# version 04
# baseline (speedup 1.0000x reference)
import numpy as np
import concourse.bass as bass
import concourse.mybir as mybir
from concourse.bass_utils import run_bass_kernel_spmd
from concourse.bass import IndirectOffsetOnAxis

F32 = mybir.dt.float32
BF16 = mybir.dt.bfloat16
U32 = mybir.dt.uint32
U16 = mybir.dt.uint16
AF = mybir.ActivationFunctionType
ALU = mybir.AluOpType
AX = mybir.AxisListType

T = 4096
D = 1024
CT = D // 128
NT = T // 128
NCH = T // 512

ENGINES = ("tensor", "vector", "scalar", "sync", "gpsimd")


class Em:

    def __init__(self, nc):
        self.nc = nc
        self.q = {e: [] for e in ENGINES}
        self.sems = {}
        self.cnt = {}
        self.nblk = 0

    def sem(self, name):
        if name not in self.sems:
            self.sems[name] = self.nc.alloc_semaphore(name)
            self.cnt[name] = 0
        return self.sems[name]

    def on(self, eng, fn, *args, _inc=None, **kwargs):
        if _inc is not None:
            name, amt = _inc
            sem = self.sem(name)
            self.cnt[name] += amt
            self.q[eng].append((fn, args, kwargs, (sem, amt)))
            return self.cnt[name]
        self.q[eng].append((fn, args, kwargs, None))
        return None

    def vop(self, fn, *args, _inc=None, **kwargs):
        self.wait("vector", "vch", self.cnt.get("vch", 0))
        if _inc is not None:
            n = self.on("vector", fn, *args, _inc=_inc, **kwargs)
            self.wait("vector", _inc[0], n)
            return n
        return self.on("vector", fn, *args, _inc=("vch", 1), **kwargs)

    def sop(self, fn, *args, _inc=None, **kwargs):
        self.wait("scalar", "sch", self.cnt.get("sch", 0))
        if _inc is not None:
            n = self.on("scalar", fn, *args, _inc=_inc, **kwargs)
            self.wait("scalar", _inc[0], n)
            return n
        return self.on("scalar", fn, *args, _inc=("sch", 1), **kwargs)

    def wait(self, eng, semname, val):
        if val is None or val <= 0:
            return
        sem = self.sem(semname)
        self.q[eng].append(("WAIT", (sem, val), {}, None))

    def wait_now(self, eng, semname):
        self.wait(eng, semname, self.cnt.get(semname, 0))

    def flush(self, name=None):
        nc = self.nc
        q = self.q
        self.q = {e: [] for e in ENGINES}
        if not any(q[e] for e in ENGINES):
            return
        self.nblk += 1
        name = f"{name}{self.nblk}" if name else f"blk{self.nblk}"

        def runner(ops):
            def body(eng):
                for fn, args, kwargs, inc in ops:
                    if fn == "WAIT":
                        eng.wait_ge(args[0], args[1])
                    else:
                        instr = fn(*args, **kwargs)
                        if inc is not None:
                            instr.then_inc(inc[0], inc[1])
            return body

        with nc.Block(name) as blk:
            for e in ENGINES:
                if q[e]:
                    getattr(blk, e)(runner(q[e]))


def ts(i, n=128):
    return slice(i * n, (i + 1) * n)



def prep_weights(inp):
    w = {}
    f32 = lambda a: np.ascontiguousarray(np.asarray(a), np.float32)
    qw = f32(inp["L0_attn_q_w"])
    kvw = f32(inp["L0_attn_kv_w"])
    head_tiles = lambda m: np.ascontiguousarray(
        m.reshape(CT, 128, 16, 64).transpose(2, 0, 1, 3))
    w["a0_wq"] = head_tiles(qw)
    w["a0_wk"] = head_tiles(kvw[:, :D])
    w["a0_wv"] = head_tiles(kvw[:, D:])
    w["a0_wo"] = np.ascontiguousarray(
        f32(inp["L0_attn_o_w"]).reshape(CT, 128, CT, 128).transpose(2, 1, 0, 3)
    ).reshape(CT, 128, CT * 128)
    w["a0_qb"] = np.ascontiguousarray(f32(inp["L0_attn_q_b"]).reshape(8, 128).T)
    w["a0_kb"] = np.ascontiguousarray(f32(inp["L0_attn_kv_b"][:D]).reshape(8, 128).T)
    w["a0_vb"] = f32(inp["L0_attn_kv_b"][D:]).reshape(1, D)
    w["a0_ob"] = np.ascontiguousarray(f32(inp["L0_attn_o_b"]).reshape(8, 128).T)

    wqkv = f32(inp["L0_ffn_wqkv"])
    hq = wqkv[:, :, 0:512].reshape(2, CT, 128, 8, 64)
    hk = wqkv[:, :, 512:1024].reshape(2, CT, 128, 8, 64)
    hv = wqkv[:, :, 1024:1536].reshape(2, CT, 128, 8, 64)
    w["f0_wqkv"] = np.ascontiguousarray(
        np.concatenate([hq, hk, hv], axis=-1).transpose(0, 3, 1, 2, 4))
    w["f0_qs"] = f32(inp["L0_ffn_q_scale"]).reshape(2, 64, 1)
    w["f0_ks"] = f32(inp["L0_ffn_k_scale"]).reshape(2, 64, 1)
    w["f0_wo"] = f32(inp["L0_ffn_wo"]).reshape(2, 4, 128, 1024)
    w["f0_lg"] = np.ascontiguousarray(
        f32(inp["L0_ffn_ln_g"]).reshape(2, 8, 128).transpose(0, 2, 1))
    w["f0_lb"] = np.ascontiguousarray(
        f32(inp["L0_ffn_ln_b"]).reshape(2, 8, 128).transpose(0, 2, 1))
    HID, FCP = 2730, 22
    w1 = f32(inp["L0_ffn_w1"])
    w1a = np.zeros((2, D, FCP * 128), np.float32)
    w1g = np.zeros((2, D, FCP * 128), np.float32)
    w1a[:, :, :HID] = w1[:, :, :HID]
    w1g[:, :, :HID] = w1[:, :, HID:]
    tile_lhs = lambda m: np.ascontiguousarray(
        m.reshape(2, CT, 128, FCP, 128).transpose(0, 3, 1, 2, 4))
    w["f0_w1a"] = tile_lhs(w1a)
    w["f0_w1g"] = tile_lhs(w1g)
    b1 = f32(inp["L0_ffn_b1"])
    b1a = np.zeros((2, FCP * 128), np.float32)
    b1g = np.zeros((2, FCP * 128), np.float32)
    b1a[:, :HID] = b1[:, :HID]
    b1g[:, :HID] = b1[:, HID:]
    w["f0_b1a"] = np.ascontiguousarray(
        b1a.reshape(2, FCP, 128).transpose(0, 2, 1))
    w["f0_b1g"] = np.ascontiguousarray(
        b1g.reshape(2, FCP, 128).transpose(0, 2, 1))
    w2 = f32(inp["L0_ffn_w2"])
    w2p = np.zeros((2, FCP * 128, D), np.float32)
    w2p[:, :HID, :] = w2
    w["f0_w2"] = np.ascontiguousarray(w2p.reshape(2, FCP, 128, D))
    w["f0_b2"] = np.ascontiguousarray(
        f32(inp["L0_ffn_b2"]).reshape(2, 8, 128).transpose(0, 2, 1))

    gat = lambda k: f32(inp["L1_attn_" + k])
    maa5 = np.stack([gat("time_maa_x"), gat("time_maa_w"), gat("time_maa_k"),
                     gat("time_maa_v"), gat("time_maa_r")])
    w["t1_maa5"] = np.ascontiguousarray(
        maa5.reshape(5, 8, 128).transpose(2, 0, 1).reshape(128, 40))
    w["t1_td"] = np.ascontiguousarray(gat("time_decay").reshape(8, 128).T)
    w["t1_mw1"] = np.ascontiguousarray(gat("time_maa_w1").reshape(8, 128, 128))
    w["t1_mw2"] = np.ascontiguousarray(
        gat("time_maa_w2").transpose(1, 0, 2).reshape(32, 4096))
    w["t1_dw1"] = np.ascontiguousarray(gat("time_decay_w1").reshape(8, 128, 64))
    w["t1_dw2"] = gat("time_decay_w2")
    w["t1_u"] = np.ascontiguousarray(gat("time_faaaa").T.reshape(64, 16))
    tile4 = lambda m: np.ascontiguousarray(
        m.reshape(8, 128, 8, 128).transpose(2, 0, 1, 3))
    w["t1_wr"] = tile4(gat("W_r"))
    w["t1_wk"] = tile4(gat("W_k"))
    w["t1_wv"] = gat("W_v")
    w["t1_wo"] = tile4(gat("W_o"))
    w["t1_lnw"] = np.ascontiguousarray(gat("ln_w").reshape(8, 128).T)
    w["t1_lnb"] = np.ascontiguousarray(gat("ln_b").reshape(8, 128).T)

    gm = lambda k: f32(inp["L1_ffn_" + k])
    w["m1_gate"] = np.ascontiguousarray(
        gm("gate_w").T.reshape(8, 128, 16).transpose(1, 0, 2).reshape(128, 128))
    exp_tiles = lambda m: np.ascontiguousarray(
        m.transpose(0, 2, 1).reshape(16, 8, 128, 16, 128)
        .transpose(0, 3, 1, 2, 4))
    w["m1_wg"] = exp_tiles(gm("Wg"))
    w["m1_wu"] = exp_tiles(gm("Wu"))
    w["m1_wd"] = np.ascontiguousarray(
        gm("Wd").transpose(0, 2, 1).reshape(16, 16, 128, 1024))
    sh_tiles = lambda m: np.ascontiguousarray(
        m.T.reshape(8, 128, 16, 128).transpose(2, 0, 1, 3))
    w["m1_wsg"] = sh_tiles(gm("Ws_g"))
    w["m1_wsu"] = sh_tiles(gm("Ws_u"))
    w["m1_wsd"] = np.ascontiguousarray(gm("Ws_d").T.reshape(16, 128, 1024))
    return w


def _rotary_tables(HD=64, WINDOW=512):
    half = np.arange(0, HD, 2, dtype=np.float32)
    inv_freq = 1.0 / 10000.0 ** (half / HD)
    t = np.arange(T, dtype=np.float32)
    freqs = t[:, None] * inv_freq[None, :]
    freqs = np.concatenate([freqs, freqs], axis=-1)
    base = (half + 0.4 * HD) / (1.4 * HD)
    power = (t - T // 2) / float(WINDOW // 2)
    scale = base[None, :] ** power[:, None]
    scale = np.concatenate([scale, scale], axis=-1)
    cos, sin = np.cos(freqs), np.sin(freqs)
    f = lambda m: np.ascontiguousarray(m.T, np.float32)
    return f(cos * scale), f(sin * scale), f(cos / scale), f(sin / scale)


def make_consts(nc):
    c = {}
    c["qpos"] = nc.inline_tensor(
        np.arange(512, dtype=np.float32).reshape(1, 512), name="qpos")
    c["kpos"] = nc.inline_tensor(
        np.ascontiguousarray(
            np.arange(NT, dtype=np.float32)[None, :] * 128
            + np.arange(128, dtype=np.float32)[:, None]), name="kpos")
    c["ident"] = nc.inline_tensor(np.eye(128, dtype=np.float32), name="identc")
    CSQ, SSQ, CSK, SSK = _rotary_tables()
    c["rotq"] = nc.inline_tensor(
        np.ascontiguousarray(np.stack([CSQ, SSQ])), name="rotq")
    c["rotk"] = nc.inline_tensor(
        np.ascontiguousarray(np.stack([CSK, SSK])), name="rotk")
    p = np.arange(128)
    c["E8"] = nc.inline_tensor(
        (p[:, None] % 16 == np.arange(16)[None, :]).astype(np.float32),
        name="m1E8")
    c["E8T"] = nc.inline_tensor(np.ascontiguousarray(
        (p[:, None] % 16 == np.arange(16)[None, :]).astype(np.float32).T),
        name="m1E8T")
    c["LB"] = nc.inline_tensor(
        ((p[:, None] % 16 == p[None, :] % 16)
         & (p[:, None] < p[None, :])).astype(np.float32), name="m1LB")
    c["iotaB"] = nc.inline_tensor(
        ((p // 16)[:, None] * 512.0
         + np.arange(512)[None, :]).astype(np.float32), name="m1iotaB")
    c["ebm1"] = nc.inline_tensor(
        ((p % 16) * 512.0 - 1.0).astype(np.float32).reshape(128, 1),
        name="m1ebm1")
    c["tids"] = nc.inline_tensor(
        (np.arange(NT)[None, :] * 128 + p[:, None]).astype(np.uint32),
        name="m1tids")
    c["Lrep"] = nc.inline_tensor(np.ascontiguousarray(np.concatenate(
        [(np.arange(16)[:, None] == 2 * (p[None, :] // 16) + o)
         .astype(np.float32) for o in range(2)], axis=1)), name="m1Lrep")
    w64 = np.arange(64)
    c["offc"] = nc.inline_tensor(np.ascontiguousarray(np.broadcast_to(
        (w64 >= 32) * 4096.0, (128, 64))).astype(np.float32), name="m1offc")
    c["jp1"] = nc.inline_tensor(
        ((w64[None, :] * 16 + (p % 16)[:, None] + 1)
         - 512.0 * (w64[None, :] >= 32)).astype(np.float32), name="m1jp1")
    c["maskc"] = nc.inline_tensor(
        ((p % 16)[:, None] == np.arange(16)[None, :]).astype(np.float32),
        name="m1maskc")
    c["cvals"] = nc.inline_tensor(
        np.array([[1e-20, 1e-5, 0.00064, 0.0, 1e-24, 0.5, 1.0, 0.0]],
                 dtype=np.float32), name="cvalsc")
    return c



class Ctx:
    pass


def setup_common(nc, em, consts):
    g = Ctx()
    g.nc, g.em = nc, em
    g.xT = nc.alloc_sbuf_tensor("xT", [128, CT * T], BF16)
    g.bigA = nc.alloc_sbuf_tensor("bigA", [128, CT * T], BF16)
    g.identf = nc.alloc_sbuf_tensor("identf", [128, 128], F32)
    g.ident = nc.alloc_sbuf_tensor("ident", [128, 128], BF16)
    g.onescol = nc.alloc_sbuf_tensor("onescol", [128, 1], BF16)
    g.onesrow = nc.alloc_sbuf_tensor("onesrow", [1, 128], BF16)
    g.ones2 = nc.alloc_sbuf_tensor("ones2", [2, 128], BF16)
    g.onesrowf = nc.alloc_sbuf_tensor("onesrowf", [1, 128], F32)
    g.kpos = nc.alloc_sbuf_tensor("kpos_sb", [128, NT], F32)
    g.p_a = [nc.alloc_psum_tensor(f"p_a{i}", [128, 512], F32) for i in range(2)]
    g.p_y = nc.alloc_psum_tensor("p_y", [128, 512], F32)
    g.p_bc = nc.alloc_psum_tensor("p_bc", [128, 512], F32)
    g.p_row = nc.alloc_psum_tensor("p_row", [1, 512], F32)
    g.p_row2 = nc.alloc_psum_tensor("p_row2", [1, 512], F32)
    g.p_v = nc.alloc_psum_tensor("p_v", [128, 128], F32)
    g.p_tp = nc.alloc_psum_tensor("p_tp", [128, 128], BF16)
    g.tp_slots = [g.p_tp[:, 0:128], g.p_v[:, :].bitcast(BF16)[:, 0:128]]
    g.cvals = nc.alloc_sbuf_tensor("cvals", [1, 8], F32)
    g.scr = nc.alloc_sbuf_tensor("g_scr", [128, 512], F32)[:, :]
    g.scr2 = nc.alloc_sbuf_tensor("g_scr2", [128, 512], F32)[:, :]
    g.sqbf = nc.alloc_sbuf_tensor("g_sqbf", [128, 512], BF16)[:, :]
    g.rbc = nc.alloc_sbuf_tensor("g_rbc", [128, 512], F32)[:, :]
    g.mrbc = nc.alloc_sbuf_tensor("g_mrbc", [128, 512], F32)[:, :]
    g.rbc64 = nc.alloc_sbuf_tensor("g_rbc64", [64, 512], F32)[:, :]
    g.recbc = nc.alloc_sbuf_tensor("g_recbc", [64, 512], F32)[:, :]
    g.rowa = nc.alloc_sbuf_tensor("g_rowa", [1, 512], F32)[:, :]
    g.rowb = nc.alloc_sbuf_tensor("g_rowb", [1, 512], F32)[:, :]
    g.lm = nc.alloc_sbuf_tensor("g_lm", [1, 512], F32)[:, :]
    g.lr = nc.alloc_sbuf_tensor("g_lr", [1, 512], F32)[:, :]
    g.lmr = nc.alloc_sbuf_tensor("g_lmr", [1, 512], F32)[:, :]
    g.scal = nc.alloc_sbuf_tensor("g_scal", [1, 64], F32)
    g.cvcol = nc.alloc_sbuf_tensor("g_cvcol", [128, 4], F32)

    em.on("sync", nc.sync.dma_start, g.identf[:], consts["ident"][:, :],
          _inc=("a0bl", 16))
    em.on("sync", nc.sync.dma_start, g.kpos[:], consts["kpos"][:, :],
          _inc=("a0bl", 16))
    em.on("sync", nc.sync.dma_start, g.cvals[:], consts["cvals"][:, :],
          _inc=("a0bl", 16))
    em.wait_now("vector", "a0bl")
    em.on("vector", nc.vector.tensor_copy, g.ident[:], g.identf[:])
    em.on("vector", nc.vector.memset, g.onescol[:], 1.0)
    em.on("vector", nc.vector.memset, g.onesrow[:], 1.0)
    em.on("vector", nc.vector.memset, g.ones2[:], 1.0)
    em.on("vector", nc.vector.memset, g.onesrowf[:], 1.0)
    em.on("vector", nc.vector.memset, g.cvcol[:, 0:1], 1e-20)
    em.on("vector", nc.vector.memset, g.cvcol[:, 1:2], 1e-5)
    em.on("vector", nc.vector.memset, g.cvcol[:, 2:3], 0.00064)
    em.on("vector", nc.vector.memset, g.cvcol[:, 3:4], 1e-24, _inc=("setup_v", 1))
    em.flush("setup")
    return g


def stage_xT(g, src_h, dst=None, cast=True):
    nc, em = g.nc, g.em
    dst = dst if dst is not None else g.xT
    from contextlib import ExitStack
    ses = ExitStack()
    fin = ses.enter_context(nc.sbuf_tensor("stage_tok", [128, 2 * D], F32))
    st = fin[:, :].bitcast(BF16)
    tpn = 0
    for tt in range(NT):
        b = tt % 2
        em.wait("gpsimd", "xtp", (tt - 1) * CT)
        ld = em.on("gpsimd", nc.gpsimd.dma_start, st[:, ts(b, D)],
                   src_h[ts(tt), :], _inc=(f"xld{b}", 16))
        em.wait("tensor", f"xld{b}", ld)
        for ct in range(CT):
            pslot = g.tp_slots[ct % 2]
            em.wait("tensor", "xev", tpn - 1)
            tpn = em.on("tensor", nc.tensor.transpose, pslot,
                        st[:, b * D + ct * 128: b * D + (ct + 1) * 128],
                        g.ident[:], _inc=("xtp", 1))
            em.wait("vector", "xtp", tpn)
            em.on("vector", nc.vector.tensor_copy,
                  dst[:, ct * T + tt * 128: ct * T + (tt + 1) * 128],
                  pslot, _inc=("xev", 1))
    em.flush("stage_xT")
    ses.close()


def finalize_out(g, y_h, deps=()):
    nc, em = g.nc, g.em
    from contextlib import ExitStack
    fes_ = ExitStack()
    st = fes_.enter_context(nc.sbuf_tensor("final_tok", [128, 2 * D], F32))
    for semname in deps:
        em.wait("tensor", semname, em.cnt.get(semname, 0))
    tpn = 0
    fst_tgt = {0: 0, 1: 0}
    for tt in range(NT):
        b = tt % 2
        em.wait("vector", f"fst{b}", fst_tgt[b])
        for ct in range(CT):
            pslot = g.tp_slots[ct % 2]
            em.wait("tensor", "fev", tpn - 1)
            tpn = em.on("tensor", nc.tensor.transpose, pslot,
                        g.xT[:, ct * T + tt * 128: ct * T + (tt + 1) * 128],
                        g.ident[:], _inc=("ftp", 1))
            em.wait("vector", "ftp", tpn)
            em.on("vector", nc.vector.tensor_copy,
                  st[:, b * D + ct * 128: b * D + (ct + 1) * 128], pslot,
                  _inc=("fev", 1))
        em.wait_now("sync", "fev")
        fst_tgt[b] = em.on("sync", nc.sync.dma_start, y_h[ts(tt), :],
                           st[:, ts(b, D)], _inc=(f"fst{b}", 16))
    em.wait_now("sync", "fst0")
    em.wait_now("sync", "fst1")
    em.flush("finalize")
    fes_.close()



def build_l0_attn(g, W, CONSTS_QPOS):
    nc, em = g.nc, g.em
    NH = 16
    from contextlib import ExitStack
    es = ExitStack()
    sb = lambda name, shape, dt: es.enter_context(nc.sbuf_tensor(name, shape, dt))

    wqkv = [sb(f"a0_w{i}", [128, CT * 192], BF16) for i in range(2)]
    qb_sb = sb("a0_qbs", [128, 8], F32)
    kb_sb = sb("a0_kbs", [128, 8], F32)
    ob_sb = sb("a0_obs", [128, 8], F32)
    vb_bf = sb("a0_vbb", [1, D], BF16)
    QTa = sb("QTa", [65, T], BF16)
    KTa = sb("KTa", [65, T], BF16)
    Vbuf = sb("Vbuf", [128, NT * 65], BF16)
    YT = g.bigA[:, :]
    PT = sb("PT", [128, 2 * 512], BF16)
    dmask = sb("dmask", [128, 4 * 512], BF16)
    r_qn2 = sb("a0_qn2", [1, T], BF16)[:, :]
    r_b8 = r_qn2
    qpos_t = sb("a0_qpos", [1, 512], F32)
    g.qpos = qpos_t
    bias_k = sb("a0_biask", [128, NT], F32)
    wo_s = sb("a0_wos", [128, 2 * D], BF16)[:, :]
    scal = g.scal
    rec_sb = None
    recbc = g.recbc
    scr = g.scr
    sqbf = g.sqbf[:, :]
    qsq = g.sqbf[0:64, :]
    l_mean, l_r, l_mr = g.lm, g.lr, g.lmr
    rbc, mrbc = g.rbc, g.mrbc
    em.on("sync", nc.sync.dma_start, qpos_t[:], CONSTS_QPOS[:, :],
          _inc=("a0bl", 16))

    for j in range(4):
        m = dmask[:, ts(j, 512)]
        n = em.on("gpsimd", nc.gpsimd.memset, m, 0.0, _inc=("a0mm", 1))
        em.wait("gpsimd", "a0mm", n)
        em.on("gpsimd", nc.gpsimd.affine_select, m, m,
              [[1, 512]], compare_op=ALU.is_ge, fill=-1e30,
              base=-128 * j, channel_multiplier=-1, _inc=("a0m", 1))
    for tt in range(NT):
        em.on("gpsimd", nc.gpsimd.memset,
              Vbuf[:, tt * 65 + 64: tt * 65 + 65], 1.0, _inc=("a0m", 1))
    em.on("vector", nc.vector.memset, KTa[64:65, :], 1.0, _inc=("a0m", 1))
    em.on("sync", nc.sync.dma_start, qb_sb[:], W["qb"][:, :], _inc=("a0bl", 16))
    em.on("sync", nc.sync.dma_start, kb_sb[:], W["kb"][:, :], _inc=("a0bl", 16))
    em.on("sync", nc.sync.dma_start, ob_sb[:], W["ob"][:, :], _inc=("a0bl", 16))
    em.on("gpsimd", nc.gpsimd.dma_start, vb_bf[:], W["vb"][:, :],
          _inc=("a0wo0", 16))
    def load_head(h, slot):
        for ct in range(CT):
            em.on("gpsimd", nc.gpsimd.dma_start,
                  wqkv[slot][:, ct * 192 + 0: ct * 192 + 64],
                  W["wq"][h, ct, :, :], _inc=(f"a0wl{slot}", 16))
            em.on("gpsimd", nc.gpsimd.dma_start,
                  wqkv[slot][:, ct * 192 + 64: ct * 192 + 128],
                  W["wk"][h, ct, :, :], _inc=(f"a0wl{slot}", 16))
            em.on("gpsimd", nc.gpsimd.dma_start,
                  wqkv[slot][:, ct * 192 + 128: ct * 192 + 192],
                  W["wv"][h, ct, :, :], _inc=(f"a0wl{slot}", 16))
        return em.cnt[f"a0wl{slot}"]

    wl_tgt = load_head(0, 0)
    em.flush("a0_setup")

    for h in range(NH):
        slot = h % 2
        hb = 64 * (h % 2)
        slope = (h + 1) / NH
        my_tgt = wl_tgt
        if h + 1 < NH:
            em.wait("gpsimd", "a0vm", em.cnt.get("a0vm", 0))
            wl_tgt = load_head(h + 1, 1 - slot)
        em.wait("tensor", f"a0wl{slot}", my_tgt)

        if h == 0:
            em.wait_now("vector", "a0bl")
        em.wait("vector", "a0s", em.cnt.get("a0s", 0))
        em.wait("vector", "a0pv", em.cnt.get("a0pv", 0))
        em.wait("vector", "a0ex", em.cnt.get("a0ex", 0))
        for hc in range(NCH):
            em.wait("tensor", "a0qe", em.cnt.get("a0qe", 0))
            for ct in range(CT):
                em.on("tensor", nc.tensor.matmul, g.p_a[0][0:64, :],
                      wqkv[slot][:, ct * 192: ct * 192 + 64],
                      g.xT[:, ct * T + hc * 512: ct * T + (hc + 1) * 512],
                      start=(ct == 0), stop=(ct == CT - 1),
                      _inc=("a0q", 1) if ct == CT - 1 else None)
            for ct in range(CT):
                em.on("tensor", nc.tensor.matmul, g.p_a[1][0:64, :],
                      wqkv[slot][:, ct * 192 + 64: ct * 192 + 128],
                      g.xT[:, ct * T + hc * 512: ct * T + (hc + 1) * 512],
                      start=(ct == 0), stop=(ct == CT - 1),
                      _inc=("a0q", 1) if ct == CT - 1 else None)
            em.wait_now("vector", "a0q")
            em.on("vector", nc.vector.tensor_scalar,
                  QTa[0:64, hc * 512:(hc + 1) * 512], g.p_a[0][0:64, :],
                  qb_sb[hb:hb + 64, h // 2:h // 2 + 1], None, op0=ALU.add,
                  _inc=("a0qe", 1))
            em.on("vector", nc.vector.tensor_scalar,
                  KTa[0:64, hc * 512:(hc + 1) * 512], g.p_a[1][0:64, :],
                  kb_sb[hb:hb + 64, h // 2:h // 2 + 1], None, op0=ALU.add,
                  _inc=("a0qe", 1))
        vhead = vb_bf[:, h * 64:(h + 1) * 64]
        for tt in range(NT):
            em.wait("tensor", "a0ve", em.cnt.get("a0ve", 0))
            for ct in range(CT):
                em.on("tensor", nc.tensor.matmul, g.p_v[:, 0:64],
                      g.xT[:, ct * T + tt * 128: ct * T + (tt + 1) * 128],
                      wqkv[slot][:, ct * 192 + 128: ct * 192 + 192],
                      start=(ct == 0), stop=False)
            if h == 0 and tt == 0:
                em.wait("tensor", "a0wo0", 16)
            em.on("tensor", nc.tensor.matmul, g.p_v[:, 0:64],
                  g.onesrow[:, :], vhead, start=False, stop=True,
                  _inc=("a0vm", 1))
            em.wait_now("vector", "a0vm")
            em.on("vector", nc.vector.tensor_copy,
                  Vbuf[:, tt * 65: tt * 65 + 64], g.p_v[:, 0:64],
                  _inc=("a0ve", 1))

        em.wait("vector", "a0qe", em.cnt["a0qe"])
        for hc in range(NCH):
            em.wait("vector", "a0nr", em.cnt.get("a0nr", 0))
            em.on("vector", nc.vector.tensor_mul, qsq[:, :],
                  QTa[0:64, hc * 512:(hc + 1) * 512],
                  QTa[0:64, hc * 512:(hc + 1) * 512], _inc=("a0sq", 1))
            em.wait("tensor", "a0sq", em.cnt["a0sq"])
            em.wait("tensor", "a0nre", em.cnt.get("a0nre", 0))
            em.on("tensor", nc.tensor.matmul, g.p_row[:, :],
                  g.onescol[0:64, :], qsq[:, :],
                  start=True, stop=True, _inc=("a0nr", 1))
            em.wait_now("vector", "a0nr")
            em.on("vector", nc.vector.tensor_copy,
                  r_qn2[:, hc * 512:(hc + 1) * 512], g.p_row[:, :],
                  _inc=("a0nre", 1))
        for hc in range(NCH):
            em.wait("vector", "a0nr", em.cnt["a0nr"])
            em.on("vector", nc.vector.tensor_mul, qsq[:, :],
                  KTa[0:64, hc * 512:(hc + 1) * 512],
                  KTa[0:64, hc * 512:(hc + 1) * 512], _inc=("a0sq", 1))
            em.wait("tensor", "a0sq", em.cnt["a0sq"])
            em.wait("tensor", "a0nre", em.cnt.get("a0nre", 0))
            em.on("tensor", nc.tensor.matmul, g.p_row[:, :],
                  g.onescol[0:64, :], qsq[:, :],
                  start=True, stop=True, _inc=("a0nr", 1))
            em.wait_now("vector", "a0nr")
            em.on("vector", nc.vector.reduce_max, scal[:, hc:hc + 1],
                  g.p_row[:, :], axis=AX.X, _inc=("a0nre", 1))
        em.wait("vector", "a0nre", em.cnt["a0nre"])
        em.on("vector", nc.vector.reduce_max, scal[:, 8:9], scal[:, 0:8],
              axis=AX.X, _inc=("a0s1", 1))
        em.wait_now("scalar", "a0s1")
        em.on("scalar", nc.scalar.activation, scal[:, 9:10], scal[:, 8:9],
              AF.Ln, bias=g.cvals[:, 0:1], scale=1.0)
        em.on("scalar", nc.scalar.activation, r_b8, r_qn2,
              AF.Ln, bias=g.cvals[:, 0:1], scale=1.0, _inc=("a0s2", 1))
        em.wait_now("vector", "a0s2")
        em.on("vector", nc.vector.tensor_scalar_mul, scal[:, 10:11],
              scal[:, 9:10], 0.5, _inc=("a0s3", 1))
        em.wait_now("scalar", "a0s3")
        em.on("scalar", nc.scalar.activation, r_b8, r_b8,
              AF.Exp, bias=scal[:, 10:11], scale=0.5, _inc=("a0s4", 1))
        em.wait_now("vector", "a0s4")
        for hc in range(NCH):
            em.wait("vector", "a0ns", em.cnt.get("a0ns", 0))
            n = em.on("vector", nc.vector.tensor_scalar,
                      g.rowa, r_b8[:, hc * 512:(hc + 1) * 512],
                      1.0, 8.0 * slope * (hc * 512),
                      op0=ALU.mult, op1=ALU.add, _inc=("a0nsp", 1))
            em.wait("vector", "a0nsp", n)
            em.on("vector", nc.vector.scalar_tensor_tensor,
                  QTa[64:65, hc * 512:(hc + 1) * 512],
                  g.qpos[:, :], -8.0 * slope, g.rowa,
                  op0=ALU.mult, op1=ALU.subtract, _inc=("a0ns", 1))
        em.on("vector", nc.vector.tensor_scalar_mul, bias_k[:, :],
              g.kpos[:, :], slope, _inc=("a0bk", 1))
        em.wait_now("tensor", "a0ns")
        em.wait("tensor", "a0ve", em.cnt["a0ve"])
        em.wait_now("scalar", "a0bk")

        for qc in range(NCH):
            ktm = 4 * qc + 4
            for kt in range(ktm):
                sb = kt % 2
                p_s = g.p_a[sb]
                em.wait("tensor", "a0ex", em.cnt.get("a0ex", 0) - 1)
                smm = em.on("tensor", nc.tensor.matmul, p_s[:, :],
                            KTa[:, ts(kt)], QTa[:, qc * 512:(qc + 1) * 512],
                            start=True, stop=True, _inc=("a0s", 1))
                if kt >= 4 * qc:
                    em.wait("vector", "a0s", smm)
                    mkn = em.on("vector", nc.vector.tensor_add,
                                p_s[:, :], p_s[:, :],
                                dmask[:, ts(kt - 4 * qc, 512)],
                                _inc=("a0mk", 1))
                    em.wait("scalar", "a0mk", mkn)
                else:
                    em.wait("scalar", "a0s", smm)
                exn = em.on("scalar", nc.scalar.activation,
                            PT[:, ts(sb, 512)], p_s[:, :], AF.Exp,
                            bias=bias_k[:, kt:kt + 1], scale=0.125,
                            _inc=("a0ex", 1))
                em.wait("tensor", "a0ex", exn)
                em.on("tensor", nc.tensor.matmul, g.p_y[0:65, :],
                      Vbuf[:, kt * 65: kt * 65 + 65], PT[:, ts(sb, 512)],
                      start=(kt == 0), stop=(kt == ktm - 1),
                      _inc=("a0pv", 1) if kt == ktm - 1 else None)
            em.wait_now("vector", "a0pv")
            rcn = em.on("vector", nc.vector.reciprocal, g.rowa,
                        g.p_y[64:65, :], _inc=("a0rc", 1))
            em.wait("tensor", "a0rc", rcn)
            bcn = em.on("tensor", nc.tensor.matmul, g.p_bc[0:64, :],
                        g.onesrowf[:, 0:64], g.rowa,
                        start=True, stop=True, _inc=("a0bc", 1))
            em.wait("vector", "a0bc", bcn)
            nbce = em.on("vector", nc.vector.tensor_copy, recbc[:, :],
                         g.p_bc[0:64, :], _inc=("a0bce", 1))
            em.wait("vector", "a0bce", nbce)
            yev = em.on("vector", nc.vector.tensor_mul,
                        YT[64 * (h % 2): 64 * (h % 2) + 64,
                           (h // 2) * T + qc * 512: (h // 2) * T + (qc + 1) * 512],
                        g.p_y[0:64, :], recbc[:, :], _inc=("a0ye", 1))
            em.wait("tensor", "a0ye", yev)
            em.wait("tensor", "a0bce", em.cnt["a0bce"])

    em.flush("a0_heads")

    won = 0
    wou = 0

    wos_tgt = {0: 0, 1: 0}

    def load_wo(ft, slot):
        em.wait("gpsimd", "a0om", wou - 1)
        wos_tgt[slot] = em.on("gpsimd", nc.gpsimd.dma_start,
                              wo_s[:, ts(slot, D)],
                              W["wo"][ft, :, :],
                              _inc=(f"a0wos{slot}", 16))

    load_wo(0, 0)
    for tc in range(NCH):
        for pss in range(2):
            for ft in range(CT):
                slot = (tc * 2 * CT + pss * CT + ft) % 2
                nft = ft + 1
                npss, ntc = pss, tc
                if nft == CT:
                    nft = 0
                    npss += 1
                    if npss == 2:
                        npss = 0
                        ntc += 1
                po = g.p_a[ft % 2]
                em.wait("tensor", f"a0wos{slot}", wos_tgt[slot])
                em.wait("tensor", "a0pre", em.cnt.get("a0pre", 0) - 1)
                for ct in range(CT):
                    em.on("tensor", nc.tensor.matmul, po[:, :],
                          wo_s[:, slot * D + ct * 128: slot * D + (ct + 1) * 128],
                          YT[:, ct * T + tc * 512: ct * T + (tc + 1) * 512],
                          start=(ct == 0), stop=(ct == CT - 1),
                          _inc=("a0om", 1) if ct == CT - 1 else None)
                wou = em.cnt["a0om"]
                if ntc < NCH:
                    load_wo(nft, 1 - slot)
                em.wait_now("vector", "a0om")
                if pss == 0:
                    em.wait("vector", "a0st1", em.cnt.get("a0st1", 0))
                    prn = em.on("vector", nc.vector.scalar_tensor_tensor,
                                sqbf[:, :], po[:, :], ob_sb[:, ft:ft + 1],
                                g.xT[:, ft * T + tc * 512: ft * T + (tc + 1) * 512],
                                op0=ALU.add, op1=ALU.add, _inc=("a0pre", 1))
                    em.wait("tensor", "a0pre", prn)
                    em.on("tensor", nc.tensor.matmul, g.p_row[:, :],
                          g.onescol[:, :], sqbf[:, :],
                          start=(ft == 0), stop=(ft == CT - 1),
                          _inc=("a0st1", 1))
                    em.wait("vector", "a0st2", em.cnt.get("a0st2", 0))
                    sqn = em.on("vector", nc.vector.tensor_mul, PT[:, 0:512],
                                sqbf[:, :], sqbf[:, :], _inc=("a0sqt", 1))
                    em.wait("tensor", "a0sqt", sqn)
                    em.on("tensor", nc.tensor.matmul, g.p_row2[:, :],
                          g.onescol[:, :], PT[:, 0:512],
                          start=(ft == 0), stop=(ft == CT - 1),
                          _inc=("a0st2", 1))
                else:
                    em.vop(nc.vector.scalar_tensor_tensor,
                           scr[:, :], po[:, :], ob_sb[:, ft:ft + 1],
                           g.xT[:, ft * T + tc * 512: ft * T + (tc + 1) * 512],
                           op0=ALU.add, op1=ALU.add)
                    em.vop(nc.vector.tensor_mul, scr[:, :],
                           scr[:, :], rbc[:, :])
                    em.vop(nc.vector.tensor_sub,
                           g.xT[:, ft * T + tc * 512: ft * T + (tc + 1) * 512],
                           scr[:, :], mrbc[:, :], _inc=("a0pre", 1))
            if pss == 0:
                em.wait_now("vector", "a0st1")
                em.wait_now("vector", "a0st2")
                em.vop(nc.vector.tensor_scalar_mul, l_mean, g.p_row[:, :],
                       1.0 / D)
                em.vop(nc.vector.tensor_scalar_mul, l_r, g.p_row2[:, :],
                       1.0 / D, _inc=("a0lv", 1))
                em.wait("tensor", "a0lv", em.cnt["a0lv"])
                em.vop(nc.vector.tensor_mul, l_mr, l_mean, l_mean)
                em.vop(nc.vector.tensor_sub, l_r, l_r, l_mr, _inc=("a0lv2", 1))
                em.wait_now("scalar", "a0lv2")
                em.on("scalar", nc.scalar.activation, l_r, l_r, AF.Ln,
                      bias=g.cvals[:, 1:2], scale=1.0, _inc=("a0lnp", 1))
                em.wait("scalar", "a0lnp", em.cnt["a0lnp"])
                em.on("scalar", nc.scalar.activation, l_r, l_r, AF.Exp,
                      bias=0.0, scale=-0.5, _inc=("a0lnr", 1))
                em.wait_now("vector", "a0lnr")
                mrn = em.vop(nc.vector.tensor_mul, l_mr, l_mean, l_r,
                             _inc=("a0hl", 1))
                em.wait("tensor", "a0hl", mrn)
                em.wait("tensor", "a0bce", em.cnt.get("a0bce", 0))
                n1 = em.on("tensor", nc.tensor.matmul, g.p_bc[:, :],
                           g.onesrowf[:, :], l_r,
                           start=True, stop=True, _inc=("a0bcc", 1))
                em.wait("vector", "a0bcc", n1)
                em.on("vector", nc.vector.tensor_copy, rbc[:, :], g.p_bc[:, :],
                      _inc=("a0bce", 1))
                em.wait("tensor", "a0bce", em.cnt["a0bce"])
                n2 = em.on("tensor", nc.tensor.matmul, g.p_bc[:, :],
                           g.onesrowf[:, :], l_mr,
                           start=True, stop=True, _inc=("a0bcc", 1))
                em.wait("vector", "a0bcc", n2)
                em.on("vector", nc.vector.tensor_copy, mrbc[:, :], g.p_bc[:, :],
                      _inc=("a0bce", 1))
    em.flush("a0_proj")
    es.close()





def build_l0_ffn(g, W, z1_hbm, consts):
    nc, em = g.nc, g.em
    from contextlib import ExitStack
    FCP = 22
    es = ExitStack()
    sb = lambda n, sh, dt: es.enter_context(nc.sbuf_tensor(n, sh, dt))

    xp = g.bigA[:, :]
    h2h = nc.dram_tensor("h2_hbm", [128, CT * T], BF16, kind="Internal")

    masks = sb("f0_masks", [128, 8 * 512], BF16)
    scal = g.scal
    rrow = g.rowa
    rrow2 = g.rowb
    rbc64 = g.rbc64
    recbc = g.recbc
    l_mean, l_r, l_mr = g.lm, g.lr, g.lmr
    rbc, mrbc = g.rbc, g.mrbc
    scr = g.scr
    scr2 = g.scr2
    sqbf = g.sqbf
    qs8 = sb("f0_qs8", [64, 2], F32)
    lgb = sb("f0_lgb", [128, 16], F32)
    b1c = sb("f0_b1c", [128, 44], F32)
    b2c = sb("f0_b2c", [128, 8], F32)

    em.on("vector", nc.vector.tensor_copy, xp, g.xT[:, :], _inc=("setup_v", 1))
    em.wait_now("sync", "setup_v")
    em.on("sync", nc.sync.dma_start, z1_hbm[:, :], g.xT[:, :], _inc=("f0z1", 16))

    for ji, j in enumerate(range(-4, 4)):
        m = masks[:, ts(ji, 512)]
        n = em.on("gpsimd", nc.gpsimd.memset, m, 0.0, _inc=("f0mg", 1))
        em.wait("gpsimd", "f0mg", n)
        n = em.on("gpsimd", nc.gpsimd.affine_select, m, m,
                  [[1, 512]], compare_op=ALU.is_ge, fill=-1e30,
                  base=-128 * j, channel_multiplier=-1, _inc=("f0mg", 1))
        em.wait("gpsimd", "f0mg", n)
        em.on("gpsimd", nc.gpsimd.affine_select, m, m,
              [[-1, 512]], compare_op=ALU.is_ge, fill=-1e30,
              base=512 + 128 * j, channel_multiplier=1, _inc=("f0mg", 1))
    em.flush("f0_setup")

    def mask_j(j):
        return masks[:, ts(j + 4, 512)]

    for l in range(2):
        tag = "a0"

        em.on("sync", nc.sync.dma_start, qs8[:, 0:1], W["qs"][l, :, :],
              _inc=(tag + "bl", 16))
        em.on("sync", nc.sync.dma_start, qs8[:, 1:2], W["ks"][l, :, :],
              _inc=(tag + "bl", 16))
        em.on("sync", nc.sync.dma_start, lgb[:, 0:8], W["lg"][l, :, :],
              _inc=(tag + "bl", 16))
        em.on("sync", nc.sync.dma_start, lgb[:, 8:16], W["lb"][l, :, :],
              _inc=(tag + "bl", 16))
        em.on("sync", nc.sync.dma_start, b1c[:, 0:22], W["b1a"][l, :, :],
              _inc=(tag + "bl", 16))
        em.on("sync", nc.sync.dma_start, b1c[:, 22:44], W["b1g"][l, :, :],
              _inc=(tag + "bl", 16))
        em.on("sync", nc.sync.dma_start, b2c[:, :], W["b2"][l, :, :],
              _inc=(tag + "bl", 16))
        em.wait_now("vector", tag + "bl")
        n = em.on("vector", nc.vector.tensor_scalar_mul, qs8[:, 0:1],
                  qs8[:, 0:1], 8.0, _inc=(tag + "qs8", 1))
        em.flush(tag + "_consts")

        with ExitStack() as aes:
            asb = lambda n_, sh, dt: aes.enter_context(
                nc.sbuf_tensor(f"L{l}{n_}", sh, dt))
            YT2 = g.xT[:, 0: 4 * T]
            QTa = asb("_QTa", [65, T], BF16)
            KTa = asb("_KTa", [65, T], BF16)
            Vbuf = asb("_Vbuf", [128, NT * 65], BF16)
            PT = asb("_PT", [128, 2 * 512], BF16)
            wqh = [asb(f"_wqh{i}", [128, CT * 192], BF16) for i in range(2)]
            rot = asb("_rot", [64, 8 * 512], BF16)
            woh = [asb(f"_woh{i}", [128, 4 * 128], BF16) for i in range(2)]
            maxc = asb("_maxc", [128, 32], F32)
            maxt = asb("_maxt", [32, 128], F32)

            for tt in range(NT):
                em.on("gpsimd", nc.gpsimd.memset,
                      Vbuf[:, tt * 65 + 64: tt * 65 + 65], 1.0,
                      _inc=(tag + "vo", 1))
            em.on("vector", nc.vector.memset, KTa[64:65, :], 1.0,
                  _inc=(tag + "vo", 1))

            def load_head(h, slot):
                for ct in range(CT):
                    em.on("gpsimd", nc.gpsimd.dma_start,
                          wqh[slot][:, ct * 192: (ct + 1) * 192],
                          W["wqkv"][l, h, ct, :, :],
                          _inc=(tag + f"wl{slot}", 16))
                return em.cnt[tag + f"wl{slot}"]

            em.wait("vector", "f0z1", 16)
            wl_tgt = load_head(0, 0)
            em.flush(tag + "_asetup")

            def rhs_x(ct, lo, width):
                off = ct * T + lo - (1 if ct >= 4 else 0)
                return xp[:, off: off + width]

            rot_t = {0: 0, 1: 0}
            for h in range(8):
                slot = h % 2
                my_tgt = wl_tgt
                if h + 1 < 8:
                    em.wait("gpsimd", tag + "vm", em.cnt.get(tag + "vm", 0))
                    wl_tgt = load_head(h + 1, 1 - slot)
                em.wait("tensor", tag + f"wl{slot}", my_tgt)

                em.wait("vector", tag + "s", em.cnt.get(tag + "s", 0))
                em.wait("vector", tag + "p1", em.cnt.get(tag + "p1", 0))
                em.wait("vector", tag + "pv", em.cnt.get(tag + "pv", 0))
                em.wait("vector", tag + "mxt", em.cnt.get(tag + "mxt", 0))
                for hc in range(NCH):
                    em.wait("tensor", tag + "qe", em.cnt.get(tag + "qe", 0))
                    for ct in range(CT):
                        em.on("tensor", nc.tensor.matmul, g.p_a[0][0:64, :],
                              wqh[slot][:, ct * 192: ct * 192 + 64],
                              rhs_x(ct, hc * 512, 512),
                              start=(ct == 0), stop=(ct == CT - 1),
                              _inc=(tag + "qm", 1) if ct == CT - 1 else None)
                    for ct in range(CT):
                        em.on("tensor", nc.tensor.matmul, g.p_a[1][0:64, :],
                              wqh[slot][:, ct * 192 + 64: ct * 192 + 128],
                              rhs_x(ct, hc * 512, 512),
                              start=(ct == 0), stop=(ct == CT - 1),
                              _inc=(tag + "qm", 1) if ct == CT - 1 else None)
                    em.wait_now("vector", tag + "qm")
                    em.on("vector", nc.vector.tensor_copy,
                          QTa[0:64, hc * 512:(hc + 1) * 512], g.p_a[0][0:64, :],
                          _inc=(tag + "qe", 1))
                    em.on("vector", nc.vector.tensor_copy,
                          KTa[0:64, hc * 512:(hc + 1) * 512], g.p_a[1][0:64, :],
                          _inc=(tag + "qe", 1))
                em.wait("tensor", tag + "qe", em.cnt[tag + "qe"])
                em.wait("tensor", tag + "ve", em.cnt.get(tag + "ve", 0))
                for which, dst in ((0, QTa), (64, KTa)):
                    for ct in range(4):
                        em.on("tensor", nc.tensor.matmul, g.p_v[0:64, 0:1],
                              wqh[slot][:, ct * 192 + which: ct * 192 + which + 64],
                              xp[:, ct * T: ct * T + 1],
                              start=(ct == 0), stop=(ct == 3),
                              _inc=(tag + "fx", 1))
                    em.wait_now("vector", tag + "fx")
                    em.on("vector", nc.vector.tensor_copy, dst[0:64, 0:1],
                          g.p_v[0:64, 0:1], _inc=(tag + "fxe", 1))
                    em.wait("tensor", tag + "fxe", em.cnt[tag + "fxe"])
                for tt in range(NT):
                    em.wait("tensor", tag + "ve", em.cnt.get(tag + "ve", 0))
                    for ct in range(CT):
                        em.on("tensor", nc.tensor.matmul, g.p_v[:, 0:64],
                              rhs_x(ct, tt * 128, 128),
                              wqh[slot][:, ct * 192 + 128: ct * 192 + 192],
                              start=(ct == 0), stop=(ct == CT - 1),
                              _inc=(tag + "vm", 1) if ct == CT - 1 else None)
                    em.wait_now("vector", tag + "vm")
                    em.on("vector", nc.vector.tensor_copy,
                          Vbuf[:, tt * 65: tt * 65 + 64], g.p_v[:, 0:64],
                          _inc=(tag + "ve", 1))
                em.wait("tensor", tag + "ve", em.cnt[tag + "ve"])
                for ct in range(4):
                    em.on("tensor", nc.tensor.matmul, g.p_v[0:1, 0:64],
                          xp[:, ct * T: ct * T + 1],
                          wqh[slot][:, ct * 192 + 128: ct * 192 + 192],
                          start=(ct == 0), stop=(ct == 3),
                          _inc=(tag + "fx", 1))
                em.wait_now("vector", tag + "fx")
                em.on("vector", nc.vector.tensor_copy, Vbuf[0:1, 0:64],
                      g.p_v[0:1, 0:64], _inc=(tag + "fxe", 1))
                em.wait("tensor", tag + "fxe", em.cnt[tag + "fxe"])

                for hc in range(NCH):
                    rs = hc % 2
                    em.wait("gpsimd", tag + "rote",
                            em.cnt.get(tag + "rote", 0) - 2)
                    rot_t[rs] = em.on(
                        "gpsimd", nc.gpsimd.dma_start,
                        rot[:, rs * 2048 + 0: rs * 2048 + 512],
                        consts["rotq"][0, :, hc * 512:(hc + 1) * 512],
                        _inc=(tag + f"rt{rs}", 16))
                    em.on("gpsimd", nc.gpsimd.dma_start,
                          rot[:, rs * 2048 + 512: rs * 2048 + 1024],
                          consts["rotq"][1, :, hc * 512:(hc + 1) * 512],
                          _inc=(tag + f"rt{rs}", 16))
                    em.on("gpsimd", nc.gpsimd.dma_start,
                          rot[:, rs * 2048 + 1024: rs * 2048 + 1536],
                          consts["rotk"][0, :, hc * 512:(hc + 1) * 512],
                          _inc=(tag + f"rt{rs}", 16))
                    rot_t[rs] = em.on(
                        "gpsimd", nc.gpsimd.dma_start,
                        rot[:, rs * 2048 + 1536: rs * 2048 + 2048],
                        consts["rotk"][1, :, hc * 512:(hc + 1) * 512],
                        _inc=(tag + f"rt{rs}", 16))
                    for (src, t0, t1, scol) in (
                            (QTa, rs * 2048, rs * 2048 + 512, 0),
                            (KTa, rs * 2048 + 1024, rs * 2048 + 1536, 1)):
                        ch = slice(hc * 512, (hc + 1) * 512)
                        em.wait("vector", tag + "nm", em.cnt.get(tag + "nm", 0))
                        n = em.on("vector", nc.vector.tensor_mul, sqbf[0:64, :],
                                  src[0:64, ch], src[0:64, ch],
                                  _inc=(tag + "sq", 1))
                        em.wait("tensor", tag + "sq", n)
                        em.wait("tensor", tag + "nre", em.cnt.get(tag + "nre", 0))
                        n = em.on("tensor", nc.tensor.matmul, g.p_row[:, :],
                                  g.onescol[0:64, :], sqbf[0:64, :],
                                  start=True, stop=True, _inc=(tag + "nm", 1))
                        em.wait("scalar", tag + "nm", n)
                        n = em.on("scalar", nc.scalar.activation, rrow[:, :],
                                  g.p_row[:, :], AF.Ln, bias=g.cvals[:, 4:5],
                                  scale=1.0, _inc=(tag + "ln1", 1))
                        em.wait("scalar", tag + "ln1", n)
                        n = em.on("scalar", nc.scalar.activation, rrow[:, :],
                                  rrow[:, :], AF.Exp, bias=0.0, scale=-0.5,
                                  _inc=(tag + "nre", 1))
                        em.wait("tensor", tag + "nre", em.cnt[tag + "nre"])
                        n = em.on("tensor", nc.tensor.matmul, g.p_bc[0:64, :],
                                  g.onesrowf[:, 0:64], rrow[:, :],
                                  start=True, stop=True, _inc=(tag + "bc", 1))
                        em.wait("vector", tag + "bc", n)
                        n = em.on("vector", nc.vector.tensor_copy, rbc64[:, :],
                                  g.p_bc[0:64, :], _inc=(tag + "bce", 1))
                        em.wait("tensor", tag + "bce", em.cnt[tag + "bce"])
                        em.wait("vector", tag + "bce", n)
                        em.vop(nc.vector.tensor_mul, scr[0:64, :],
                               src[0:64, ch], rbc64[:, :])
                        if l == 0 and h == 0 and hc == 0 and scol == 0:
                            em.wait("vector", tag + "qs8", 1)
                        em.vop(nc.vector.tensor_scalar_mul, scr[0:64, :],
                               scr[0:64, :], qs8[:, scol:scol + 1])
                        em.wait("vector", tag + f"rt{rs}", rot_t[rs])
                        em.vop(nc.vector.tensor_mul, scr2[0:64, :],
                               scr[0:64, :], rot[:, t0:t0 + 512])
                        em.vop(nc.vector.tensor_scalar_mul, sqbf[0:32, :],
                               scr[32:64, :], -1.0)
                        em.vop(nc.vector.tensor_copy, sqbf[32:64, :],
                               scr[0:32, :])
                        em.vop(nc.vector.tensor_mul, scr[0:64, :],
                               sqbf[0:64, :], rot[:, t1:t1 + 512])
                        em.vop(nc.vector.tensor_add, src[0:64, ch],
                               scr2[0:64, :], scr[0:64, :],
                               _inc=(tag + "rote", 1))
                em.wait("tensor", tag + "rote", em.cnt[tag + "rote"])
                p1_step = [0]
                for qt in range(NT):
                    kcs = [kc for kc in (qt // 4 - 1, qt // 4) if kc >= 0]
                    first = True
                    for kc in kcs:
                        j = qt - 4 * kc - 4
                        pb = g.p_a[p1_step[0] % 2]
                        p1_step[0] += 1
                        em.wait("tensor", tag + "mx",
                                em.cnt.get(tag + "mx", 0) - 1)
                        n = em.on("tensor", nc.tensor.matmul, pb[:, :],
                                  QTa[0:64, ts(qt)],
                                  KTa[0:64, kc * 512:(kc + 1) * 512],
                                  start=True, stop=True, _inc=(tag + "p1", 1))
                        em.wait("vector", tag + "p1", n)
                        n = em.on("vector", nc.vector.tensor_add, pb[:, :],
                                  pb[:, :], mask_j(j), _inc=(tag + "p1m", 1))
                        em.wait("vector", tag + "p1m", n)
                        if first:
                            n = em.on("vector", nc.vector.reduce_max,
                                      maxc[:, qt:qt + 1], pb[:, :], axis=AX.X,
                                      _inc=(tag + "mx", 1))
                        else:
                            n = em.on("vector", nc.vector.reduce_max,
                                      scr[:, 0:1], pb[:, :], axis=AX.X,
                                      _inc=(tag + "mx", 1))
                            em.wait("vector", tag + "mx", n)
                            n = em.on("vector", nc.vector.tensor_max,
                                      maxc[:, qt:qt + 1], maxc[:, qt:qt + 1],
                                      scr[:, 0:1], _inc=(tag + "mx", 1))
                        first = False
                em.wait("vector", tag + "mx", em.cnt[tag + "mx"])
                n = em.on("vector", nc.vector.tensor_scalar_mul, maxc[:, :],
                          maxc[:, :], -1.0, _inc=(tag + "mxn", 1))
                em.wait("tensor", tag + "mxn", n)
                n = em.on("tensor", nc.tensor.transpose, g.p_v[0:32, :],
                          maxc[:, :], g.identf[:], _inc=(tag + "mxt", 1))
                em.wait("vector", tag + "mxt", n)
                n = em.on("vector", nc.vector.tensor_copy, maxt[:, :],
                          g.p_v[0:32, :], _inc=(tag + "mxe", 1))
                em.wait("gpsimd", tag + "mxe", em.cnt[tag + "mxe"])
                nsh = em.on("gpsimd", nc.gpsimd.dma_start, QTa[64:65, :],
                            maxt[:, :], _inc=(tag + "nsh", 16))
                em.wait("tensor", tag + "nsh", nsh)

                for qc in range(NCH):
                    kts = list(range(max(0, 4 * qc - 4), 4 * qc + 4))
                    for ki, kt in enumerate(kts):
                        sbk = kt % 2
                        p_s = g.p_a[sbk]
                        em.wait("tensor", tag + "ex",
                                em.cnt.get(tag + "ex", 0) - 1)
                        smm = em.on("tensor", nc.tensor.matmul, p_s[:, :],
                                    KTa[:, ts(kt)],
                                    QTa[:, qc * 512:(qc + 1) * 512],
                                    start=True, stop=True, _inc=(tag + "s", 1))
                        em.wait("vector", tag + "s", smm)
                        j = kt - 4 * qc
                        mkn = em.on("vector", nc.vector.tensor_add,
                                    p_s[:, :], p_s[:, :], mask_j(j),
                                    _inc=(tag + "mk", 1))
                        em.wait("scalar", tag + "mk", mkn)
                        exn = em.on("scalar", nc.scalar.activation,
                                    PT[:, ts(sbk, 512)], p_s[:, :], AF.Exp,
                                    bias=0.0, scale=1.0, _inc=(tag + "ex", 1))
                        em.wait("tensor", tag + "ex", exn)
                        em.on("tensor", nc.tensor.matmul, g.p_y[0:65, :],
                              Vbuf[:, kt * 65: kt * 65 + 65],
                              PT[:, ts(sbk, 512)],
                              start=(ki == 0), stop=(ki == len(kts) - 1),
                              _inc=(tag + "pv", 1) if ki == len(kts) - 1 else None)
                    em.wait_now("vector", tag + "pv")
                    rcn = em.on("vector", nc.vector.reciprocal, rrow2[:, :],
                                g.p_y[64:65, :], _inc=(tag + "rc2", 1))
                    em.wait("tensor", tag + "rc2", rcn)
                    bcn = em.on("tensor", nc.tensor.matmul, g.p_bc[0:64, :],
                                g.onesrowf[:, 0:64], rrow2[:, :],
                                start=True, stop=True, _inc=(tag + "bc2", 1))
                    em.wait("vector", tag + "bc2", bcn)
                    nbce = em.on("vector", nc.vector.tensor_copy, recbc[:, :],
                                 g.p_bc[0:64, :], _inc=(tag + "bc2e", 1))
                    em.wait("vector", tag + "bc2e", nbce)
                    yev = em.on("vector", nc.vector.tensor_mul,
                                YT2[64 * (h % 2): 64 * (h % 2) + 64,
                                    (h // 2) * T + qc * 512:
                                    (h // 2) * T + (qc + 1) * 512],
                                g.p_y[0:64, :], recbc[:, :],
                                _inc=(tag + "ye", 1))
                    em.wait("tensor", tag + "ye", yev)
                    em.wait("tensor", tag + "bc2e", em.cnt[tag + "bc2e"])

            wot = {0: 0, 1: 0}
            wou = [0]

            def load_wo(ft, wslot):
                em.wait("gpsimd", tag + "om", wou[0] - 1)
                wot[wslot] = em.on("gpsimd", nc.gpsimd.dma_start,
                                   woh[wslot][:, :].rearrange(
                                       "c (ct f) -> c ct f", ct=4),
                                   W["wo"][l, :, :, ft * 128:(ft + 1) * 128]
                                   .rearrange("ct c f -> c ct f"),
                                   _inc=(tag + f"wo{wslot}", 16))

            load_wo(0, 0)
            for tc in range(NCH):
                for ft in range(CT):
                    i = tc * CT + ft
                    wslot = i % 2
                    po = g.p_a[ft % 2]
                    em.wait("tensor", tag + f"wo{wslot}", wot[wslot])
                    em.wait("tensor", tag + "oe", em.cnt.get(tag + "oe", 0) - 1)
                    for ct in range(4):
                        em.on("tensor", nc.tensor.matmul, po[:, :],
                              woh[wslot][:, ct * 128: (ct + 1) * 128],
                              YT2[:, ct * T + tc * 512: ct * T + (tc + 1) * 512],
                              start=(ct == 0), stop=(ct == 3),
                              _inc=(tag + "om", 1) if ct == 3 else None)
                    wou[0] = em.cnt[tag + "om"]
                    ni = i + 1
                    if ni < NCH * CT:
                        load_wo(ni % CT, ni % 2)
                    em.wait_now("vector", tag + "om")
                    em.on("vector", nc.vector.tensor_add,
                          xp[:, ft * T + tc * 512: ft * T + (tc + 1) * 512],
                          xp[:, ft * T + tc * 512: ft * T + (tc + 1) * 512],
                          po[:, :], _inc=(tag + "oe", 1))
        em.flush(tag + "_attn")

        with ExitStack() as fes:
            fsb = lambda n_, sh, dt: fes.enter_context(
                nc.sbuf_tensor(f"F{l}{n_}", sh, dt))
            w1a_res = g.xT[:, 0:FCP * CT * 128]
            act = fsb("_act", [128, FCP * 512], BF16)
            h2ch = fsb("_h2", [128, CT * 512], BF16)
            shch = fsb("_sh", [128, CT * 512], BF16)
            w1g_s = [fsb(f"_w1g{i}", [128, CT * 128], BF16) for i in range(2)]
            w2_s = [fsb(f"_w2{i}", [128, D], BF16) for i in range(2)]
            gel = g.sqbf

            em.wait("gpsimd", "f0z1", 16)
            for fc in range(FCP):
                em.on("gpsimd", nc.gpsimd.dma_start,
                      w1a_res[:, fc * 1024: (fc + 1) * 1024],
                      W["w1a"][l, fc].rearrange("ct c f -> c ct f"),
                      _inc=(tag + "w1a", 16))
            w1a_tgt = em.cnt[tag + "w1a"]
            w1g_t = {0: 0, 1: 0}
            w2_t = {0: 0, 1: 0}

            def load_w1g(fc, wslot, chunk_i):
                w1g_t[wslot] = em.on(
                    "gpsimd", nc.gpsimd.dma_start, w1g_s[wslot][:, :],
                    W["w1g"][l, fc].rearrange("ct c f -> c ct f"),
                    _inc=(tag + f"g{wslot}", 16))

            def load_w2(fc, wslot):
                w2_t[wslot] = em.on(
                    "gpsimd", nc.gpsimd.dma_start, w2_s[wslot][:, :],
                    W["w2"][l, fc, :, :], _inc=(tag + f"h{wslot}", 16))

            load_w1g(0, 0, 0)
            for tc in range(NCH):
                em.wait("vector", tag + "h2u", em.cnt.get(tag + "h2u", 0))
                em.wait("vector", tag + "st1", em.cnt.get(tag + "st1", 0))
                em.wait("vector", tag + "st2", em.cnt.get(tag + "st2", 0))
                for ct in range(CT):
                    off = ct * T + tc * 512 - (1 if ct >= 4 else 0)
                    n = em.on("vector", nc.vector.tensor_copy,
                              shch[:, ts(ct, 512)], xp[:, off: off + 512],
                              _inc=(tag + "shc", 1))
                    if ct >= 4 and tc == 0:
                        em.wait("vector", tag + "shc", n)
                        em.on("vector", nc.vector.memset,
                              shch[:, ct * 512: ct * 512 + 1], 0.0,
                              _inc=(tag + "shc", 1))
                em.wait("tensor", tag + "shc", em.cnt[tag + "shc"])
                em.wait("vector", tag + "shc", em.cnt[tag + "shc"])
                for ct in range(CT):
                    em.on("tensor", nc.tensor.matmul, g.p_row[:, :],
                          g.onescol[:, :], shch[:, ts(ct, 512)],
                          start=(ct == 0), stop=(ct == CT - 1),
                          _inc=(tag + "st1", 1))
                    em.wait("vector", tag + "st2", em.cnt.get(tag + "st2", 0))
                    n = em.on("vector", nc.vector.tensor_mul, sqbf[:, :],
                              shch[:, ts(ct, 512)], shch[:, ts(ct, 512)],
                              _inc=(tag + "sq2", 1))
                    em.wait("tensor", tag + "sq2", n)
                    em.on("tensor", nc.tensor.matmul, g.p_row2[:, :],
                          g.onescol[:, :], sqbf[:, :],
                          start=(ct == 0), stop=(ct == CT - 1),
                          _inc=(tag + "st2", 1))
                em.wait_now("vector", tag + "st1")
                em.wait_now("vector", tag + "st2")
                em.vop(nc.vector.tensor_scalar_mul, l_mean, g.p_row[:, :],
                       1.0 / D)
                em.vop(nc.vector.tensor_scalar_mul, l_r, g.p_row2[:, :],
                       1.0 / D, _inc=(tag + "lv", 1))
                em.wait("tensor", tag + "lv", em.cnt[tag + "lv"])
                em.vop(nc.vector.tensor_mul, l_mr, l_mean, l_mean)
                em.vop(nc.vector.tensor_sub, l_r, l_r, l_mr,
                       _inc=(tag + "lv2", 1))
                em.wait_now("scalar", tag + "lv2")
                n = em.on("scalar", nc.scalar.activation, l_r, l_r, AF.Ln,
                          bias=g.cvals[:, 1:2], scale=1.0, _inc=(tag + "lnp", 1))
                em.wait("scalar", tag + "lnp", n)
                em.on("scalar", nc.scalar.activation, l_r, l_r, AF.Exp,
                      bias=0.0, scale=-0.5, _inc=(tag + "lnr", 1))
                em.wait_now("vector", tag + "lnr")
                n = em.vop(nc.vector.tensor_mul, l_mr, l_mean, l_r,
                           _inc=(tag + "hl", 1))
                em.wait("tensor", tag + "hl", em.cnt[tag + "hl"])
                em.wait("tensor", tag + "bfre", em.cnt.get(tag + "bfre", 0))
                n1 = em.on("tensor", nc.tensor.matmul, g.p_bc[:, :],
                           g.onesrowf[:, :], l_r, start=True, stop=True,
                           _inc=(tag + "bcf", 1))
                em.wait("vector", tag + "bcf", n1)
                em.on("vector", nc.vector.tensor_copy, rbc[:, :], g.p_bc[:, :],
                      _inc=(tag + "bfre", 1))
                em.wait("tensor", tag + "bfre", em.cnt[tag + "bfre"])
                n2 = em.on("tensor", nc.tensor.matmul, g.p_bc[:, :],
                           g.onesrowf[:, :], l_mr, start=True, stop=True,
                           _inc=(tag + "bcf", 1))
                em.wait("vector", tag + "bcf", n2)
                em.on("vector", nc.vector.tensor_copy, mrbc[:, :], g.p_bc[:, :],
                      _inc=(tag + "bfre", 1))
                em.wait("vector", tag + "am", em.cnt.get(tag + "am", 0))
                em.wait("vector", tag + "gm", em.cnt.get(tag + "gm", 0))
                for ct in range(CT):
                    em.vop(nc.vector.tensor_mul, scr[:, :],
                           shch[:, ts(ct, 512)], rbc[:, :])
                    em.vop(nc.vector.tensor_sub, scr[:, :], scr[:, :],
                           mrbc[:, :])
                    em.vop(nc.vector.tensor_scalar, h2ch[:, ts(ct, 512)],
                           scr[:, :], lgb[:, ct:ct + 1], lgb[:, 8 + ct:9 + ct],
                           op0=ALU.mult, op1=ALU.add, _inc=(tag + "h2", 1))
                em.wait("sync", tag + "h2", em.cnt[tag + "h2"])
                em.wait("sync", tag + "bl", em.cnt[tag + "bl"])
                em.on("sync", nc.sync.dma_start,
                      h2h[:, tc * 4096:(tc + 1) * 4096], h2ch[:, :],
                      _inc=(tag + "bl", 16))
                em.wait("vector", tag + "bl", em.cnt[tag + "bl"])

            h2sl = [h2ch, shch]
            em.wait("sync", tag + "bl", em.cnt[tag + "bl"])
            em.wait("sync", tag + "st1", em.cnt.get(tag + "st1", 0))
            em.wait("sync", tag + "sq2", em.cnt.get(tag + "sq2", 0))
            em.on("sync", nc.sync.dma_start, h2sl[0][:, :], h2h[:, 0:4096],
                  _inc=(tag + "bl", 16))
            h2ld = {0: em.cnt[tag + "bl"], 1: 0}
            for tc in range(NCH):
                sl = tc % 2
                h2c = h2sl[sl]
                em.wait("sync", tag + "bl", h2ld[sl])
                if tc + 1 < NCH:
                    em.wait("sync", tag + "am", em.cnt.get(tag + "am", 0))
                    em.wait("sync", tag + "gm", em.cnt.get(tag + "gm", 0))
                    em.on("sync", nc.sync.dma_start, h2sl[1 - sl][:, :],
                          h2h[:, (tc + 1) * 4096:(tc + 2) * 4096],
                          _inc=(tag + "bl", 16))
                    h2ld[1 - sl] = em.cnt[tag + "bl"]
                em.wait("tensor", tag + "bl", h2ld[sl])

                for fc in range(FCP):
                    wslot = fc % 2
                    if tc == 0 and fc == 0:
                        em.wait("tensor", tag + "w1a", w1a_tgt)
                    em.wait("tensor", tag + f"g{wslot}", w1g_t[wslot])
                    em.wait("tensor", tag + "ae",
                            em.cnt.get(tag + "ae", 0))
                    for ct in range(CT):
                        em.on("tensor", nc.tensor.matmul, g.p_a[0][:, :],
                              w1a_res[:, fc * 1024 + ct * 128:
                                      fc * 1024 + (ct + 1) * 128],
                              h2c[:, ts(ct, 512)],
                              start=(ct == 0), stop=(ct == CT - 1),
                              _inc=(tag + "am", 1) if ct == CT - 1 else None)
                    for ct in range(CT):
                        em.on("tensor", nc.tensor.matmul, g.p_a[1][:, :],
                              w1g_s[wslot][:, ct * 128: (ct + 1) * 128],
                              h2c[:, ts(ct, 512)],
                              start=(ct == 0), stop=(ct == CT - 1),
                              _inc=(tag + "gm", 1) if ct == CT - 1 else None)
                    nfc = fc + 1 if fc + 1 < FCP else (0 if tc + 1 < NCH else None)
                    if nfc is not None:
                        em.wait("gpsimd", tag + "gm", em.cnt[tag + "gm"] - 1)
                        load_w1g(nfc, 1 - wslot, tc)
                    em.wait_now("scalar", tag + "gm")
                    gn = em.on("scalar", nc.scalar.activation, gel[:, :],
                               g.p_a[1][:, :], AF.Gelu,
                               bias=b1c[:, 22 + fc: 23 + fc], scale=1.0,
                               _inc=(tag + "gel", 1))
                    em.wait("vector", tag + "am", em.cnt[tag + "am"])
                    em.wait("vector", tag + "gel", gn)
                    em.wait("vector", tag + "wm", em.cnt.get(tag + "wm", 0))
                    em.vop(nc.vector.tensor_scalar, scr[:, :], g.p_a[0][:, :],
                           b1c[:, fc:fc + 1], None, op0=ALU.add)
                    em.vop(nc.vector.tensor_mul, act[:, ts(fc, 512)],
                           scr[:, :], gel[:, :], _inc=(tag + "ae", 1))
                em.wait("tensor", tag + "ae", em.cnt[tag + "ae"])
                em.wait("tensor", tag + "bfre", em.cnt.get(tag + "bfre", 0))
                pbanks = [g.p_a[0], g.p_a[1], g.p_y, g.p_bc,
                          None, None, None, None]
                for half in range(2):
                    for fc in range(FCP):
                        wslot = fc % 2
                        em.wait("gpsimd", tag + "wm",
                                em.cnt.get(tag + "wm", 0) - 1)
                        load_w2(fc, wslot)
                        em.wait("tensor", tag + f"h{wslot}", w2_t[wslot])
                        for dq in range(4):
                            dt = half * 4 + dq
                            em.on("tensor", nc.tensor.matmul,
                                  pbanks[dq][:, :],
                                  w2_s[wslot][:, dt * 128: (dt + 1) * 128],
                                  act[:, ts(fc, 512)],
                                  start=(fc == 0), stop=(fc == FCP - 1),
                                  _inc=(tag + "wm", 1) if dq == 3 else None)
                    em.wait_now("vector", tag + "wm")
                    for dq in range(4):
                        dt = half * 4 + dq
                        em.vop(nc.vector.tensor_scalar, scr[:, :],
                               pbanks[dq][:, :], b2c[:, dt:dt + 1], None,
                               op0=ALU.add)
                        em.vop(nc.vector.tensor_add,
                               xp[:, dt * T + tc * 512: dt * T + (tc + 1) * 512],
                               xp[:, dt * T + tc * 512: dt * T + (tc + 1) * 512],
                               scr[:, :], _inc=(tag + "h2u", 1))
                    em.wait("tensor", tag + "h2u", em.cnt[tag + "h2u"])
        em.flush(tag + "_ffn")
    with ExitStack() as les:
        lsb = lambda n_, sh, dt: les.enter_context(nc.sbuf_tensor(n_, sh, dt))
        z1ch = lsb("fL_z1", [128, CT * 512], BF16)
        pre = lsb("fL_pre", [128, CT * 512], BF16)
        z1v = z1_hbm[:, :]
        em.wait("gpsimd", "f0z1", 16)
        zt = {}
        for tc in range(NCH):
            em.wait("gpsimd", "a0prf", em.cnt.get("a0prf", 0))
            zt[tc] = em.on("gpsimd", nc.gpsimd.dma_start, z1ch[:, :],
                           z1v.rearrange("p (c t) -> p c t", c=CT)
                           [:, :, tc * 512:(tc + 1) * 512],
                           _inc=("a0zl", 16))
            em.wait("vector", "a0zl", zt[tc])
            for ct in range(CT):
                em.on("vector", nc.vector.tensor_add, pre[:, ts(ct, 512)],
                      z1ch[:, ts(ct, 512)],
                      xp[:, ct * T + tc * 512: ct * T + (tc + 1) * 512],
                      _inc=("a0prf", 1))
            em.wait("tensor", "a0prf", em.cnt["a0prf"])
            em.wait("vector", "a0prf", em.cnt["a0prf"])
            for ct in range(CT):
                em.on("tensor", nc.tensor.matmul, g.p_row[:, :],
                      g.onescol[:, :], pre[:, ts(ct, 512)],
                      start=(ct == 0), stop=(ct == CT - 1),
                      _inc=("a0st1", 1))
                em.wait("vector", "a0st2", em.cnt.get("a0st2", 0))
                n = em.on("vector", nc.vector.tensor_mul, sqbf[:, :],
                          pre[:, ts(ct, 512)], pre[:, ts(ct, 512)],
                          _inc=("a0sqt", 1))
                em.wait("tensor", "a0sqt", n)
                em.on("tensor", nc.tensor.matmul, g.p_row2[:, :],
                      g.onescol[:, :], sqbf[:, :],
                      start=(ct == 0), stop=(ct == CT - 1),
                      _inc=("a0st2", 1))
            em.wait_now("vector", "a0st1")
            em.wait_now("vector", "a0st2")
            em.vop(nc.vector.tensor_scalar_mul, l_mean, g.p_row[:, :], 1.0 / D)
            em.vop(nc.vector.tensor_scalar_mul, l_r, g.p_row2[:, :], 1.0 / D,
                   _inc=("a0lv", 1))
            em.wait("tensor", "a0lv", em.cnt["a0lv"])
            em.vop(nc.vector.tensor_mul, l_mr, l_mean, l_mean)
            em.vop(nc.vector.tensor_sub, l_r, l_r, l_mr, _inc=("a0lv2", 1))
            em.wait_now("scalar", "a0lv2")
            n = em.on("scalar", nc.scalar.activation, l_r, l_r, AF.Ln,
                      bias=g.cvals[:, 1:2], scale=1.0, _inc=("a0lnp", 1))
            em.wait("scalar", "a0lnp", n)
            em.on("scalar", nc.scalar.activation, l_r, l_r, AF.Exp,
                  bias=0.0, scale=-0.5, _inc=("a0lnr", 1))
            em.wait_now("vector", "a0lnr")
            em.vop(nc.vector.tensor_mul, l_mr, l_mean, l_r, _inc=("a0hl", 1))
            em.wait("tensor", "a0hl", em.cnt["a0hl"])
            em.wait("tensor", "a0bfre", em.cnt.get("a0bfre", 0))
            n1 = em.on("tensor", nc.tensor.matmul, g.p_bc[:, :],
                       g.onesrowf[:, :], l_r, start=True, stop=True,
                       _inc=("a0bcf", 1))
            em.wait("vector", "a0bcf", n1)
            em.on("vector", nc.vector.tensor_copy, rbc[:, :], g.p_bc[:, :],
                  _inc=("a0bfre", 1))
            em.wait("tensor", "a0bfre", em.cnt["a0bfre"])
            n2 = em.on("tensor", nc.tensor.matmul, g.p_bc[:, :],
                       g.onesrowf[:, :], l_mr, start=True, stop=True,
                       _inc=("a0bcf", 1))
            em.wait("vector", "a0bcf", n2)
            em.on("vector", nc.vector.tensor_copy, mrbc[:, :], g.p_bc[:, :],
                  _inc=("a0bfre", 1))
            for ct in range(CT):
                em.vop(nc.vector.tensor_mul, scr[:, :], pre[:, ts(ct, 512)],
                       rbc[:, :])
                em.vop(nc.vector.tensor_sub,
                       g.xT[:, ct * T + tc * 512: ct * T + (tc + 1) * 512],
                       scr[:, :], mrbc[:, :], _inc=("a0fin", 1))
    em.flush("fL_final")
    es.close()



def build_l1_attn(g, W, z2_hbm):
    nc, em = g.nc, g.em
    from contextlib import ExitStack
    es = ExitStack()
    sb = lambda n_, sh, dt: es.enter_context(nc.sbuf_tensor(n_, sh, dt))
    tag = "a0"
    SUBS = 4

    em.wait_now("sync", "a0fin")
    em.on("sync", nc.sync.dma_start, z2_hbm[:, :], g.xT[:, :], _inc=("f0z1", 16))

    WR, WK, WV, WO = 0, CT * 1024, 2 * CT * 1024, 3 * CT * 1024
    maa5 = sb("t1_maa5s", [128, 40], F32)
    tdc = sb("t1_tds", [128, 8], F32)
    mw1 = sb("t1_mw1s", [128, CT * 128], BF16)
    mw2 = [sb(f"t1_mw2s{i}", [32, 1024], BF16) for i in range(4)]
    dw1 = sb("t1_dw1s", [128, CT * 64], BF16)
    dw2 = sb("t1_dw2s", [64, 1024], BF16)
    ucol = sb("t1_us", [64, 16], F32)
    lnbc_ = sb("t1_lnbs", [128, 8], BF16)
    lnwc = sb("t1_lnws", [128, 8], F32)
    biasW = sb("t1_biasW", [128, 8], F32)
    xch = sb("t1_xch", [128, CT * 513], BF16)
    mixf = g.bigA[:, WV: WV + CT * 512]
    vtok = g.bigA[:, WV + CT * 512: WV + CT * 512 + SUBS * 1024]
    xxx_rT = sb("t1_xxxrT", [128, CT * 512], BF16)
    kT = sb("t1_kT", [128, CT * 512], BF16)
    wv_s = [sb(f"t1_wv{i}", [128, 512], BF16) for i in range(2)]
    th_all = sb("t1_thall", [128, 512], BF16)
    th_f = [th_all[0:32, :]] + [sb(f"t1_th{i}", [32, 512], BF16)[:, :]
                                for i in range(1, 4)]
    tdh = sb("t1_tdh", [64, 512], BF16)
    wtile = sb("t1_wtile", [128, 128], F32)
    ybuf = sb("t1_ybuf", [128, 1024], BF16)
    ynT = sb("t1_ynT", [128, CT * 128], BF16)
    S32 = sb("t1_S", [64, 16 * 64], F32)
    Sbf = g.rbc64.bitcast(BF16)[0:64, 256:320]
    A_sb = sb("t1_A", [64, 128], F32)
    Ash = sb("t1_Ash", [64, 128], F32)
    rtil = g.scr2.bitcast(BF16)[0:64, 128:256]
    ktil = g.scr2.bitcast(BF16)[0:64, 256:384]
    r0 = g.scr2.bitcast(BF16)[0:64, 384:512]
    rA = g.scr[0:64, 128:256]
    k0 = g.scr.bitcast(BF16)[0:64, 768:896]
    khat = g.sqbf[0:64, 256:384]
    khtok = g.sqbf[:, 384:448]
    dcol = sb("t1_dcol", [128, 1], F32)
    rku = g.sqbf[0:64, 128:256]
    drow = g.rowa[:, 0:128]
    zeros128 = g.mrbc[:, 0:128]
    mstrict = sb("t1_mstrict", [128, 128], BF16)
    m16 = g.rbc.bitcast(F32)[:, 160:176] if False else g.rbc[:, 160:176]
    v16 = g.rbc[:, 176:192]
    sq16 = g.rbc[:, 192:208]

    em.on("sync", nc.sync.dma_start, maa5[:], W["maa5"][:, :], _inc=(tag + "bl", 16))
    em.on("sync", nc.sync.dma_start, tdc[:], W["td"][:, :], _inc=(tag + "bl", 16))
    em.on("sync", nc.sync.dma_start, ucol[:], W["u"][:, :], _inc=(tag + "bl", 16))
    em.on("sync", nc.sync.dma_start, lnwc[:], W["lnw"][:, :], _inc=(tag + "bl", 16))
    em.on("gpsimd", nc.gpsimd.dma_start, lnbc_[:], W["lnb"][:, :], _inc=(tag + "thd", 16))
    for f_ in range(4):
        em.on("gpsimd", nc.gpsimd.dma_start, mw2[f_][:],
              W["mw2"][:, f_ * 1024:(f_ + 1) * 1024], _inc=(tag + "thd", 16))
    em.on("gpsimd", nc.gpsimd.dma_start, dw2[:], W["dw2"][:, :], _inc=(tag + "thd", 16))
    for ct in range(CT):
        em.on("gpsimd", nc.gpsimd.dma_start, mw1[:, ct * 128:(ct + 1) * 128],
              W["mw1"][ct], _inc=(tag + "thd", 16))
        em.on("gpsimd", nc.gpsimd.dma_start, dw1[:, ct * 64:(ct + 1) * 64],
              W["dw1"][ct], _inc=(tag + "thd", 16))
    for ft in range(CT):
        for ct in range(CT):
            em.on("gpsimd", nc.gpsimd.dma_start,
                  g.bigA[:, WR + ft * 1024 + ct * 128: WR + ft * 1024 + (ct + 1) * 128],
                  W["wr"][ft, ct], _inc=(tag + "w1a", 16))
            em.on("gpsimd", nc.gpsimd.dma_start,
                  g.bigA[:, WK + ft * 1024 + ct * 128: WK + ft * 1024 + (ct + 1) * 128],
                  W["wk"][ft, ct], _inc=(tag + "w1a", 16))
            em.on("gpsimd", nc.gpsimd.dma_start,
                  g.bigA[:, WO + ft * 1024 + ct * 128: WO + ft * 1024 + (ct + 1) * 128],
                  W["wo"][ft, ct], _inc=(tag + "w1a", 16))
    wb_tgt = em.cnt[tag + "w1a"]
    n = em.on("gpsimd", nc.gpsimd.memset, mstrict[:], 1.0, _inc=(tag + "bc2", 1))
    em.wait("gpsimd", tag + "bc2", n)
    em.on("gpsimd", nc.gpsimd.affine_select, mstrict[:], mstrict[:],
          [[1, 128]], compare_op=ALU.is_ge, fill=0.0,
          base=-1, channel_multiplier=-1, _inc=(tag + "bc2", 1))
    em.on("vector", nc.vector.memset, zeros128[:, :], 0.0)
    em.on("vector", nc.vector.memset, S32[:], 0.0, _inc=(tag + "qe", 1))
    em.flush("t1_setup")

    em.wait("tensor", tag + "w1a", wb_tgt)
    em.wait("tensor", tag + "bl", em.cnt[tag + "bl"])
    em.wait("tensor", tag + "thd", em.cnt[tag + "thd"])
    for ft in range(CT):
        for ct in range(CT):
            em.on("tensor", nc.tensor.matmul, g.p_row[0:128, 0:1].bitcast(F32)
                  if False else g.p_v[:, 0:1],
                  g.bigA[:, WO + ft * 1024 + ct * 128: WO + ft * 1024 + (ct + 1) * 128],
                  lnbc_[:, ct:ct + 1],
                  start=(ct == 0), stop=(ct == CT - 1),
                  _inc=(tag + "vm", 1) if ct == CT - 1 else None)
        em.wait_now("vector", tag + "vm")
        em.on("vector", nc.vector.tensor_copy, biasW[:, ft:ft + 1],
              g.p_v[:, 0:1], _inc=(tag + "ve", 1))
        em.wait("tensor", tag + "ve", em.cnt[tag + "ve"])
    em.wait("vector", tag + "vm", em.cnt[tag + "vm"])
    for ft in range(CT):
        for ct in range(CT):
            em.vop(nc.vector.tensor_scalar_mul,
                   g.bigA[:, WO + ft * 1024 + ct * 128: WO + ft * 1024 + (ct + 1) * 128],
                   g.bigA[:, WO + ft * 1024 + ct * 128: WO + ft * 1024 + (ct + 1) * 128],
                   lnwc[:, ct:ct + 1], _inc=(tag + "sq", 1))
    em.flush("t1_biasw")

    STR = {"w": 1, "k": 2, "v": 3, "r": 4}

    for tc in range(NCH):
        em.wait("sync", tag + "oe", em.cnt.get(tag + "oe", 0))
        if tc == 0:
            em.wait_now("sync", "f0z1")
            em.wait_now("vector", "f0z1")
            em.wait_now("vector", tag + "bl")
            em.wait_now("scalar", tag + "bl")
            em.wait_now("vector", tag + "thd")
            em.wait_now("scalar", tag + "thd")
            em.wait_now("tensor", tag + "thd")
            em.wait("tensor", tag + "sq", em.cnt[tag + "sq"])
            for ct in range(CT):
                em.on("sync", nc.sync.dma_start, xch[:, ct * 513 + 1: (ct + 1) * 513],
                      z2_hbm[:, ct * T: ct * T + 512], _inc=(tag + "nr", 16))
            em.wait_now("vector", tag + "nr")
            for ct in range(CT):
                em.on("vector", nc.vector.memset, xch[:, ct * 513: ct * 513 + 1],
                      0.0, _inc=(tag + "nre", 1))
            em.wait("vector", tag + "nre", em.cnt[tag + "nre"])
        else:
            for ct in range(CT):
                em.on("sync", nc.sync.dma_start, xch[:, ct * 513: (ct + 1) * 513],
                      z2_hbm[:, ct * T + tc * 512 - 1: ct * T + (tc + 1) * 512],
                      _inc=(tag + "nr", 16))
            em.wait_now("vector", tag + "nr")
        em.wait_now("tensor", tag + "nr")
        xv = lambda ct: xch[:, ct * 513 + 1: (ct + 1) * 513]
        xs = lambda ct: xch[:, ct * 513: (ct + 1) * 513 - 1]

        for ct in range(CT):
            em.vop(nc.vector.tensor_sub, g.scr, xs(ct), xv(ct))
            em.vop(nc.vector.scalar_tensor_tensor, xxx_rT[:, ts(ct, 512)],
                   g.scr, maa5[:, 0 * 8 + ct: 0 * 8 + ct + 1], xv(ct),
                   op0=ALU.mult, op1=ALU.add, _inc=(tag + "s1", 1))
        em.wait("tensor", tag + "s1", em.cnt[tag + "s1"])
        em.wait("tensor", tag + "s2", em.cnt.get(tag + "s2", 0))
        em.wait("tensor", tag + "pv", em.cnt.get(tag + "pv", 0))
        em.wait("tensor", tag + "s", em.cnt.get(tag + "s", 0))
        for ct in range(CT):
            em.on("tensor", nc.tensor.matmul, g.p_a[0][:, :],
                  mw1[:, ct * 128:(ct + 1) * 128], xxx_rT[:, ts(ct, 512)],
                  start=(ct == 0), stop=(ct == CT - 1),
                  _inc=(tag + "s3", 1) if ct == CT - 1 else None)
        em.wait_now("scalar", tag + "s3")
        em.wait("scalar", tag + "s4", em.cnt.get(tag + "s4", 0))
        em.wait("scalar", tag + "thd", em.cnt.get(tag + "thd", 0))
        em.sop(nc.scalar.activation, th_all[:, :], g.p_a[0][:, :], AF.Tanh,
               bias=0.0, scale=1.0, _inc=(tag + "s2", 1))
        em.wait("gpsimd", tag + "s2", em.cnt[tag + "s2"])
        for f_ in range(1, 4):
            em.on("gpsimd", nc.gpsimd.dma_start, th_f[f_],
                  th_all[f_ * 32:(f_ + 1) * 32, :], _inc=(tag + "thd", 16))
        em.wait_now("tensor", tag + "thd")
        em.wait_now("tensor", tag + "s2")

        for f, fname in enumerate(("w", "k", "v", "r")):
            em.wait("vector", tag + "ex", em.cnt.get(tag + "ex", 0))
            em.wait("vector", tag + "ns", em.cnt.get(tag + "ns", 0))
            em.wait("tensor", tag + "pv", em.cnt.get(tag + "pv", 0))
            mcol = STR[fname]
            for ct in range(CT):
                em.wait("tensor", tag + "nsp", em.cnt.get(tag + "nsp", 0))
                em.on("tensor", nc.tensor.matmul, g.p_a[1][:, :],
                      mw2[f][0:32, ct * 128:(ct + 1) * 128], th_f[f],
                      start=True, stop=True, _inc=(tag + "s4", 1))
                em.wait_now("vector", tag + "s4")
                em.vop(nc.vector.tensor_sub, g.scr, xs(ct), xv(ct))
                em.vop(nc.vector.tensor_scalar, g.scr2, g.p_a[1][:, :],
                       maa5[:, mcol * 8 + ct: mcol * 8 + ct + 1], None,
                       op0=ALU.add)
                em.vop(nc.vector.tensor_mul, g.scr, g.scr, g.scr2)
                em.vop(nc.vector.tensor_add, mixf[:, ts(ct, 512)], g.scr,
                       xv(ct), _inc=(tag + "nsp", 1))
            em.wait("tensor", tag + "nsp", em.cnt[tag + "nsp"])
            if fname == "w":
                em.wait("tensor", tag + "bk", em.cnt.get(tag + "bk", 0))
                for ct in range(CT):
                    em.on("tensor", nc.tensor.matmul, g.p_a[0][0:64, :],
                          dw1[:, ct * 64:(ct + 1) * 64], mixf[:, ts(ct, 512)],
                          start=(ct == 0), stop=(ct == CT - 1),
                          _inc=(tag + "ns", 1) if ct == CT - 1 else None)
                em.wait_now("scalar", tag + "ns")
                em.wait("scalar", tag + "mk", em.cnt.get(tag + "mk", 0))
                em.sop(nc.scalar.activation, tdh[:, :], g.p_a[0][0:64, :],
                       AF.Tanh, bias=0.0, scale=1.0, _inc=(tag + "bk", 1))
                em.wait_now("tensor", tag + "bk")
            elif fname in ("k", "r"):
                dst = kT if fname == "k" else xxx_rT
                WX = WK if fname == "k" else WR
                for ft in range(CT):
                    em.wait("tensor", tag + "pv", em.cnt.get(tag + "pv", 0) - 1)
                    for ct in range(CT):
                        em.on("tensor", nc.tensor.matmul,
                              g.p_a[ft % 2][:, :],
                              g.bigA[:, WX + ft * 1024 + ct * 128:
                                     WX + ft * 1024 + (ct + 1) * 128],
                              mixf[:, ts(ct, 512)],
                              start=(ct == 0), stop=(ct == CT - 1),
                              _inc=(tag + "ex", 1) if ct == CT - 1 else None)
                    em.wait_now("vector", tag + "ex")
                    em.on("vector", nc.vector.tensor_copy, dst[:, ts(ft, 512)],
                          g.p_a[ft % 2][:, :], _inc=(tag + "pv", 1))
            elif fname == "v":
                em.wait("tensor", tag + "ye", em.cnt.get(tag + "ye", 0))
                em.wait("tensor", tag + "p1", em.cnt.get(tag + "p1", 0))
                em.wait("tensor", tag + "bfre", em.cnt.get(tag + "bfre", 0))
                banks = [g.p_a[0], g.p_a[1], g.p_y, g.p_bc]
                wv_t = {0: 0, 1: 0}
                vstep = []

                def load_wv(fh, ct, slot, step):
                    if step >= 2:
                        em.wait("gpsimd", tag + "ln1", vstep[step - 2])
                    wv_t[slot] = em.on(
                        "gpsimd", nc.gpsimd.dma_start, wv_s[slot][:, :],
                        W["wv"][ct * 128:(ct + 1) * 128, fh * 512:(fh + 1) * 512],
                        _inc=(tag + f"wv{slot}", 16))

                load_wv(0, 0, 0, 0)
                step = 0
                for fh in range(2):
                    for ct in range(CT):
                        slot = step % 2
                        em.wait("tensor", tag + f"wv{slot}", wv_t[slot])
                        for tt in range(SUBS):
                            em.on("tensor", nc.tensor.matmul, banks[tt][:, :],
                                  mixf[:, ct * 512 + tt * 128:
                                       ct * 512 + (tt + 1) * 128],
                                  wv_s[slot][:, :],
                                  start=(ct == 0), stop=(ct == CT - 1),
                                  _inc=(tag + "ln1", 1) if tt == SUBS - 1 else None)
                        vstep.append(em.cnt[tag + "ln1"])
                        if step + 1 < 16:
                            nfh, nct = divmod(step + 1, CT)
                            load_wv(nfh, nct, 1 - slot, step + 1)
                        step += 1
                    em.wait_now("vector", tag + "ln1")
                    for tt in range(SUBS):
                        em.on("vector", nc.vector.tensor_copy,
                              vtok[:, tt * 1024 + fh * 512:
                                   tt * 1024 + (fh + 1) * 512],
                              banks[tt][:, :], _inc=(tag + "pv", 1))
                    em.wait("tensor", tag + "pv", em.cnt[tag + "pv"])
        em.wait("tensor", tag + "pv", em.cnt[tag + "pv"])

        for sub in range(SUBS):
            for h in range(16):
                ct = h // 2
                hb = 64 * (h % 2)
                em.wait("tensor", tag + "oe", em.cnt.get(tag + "oe", 0))
                if h % 2 == 0:
                    em.wait("tensor", tag + "s", em.cnt.get(tag + "s", 0))
                    em.on("tensor", nc.tensor.matmul, g.p_a[1][:, 0:128],
                          dw2[:, ct * 128:(ct + 1) * 128],
                          tdh[:, sub * 128:(sub + 1) * 128],
                          start=True, stop=True, _inc=(tag + "mk", 1))
                    em.wait_now("scalar", tag + "mk")
                    em.sop(nc.scalar.activation, wtile[:, :], g.p_a[1][:, 0:128],
                           AF.Exp, bias=tdc[:, ct:ct + 1], scale=1.0)
                    em.sop(nc.scalar.activation, wtile[:, :], wtile[:, :],
                           AF.Exp, bias=0.0, scale=-1.0, _inc=(tag + "s", 1))
                    em.wait_now("vector", tag + "s")
                rsl = xxx_rT[hb:hb + 64, ct * 512 + sub * 128: ct * 512 + (sub + 1) * 128]
                ksl = kT[hb:hb + 64, ct * 512 + sub * 128: ct * 512 + (sub + 1) * 128]
                vsl = vtok[:, sub * 1024 + h * 64: sub * 1024 + (h + 1) * 64]
                wsl = wtile[hb:hb + 64, :]
                em.vop(nc.vector.tensor_copy, r0[:, :], rsl)
                em.vop(nc.vector.tensor_copy, k0[:, :], ksl)
                em.vop(nc.vector.tensor_tensor_scan, A_sb[:, :], wsl,
                       zeros128[hb:hb + 64, :], 1.0, op0=ALU.mult, op1=ALU.add)
                em.vop(nc.vector.tensor_copy, Ash[:, 1:128], A_sb[:, 0:127])
                em.vop(nc.vector.memset, Ash[:, 0:1], 1.0)
                em.vop(nc.vector.tensor_mul, rtil[:, :], r0[:, :], Ash[:, :])
                em.vop(nc.vector.reciprocal, rA[:, :], A_sb[:, :])
                em.vop(nc.vector.tensor_mul, ktil[:, :], k0[:, :], rA[:, :])
                em.vop(nc.vector.tensor_scalar_mul, khat[:, :], ktil[:, :],
                       A_sb[:, 127:128])
                em.vop(nc.vector.tensor_scalar_mul, g.scr[0:64, 0:128], r0[:, :],
                       ucol[:, h:h + 1])
                em.vop(nc.vector.tensor_mul, rku[:, :],
                       g.scr[0:64, 0:128], k0[:, :], _inc=(tag + "rc", 1))
                em.wait("tensor", tag + "rc", em.cnt[tag + "rc"])
                em.on("tensor", nc.tensor.transpose,
                      g.p_v[:, :].bitcast(BF16)[:, 0:64], khat[:, :],
                      g.ident[0:64, 0:64], _inc=(tag + "bc", 1))
                em.wait_now("vector", tag + "bc")
                em.on("vector", nc.vector.tensor_copy, khtok[:, :],
                      g.p_v[:, :].bitcast(BF16)[:, 0:64], _inc=(tag + "bce", 1))
                em.on("tensor", nc.tensor.matmul, g.p_row[:, 0:128],
                      g.onescol[0:64, :], rku[:, :], start=True, stop=True,
                      _inc=(tag + "qm", 1))
                em.wait_now("vector", tag + "qm")
                em.vop(nc.vector.tensor_copy, drow[:, :], g.p_row[:, 0:128],
                       _inc=(tag + "sq2", 1))
                em.wait("tensor", tag + "sq2", em.cnt[tag + "sq2"])
                em.on("tensor", nc.tensor.matmul, g.p_tp[:, :].bitcast(F32)[:, 0:1],
                      drow[:, :], g.onesrowf[:, 0:1], start=True, stop=True,
                      _inc=(tag + "om", 1))
                em.wait_now("vector", tag + "om")
                em.vop(nc.vector.tensor_copy, dcol[:, :],
                       g.p_tp[:, :].bitcast(F32)[:, 0:1], _inc=(tag + "pre", 1))
                em.wait("tensor", tag + "fx", em.cnt.get(tag + "fx", 0))
                em.on("tensor", nc.tensor.matmul, g.p_a[0][:, 0:128],
                      ktil[:, :], rtil[:, :], start=True, stop=True,
                      _inc=(tag + "bcc", 1))
                em.wait_now("vector", tag + "bcc")
                em.vop(nc.vector.tensor_mul, g.sqbf[:, 0:128],
                       g.p_a[0][:, 0:128], mstrict[:, :], _inc=(tag + "fx", 1))
                em.vop(nc.vector.tensor_copy, Sbf[:, :],
                       S32[:, h * 64:(h + 1) * 64], _inc=(tag + "fxe", 1))
                em.wait("tensor", tag + "fx", em.cnt[tag + "fx"])
                em.wait("tensor", tag + "fxe", em.cnt[tag + "fxe"])
                em.wait("tensor", tag + "ye", em.cnt.get(tag + "ye", 0))
                em.on("tensor", nc.tensor.matmul, g.p_y[:, 0:64],
                      g.sqbf[:, 0:128], vsl, start=True, stop=False)
                em.on("tensor", nc.tensor.matmul, g.p_y[:, 0:64],
                      rtil[:, :], Sbf[:, :], start=False, stop=True,
                      _inc=(tag + "rote", 1))
                em.wait("tensor", tag + "bce", em.cnt[tag + "bce"])
                em.wait("tensor", tag + "p1", em.cnt.get(tag + "p1", 0))
                em.on("tensor", nc.tensor.matmul, g.p_bc[0:64, 0:64],
                      khtok[:, :], vsl, start=True, stop=True,
                      _inc=(tag + "p1m", 1))
                em.wait_now("vector", tag + "p1m")
                em.vop(nc.vector.tensor_scalar_mul, g.scr2[0:64, 0:64],
                       S32[:, h * 64:(h + 1) * 64], A_sb[:, 127:128])
                em.vop(nc.vector.tensor_add, S32[:, h * 64:(h + 1) * 64],
                       g.scr2[0:64, 0:64], g.p_bc[0:64, 0:64],
                       _inc=(tag + "p1", 1))
                em.wait("vector", tag + "pre", em.cnt[tag + "pre"])
                em.wait("vector", tag + "rote", em.cnt[tag + "rote"])
                em.vop(nc.vector.tensor_scalar_mul, g.scr[:, 0:64], vsl,
                       dcol[:, :])
                em.vop(nc.vector.tensor_add, ybuf[:, h * 64:(h + 1) * 64],
                       g.p_y[:, 0:64], g.scr[:, 0:64], _inc=(tag + "ye", 1))
            em.wait("vector", tag + "ye", em.cnt[tag + "ye"])
            em.vop(nc.vector.reduce_sum, m16[:, :],
                   ybuf[:, :].rearrange("p (h j) -> p h j", j=64), axis=AX.X)
            em.vop(nc.vector.tensor_mul, g.scr.bitcast(BF16)[:, 0:1024],
                   ybuf[:, :], ybuf[:, :])
            em.vop(nc.vector.reduce_sum, sq16[:, :],
                   g.scr.bitcast(BF16)[:, 0:1024]
                   .rearrange("p (h j) -> p h j", j=64), axis=AX.X)
            em.vop(nc.vector.tensor_scalar_mul, m16[:, :], m16[:, :], 1.0 / 64)
            em.vop(nc.vector.tensor_scalar_mul, sq16[:, :], sq16[:, :], 1.0 / 64)
            em.vop(nc.vector.tensor_mul, v16[:, :], m16[:, :], m16[:, :])
            em.vop(nc.vector.tensor_sub, v16[:, :], sq16[:, :], v16[:, :],
                   _inc=(tag + "mx", 1))
            em.wait_now("scalar", tag + "mx")
            em.sop(nc.scalar.activation, v16[:, :], v16[:, :], AF.Ln,
                   bias=g.cvcol[:, 2:3], scale=1.0)
            em.sop(nc.scalar.activation, v16[:, :], v16[:, :], AF.Exp,
                   bias=0.0, scale=-0.5, _inc=(tag + "mxn", 1))
            em.wait_now("vector", tag + "mxn")
            for h in range(16):
                em.vop(nc.vector.tensor_scalar, ybuf[:, h * 64:(h + 1) * 64],
                       ybuf[:, h * 64:(h + 1) * 64],
                       m16[:, h:h + 1], v16[:, h:h + 1],
                       op0=ALU.subtract, op1=ALU.mult, _inc=(tag + "mxt", 1))
            em.wait("tensor", tag + "mxt", em.cnt[tag + "mxt"])
            for ct in range(CT):
                em.wait("tensor", tag + "ae", em.cnt.get(tag + "ae", 0) - 1)
                em.on("tensor", nc.tensor.transpose, g.tp_slots[ct % 2],
                      ybuf[:, ct * 128:(ct + 1) * 128], g.ident[:],
                      _inc=(tag + "mxe", 1))
                em.wait("vector", tag + "mxe", em.cnt[tag + "mxe"])
                em.on("vector", nc.vector.tensor_copy, ynT[:, ts(ct)],
                      g.tp_slots[ct % 2], _inc=(tag + "ae", 1))
            em.wait("tensor", tag + "ae", em.cnt[tag + "ae"])
            for ft in range(CT):
                em.wait("tensor", tag + "oe", em.cnt.get(tag + "oe", 0) - 1)
                for ct in range(CT):
                    em.on("tensor", nc.tensor.matmul, g.p_a[ft % 2][:, 0:128],
                          g.bigA[:, WO + ft * 1024 + ct * 128:
                                 WO + ft * 1024 + (ct + 1) * 128],
                          ynT[:, ts(ct)], start=(ct == 0), stop=(ct == CT - 1),
                          _inc=(tag + "rc2", 1) if ct == CT - 1 else None)
                em.wait_now("vector", tag + "rc2")
                em.vop(nc.vector.tensor_scalar, g.scr[:, 0:128],
                       g.p_a[ft % 2][:, 0:128], biasW[:, ft:ft + 1], None,
                       op0=ALU.add)
                em.vop(nc.vector.tensor_add,
                       g.xT[:, ft * T + tc * 512 + sub * 128:
                            ft * T + tc * 512 + (sub + 1) * 128],
                       g.scr[:, 0:128],
                       xch[:, ft * 513 + 1 + sub * 128: ft * 513 + 1 + (sub + 1) * 128],
                       _inc=(tag + "oe", 1))
        em.wait("tensor", tag + "oe", em.cnt[tag + "oe"])
        em.wait("vector", tag + "oe", em.cnt[tag + "oe"])
        for ct in range(CT):
            em.on("tensor", nc.tensor.matmul, g.p_row[:, :], g.onescol[:, :],
                  g.xT[:, ct * T + tc * 512: ct * T + (tc + 1) * 512],
                  start=(ct == 0), stop=(ct == CT - 1), _inc=(tag + "st1", 1))
            em.wait("vector", tag + "st2", em.cnt.get(tag + "st2", 0))
            n = em.on("vector", nc.vector.tensor_mul, g.sqbf[:, :],
                      g.xT[:, ct * T + tc * 512: ct * T + (tc + 1) * 512],
                      g.xT[:, ct * T + tc * 512: ct * T + (tc + 1) * 512],
                      _inc=(tag + "sqt", 1))
            em.wait("tensor", tag + "sqt", n)
            em.on("tensor", nc.tensor.matmul, g.p_row2[:, :], g.onescol[:, :],
                  g.sqbf[:, :], start=(ct == 0), stop=(ct == CT - 1),
                  _inc=(tag + "st2", 1))
        em.wait_now("vector", tag + "st1")
        em.wait_now("vector", tag + "st2")
        em.vop(nc.vector.tensor_scalar_mul, g.lm, g.p_row[:, :], 1.0 / D)
        em.vop(nc.vector.tensor_scalar_mul, g.lr, g.p_row2[:, :], 1.0 / D,
               _inc=(tag + "lv", 1))
        em.wait("tensor", tag + "lv", em.cnt[tag + "lv"])
        em.vop(nc.vector.tensor_mul, g.lmr, g.lm, g.lm)
        em.vop(nc.vector.tensor_sub, g.lr, g.lr, g.lmr, _inc=(tag + "lv2", 1))
        em.wait_now("scalar", tag + "lv2")
        em.sop(nc.scalar.activation, g.lr, g.lr, AF.Ln,
               bias=g.cvals[:, 1:2], scale=1.0)
        em.sop(nc.scalar.activation, g.lr, g.lr, AF.Exp,
               bias=0.0, scale=-0.5, _inc=(tag + "lnr", 1))
        em.wait_now("vector", tag + "lnr")
        em.vop(nc.vector.tensor_mul, g.lmr, g.lm, g.lr, _inc=(tag + "hl", 1))
        em.wait("tensor", tag + "hl", em.cnt[tag + "hl"])
        em.wait("tensor", tag + "bfre", em.cnt.get(tag + "bfre", 0))
        n1 = em.on("tensor", nc.tensor.matmul, g.p_bc[:, :], g.onesrowf[:, :],
                   g.lr, start=True, stop=True, _inc=(tag + "bcf", 1))
        em.wait("vector", tag + "bcf", n1)
        em.on("vector", nc.vector.tensor_copy, g.rbc, g.p_bc[:, :],
              _inc=(tag + "bfre", 1))
        em.wait("tensor", tag + "bfre", em.cnt[tag + "bfre"])
        n2 = em.on("tensor", nc.tensor.matmul, g.p_bc[:, :], g.onesrowf[:, :],
                   g.lmr, start=True, stop=True, _inc=(tag + "bcf", 1))
        em.wait("vector", tag + "bcf", n2)
        em.on("vector", nc.vector.tensor_copy, g.mrbc, g.p_bc[:, :],
              _inc=(tag + "bfre", 1))
        for ct in range(CT):
            em.vop(nc.vector.tensor_mul, g.scr,
                   g.xT[:, ct * T + tc * 512: ct * T + (tc + 1) * 512], g.rbc)
            em.vop(nc.vector.tensor_sub,
                   g.xT[:, ct * T + tc * 512: ct * T + (tc + 1) * 512],
                   g.scr, g.mrbc, _inc=("a0fin", 1))
        em.vop(nc.vector.memset, zeros128[:, :], 0.0, _inc=(tag + "z0", 1))
    em.flush("t1_main")
    es.close()




def build_l1_ffn(g, W, C4, eoT_hbm, moe_hbm, out_hbm, y_h, scr_dram):
    nc, em = g.nc, g.em
    from contextlib import ExitStack
    es = ExitStack()
    sb = lambda n_, sh, dt: es.enter_context(nc.sbuf_tensor(n_, sh, dt))
    tag = "a0"
    CAP = 512

    H0 = 0
    WD0 = 16 * 512
    XET0 = WD0 + 16 * 1024
    hbuf = lambda ft: g.bigA[:, H0 + ft * 512: H0 + (ft + 1) * 512]
    wdt = lambda ft: g.bigA[:, WD0 + ft * 1024: WD0 + (ft + 1) * 1024]
    xeT = g.bigA[:, XET0: XET0 + CT * 512]
    bigf = g.bigA[:, :].bitcast(F32)
    bigu = g.bigA[:, :].bitcast(U16)
    Ctab = bigu[:, 0:8192]
    sel_em = bigf[0:16, 4096:8192]
    C_em = bigf[0:16, 8192:12288]
    inv_em = bigf[0:16, 12288:16384]
    smajS = bigf[:, 0:4096]
    dtile = lambda e_: g.bigA[:, e_ * 520:(e_ + 1) * 520]

    gate_s = sb("m1_gate_s", [128, CT * 16], BF16)
    lsb = sb("m1_lsb", [16, 512], BF16)
    ltok = sb("m1_ltok", [128, 512], BF16)
    stok = sb("m1_stok", [128, 512], F32)
    sblk = sb("m1_sblk", [128, 512], F32)
    E8 = sb("m1_E8", [128, 16], F32)
    E8T = sb("m1_E8T", [16, 128], F32)
    LB = sb("m1_LB", [128, 128], F32)
    iotaB = sb("m1_iotaB", [128, 512], F32)
    Lrep = sb("m1_Lrep", [16, 256], F32)
    offc = sb("m1_offc", [128, 64], F32)
    jp1 = sb("m1_jp1", [128, 64], F32)
    maskc = sb("m1_maskc", [128, 16], F32)
    lo_c = sb("m1_lo", [128, 1], F32)
    hi_c = sb("m1_hi", [128, 1], F32)
    mid_c = sb("m1_mid", [128, 1], F32)
    cnt_c = sb("m1_cnt", [128, 1], F32)
    cond_c = sb("m1_cond", [128, 1], F32)
    need_c = sb("m1_need", [128, 1], F32)
    cnte = sb("m1_cnte", [16, 1], F32)
    t_d = sb("m1_td_", [128, 1], F32)
    t_n = sb("m1_tn_", [128, 1], F32)
    gemask = sb("m1_ge", [128, 512], BF16)
    gehi = sb("m1_gehi", [128, 512], BF16)
    band = sb("m1_band", [128, 512], BF16)
    scst = [sb(f"m1_scst{i}", [16, 128], F32) for i in range(2)]
    lo2 = sb("m1_lo2", [128, 64], F32)
    hi2 = sb("m1_hi2", [128, 64], F32)
    midf = sb("m1_midf", [128, 64], F32)
    mids = sb("m1_mids", [128, 64], F32)
    midu = sb("m1_midu", [128, 64], U16)
    vcp = sb("m1_vcp", [128, 64], F32)
    val = sb("m1_val", [128, 64], F32)
    cond2 = sb("m1_cond2", [128, 64], F32)
    idw = sb("m1_idw", [128, 512], U16)
    invw = sb("m1_invw", [128, 4096], U16)
    wcolsAll = sb("m1_wca", [128, 64], F32)
    eoTb = sb("m1_eoTb", [128, 8 * 520], BF16)
    gbuf = eoTb[:, :].bitcast(U16)[:, 0:1024]
    scg = eoTb[:, :].bitcast(F32)[:, 512:1024]
    selm = eoTb[:, :].bitcast(F32)[:, 1024:1536]
    rank = eoTb[:, :].bitcast(F32)[:, 1536:2048]
    wg_s = [sb(f"m1_wg{i}", [128, 2048], BF16) for i in range(2)]
    wu_s = [sb(f"m1_wu{i}", [128, 2048], BF16) for i in range(2)]
    eo = sb("m1_eo", [128, 1024], F32)
    ysb = eo[:, :]
    gathw = g.bigA[:, 9344: 9344 + 16 * 512]
    accf = bigf[:, 8768:9280]
    mts = [bigf[:, 9280 + 128 * i: 9280 + 128 * (i + 1)] for i in range(2)]
    osb = [invw[:, :].bitcast(F32)[:, 1024 * i: 1024 * (i + 1)]
           for i in range(2)]
    msb = [eoTb[:, :].bitcast(F32)[:, 1024 * i: 1024 * (i + 1)]
           for i in range(2)]
    xtk = g.bigA[:, 0:1024]

    em.on("sync", nc.sync.dma_start, E8[:], C4["E8"][:, :], _inc=(tag + "bl", 16))
    em.on("sync", nc.sync.dma_start, E8T[:], C4["E8T"][:, :], _inc=(tag + "bl", 16))
    em.on("sync", nc.sync.dma_start, LB[:], C4["LB"][:, :], _inc=(tag + "bl", 16))
    em.on("sync", nc.sync.dma_start, iotaB[:], C4["iotaB"][:, :], _inc=(tag + "bl", 16))
    em.on("sync", nc.sync.dma_start, Lrep[:], C4["Lrep"][:, :], _inc=(tag + "bl", 16))
    em.on("sync", nc.sync.dma_start, offc[:], C4["offc"][:, :], _inc=(tag + "bl", 16))
    em.on("sync", nc.sync.dma_start, jp1[:], C4["jp1"][:, :], _inc=(tag + "bl", 16))
    em.on("sync", nc.sync.dma_start, maskc[:], C4["maskc"][:, :], _inc=(tag + "bl", 16))
    em.on("gpsimd", nc.gpsimd.dma_start, gate_s[:], W["gate"][:, :],
          _inc=(tag + "thd", 16))
    em.flush("m1_setup")

    em.wait_now("tensor", tag + "thd")
    for tc in range(NCH):
        em.wait("tensor", tag + "qe", em.cnt.get(tag + "qe", 0))
        for ct in range(CT):
            em.on("tensor", nc.tensor.matmul, g.p_a[0][0:16, :],
                  gate_s[:, ct * 16:(ct + 1) * 16],
                  g.xT[:, ct * T + tc * 512: ct * T + (tc + 1) * 512],
                  start=(ct == 0), stop=(ct == CT - 1),
                  _inc=("a0om", 1) if ct == CT - 1 else None)
        em.wait_now("vector", "a0om")
        em.on("vector", nc.vector.tensor_copy, lsb[:, :], g.p_a[0][0:16, :],
              _inc=(tag + "qe", 1))
        em.wait("tensor", tag + "qe", em.cnt[tag + "qe"])
        for q in range(4):
            tt = tc * 4 + q
            em.wait("tensor", tag + "mk", em.cnt.get(tag + "mk", 0) - 1)
            em.on("tensor", nc.tensor.transpose, g.tp_slots[q % 2][:, 0:16],
                  lsb[:, q * 128:(q + 1) * 128],
                  g.ident[0:16, 0:16], _inc=(tag + "s", 1))
            em.wait("vector", tag + "s", em.cnt[tag + "s"])
            em.on("vector", nc.vector.tensor_copy,
                  ltok[:, tt * 16:(tt + 1) * 16],
                  g.tp_slots[q % 2][:, 0:16], _inc=(tag + "mk", 1))
    em.wait("vector", tag + "mk", em.cnt[tag + "mk"])
    for tt in range(NT):
        sl = ltok[:, tt * 16:(tt + 1) * 16]
        em.vop(nc.vector.reduce_max, g.scr[:, 0:1], sl, axis=AX.X)
        em.vop(nc.vector.tensor_scalar_mul, g.scr[:, 1:2], g.scr[:, 0:1], -1.0,
               _inc=(tag + "s1", 1))
        em.wait_now("scalar", tag + "s1")
        em.sop(nc.scalar.activation, g.sqbf[:, 0:16], sl, AF.Exp,
               bias=g.scr[:, 1:2], scale=1.0, _inc=(tag + "s2", 1))
        em.wait_now("vector", tag + "s2")
        em.vop(nc.vector.reduce_sum, g.scr[:, 2:3], g.sqbf[:, 0:16], axis=AX.X)
        em.vop(nc.vector.reciprocal, g.scr[:, 3:4], g.scr[:, 2:3])
        em.vop(nc.vector.tensor_scalar_mul, stok[:, tt * 16:(tt + 1) * 16],
               g.sqbf[:, 0:16], g.scr[:, 3:4], _inc=(tag + "s3", 1))
    em.wait("tensor", tag + "s3", em.cnt[tag + "s3"])
    base_s = em.cnt.get(tag + "s", 0)
    base_mk = em.cnt.get(tag + "mk", 0)
    base_bl = em.cnt[tag + "bl"]
    for tt in range(NT):
        b_, s_ = tt // 4, tt % 4
        slot = tt % 2
        pb = g.p_y if slot == 0 else g.p_bc
        if tt >= 2:
            em.wait("tensor", tag + "mk", base_mk + tt - 1)
        em.on("tensor", nc.tensor.transpose, pb[0:16, 0:128],
              stok[:, tt * 16:(tt + 1) * 16], g.identf[:],
              _inc=(tag + "s", 1))
        if tt >= 2:
            em.wait("vector", tag + "bl", base_bl + (tt - 1) * 16)
        em.wait("vector", tag + "s", base_s + tt + 1)
        em.vop(nc.vector.tensor_copy, scst[slot][:, :], pb[0:16, 0:128],
               _inc=(tag + "mk", 1))
        em.wait("sync", tag + "mk", base_mk + tt + 1)
        em.wait("sync", tag + "bl", base_bl + tt * 16)
        em.on("sync", nc.sync.dma_start,
              sblk[b_ * 16: b_ * 16 + 16, s_ * 128:(s_ + 1) * 128],
              scst[slot][:, :], _inc=(tag + "bl", 16))
    em.flush("m1_gating")

    em.wait("tensor", tag + "bl", em.cnt[tag + "bl"])
    em.wait("vector", tag + "bl", em.cnt[tag + "bl"])
    em.on("vector", nc.vector.memset, lo_c[:], 0.0)
    em.on("vector", nc.vector.memset, hi_c[:], 1.0, _inc=("vch", 1))

    def count_ge(mask_ap):
        em.vop(nc.vector.reduce_sum, cnt_c[:], mask_ap, axis=AX.X,
               _inc=(tag + "p1", 1))
        em.wait("tensor", tag + "p1", em.cnt[tag + "p1"])
        em.wait("tensor", tag + "mxn", em.cnt.get(tag + "mxn", 0))
        n1 = em.on("tensor", nc.tensor.matmul, g.p_v[0:16, 0:1], E8[:], cnt_c[:],
                   start=True, stop=True, _inc=(tag + "p1m", 1))
        em.wait("vector", tag + "p1m", n1)
        em.vop(nc.vector.tensor_copy, cnte[:], g.p_v[0:16, 0:1],
               _inc=(tag + "mx", 1))
        em.wait("tensor", tag + "mx", em.cnt[tag + "mx"])
        n2 = em.on("tensor", nc.tensor.matmul, g.p_v[0:128, 1:2], E8T[:], cnte[:],
                   start=True, stop=True, _inc=(tag + "mxn", 1))
        em.wait("vector", tag + "mxn", n2)
        em.vop(nc.vector.tensor_copy, cnt_c[:], g.p_v[:, 1:2],
               _inc=(tag + "ns", 1))

    for it in range(45):
        em.vop(nc.vector.tensor_add, mid_c[:], lo_c[:], hi_c[:])
        em.vop(nc.vector.tensor_scalar_mul, mid_c[:], mid_c[:], 0.5)
        em.vop(nc.vector.tensor_scalar, gemask[:, :], sblk[:, :], mid_c[:],
               None, op0=ALU.is_ge)
        count_ge(gemask[:, :])
        em.vop(nc.vector.tensor_scalar, cond_c[:], cnt_c[:], float(CAP), None,
               op0=ALU.is_ge, _inc=(tag + "nsp", 1))
        em.wait("vector", tag + "nsp", em.cnt[tag + "nsp"])
        em.vop(nc.vector.tensor_scalar, t_n[:], cond_c[:], -1.0, 1.0,
               op0=ALU.mult, op1=ALU.add)
        em.vop(nc.vector.tensor_sub, t_d[:], mid_c[:], lo_c[:])
        em.vop(nc.vector.tensor_mul, t_d[:], t_d[:], cond_c[:])
        em.vop(nc.vector.tensor_add, lo_c[:], lo_c[:], t_d[:])
        em.vop(nc.vector.tensor_sub, t_d[:], mid_c[:], hi_c[:])
        em.vop(nc.vector.tensor_mul, t_d[:], t_d[:], t_n[:])
        em.vop(nc.vector.tensor_add, hi_c[:], hi_c[:], t_d[:])
    em.vop(nc.vector.tensor_scalar, gehi[:, :], sblk[:, :], hi_c[:], None,
           op0=ALU.is_ge)
    count_ge(gehi[:, :])
    em.vop(nc.vector.tensor_scalar, need_c[:], cnt_c[:], -1.0, float(CAP),
           op0=ALU.mult, op1=ALU.add)
    em.vop(nc.vector.tensor_scalar, gemask[:, :], sblk[:, :], lo_c[:], None,
           op0=ALU.is_ge)
    em.vop(nc.vector.tensor_sub, band[:, :], gemask[:, :], gehi[:, :])
    em.vop(nc.vector.memset, lo_c[:], -1.0)
    em.vop(nc.vector.memset, hi_c[:], 4095.0, _inc=("vch", 1))
    for it in range(13):
        em.vop(nc.vector.tensor_add, mid_c[:], lo_c[:], hi_c[:])
        em.vop(nc.vector.tensor_scalar_mul, mid_c[:], mid_c[:], 0.5)
        em.vop(nc.vector.tensor_scalar, selm[:, :], iotaB[:, :], mid_c[:],
               None, op0=ALU.is_le)
        em.vop(nc.vector.tensor_mul, selm[:, :], selm[:, :], band[:, :])
        count_ge(selm[:, :])
        em.vop(nc.vector.tensor_tensor, cond_c[:], cnt_c[:], need_c[:],
               op=ALU.is_ge, _inc=(tag + "nsp", 1))
        em.wait("vector", tag + "nsp", em.cnt[tag + "nsp"])
        em.vop(nc.vector.tensor_scalar, t_n[:], cond_c[:], -1.0, 1.0,
               op0=ALU.mult, op1=ALU.add)
        em.vop(nc.vector.tensor_sub, t_d[:], mid_c[:], hi_c[:])
        em.vop(nc.vector.tensor_mul, t_d[:], t_d[:], cond_c[:])
        em.vop(nc.vector.tensor_add, hi_c[:], hi_c[:], t_d[:])
        em.vop(nc.vector.tensor_sub, t_d[:], mid_c[:], lo_c[:])
        em.vop(nc.vector.tensor_mul, t_d[:], t_d[:], t_n[:])
        em.vop(nc.vector.tensor_add, lo_c[:], lo_c[:], t_d[:])
    em.vop(nc.vector.tensor_scalar, selm[:, :], iotaB[:, :], hi_c[:], None,
           op0=ALU.is_le)
    em.vop(nc.vector.tensor_mul, selm[:, :], selm[:, :], band[:, :])
    em.vop(nc.vector.tensor_add, selm[:, :], selm[:, :], gehi[:, :])
    em.vop(nc.vector.memset, g.scr2[:, :], 1.0)
    em.vop(nc.vector.tensor_tensor_scan, rank[:, :], g.scr2[:, :], selm[:, :],
           0.0, op0=ALU.mult, op1=ALU.add)
    em.vop(nc.vector.reduce_sum, cnt_c[:], selm[:, :], axis=AX.X,
           _inc=(tag + "bk", 1))
    em.wait("tensor", tag + "bk", em.cnt[tag + "bk"])
    n1 = em.on("tensor", nc.tensor.matmul, g.p_v[:, 2:3], LB[:], cnt_c[:],
               start=True, stop=True, _inc=(tag + "ye", 1))
    em.wait("vector", tag + "ye", n1)
    em.vop(nc.vector.tensor_copy, cond_c[:], g.p_v[:, 2:3])
    em.vop(nc.vector.tensor_scalar_add, rank[:, :], rank[:, :], cond_c[:])
    em.vop(nc.vector.tensor_mul, rank[:, :], rank[:, :], selm[:, :])
    em.vop(nc.vector.tensor_scalar, g.scr[:, :], selm[:, :], -513.0,
           512.0, op0=ALU.mult, op1=ALU.add)
    em.vop(nc.vector.tensor_add, rank[:, :], rank[:, :], g.scr[:, :],
           _inc=(tag + "rc", 1))
    em.flush("m1_topk")

    em.wait("sync", tag + "rc", em.cnt[tag + "rc"])
    em.on("sync", nc.sync.dma_start, scr_dram[:, :], selm[:, :],
          _inc=(tag + "bl", 16))
    em.wait("sync", tag + "bl", em.cnt[tag + "bl"])
    em.on("sync", nc.sync.dma_start,
          sel_em.rearrange("p (b f) -> p b f", b=8),
          scr_dram[:, :].rearrange("(b e) f -> e b f", e=16),
          _inc=(tag + "bl", 16))
    em.wait("vector", tag + "bl", em.cnt[tag + "bl"])
    em.vop(nc.vector.memset, inv_em, 1.0)
    em.vop(nc.vector.tensor_tensor_scan, C_em, inv_em, sel_em, 0.0,
           op0=ALU.mult, op1=ALU.add)
    em.vop(nc.vector.tensor_mul, inv_em, C_em, sel_em)
    em.vop(nc.vector.tensor_scalar, sel_em, sel_em, -513.0, 512.0,
           op0=ALU.mult, op1=ALU.add)
    em.vop(nc.vector.tensor_add, inv_em, inv_em, sel_em, _inc=(tag + "ve", 1))
    base_mk = em.cnt.get(tag + "mk", 0)
    nrep = 0
    for o in range(2):
        for ch in range(8):
            pb = g.p_y if nrep % 2 == 0 else g.p_bc
            if nrep == 0:
                em.wait("tensor", tag + "ve", em.cnt[tag + "ve"])
            em.wait("tensor", tag + "mk", em.cnt[tag + "mk"] - 1)
            em.on("tensor", nc.tensor.matmul, pb[:, 0:512],
                  Lrep[:, o * 128:(o + 1) * 128],
                  C_em[:, ch * 512:(ch + 1) * 512],
                  start=True, stop=True, _inc=(tag + "s", 1))
            em.wait("vector", tag + "s", em.cnt[tag + "s"])
            em.vop(nc.vector.tensor_copy,
                   Ctab[:, o * 4096 + ch * 512: o * 4096 + (ch + 1) * 512],
                   pb[:, 0:512], _inc=(tag + "mk", 1))
            nrep += 1
    em.wait("gpsimd", tag + "mk", em.cnt[tag + "mk"])

    em.vop(nc.vector.memset, lo2[:], -1.0)
    em.vop(nc.vector.memset, hi2[:], 4095.0, _inc=("vch", 1))
    gb3 = gbuf[:, :].rearrange("p (w m) -> p w m", m=16)
    for it in range(12):
        em.vop(nc.vector.tensor_add, midf[:], lo2[:], hi2[:])
        em.vop(nc.vector.tensor_scalar_mul, midf[:], midf[:], 0.5)
        em.vop(nc.vector.tensor_copy, midu[:], midf[:])
        em.vop(nc.vector.tensor_copy, midf[:], midu[:])
        em.vop(nc.vector.tensor_add, mids[:], midf[:], offc[:])
        em.vop(nc.vector.tensor_copy, midu[:], mids[:], _inc=(tag + "ve", 1))
        em.wait("gpsimd", tag + "ve", em.cnt[tag + "ve"])
        ng = em.on("gpsimd", nc.gpsimd.indirect_copy, gbuf[:, :], Ctab,
                   midu[:, :], i_know_ap_gather_is_preferred=True,
                   _inc=(tag + "ex", 1))
        em.wait("vector", tag + "ex", ng)
        for m in range(16):
            em.vop(nc.vector.tensor_copy, vcp[:], gb3[:, :, m])
            if m == 0:
                em.vop(nc.vector.tensor_scalar_mul, val[:], vcp[:],
                       maskc[:, 0:1])
            else:
                em.vop(nc.vector.scalar_tensor_tensor, val[:], vcp[:],
                       maskc[:, m:m + 1], val[:], op0=ALU.mult, op1=ALU.add)
        em.vop(nc.vector.tensor_tensor, cond2[:], val[:], jp1[:],
               op=ALU.is_ge)
        em.vop(nc.vector.tensor_scalar, mids[:], cond2[:], -1.0, 1.0,
               op0=ALU.mult, op1=ALU.add)
        em.vop(nc.vector.tensor_sub, vcp[:], midf[:], hi2[:])
        em.vop(nc.vector.tensor_mul, vcp[:], vcp[:], cond2[:])
        em.vop(nc.vector.tensor_add, hi2[:], hi2[:], vcp[:])
        em.vop(nc.vector.tensor_sub, vcp[:], midf[:], lo2[:])
        em.vop(nc.vector.tensor_mul, vcp[:], vcp[:], mids[:])
        em.vop(nc.vector.tensor_add, lo2[:], lo2[:], vcp[:],
               _inc=(tag + "ve", 1))
    em.vop(nc.vector.tensor_copy, midu[:], hi2[:], _inc=(tag + "ve", 1))
    em.wait("sync", tag + "ve", em.cnt[tag + "ve"])
    scr_u = scr_dram[:, :].bitcast(U16)
    em.on("sync", nc.sync.dma_start, scr_u[:, 0:64], midu[:, :],
          _inc=(tag + "bl", 16))
    em.wait("sync", tag + "bl", em.cnt[tag + "bl"])
    for gp in range(8):
        em.on("sync", nc.sync.dma_start,
              idw[16 * gp:16 * gp + 16, :].rearrange("p (gr w) -> p gr w", gr=8),
              scr_u[:, 0:64].rearrange("(gr r) w -> r gr w", gr=8),
              _inc=(tag + "bl", 16))

    base_s = em.cnt[tag + "s"]
    base_mk = em.cnt[tag + "mk"]
    wrI3 = invw[0:16, :].rearrange("r (e w) -> r e w", w=256)
    for w in range(256):
        pb = g.p_y if w % 2 == 0 else g.p_bc
        em.wait("tensor", tag + "mk", em.cnt[tag + "mk"] - 1)
        em.on("tensor", nc.tensor.transpose, pb[0:16, 0:16],
              inv_em[:, w * 16:(w + 1) * 16], g.identf[0:16, 0:16],
              _inc=(tag + "s", 1))
        em.wait("vector", tag + "s", base_s + w + 1)
        em.vop(nc.vector.tensor_copy, wrI3[:, :, w], pb[0:16, 0:16],
               _inc=(tag + "mk", 1))
    em.wait("sync", tag + "mk", em.cnt[tag + "mk"])
    em.wait("sync", tag + "bl", em.cnt[tag + "bl"])
    em.on("sync", nc.sync.dma_start,
          scr_u[0:64, :].rearrange("(r c) w -> r c w", c=4),
          invw[0:16, :].rearrange("p (c w) -> p c w", c=4),
          _inc=(tag + "bl", 16))
    em.wait("sync", tag + "bl", em.cnt[tag + "bl"])
    srcw = scr_u[0:64, :].rearrange("(r c) w -> r (c w)", c=4)
    for gp in range(1, 8):
        em.on("sync", nc.sync.dma_start, invw[16 * gp:16 * gp + 16, :], srcw,
              _inc=(tag + "bl", 16))
    em.flush("m1_index")

    em.wait("sync", tag + "ex", em.cnt[tag + "ex"])
    em.wait("sync", tag + "bl", em.cnt[tag + "bl"])
    em.on("sync", nc.sync.dma_start, scr_dram[:, :], sblk[:, :],
          _inc=(tag + "bl", 16))
    em.wait("sync", tag + "bl", em.cnt[tag + "bl"])
    for gp in range(8):
        em.on("sync", nc.sync.dma_start,
              smajS[16 * gp:16 * gp + 16, :].rearrange("p (b f) -> p b f", b=8),
              scr_dram[:, :].rearrange("(b e) f -> e b f", e=16),
              _inc=(tag + "bl", 16))
    em.wait("gpsimd", tag + "bl", em.cnt[tag + "bl"])
    base_s = em.cnt[tag + "s"]
    for e in range(16):
        ew = idw[:, (e // 2) * 64 + (e % 2) * 32: (e // 2) * 64 + (e % 2) * 32 + 32]
        if e > 0:
            em.wait("gpsimd", tag + "s", base_s + 4 * e)
        ng = em.on("gpsimd", nc.gpsimd.indirect_copy, scg[:, :], smajS,
                   ew, i_know_ap_gather_is_preferred=True, _inc=(tag + "ex", 1))
        em.wait("tensor", tag + "ex", ng)
        for q in range(4):
            pb = g.p_y if q % 2 == 0 else g.p_bc
            em.wait("tensor", tag + "mk", em.cnt[tag + "mk"] - 1)
            em.on("tensor", nc.tensor.transpose, pb[:, 0:16],
                  scg[0:16, q * 128:(q + 1) * 128], g.identf[0:16, 0:16],
                  _inc=(tag + "s", 1))
            em.wait("vector", tag + "s", em.cnt[tag + "s"])
            em.vop(nc.vector.tensor_copy,
                   wcolsAll[:, e * 4 + q: e * 4 + q + 1], pb[:, e:e + 1],
                   _inc=(tag + "mk", 1))
    em.wait("gpsimd", tag + "mk", em.cnt[tag + "mk"])
    em.wait("vector", tag + "mk", em.cnt[tag + "mk"])
    em.flush("m1_scores")

    wl_t = {0: 0, 1: 0}
    wu_t = {0: 0, 1: 0}

    def load_wg(src, p, slot):
        em.wait("gpsimd", tag + "gm", em.cnt.get(tag + "gm", 0))
        wl_t[slot] = em.on(
            "gpsimd", nc.gpsimd.dma_start,
            wg_s[slot][:, :].rearrange("c (t ct f) -> c t ct f", t=2, ct=8),
            src[2 * p:2 * p + 2].rearrange("t ct c f -> c t ct f"),
            _inc=(tag + f"wl{slot}", 16))

    def load_wu(src, p, slot):
        em.wait("gpsimd", tag + "sq", em.cnt.get(tag + "sq", 0))
        wu_t[slot] = em.on(
            "gpsimd", nc.gpsimd.dma_start,
            wu_s[slot][:, :].rearrange("c (t ct f) -> c t ct f", t=2, ct=8),
            src[2 * p:2 * p + 2].rearrange("t ct c f -> c t ct f"),
            _inc=(tag + f"g{slot}", 16))

    sq2 = g.scr2[:, :].bitcast(BF16)[:, 0:512]

    def geglu_512(rhs_of, wg_src, wu_src):
        em.wait("tensor", tag + "h2", em.cnt.get(tag + "h2", 0))
        for ft in range(16):
            p, half = ft // 2, ft % 2
            slot = p % 2
            pg = g.p_a[0][:, :] if half == 0 else g.p_y[:, :]
            pu = g.p_a[1][:, :] if half == 0 else g.p_bc[:, :]
            sqb = g.sqbf[:, 0:512] if half == 0 else sq2
            if half == 0 and p + 1 < 8:
                load_wg(wg_src, p + 1, 1 - slot)
                load_wu(wu_src, p + 1, 1 - slot)
            em.wait("tensor", tag + f"wl{slot}", wl_t[slot])
            em.wait("tensor", tag + f"g{slot}", wu_t[slot])
            em.wait("tensor", tag + "hl", em.cnt.get(tag + "hl", 0) - 1)
            for ct in range(CT):
                em.on("tensor", nc.tensor.matmul, pg,
                      wg_s[slot][:, half * 1024 + ct * 128:
                                 half * 1024 + (ct + 1) * 128], rhs_of(ct),
                      start=(ct == 0), stop=(ct == CT - 1),
                      _inc=(tag + "gm", 1) if ct == CT - 1 else None)
            for ct in range(CT):
                em.on("tensor", nc.tensor.matmul, pu,
                      wu_s[slot][:, half * 1024 + ct * 128:
                                 half * 1024 + (ct + 1) * 128], rhs_of(ct),
                      start=(ct == 0), stop=(ct == CT - 1),
                      _inc=(tag + "sq", 1) if ct == CT - 1 else None)
            em.wait_now("scalar", tag + "gm")
            em.wait("scalar", tag + "hl", em.cnt.get(tag + "hl", 0) - 1)
            gn = em.sop(nc.scalar.activation, sqb, pg,
                        AF.Gelu, bias=0.0, scale=1.0, _inc=(tag + "gel", 1))
            em.wait("vector", tag + "gel", gn)
            em.wait("vector", tag + "sq", em.cnt[tag + "sq"])
            em.vop(nc.vector.tensor_mul, hbuf(ft), pu, sqb,
                   _inc=(tag + "hl", 1))
        em.wait("tensor", tag + "hl", em.cnt[tag + "hl"])

    def down_512(wd_of, out_rows_fn, wcol=None):
        for tt in range(4):
            for dh in range(2):
                pb = g.p_y if dh == 0 else g.p_bc
                em.wait("tensor", tag + "h2", em.cnt.get(tag + "h2", 0))
                for ft in range(16):
                    em.on("tensor", nc.tensor.matmul, pb[:, :],
                          g.bigA[:, H0 + ft * 512 + tt * 128:
                                 H0 + ft * 512 + (tt + 1) * 128],
                          wd_of(ft)[:, dh * 512:(dh + 1) * 512],
                          start=(ft == 0), stop=(ft == 15),
                          _inc=(tag + "h2u", 1) if ft == 15 else None)
                em.wait_now("vector", tag + "h2u")
                em.wait("vector", tag + "s", em.cnt.get(tag + "s", 0))
                if wcol is not None:
                    em.vop(nc.vector.tensor_scalar_mul,
                           eo[:, dh * 512:(dh + 1) * 512], pb[:, :],
                           wcol(tt), _inc=(tag + "h2", 1))
                else:
                    em.vop(nc.vector.tensor_copy,
                           eo[:, dh * 512:(dh + 1) * 512], pb[:, :],
                           _inc=(tag + "h2", 1))
            out_rows_fn(tt)

    load_wg(W["wsg"], 0, 0)
    load_wu(W["wsu"], 0, 0)
    for ft in range(16):
        em.on("gpsimd", nc.gpsimd.dma_start, wdt(ft), W["wsd"][ft],
              _inc=(tag + "h0", 16))
    em.wait("tensor", tag + "h0", em.cnt[tag + "h0"])
    for tc in range(NCH):
        rhs_of = (lambda _tc: (lambda ct: g.xT[
            :, ct * T + _tc * 512: ct * T + (_tc + 1) * 512]))(tc)
        geglu_512(rhs_of, W["wsg"], W["wsu"])
        if tc + 1 < NCH:
            load_wg(W["wsg"], 0, 0)
            load_wu(W["wsu"], 0, 0)

        def store_rows(tt, _tc=tc):
            em.wait_now("sync", tag + "h2")
            em.on("sync", nc.sync.dma_start,
                  out_hbm[_tc * 512 + tt * 128: _tc * 512 + (tt + 1) * 128, :],
                  eo[:, :], _inc=(tag + "am", 16))
            em.wait("vector", tag + "am", em.cnt[tag + "am"])

        down_512(wdt, store_rows)
    em.flush("m1_shared")

    em.vop(nc.vector.memset, eoTb[:, :], 0.0, _inc=(tag + "mk", 1))
    for e in range(16):
        ew = idw[:, (e // 2) * 64 + (e % 2) * 32: (e // 2) * 64 + (e % 2) * 32 + 32]
        em.wait("gpsimd", tag + "gm", em.cnt.get(tag + "gm", 0))
        em.wait("gpsimd", tag + "sq", em.cnt.get(tag + "sq", 0))
        for ct in range(CT):
            em.on("gpsimd", nc.gpsimd.indirect_copy,
                  xeT[:, ct * 512:(ct + 1) * 512],
                  g.xT[:, ct * T:(ct + 1) * T], ew,
                  i_know_ap_gather_is_preferred=True, _inc=(tag + "ex", 1))
        em.wait("gpsimd", tag + "h2u", em.cnt.get(tag + "h2u", 0))
        for ft in range(16):
            em.on("gpsimd", nc.gpsimd.dma_start, wdt(ft), W["wd"][e, ft],
                  _inc=(tag + "h0", 16))
        em.wait("tensor", tag + "ex", em.cnt[tag + "ex"])
        wg_src = W["wg"][e]
        wu_src = W["wu"][e]
        if e == 0:
            load_wg(wg_src, 0, 0)
            load_wu(wu_src, 0, 0)
        em.wait("tensor", tag + "mk", em.cnt[tag + "mk"])
        geglu_512(lambda ct: xeT[:, ct * 512:(ct + 1) * 512], wg_src, wu_src)
        if e + 1 < 16:
            load_wg(W["wg"][e + 1], 0, 0)
            load_wu(W["wu"][e + 1], 0, 0)
        em.wait("tensor", tag + "h0", em.cnt[tag + "h0"])
        em.wait("vector", tag + "am", em.cnt.get(tag + "am", 0))

        def eoT_rows(tt, _e=e):
            em.wait("tensor", tag + "mk", em.cnt[tag + "mk"])
            em.wait("tensor", tag + "h2", em.cnt[tag + "h2"])
            for dt in range(8):
                pb2 = g.p_a[0] if dt < 4 else g.p_a[1]
                col = (dt % 4) * 128
                em.on("tensor", nc.tensor.transpose, pb2[:, col:col + 128],
                      eo[:, dt * 128:(dt + 1) * 128], g.identf[:],
                      _inc=(tag + "s", 1))
            em.wait("vector", tag + "s", em.cnt[tag + "s"])
            for dt in range(8):
                pb2 = g.p_a[0] if dt < 4 else g.p_a[1]
                col = (dt % 4) * 128
                em.vop(nc.vector.tensor_copy,
                       eoTb[:, dt * 520 + tt * 128: dt * 520 + (tt + 1) * 128],
                       pb2[:, col:col + 128],
                       _inc=(tag + "mk", 1) if dt == 7 else None)
            em.wait("vector", tag + "s", em.cnt[tag + "s"])

        down_512(wdt, eoT_rows, wcol=lambda tt: wcolsAll[:, e * 4 + tt:
                                                         e * 4 + tt + 1])
        em.wait("sync", tag + "mk", em.cnt[tag + "mk"])
        em.wait("sync", tag + "am", em.cnt.get(tag + "am", 0))
        em.on("sync", nc.sync.dma_start, eoT_hbm[e], eoTb[:, :],
              _inc=(tag + "am", 16))
    em.flush("m1_experts")

    em.wait("sync", tag + "am", em.cnt[tag + "am"])
    for dt in range(8):
        for e in range(16):
            if dt > 0:
                em.wait("sync", tag + "pv", em.cnt.get(tag + "pv", 0))
            em.on("sync", nc.sync.dma_start, dtile(e),
                  eoT_hbm[e][:, dt * 520:(dt + 1) * 520],
                  _inc=(tag + "nr", 16))
        em.wait("gpsimd", tag + "nr", em.cnt[tag + "nr"])
        for tc in range(8):
            em.wait("gpsimd", tag + "ye", em.cnt.get(tag + "ye", 0))
            for e in range(16):
                ng = em.on("gpsimd", nc.gpsimd.indirect_copy,
                           gathw[:, e * 512:(e + 1) * 512], dtile(e),
                           invw[:, e * 256 + tc * 32: e * 256 + (tc + 1) * 32],
                           i_know_ap_gather_is_preferred=True,
                           _inc=(tag + "pv", 1))
            em.wait("tensor", tag + "pv", ng)
            for e in range(16):
                em.on("tensor", nc.tensor.matmul, g.p_a[0][:, :],
                      g.ident[:], gathw[:, e * 512:(e + 1) * 512],
                      start=(e == 0), stop=(e == 15),
                      _inc=(tag + "ye", 1) if e == 15 else None)
            em.wait("vector", tag + "ye", em.cnt[tag + "ye"])
            em.wait("vector", tag + "s", em.cnt[tag + "s"])
            em.vop(nc.vector.tensor_copy, accf[:, :], g.p_a[0][:, :],
                   _inc=(tag + "h2", 1))
            em.wait("tensor", tag + "h2", em.cnt[tag + "h2"])
            for s_ in range(4):
                pb = g.p_y if s_ % 2 == 0 else g.p_bc
                em.wait("tensor", tag + "mk", em.cnt[tag + "mk"] - 1)
                em.on("tensor", nc.tensor.transpose, pb[:, 0:128],
                      accf[:, s_ * 128:(s_ + 1) * 128], g.identf[:],
                      _inc=(tag + "s", 1))
                em.wait("vector", tag + "wm", em.cnt.get(tag + "wm", 0) - 16)
                em.wait("vector", tag + "s", em.cnt[tag + "s"])
                em.vop(nc.vector.tensor_copy, mts[s_ % 2][:, :],
                       pb[:, 0:128], _inc=(tag + "mk", 1))
                em.wait("sync", tag + "mk", em.cnt[tag + "mk"])
                em.wait("sync", tag + "wm", em.cnt.get(tag + "wm", 0))
                em.on("sync", nc.sync.dma_start,
                      moe_hbm[tc * 512 + s_ * 128: tc * 512 + (s_ + 1) * 128,
                              dt * 128:(dt + 1) * 128],
                      mts[s_ % 2][:, :], _inc=(tag + "wm", 16))
    em.flush("m1_combine")

    for tt in range(NT):
        oslot = tt % 2
        em.wait("sync", tag + "shc", em.cnt.get(tag + "shc", 0))
        if tt == 0:
            em.wait("sync", tag + "wm", em.cnt.get(tag + "wm", 0))
        ld = em.on("sync", nc.sync.dma_start, osb[oslot][:, :],
                   out_hbm[tt * 128:(tt + 1) * 128, :], _inc=(tag + "nre", 16))
        ld = em.on("sync", nc.sync.dma_start, msb[oslot][:, :],
                   moe_hbm[tt * 128:(tt + 1) * 128, :], _inc=(tag + "nre", 16))
        for ct in range(CT):
            em.wait("tensor", tag + "fxe", em.cnt.get(tag + "fxe", 0) - 1)
            em.on("tensor", nc.tensor.transpose, g.tp_slots[ct % 2],
                  g.xT[:, ct * T + tt * 128: ct * T + (tt + 1) * 128],
                  g.ident[:], _inc=(tag + "fx", 1))
            em.wait("vector", tag + "fx", em.cnt[tag + "fx"])
            em.on("vector", nc.vector.tensor_copy, xtk[:, ts(ct)],
                  g.tp_slots[ct % 2], _inc=(tag + "fxe", 1))
        em.wait("vector", tag + "nre", ld)
        pre4 = osb[oslot][:, :]
        em.vop(nc.vector.tensor_add, pre4, osb[oslot][:, :], msb[oslot][:, :])
        em.vop(nc.vector.tensor_add, pre4, pre4, xtk[:, :])
        em.vop(nc.vector.reduce_sum, g.scr[:, 0:1], pre4, axis=AX.X)
        em.vop(nc.vector.tensor_scalar_mul, g.scr[:, 0:1], g.scr[:, 0:1],
               1.0 / D)
        em.vop(nc.vector.tensor_mul, g.scr2[:, :], pre4[:, 0:512],
               pre4[:, 0:512])
        em.vop(nc.vector.reduce_sum, g.scr[:, 1:2], g.scr2[:, :], axis=AX.X)
        em.vop(nc.vector.tensor_mul, g.scr2[:, :], pre4[:, 512:1024],
               pre4[:, 512:1024])
        em.vop(nc.vector.reduce_sum, g.scr[:, 2:3], g.scr2[:, :], axis=AX.X)
        em.vop(nc.vector.tensor_add, g.scr[:, 1:2], g.scr[:, 1:2],
               g.scr[:, 2:3])
        em.vop(nc.vector.tensor_scalar_mul, g.scr[:, 1:2], g.scr[:, 1:2],
               1.0 / D)
        em.vop(nc.vector.tensor_mul, g.scr[:, 2:3], g.scr[:, 0:1],
               g.scr[:, 0:1])
        em.vop(nc.vector.tensor_sub, g.scr[:, 1:2], g.scr[:, 1:2],
               g.scr[:, 2:3], _inc=(tag + "lv", 1))
        em.wait_now("scalar", tag + "lv")
        em.sop(nc.scalar.activation, g.scr[:, 1:2], g.scr[:, 1:2], AF.Ln,
               bias=g.cvcol[:, 1:2], scale=1.0)
        em.sop(nc.scalar.activation, g.scr[:, 1:2], g.scr[:, 1:2], AF.Exp,
               bias=0.0, scale=-0.5, _inc=(tag + "lnr", 1))
        em.wait_now("vector", tag + "lnr")
        em.wait("vector", tag + "wm", em.cnt.get(tag + "wm", 0))
        em.vop(nc.vector.tensor_scalar, ysb[:, :], pre4,
               g.scr[:, 0:1], g.scr[:, 1:2],
               op0=ALU.subtract, op1=ALU.mult, _inc=(tag + "shc", 1))
        em.wait("sync", tag + "shc", em.cnt[tag + "shc"])
        em.on("sync", nc.sync.dma_start, y_h[ts(tt), :], ysb[:, :],
              _inc=(tag + "wm", 16))
    em.wait_now("sync", tag + "wm")
    em.flush("m1_final")
    es.close()



def build_program(nphases=1):
    nc = bass.Bass("TRN2", target_bir_lowering=False)
    x0_h = nc.dram_tensor("x0", [T, D], F32, kind="ExternalInput")
    y_h = nc.dram_tensor("y", [T, D], F32, kind="ExternalOutput")
    Wh = {}
    for name, shape in [("a0_wq", [16, CT, 128, 64]), ("a0_wk", [16, CT, 128, 64]),
                        ("a0_wv", [16, CT, 128, 64]), ("a0_wo", [CT, 128, CT * 128]),
                        ("a0_qb", [128, 8]), ("a0_kb", [128, 8]),
                        ("a0_vb", [1, D]), ("a0_ob", [128, 8])]:
        Wh[name] = nc.dram_tensor(name, shape, F32, kind="ExternalInput")

    for name, shape in [("f0_wqkv", [2, 8, CT, 128, 192]),
                        ("f0_qs", [2, 64, 1]), ("f0_ks", [2, 64, 1]),
                        ("f0_wo", [2, 4, 128, D]),
                        ("f0_lg", [2, 128, 8]), ("f0_lb", [2, 128, 8]),
                        ("f0_w1a", [2, 22, CT, 128, 128]),
                        ("f0_w1g", [2, 22, CT, 128, 128]),
                        ("f0_b1a", [2, 128, 22]), ("f0_b1g", [2, 128, 22]),
                        ("f0_w2", [2, 22, 128, D]), ("f0_b2", [2, 128, 8])]:
        Wh[name] = nc.dram_tensor(name, shape, F32, kind="ExternalInput")
    z1_hbm = nc.dram_tensor("z1_hbm", [128, CT * T], BF16, kind="Internal")

    consts = make_consts(nc)
    em = Em(nc)
    g = setup_common(nc, em, consts)
    stage_xT(g, x0_h)
    run_finalize = nphases < 4
    if nphases >= 1:
        build_l0_attn(g, dict(wq=Wh["a0_wq"], wk=Wh["a0_wk"], wv=Wh["a0_wv"],
                              wo=Wh["a0_wo"], qb=Wh["a0_qb"], kb=Wh["a0_kb"],
                              vb=Wh["a0_vb"], ob=Wh["a0_ob"]), consts["qpos"])
    if nphases >= 2:
        build_l0_ffn(g, dict(wqkv=Wh["f0_wqkv"], qs=Wh["f0_qs"], ks=Wh["f0_ks"],
                             wo=Wh["f0_wo"], lg=Wh["f0_lg"], lb=Wh["f0_lb"],
                             w1a=Wh["f0_w1a"], w1g=Wh["f0_w1g"],
                             b1a=Wh["f0_b1a"], b1g=Wh["f0_b1g"],
                             w2=Wh["f0_w2"], b2=Wh["f0_b2"]),
                     z1_hbm, consts)
    if nphases >= 3:
        for name, shape in [("t1_maa5", [128, 40]), ("t1_td", [128, 8]),
                            ("t1_mw1", [8, 128, 128]), ("t1_mw2", [32, 4096]),
                            ("t1_dw1", [8, 128, 64]), ("t1_dw2", [64, 1024]),
                            ("t1_u", [64, 16]), ("t1_wr", [8, 8, 128, 128]),
                            ("t1_wk", [8, 8, 128, 128]), ("t1_wv", [1024, 1024]),
                            ("t1_wo", [8, 8, 128, 128]),
                            ("t1_lnw", [128, 8]), ("t1_lnb", [128, 8])]:
            Wh[name] = nc.dram_tensor(name, shape, F32, kind="ExternalInput")
        build_l1_attn(g, dict(maa5=Wh["t1_maa5"], td=Wh["t1_td"],
                              mw1=Wh["t1_mw1"], mw2=Wh["t1_mw2"],
                              dw1=Wh["t1_dw1"], dw2=Wh["t1_dw2"],
                              u=Wh["t1_u"], wr=Wh["t1_wr"], wk=Wh["t1_wk"],
                              wv=Wh["t1_wv"], wo=Wh["t1_wo"],
                              lnw=Wh["t1_lnw"], lnb=Wh["t1_lnb"]), z1_hbm)
    if nphases >= 4:
        for name, shape, dt_ in [
                ("m1_gate", [128, 128], F32),
                ("m1_wg", [16, 16, 8, 128, 128], F32),
                ("m1_wu", [16, 16, 8, 128, 128], F32),
                ("m1_wd", [16, 16, 128, 1024], F32),
                ("m1_wsg", [16, 8, 128, 128], F32),
                ("m1_wsu", [16, 8, 128, 128], F32),
                ("m1_wsd", [16, 128, 1024], F32)]:
            Wh[name] = nc.dram_tensor(name, shape, dt_, kind="ExternalInput")
        eoT_hbm = nc.dram_tensor("eoT_hbm", [16, 128, 8 * 520], BF16,
                                 kind="Internal")
        moe_hbm = nc.dram_tensor("moe_hbm", [T, D], F32, kind="Internal")
        out_hbm = nc.dram_tensor("m1_out", [T, D], F32, kind="Internal")
        build_l1_ffn(g, dict(gate=Wh["m1_gate"], wg=Wh["m1_wg"],
                             wu=Wh["m1_wu"], wd=Wh["m1_wd"],
                             wsg=Wh["m1_wsg"], wsu=Wh["m1_wsu"],
                             wsd=Wh["m1_wsd"]),
                     dict(E8=consts["E8"], E8T=consts["E8T"], LB=consts["LB"],
                          iotaB=consts["iotaB"], Lrep=consts["Lrep"],
                          offc=consts["offc"], jp1=consts["jp1"],
                          maskc=consts["maskc"]),
                     eoT_hbm, moe_hbm, out_hbm, y_h,
                     nc.dram_tensor("m1_scr", [128, 512], F32, kind="Internal"))
    if run_finalize:
        finalize_out(g, y_h, deps=("a0fin",))
    return nc


_NC_CACHE = {}


def get_program(nphases):
    if nphases not in _NC_CACHE:
        _NC_CACHE[nphases] = build_program(nphases)
    return _NC_CACHE[nphases]


def kernel(**inputs):
    w = prep_weights(inputs)
    x = np.ascontiguousarray(np.asarray(inputs["L0_attn_x"]), np.float32)
    nc = get_program(4)
    in_maps = []
    for b in range(8):
        m = {"x0": np.ascontiguousarray(x[b])}
        m.update(w)
        in_maps.append(m)
    res = run_bass_kernel_spmd(nc, in_maps, core_ids=list(range(8)))
    out = np.stack([np.asarray(r["y"]) for r in res.results], axis=0)
    return out.astype(np.float32)


# version 05
# speedup vs baseline: 1.2065x; 1.2065x over previous
import numpy as np
import ml_dtypes
import concourse.bass as bass
import concourse.mybir as mybir
from concourse.bass_utils import run_bass_kernel_spmd
from concourse.bass import IndirectOffsetOnAxis

F32 = mybir.dt.float32
BF16 = mybir.dt.bfloat16
U32 = mybir.dt.uint32
U16 = mybir.dt.uint16
AF = mybir.ActivationFunctionType
ALU = mybir.AluOpType
AX = mybir.AxisListType

T = 4096
D = 1024
CT = D // 128
NT = T // 128
NCH = T // 512

ENGINES = ("tensor", "vector", "scalar", "sync", "gpsimd")


class Em:

    def __init__(self, nc):
        self.nc = nc
        self.q = {e: [] for e in ENGINES}
        self.sems = {}
        self.cnt = {}
        self.nblk = 0

    def sem(self, name):
        if name not in self.sems:
            self.sems[name] = self.nc.alloc_semaphore(name)
            self.cnt[name] = 0
        return self.sems[name]

    def on(self, eng, fn, *args, _inc=None, **kwargs):
        if _inc is not None:
            name, amt = _inc
            sem = self.sem(name)
            self.cnt[name] += amt
            self.q[eng].append((fn, args, kwargs, (sem, amt)))
            return self.cnt[name]
        self.q[eng].append((fn, args, kwargs, None))
        return None

    def vop(self, fn, *args, _inc=None, **kwargs):
        self.wait("vector", "vch", self.cnt.get("vch", 0))
        if _inc is not None:
            n = self.on("vector", fn, *args, _inc=_inc, **kwargs)
            self.wait("vector", _inc[0], n)
            return n
        return self.on("vector", fn, *args, _inc=("vch", 1), **kwargs)

    def sop(self, fn, *args, _inc=None, **kwargs):
        self.wait("scalar", "sch", self.cnt.get("sch", 0))
        if _inc is not None:
            n = self.on("scalar", fn, *args, _inc=_inc, **kwargs)
            self.wait("scalar", _inc[0], n)
            return n
        return self.on("scalar", fn, *args, _inc=("sch", 1), **kwargs)

    def wait(self, eng, semname, val):
        if val is None or val <= 0:
            return
        sem = self.sem(semname)
        self.q[eng].append(("WAIT", (sem, val), {}, None))

    def wait_now(self, eng, semname):
        self.wait(eng, semname, self.cnt.get(semname, 0))

    def flush(self, name=None):
        nc = self.nc
        q = self.q
        self.q = {e: [] for e in ENGINES}
        if not any(q[e] for e in ENGINES):
            return
        self.nblk += 1
        name = f"{name}{self.nblk}" if name else f"blk{self.nblk}"

        def runner(ops):
            def body(eng):
                for fn, args, kwargs, inc in ops:
                    if fn == "WAIT":
                        eng.wait_ge(args[0], args[1])
                    else:
                        instr = fn(*args, **kwargs)
                        if inc is not None:
                            instr.then_inc(inc[0], inc[1])
            return body

        with nc.Block(name) as blk:
            for e in ENGINES:
                if q[e]:
                    getattr(blk, e)(runner(q[e]))


def ts(i, n=128):
    return slice(i * n, (i + 1) * n)



def prep_weights(inp):
    w = {}
    f32 = lambda a: np.ascontiguousarray(np.asarray(a), np.float32)
    qw = f32(inp["L0_attn_q_w"])
    kvw = f32(inp["L0_attn_kv_w"])
    head_tiles = lambda m: np.ascontiguousarray(
        m.reshape(CT, 128, 16, 64).transpose(2, 0, 1, 3))
    w["a0_wq"] = head_tiles(qw)
    w["a0_wk"] = head_tiles(kvw[:, :D])
    w["a0_wv"] = head_tiles(kvw[:, D:])
    w["a0_wo"] = np.ascontiguousarray(
        f32(inp["L0_attn_o_w"]).reshape(CT, 128, CT, 128).transpose(2, 1, 0, 3)
    ).reshape(CT, 128, CT * 128)
    w["a0_qb"] = np.ascontiguousarray(f32(inp["L0_attn_q_b"]).reshape(8, 128).T)
    w["a0_kb"] = np.ascontiguousarray(f32(inp["L0_attn_kv_b"][:D]).reshape(8, 128).T)
    w["a0_vb"] = f32(inp["L0_attn_kv_b"][D:]).reshape(1, D)
    w["a0_ob"] = np.ascontiguousarray(f32(inp["L0_attn_o_b"]).reshape(8, 128).T)

    wqkv = f32(inp["L0_ffn_wqkv"])
    hq = wqkv[:, :, 0:512].reshape(2, CT, 128, 8, 64)
    hk = wqkv[:, :, 512:1024].reshape(2, CT, 128, 8, 64)
    hv = wqkv[:, :, 1024:1536].reshape(2, CT, 128, 8, 64)
    w["f0_wqkv"] = np.ascontiguousarray(
        np.concatenate([hq, hk, hv], axis=-1).transpose(0, 3, 1, 2, 4))
    w["f0_qs"] = f32(inp["L0_ffn_q_scale"]).reshape(2, 64, 1)
    w["f0_ks"] = f32(inp["L0_ffn_k_scale"]).reshape(2, 64, 1)
    w["f0_wo"] = f32(inp["L0_ffn_wo"]).reshape(2, 4, 128, 1024)
    w["f0_lg"] = np.ascontiguousarray(
        f32(inp["L0_ffn_ln_g"]).reshape(2, 8, 128).transpose(0, 2, 1))
    w["f0_lb"] = np.ascontiguousarray(
        f32(inp["L0_ffn_ln_b"]).reshape(2, 8, 128).transpose(0, 2, 1))
    HID, FCP = 2730, 22
    w1 = f32(inp["L0_ffn_w1"])
    w1a = np.zeros((2, D, FCP * 128), np.float32)
    w1g = np.zeros((2, D, FCP * 128), np.float32)
    w1a[:, :, :HID] = w1[:, :, :HID]
    w1g[:, :, :HID] = w1[:, :, HID:]
    tile_lhs = lambda m: np.ascontiguousarray(
        m.reshape(2, CT, 128, FCP, 128).transpose(0, 3, 1, 2, 4))
    w["f0_w1a"] = tile_lhs(w1a)
    w["f0_w1g"] = tile_lhs(w1g)
    b1 = f32(inp["L0_ffn_b1"])
    b1a = np.zeros((2, FCP * 128), np.float32)
    b1g = np.zeros((2, FCP * 128), np.float32)
    b1a[:, :HID] = b1[:, :HID]
    b1g[:, :HID] = b1[:, HID:]
    w["f0_b1a"] = np.ascontiguousarray(
        b1a.reshape(2, FCP, 128).transpose(0, 2, 1))
    w["f0_b1g"] = np.ascontiguousarray(
        b1g.reshape(2, FCP, 128).transpose(0, 2, 1))
    w2 = f32(inp["L0_ffn_w2"])
    w2p = np.zeros((2, FCP * 128, D), np.float32)
    w2p[:, :HID, :] = w2
    w["f0_w2"] = np.ascontiguousarray(w2p.reshape(2, FCP, 128, D))
    w["f0_b2"] = np.ascontiguousarray(
        f32(inp["L0_ffn_b2"]).reshape(2, 8, 128).transpose(0, 2, 1))

    gat = lambda k: f32(inp["L1_attn_" + k])
    maa5 = np.stack([gat("time_maa_x"), gat("time_maa_w"), gat("time_maa_k"),
                     gat("time_maa_v"), gat("time_maa_r")])
    w["t1_maa5"] = np.ascontiguousarray(
        maa5.reshape(5, 8, 128).transpose(2, 0, 1).reshape(128, 40))
    w["t1_td"] = np.ascontiguousarray(gat("time_decay").reshape(8, 128).T)
    w["t1_mw1"] = np.ascontiguousarray(gat("time_maa_w1").reshape(8, 128, 128))
    w["t1_mw2"] = np.ascontiguousarray(
        gat("time_maa_w2").transpose(1, 0, 2).reshape(32, 4096))
    w["t1_dw1"] = np.ascontiguousarray(gat("time_decay_w1").reshape(8, 128, 64))
    w["t1_dw2"] = gat("time_decay_w2")
    w["t1_u"] = np.ascontiguousarray(gat("time_faaaa").T.reshape(64, 16))
    tile4 = lambda m: np.ascontiguousarray(
        m.reshape(8, 128, 8, 128).transpose(2, 0, 1, 3))
    w["t1_wr"] = tile4(gat("W_r"))
    w["t1_wk"] = tile4(gat("W_k"))
    w["t1_wv"] = gat("W_v")
    w["t1_wo"] = tile4(gat("W_o"))
    w["t1_lnw"] = np.ascontiguousarray(gat("ln_w").reshape(8, 128).T)
    w["t1_lnb"] = np.ascontiguousarray(gat("ln_b").reshape(8, 128).T)

    gm = lambda k: f32(inp["L1_ffn_" + k])
    w["m1_gate"] = np.ascontiguousarray(
        gm("gate_w").T.reshape(8, 128, 16).transpose(1, 0, 2).reshape(128, 128))
    bf = lambda m: np.ascontiguousarray(m).astype(ml_dtypes.bfloat16)
    exp_tiles = lambda m: bf(
        m.transpose(0, 2, 1).reshape(16, 8, 128, 16, 128)
        .transpose(0, 3, 1, 2, 4)
        .reshape(16, 8, 2, 8, 128, 128).transpose(0, 1, 4, 2, 3, 5)
        .reshape(16, 8, 128, 2048))
    w["m1_wg"] = exp_tiles(gm("Wg"))
    w["m1_wu"] = exp_tiles(gm("Wu"))
    w["m1_wd"] = bf(
        gm("Wd").transpose(0, 2, 1).reshape(16, 16, 128, 1024))
    sh_tiles = lambda m: bf(
        m.T.reshape(8, 128, 16, 128).transpose(2, 0, 1, 3)
        .reshape(8, 2, 8, 128, 128).transpose(0, 3, 1, 2, 4)
        .reshape(8, 128, 2048))
    w["m1_wsg"] = sh_tiles(gm("Ws_g"))
    w["m1_wsu"] = sh_tiles(gm("Ws_u"))
    w["m1_wsd"] = bf(gm("Ws_d").T.reshape(16, 128, 1024))
    for k in ("a0_wq", "a0_wk", "a0_wv", "a0_wo", "f0_wqkv", "f0_wo",
              "f0_w1a", "f0_w1g", "f0_w2"):
        w[k] = w[k].astype(ml_dtypes.bfloat16)
    return w


def _rotary_tables(HD=64, WINDOW=512):
    half = np.arange(0, HD, 2, dtype=np.float32)
    inv_freq = 1.0 / 10000.0 ** (half / HD)
    t = np.arange(T, dtype=np.float32)
    freqs = t[:, None] * inv_freq[None, :]
    freqs = np.concatenate([freqs, freqs], axis=-1)
    base = (half + 0.4 * HD) / (1.4 * HD)
    power = (t - T // 2) / float(WINDOW // 2)
    scale = base[None, :] ** power[:, None]
    scale = np.concatenate([scale, scale], axis=-1)
    cos, sin = np.cos(freqs), np.sin(freqs)
    f = lambda m: np.ascontiguousarray(m.T, np.float32)
    return f(cos * scale), f(sin * scale), f(cos / scale), f(sin / scale)


def make_consts(nc):
    c = {}
    c["qpos"] = nc.inline_tensor(
        np.arange(512, dtype=np.float32).reshape(1, 512), name="qpos")
    c["kpos"] = nc.inline_tensor(
        np.ascontiguousarray(
            np.arange(NT, dtype=np.float32)[None, :] * 128
            + np.arange(128, dtype=np.float32)[:, None]), name="kpos")
    c["ident"] = nc.inline_tensor(np.eye(128, dtype=np.float32), name="identc")
    CSQ, SSQ, CSK, SSK = _rotary_tables()
    c["rotq"] = nc.inline_tensor(
        np.ascontiguousarray(np.stack([CSQ, SSQ])), name="rotq")
    c["rotk"] = nc.inline_tensor(
        np.ascontiguousarray(np.stack([CSK, SSK])), name="rotk")
    p = np.arange(128)
    c["E8"] = nc.inline_tensor(
        (p[:, None] % 16 == np.arange(16)[None, :]).astype(np.float32),
        name="m1E8")
    c["E8T"] = nc.inline_tensor(np.ascontiguousarray(
        (p[:, None] % 16 == np.arange(16)[None, :]).astype(np.float32).T),
        name="m1E8T")
    c["LB"] = nc.inline_tensor(
        ((p[:, None] % 16 == p[None, :] % 16)
         & (p[:, None] < p[None, :])).astype(np.float32), name="m1LB")
    c["iotaB"] = nc.inline_tensor(
        ((p // 16)[:, None] * 512.0
         + np.arange(512)[None, :]).astype(np.float32), name="m1iotaB")
    c["ebm1"] = nc.inline_tensor(
        ((p % 16) * 512.0 - 1.0).astype(np.float32).reshape(128, 1),
        name="m1ebm1")
    c["tids"] = nc.inline_tensor(
        (np.arange(NT)[None, :] * 128 + p[:, None]).astype(np.uint32),
        name="m1tids")
    c["Lrep"] = nc.inline_tensor(np.ascontiguousarray(np.concatenate(
        [(np.arange(16)[:, None] == 2 * (p[None, :] // 16) + o)
         .astype(np.float32) for o in range(2)], axis=1)), name="m1Lrep")
    w64 = np.arange(64)
    c["offc"] = nc.inline_tensor(np.ascontiguousarray(np.broadcast_to(
        (w64 >= 32) * 4096.0, (128, 64))).astype(np.float32), name="m1offc")
    c["jp1"] = nc.inline_tensor(
        ((w64[None, :] * 16 + (p % 16)[:, None] + 1)
         - 512.0 * (w64[None, :] >= 32)).astype(np.float32), name="m1jp1")
    c["maskc"] = nc.inline_tensor(
        ((p % 16)[:, None] == np.arange(16)[None, :]).astype(np.float32),
        name="m1maskc")
    c["cvals"] = nc.inline_tensor(
        np.array([[1e-20, 1e-5, 0.00064, 0.0, 1e-24, 0.5, 1.0, 0.0]],
                 dtype=np.float32), name="cvalsc")
    return c



class Ctx:
    pass


def setup_common(nc, em, consts):
    g = Ctx()
    g.nc, g.em = nc, em
    g.xT = nc.alloc_sbuf_tensor("xT", [128, CT * T], BF16)
    g.bigA = nc.alloc_sbuf_tensor("bigA", [128, CT * T], BF16)
    g.identf = nc.alloc_sbuf_tensor("identf", [128, 128], F32)
    g.ident = nc.alloc_sbuf_tensor("ident", [128, 128], BF16)
    g.onescol = nc.alloc_sbuf_tensor("onescol", [128, 1], BF16)
    g.onesrow = nc.alloc_sbuf_tensor("onesrow", [1, 128], BF16)
    g.ones2 = nc.alloc_sbuf_tensor("ones2", [2, 128], BF16)
    g.onesrowf = nc.alloc_sbuf_tensor("onesrowf", [1, 128], F32)
    g.kpos = nc.alloc_sbuf_tensor("kpos_sb", [128, NT], F32)
    g.p_a = [nc.alloc_psum_tensor(f"p_a{i}", [128, 512], F32) for i in range(2)]
    g.p_y = nc.alloc_psum_tensor("p_y", [128, 512], F32)
    g.p_bc = nc.alloc_psum_tensor("p_bc", [128, 512], F32)
    g.p_row = nc.alloc_psum_tensor("p_row", [1, 512], F32)
    g.p_row2 = nc.alloc_psum_tensor("p_row2", [1, 512], F32)
    g.p_v = nc.alloc_psum_tensor("p_v", [128, 128], F32)
    g.p_tp = nc.alloc_psum_tensor("p_tp", [128, 128], BF16)
    g.tp_slots = [g.p_tp[:, 0:128], g.p_v[:, :].bitcast(BF16)[:, 0:128]]
    g.cvals = nc.alloc_sbuf_tensor("cvals", [1, 8], F32)
    g.scr = nc.alloc_sbuf_tensor("g_scr", [128, 512], F32)[:, :]
    g.scr2 = nc.alloc_sbuf_tensor("g_scr2", [128, 512], F32)[:, :]
    g.sqbf = nc.alloc_sbuf_tensor("g_sqbf", [128, 512], BF16)[:, :]
    g.rbc = nc.alloc_sbuf_tensor("g_rbc", [128, 512], F32)[:, :]
    g.mrbc = nc.alloc_sbuf_tensor("g_mrbc", [128, 512], F32)[:, :]
    g.rbc64 = nc.alloc_sbuf_tensor("g_rbc64", [64, 512], F32)[:, :]
    g.recbc = nc.alloc_sbuf_tensor("g_recbc", [64, 512], F32)[:, :]
    g.rowa = nc.alloc_sbuf_tensor("g_rowa", [1, 512], F32)[:, :]
    g.rowb = nc.alloc_sbuf_tensor("g_rowb", [1, 512], F32)[:, :]
    g.lm = nc.alloc_sbuf_tensor("g_lm", [1, 512], F32)[:, :]
    g.lr = nc.alloc_sbuf_tensor("g_lr", [1, 512], F32)[:, :]
    g.lmr = nc.alloc_sbuf_tensor("g_lmr", [1, 512], F32)[:, :]
    g.scal = nc.alloc_sbuf_tensor("g_scal", [1, 64], F32)
    g.cvcol = nc.alloc_sbuf_tensor("g_cvcol", [128, 4], F32)

    em.on("sync", nc.sync.dma_start, g.identf[:], consts["ident"][:, :],
          _inc=("a0bl", 16))
    em.on("sync", nc.sync.dma_start, g.kpos[:], consts["kpos"][:, :],
          _inc=("a0bl", 16))
    em.on("sync", nc.sync.dma_start, g.cvals[:], consts["cvals"][:, :],
          _inc=("a0bl", 16))
    em.wait_now("vector", "a0bl")
    em.on("vector", nc.vector.tensor_copy, g.ident[:], g.identf[:])
    em.on("vector", nc.vector.memset, g.onescol[:], 1.0)
    em.on("vector", nc.vector.memset, g.onesrow[:], 1.0)
    em.on("vector", nc.vector.memset, g.ones2[:], 1.0)
    em.on("vector", nc.vector.memset, g.onesrowf[:], 1.0)
    em.on("vector", nc.vector.memset, g.cvcol[:, 0:1], 1e-20)
    em.on("vector", nc.vector.memset, g.cvcol[:, 1:2], 1e-5)
    em.on("vector", nc.vector.memset, g.cvcol[:, 2:3], 0.00064)
    em.on("vector", nc.vector.memset, g.cvcol[:, 3:4], 1e-24, _inc=("setup_v", 1))
    em.flush("setup")
    return g


def stage_xT(g, src_h, dst=None, cast=True):
    nc, em = g.nc, g.em
    dst = dst if dst is not None else g.xT
    from contextlib import ExitStack
    ses = ExitStack()
    fin = ses.enter_context(nc.sbuf_tensor("stage_tok", [128, 2 * D], F32))
    st = fin[:, :].bitcast(BF16)
    tpn = 0
    for tt in range(NT):
        b = tt % 2
        em.wait("gpsimd", "xtp", (tt - 1) * CT)
        ld = em.on("gpsimd", nc.gpsimd.dma_start, st[:, ts(b, D)],
                   src_h[ts(tt), :], _inc=(f"xld{b}", 16))
        em.wait("tensor", f"xld{b}", ld)
        for ct in range(CT):
            pslot = g.tp_slots[ct % 2]
            em.wait("tensor", "xev", tpn - 1)
            tpn = em.on("tensor", nc.tensor.transpose, pslot,
                        st[:, b * D + ct * 128: b * D + (ct + 1) * 128],
                        g.ident[:], _inc=("xtp", 1))
            em.wait("vector", "xtp", tpn)
            em.on("vector", nc.vector.tensor_copy,
                  dst[:, ct * T + tt * 128: ct * T + (tt + 1) * 128],
                  pslot, _inc=("xev", 1))
    em.flush("stage_xT")
    ses.close()


def finalize_out(g, y_h, deps=()):
    nc, em = g.nc, g.em
    from contextlib import ExitStack
    fes_ = ExitStack()
    st = fes_.enter_context(nc.sbuf_tensor("final_tok", [128, 2 * D], F32))
    for semname in deps:
        em.wait("tensor", semname, em.cnt.get(semname, 0))
    tpn = 0
    fst_tgt = {0: 0, 1: 0}
    for tt in range(NT):
        b = tt % 2
        em.wait("vector", f"fst{b}", fst_tgt[b])
        for ct in range(CT):
            pslot = g.tp_slots[ct % 2]
            em.wait("tensor", "fev", tpn - 1)
            tpn = em.on("tensor", nc.tensor.transpose, pslot,
                        g.xT[:, ct * T + tt * 128: ct * T + (tt + 1) * 128],
                        g.ident[:], _inc=("ftp", 1))
            em.wait("vector", "ftp", tpn)
            em.on("vector", nc.vector.tensor_copy,
                  st[:, b * D + ct * 128: b * D + (ct + 1) * 128], pslot,
                  _inc=("fev", 1))
        em.wait_now("sync", "fev")
        fst_tgt[b] = em.on("sync", nc.sync.dma_start, y_h[ts(tt), :],
                           st[:, ts(b, D)], _inc=(f"fst{b}", 16))
    em.wait_now("sync", "fst0")
    em.wait_now("sync", "fst1")
    em.flush("finalize")
    fes_.close()



def build_l0_attn(g, W, CONSTS_QPOS):
    nc, em = g.nc, g.em
    NH = 16
    from contextlib import ExitStack
    es = ExitStack()
    sb = lambda name, shape, dt: es.enter_context(nc.sbuf_tensor(name, shape, dt))

    wqkv = [sb(f"a0_w{i}", [128, CT * 192], BF16) for i in range(2)]
    qb_sb = sb("a0_qbs", [128, 8], F32)
    kb_sb = sb("a0_kbs", [128, 8], F32)
    ob_sb = sb("a0_obs", [128, 8], F32)
    vb_bf = sb("a0_vbb", [1, D], BF16)
    QTa = sb("QTa", [65, T], BF16)
    KTa = sb("KTa", [65, T], BF16)
    Vbuf = sb("Vbuf", [128, NT * 65], BF16)
    YT = g.bigA[:, :]
    PT = sb("PT", [128, 2 * 512], BF16)
    dmask = sb("dmask", [128, 4 * 512], BF16)
    r_qn2 = sb("a0_qn2", [1, T], BF16)[:, :]
    r_b8 = r_qn2
    qpos_t = sb("a0_qpos", [1, 512], F32)
    g.qpos = qpos_t
    bias_k = sb("a0_biask", [128, NT], F32)
    wo_s = sb("a0_wos", [128, 2 * D], BF16)[:, :]
    scal = g.scal
    rec_sb = None
    recbc = g.recbc
    scr = g.scr
    sqbf = g.sqbf[:, :]
    qsq = g.sqbf[0:64, :]
    l_mean, l_r, l_mr = g.lm, g.lr, g.lmr
    rbc, mrbc = g.rbc, g.mrbc
    em.on("sync", nc.sync.dma_start, qpos_t[:], CONSTS_QPOS[:, :],
          _inc=("a0bl", 16))

    for j in range(4):
        m = dmask[:, ts(j, 512)]
        n = em.on("gpsimd", nc.gpsimd.memset, m, 0.0, _inc=("a0mm", 1))
        em.wait("gpsimd", "a0mm", n)
        em.on("gpsimd", nc.gpsimd.affine_select, m, m,
              [[1, 512]], compare_op=ALU.is_ge, fill=-1e30,
              base=-128 * j, channel_multiplier=-1, _inc=("a0m", 1))
    for tt in range(NT):
        em.on("gpsimd", nc.gpsimd.memset,
              Vbuf[:, tt * 65 + 64: tt * 65 + 65], 1.0, _inc=("a0m", 1))
    em.on("vector", nc.vector.memset, KTa[64:65, :], 1.0, _inc=("a0m", 1))
    em.on("sync", nc.sync.dma_start, qb_sb[:], W["qb"][:, :], _inc=("a0bl", 16))
    em.on("sync", nc.sync.dma_start, kb_sb[:], W["kb"][:, :], _inc=("a0bl", 16))
    em.on("sync", nc.sync.dma_start, ob_sb[:], W["ob"][:, :], _inc=("a0bl", 16))
    em.on("gpsimd", nc.gpsimd.dma_start, vb_bf[:], W["vb"][:, :],
          _inc=("a0wo0", 16))
    def load_head(h, slot):
        for ct in range(CT):
            em.on("gpsimd", nc.gpsimd.dma_start,
                  wqkv[slot][:, ct * 192 + 0: ct * 192 + 64],
                  W["wq"][h, ct, :, :], _inc=(f"a0wl{slot}", 16))
            em.on("gpsimd", nc.gpsimd.dma_start,
                  wqkv[slot][:, ct * 192 + 64: ct * 192 + 128],
                  W["wk"][h, ct, :, :], _inc=(f"a0wl{slot}", 16))
            em.on("gpsimd", nc.gpsimd.dma_start,
                  wqkv[slot][:, ct * 192 + 128: ct * 192 + 192],
                  W["wv"][h, ct, :, :], _inc=(f"a0wl{slot}", 16))
        return em.cnt[f"a0wl{slot}"]

    wl_tgt = load_head(0, 0)
    em.flush("a0_setup")

    for h in range(NH):
        slot = h % 2
        hb = 64 * (h % 2)
        slope = (h + 1) / NH
        my_tgt = wl_tgt
        if h + 1 < NH:
            em.wait("gpsimd", "a0vm", em.cnt.get("a0vm", 0))
            wl_tgt = load_head(h + 1, 1 - slot)
        em.wait("tensor", f"a0wl{slot}", my_tgt)

        if h == 0:
            em.wait_now("vector", "a0bl")
        em.wait("vector", "a0s", em.cnt.get("a0s", 0))
        em.wait("vector", "a0pv", em.cnt.get("a0pv", 0))
        em.wait("vector", "a0ex", em.cnt.get("a0ex", 0))
        for hc in range(NCH):
            em.wait("tensor", "a0qe", em.cnt.get("a0qe", 0))
            for ct in range(CT):
                em.on("tensor", nc.tensor.matmul, g.p_a[0][0:64, :],
                      wqkv[slot][:, ct * 192: ct * 192 + 64],
                      g.xT[:, ct * T + hc * 512: ct * T + (hc + 1) * 512],
                      start=(ct == 0), stop=(ct == CT - 1),
                      _inc=("a0q", 1) if ct == CT - 1 else None)
            for ct in range(CT):
                em.on("tensor", nc.tensor.matmul, g.p_a[1][0:64, :],
                      wqkv[slot][:, ct * 192 + 64: ct * 192 + 128],
                      g.xT[:, ct * T + hc * 512: ct * T + (hc + 1) * 512],
                      start=(ct == 0), stop=(ct == CT - 1),
                      _inc=("a0q", 1) if ct == CT - 1 else None)
            em.wait_now("vector", "a0q")
            em.on("vector", nc.vector.tensor_scalar,
                  QTa[0:64, hc * 512:(hc + 1) * 512], g.p_a[0][0:64, :],
                  qb_sb[hb:hb + 64, h // 2:h // 2 + 1], None, op0=ALU.add,
                  _inc=("a0qe", 1))
            em.on("vector", nc.vector.tensor_scalar,
                  KTa[0:64, hc * 512:(hc + 1) * 512], g.p_a[1][0:64, :],
                  kb_sb[hb:hb + 64, h // 2:h // 2 + 1], None, op0=ALU.add,
                  _inc=("a0qe", 1))
        vhead = vb_bf[:, h * 64:(h + 1) * 64]
        for tt in range(NT):
            em.wait("tensor", "a0ve", em.cnt.get("a0ve", 0))
            for ct in range(CT):
                em.on("tensor", nc.tensor.matmul, g.p_v[:, 0:64],
                      g.xT[:, ct * T + tt * 128: ct * T + (tt + 1) * 128],
                      wqkv[slot][:, ct * 192 + 128: ct * 192 + 192],
                      start=(ct == 0), stop=False)
            if h == 0 and tt == 0:
                em.wait("tensor", "a0wo0", 16)
            em.on("tensor", nc.tensor.matmul, g.p_v[:, 0:64],
                  g.onesrow[:, :], vhead, start=False, stop=True,
                  _inc=("a0vm", 1))
            em.wait_now("vector", "a0vm")
            em.on("vector", nc.vector.tensor_copy,
                  Vbuf[:, tt * 65: tt * 65 + 64], g.p_v[:, 0:64],
                  _inc=("a0ve", 1))

        em.wait("vector", "a0qe", em.cnt["a0qe"])
        for hc in range(NCH):
            em.wait("vector", "a0nr", em.cnt.get("a0nr", 0))
            em.on("vector", nc.vector.tensor_mul, qsq[:, :],
                  QTa[0:64, hc * 512:(hc + 1) * 512],
                  QTa[0:64, hc * 512:(hc + 1) * 512], _inc=("a0sq", 1))
            em.wait("tensor", "a0sq", em.cnt["a0sq"])
            em.wait("tensor", "a0nre", em.cnt.get("a0nre", 0))
            em.on("tensor", nc.tensor.matmul, g.p_row[:, :],
                  g.onescol[0:64, :], qsq[:, :],
                  start=True, stop=True, _inc=("a0nr", 1))
            em.wait_now("vector", "a0nr")
            em.on("vector", nc.vector.tensor_copy,
                  r_qn2[:, hc * 512:(hc + 1) * 512], g.p_row[:, :],
                  _inc=("a0nre", 1))
        for hc in range(NCH):
            em.wait("vector", "a0nr", em.cnt["a0nr"])
            em.on("vector", nc.vector.tensor_mul, qsq[:, :],
                  KTa[0:64, hc * 512:(hc + 1) * 512],
                  KTa[0:64, hc * 512:(hc + 1) * 512], _inc=("a0sq", 1))
            em.wait("tensor", "a0sq", em.cnt["a0sq"])
            em.wait("tensor", "a0nre", em.cnt.get("a0nre", 0))
            em.on("tensor", nc.tensor.matmul, g.p_row[:, :],
                  g.onescol[0:64, :], qsq[:, :],
                  start=True, stop=True, _inc=("a0nr", 1))
            em.wait_now("vector", "a0nr")
            em.on("vector", nc.vector.reduce_max, scal[:, hc:hc + 1],
                  g.p_row[:, :], axis=AX.X, _inc=("a0nre", 1))
        em.wait("vector", "a0nre", em.cnt["a0nre"])
        em.on("vector", nc.vector.reduce_max, scal[:, 8:9], scal[:, 0:8],
              axis=AX.X, _inc=("a0s1", 1))
        em.wait_now("scalar", "a0s1")
        em.on("scalar", nc.scalar.activation, scal[:, 9:10], scal[:, 8:9],
              AF.Ln, bias=g.cvals[:, 0:1], scale=1.0)
        em.on("scalar", nc.scalar.activation, r_b8, r_qn2,
              AF.Ln, bias=g.cvals[:, 0:1], scale=1.0, _inc=("a0s2", 1))
        em.wait_now("vector", "a0s2")
        em.on("vector", nc.vector.tensor_scalar_mul, scal[:, 10:11],
              scal[:, 9:10], 0.5, _inc=("a0s3", 1))
        em.wait_now("scalar", "a0s3")
        em.on("scalar", nc.scalar.activation, r_b8, r_b8,
              AF.Exp, bias=scal[:, 10:11], scale=0.5, _inc=("a0s4", 1))
        em.wait_now("vector", "a0s4")
        for hc in range(NCH):
            em.wait("vector", "a0ns", em.cnt.get("a0ns", 0))
            n = em.on("vector", nc.vector.tensor_scalar,
                      g.rowa, r_b8[:, hc * 512:(hc + 1) * 512],
                      1.0, 8.0 * slope * (hc * 512),
                      op0=ALU.mult, op1=ALU.add, _inc=("a0nsp", 1))
            em.wait("vector", "a0nsp", n)
            em.on("vector", nc.vector.scalar_tensor_tensor,
                  QTa[64:65, hc * 512:(hc + 1) * 512],
                  g.qpos[:, :], -8.0 * slope, g.rowa,
                  op0=ALU.mult, op1=ALU.subtract, _inc=("a0ns", 1))
        em.on("vector", nc.vector.tensor_scalar_mul, bias_k[:, :],
              g.kpos[:, :], slope, _inc=("a0bk", 1))
        em.wait_now("tensor", "a0ns")
        em.wait("tensor", "a0ve", em.cnt["a0ve"])
        em.wait_now("scalar", "a0bk")

        for qc in range(NCH):
            ktm = 4 * qc + 4
            for kt in range(ktm):
                sb = kt % 2
                p_s = g.p_a[sb]
                em.wait("tensor", "a0ex", em.cnt.get("a0ex", 0) - 1)
                smm = em.on("tensor", nc.tensor.matmul, p_s[:, :],
                            KTa[:, ts(kt)], QTa[:, qc * 512:(qc + 1) * 512],
                            start=True, stop=True, _inc=("a0s", 1))
                if kt >= 4 * qc:
                    em.wait("vector", "a0s", smm)
                    mkn = em.on("vector", nc.vector.tensor_add,
                                p_s[:, :], p_s[:, :],
                                dmask[:, ts(kt - 4 * qc, 512)],
                                _inc=("a0mk", 1))
                    em.wait("scalar", "a0mk", mkn)
                else:
                    em.wait("scalar", "a0s", smm)
                exn = em.on("scalar", nc.scalar.activation,
                            PT[:, ts(sb, 512)], p_s[:, :], AF.Exp,
                            bias=bias_k[:, kt:kt + 1], scale=0.125,
                            _inc=("a0ex", 1))
                em.wait("tensor", "a0ex", exn)
                em.on("tensor", nc.tensor.matmul, g.p_y[0:65, :],
                      Vbuf[:, kt * 65: kt * 65 + 65], PT[:, ts(sb, 512)],
                      start=(kt == 0), stop=(kt == ktm - 1),
                      _inc=("a0pv", 1) if kt == ktm - 1 else None)
            em.wait_now("vector", "a0pv")
            rcn = em.on("vector", nc.vector.reciprocal, g.rowa,
                        g.p_y[64:65, :], _inc=("a0rc", 1))
            em.wait("tensor", "a0rc", rcn)
            bcn = em.on("tensor", nc.tensor.matmul, g.p_bc[0:64, :],
                        g.onesrowf[:, 0:64], g.rowa,
                        start=True, stop=True, _inc=("a0bc", 1))
            em.wait("vector", "a0bc", bcn)
            nbce = em.on("vector", nc.vector.tensor_copy, recbc[:, :],
                         g.p_bc[0:64, :], _inc=("a0bce", 1))
            em.wait("vector", "a0bce", nbce)
            yev = em.on("vector", nc.vector.tensor_mul,
                        YT[64 * (h % 2): 64 * (h % 2) + 64,
                           (h // 2) * T + qc * 512: (h // 2) * T + (qc + 1) * 512],
                        g.p_y[0:64, :], recbc[:, :], _inc=("a0ye", 1))
            em.wait("tensor", "a0ye", yev)
            em.wait("tensor", "a0bce", em.cnt["a0bce"])

    em.flush("a0_heads")

    won = 0
    wou = 0

    wos_tgt = {0: 0, 1: 0}

    def load_wo(ft, slot):
        em.wait("gpsimd", "a0om", wou - 1)
        wos_tgt[slot] = em.on("gpsimd", nc.gpsimd.dma_start,
                              wo_s[:, ts(slot, D)],
                              W["wo"][ft, :, :],
                              _inc=(f"a0wos{slot}", 16))

    load_wo(0, 0)
    for tc in range(NCH):
        for pss in range(2):
            for ft in range(CT):
                slot = (tc * 2 * CT + pss * CT + ft) % 2
                nft = ft + 1
                npss, ntc = pss, tc
                if nft == CT:
                    nft = 0
                    npss += 1
                    if npss == 2:
                        npss = 0
                        ntc += 1
                po = g.p_a[ft % 2]
                em.wait("tensor", f"a0wos{slot}", wos_tgt[slot])
                em.wait("tensor", "a0pre", em.cnt.get("a0pre", 0) - 1)
                for ct in range(CT):
                    em.on("tensor", nc.tensor.matmul, po[:, :],
                          wo_s[:, slot * D + ct * 128: slot * D + (ct + 1) * 128],
                          YT[:, ct * T + tc * 512: ct * T + (tc + 1) * 512],
                          start=(ct == 0), stop=(ct == CT - 1),
                          _inc=("a0om", 1) if ct == CT - 1 else None)
                wou = em.cnt["a0om"]
                if ntc < NCH:
                    load_wo(nft, 1 - slot)
                em.wait_now("vector", "a0om")
                if pss == 0:
                    em.wait("vector", "a0st1", em.cnt.get("a0st1", 0))
                    prn = em.on("vector", nc.vector.scalar_tensor_tensor,
                                sqbf[:, :], po[:, :], ob_sb[:, ft:ft + 1],
                                g.xT[:, ft * T + tc * 512: ft * T + (tc + 1) * 512],
                                op0=ALU.add, op1=ALU.add, _inc=("a0pre", 1))
                    em.wait("tensor", "a0pre", prn)
                    em.on("tensor", nc.tensor.matmul, g.p_row[:, :],
                          g.onescol[:, :], sqbf[:, :],
                          start=(ft == 0), stop=(ft == CT - 1),
                          _inc=("a0st1", 1))
                    em.wait("vector", "a0st2", em.cnt.get("a0st2", 0))
                    sqn = em.on("vector", nc.vector.tensor_mul, PT[:, 0:512],
                                sqbf[:, :], sqbf[:, :], _inc=("a0sqt", 1))
                    em.wait("tensor", "a0sqt", sqn)
                    em.on("tensor", nc.tensor.matmul, g.p_row2[:, :],
                          g.onescol[:, :], PT[:, 0:512],
                          start=(ft == 0), stop=(ft == CT - 1),
                          _inc=("a0st2", 1))
                else:
                    em.vop(nc.vector.scalar_tensor_tensor,
                           scr[:, :], po[:, :], ob_sb[:, ft:ft + 1],
                           g.xT[:, ft * T + tc * 512: ft * T + (tc + 1) * 512],
                           op0=ALU.add, op1=ALU.add)
                    em.vop(nc.vector.tensor_mul, scr[:, :],
                           scr[:, :], rbc[:, :])
                    em.vop(nc.vector.tensor_sub,
                           g.xT[:, ft * T + tc * 512: ft * T + (tc + 1) * 512],
                           scr[:, :], mrbc[:, :], _inc=("a0pre", 1))
            if pss == 0:
                em.wait_now("vector", "a0st1")
                em.wait_now("vector", "a0st2")
                em.vop(nc.vector.tensor_scalar_mul, l_mean, g.p_row[:, :],
                       1.0 / D)
                em.vop(nc.vector.tensor_scalar_mul, l_r, g.p_row2[:, :],
                       1.0 / D, _inc=("a0lv", 1))
                em.wait("tensor", "a0lv", em.cnt["a0lv"])
                em.vop(nc.vector.tensor_mul, l_mr, l_mean, l_mean)
                em.vop(nc.vector.tensor_sub, l_r, l_r, l_mr, _inc=("a0lv2", 1))
                em.wait_now("scalar", "a0lv2")
                em.on("scalar", nc.scalar.activation, l_r, l_r, AF.Ln,
                      bias=g.cvals[:, 1:2], scale=1.0, _inc=("a0lnp", 1))
                em.wait("scalar", "a0lnp", em.cnt["a0lnp"])
                em.on("scalar", nc.scalar.activation, l_r, l_r, AF.Exp,
                      bias=0.0, scale=-0.5, _inc=("a0lnr", 1))
                em.wait_now("vector", "a0lnr")
                mrn = em.vop(nc.vector.tensor_mul, l_mr, l_mean, l_r,
                             _inc=("a0hl", 1))
                em.wait("tensor", "a0hl", mrn)
                em.wait("tensor", "a0bce", em.cnt.get("a0bce", 0))
                n1 = em.on("tensor", nc.tensor.matmul, g.p_bc[:, :],
                           g.onesrowf[:, :], l_r,
                           start=True, stop=True, _inc=("a0bcc", 1))
                em.wait("vector", "a0bcc", n1)
                em.on("vector", nc.vector.tensor_copy, rbc[:, :], g.p_bc[:, :],
                      _inc=("a0bce", 1))
                em.wait("tensor", "a0bce", em.cnt["a0bce"])
                n2 = em.on("tensor", nc.tensor.matmul, g.p_bc[:, :],
                           g.onesrowf[:, :], l_mr,
                           start=True, stop=True, _inc=("a0bcc", 1))
                em.wait("vector", "a0bcc", n2)
                em.on("vector", nc.vector.tensor_copy, mrbc[:, :], g.p_bc[:, :],
                      _inc=("a0bce", 1))
    em.flush("a0_proj")
    es.close()





def build_l0_ffn(g, W, z1_hbm, consts):
    nc, em = g.nc, g.em
    from contextlib import ExitStack
    FCP = 22
    es = ExitStack()
    sb = lambda n, sh, dt: es.enter_context(nc.sbuf_tensor(n, sh, dt))

    xp = g.bigA[:, :]
    h2h = nc.dram_tensor("h2_hbm", [128, CT * T], BF16, kind="Internal")

    masks = sb("f0_masks", [128, 8 * 512], BF16)
    scal = g.scal
    rrow = g.rowa
    rrow2 = g.rowb
    rbc64 = g.rbc64
    recbc = g.recbc
    l_mean, l_r, l_mr = g.lm, g.lr, g.lmr
    rbc, mrbc = g.rbc, g.mrbc
    scr = g.scr
    scr2 = g.scr2
    sqbf = g.sqbf
    qs8 = sb("f0_qs8", [64, 2], F32)
    lgb = sb("f0_lgb", [128, 16], F32)
    b1c = sb("f0_b1c", [128, 44], F32)
    b2c = sb("f0_b2c", [128, 8], F32)

    em.on("vector", nc.vector.tensor_copy, xp, g.xT[:, :], _inc=("setup_v", 1))
    em.wait_now("sync", "setup_v")
    em.on("sync", nc.sync.dma_start, z1_hbm[:, :], g.xT[:, :], _inc=("f0z1", 16))

    for ji, j in enumerate(range(-4, 4)):
        m = masks[:, ts(ji, 512)]
        n = em.on("gpsimd", nc.gpsimd.memset, m, 0.0, _inc=("f0mg", 1))
        em.wait("gpsimd", "f0mg", n)
        n = em.on("gpsimd", nc.gpsimd.affine_select, m, m,
                  [[1, 512]], compare_op=ALU.is_ge, fill=-1e30,
                  base=-128 * j, channel_multiplier=-1, _inc=("f0mg", 1))
        em.wait("gpsimd", "f0mg", n)
        em.on("gpsimd", nc.gpsimd.affine_select, m, m,
              [[-1, 512]], compare_op=ALU.is_ge, fill=-1e30,
              base=512 + 128 * j, channel_multiplier=1, _inc=("f0mg", 1))
    em.flush("f0_setup")

    def mask_j(j):
        return masks[:, ts(j + 4, 512)]

    for l in range(2):
        tag = "a0"

        em.on("sync", nc.sync.dma_start, qs8[:, 0:1], W["qs"][l, :, :],
              _inc=(tag + "bl", 16))
        em.on("sync", nc.sync.dma_start, qs8[:, 1:2], W["ks"][l, :, :],
              _inc=(tag + "bl", 16))
        em.on("sync", nc.sync.dma_start, lgb[:, 0:8], W["lg"][l, :, :],
              _inc=(tag + "bl", 16))
        em.on("sync", nc.sync.dma_start, lgb[:, 8:16], W["lb"][l, :, :],
              _inc=(tag + "bl", 16))
        em.on("sync", nc.sync.dma_start, b1c[:, 0:22], W["b1a"][l, :, :],
              _inc=(tag + "bl", 16))
        em.on("sync", nc.sync.dma_start, b1c[:, 22:44], W["b1g"][l, :, :],
              _inc=(tag + "bl", 16))
        em.on("sync", nc.sync.dma_start, b2c[:, :], W["b2"][l, :, :],
              _inc=(tag + "bl", 16))
        em.wait_now("vector", tag + "bl")
        n = em.on("vector", nc.vector.tensor_scalar_mul, qs8[:, 0:1],
                  qs8[:, 0:1], 8.0, _inc=(tag + "qs8", 1))
        em.flush(tag + "_consts")

        with ExitStack() as aes:
            asb = lambda n_, sh, dt: aes.enter_context(
                nc.sbuf_tensor(f"L{l}{n_}", sh, dt))
            YT2 = g.xT[:, 0: 4 * T]
            QTa = asb("_QTa", [65, T], BF16)
            KTa = asb("_KTa", [65, T], BF16)
            Vbuf = asb("_Vbuf", [128, NT * 65], BF16)
            PT = asb("_PT", [128, 2 * 512], BF16)
            wqh = [asb(f"_wqh{i}", [128, CT * 192], BF16) for i in range(2)]
            rot = asb("_rot", [64, 8 * 512], BF16)
            woh = [asb(f"_woh{i}", [128, 4 * 128], BF16) for i in range(2)]
            maxc = asb("_maxc", [128, 32], F32)
            maxt = asb("_maxt", [32, 128], F32)

            for tt in range(NT):
                em.on("gpsimd", nc.gpsimd.memset,
                      Vbuf[:, tt * 65 + 64: tt * 65 + 65], 1.0,
                      _inc=(tag + "vo", 1))
            em.on("vector", nc.vector.memset, KTa[64:65, :], 1.0,
                  _inc=(tag + "vo", 1))

            def load_head(h, slot):
                for ct in range(CT):
                    em.on("gpsimd", nc.gpsimd.dma_start,
                          wqh[slot][:, ct * 192: (ct + 1) * 192],
                          W["wqkv"][l, h, ct, :, :],
                          _inc=(tag + f"wl{slot}", 16))
                return em.cnt[tag + f"wl{slot}"]

            em.wait("vector", "f0z1", 16)
            wl_tgt = load_head(0, 0)
            em.flush(tag + "_asetup")

            def rhs_x(ct, lo, width):
                off = ct * T + lo - (1 if ct >= 4 else 0)
                return xp[:, off: off + width]

            rot_t = {0: 0, 1: 0}
            for h in range(8):
                slot = h % 2
                my_tgt = wl_tgt
                if h + 1 < 8:
                    em.wait("gpsimd", tag + "vm", em.cnt.get(tag + "vm", 0))
                    wl_tgt = load_head(h + 1, 1 - slot)
                em.wait("tensor", tag + f"wl{slot}", my_tgt)

                em.wait("vector", tag + "s", em.cnt.get(tag + "s", 0))
                em.wait("vector", tag + "p1", em.cnt.get(tag + "p1", 0))
                em.wait("vector", tag + "pv", em.cnt.get(tag + "pv", 0))
                em.wait("vector", tag + "mxt", em.cnt.get(tag + "mxt", 0))
                for hc in range(NCH):
                    em.wait("tensor", tag + "qe", em.cnt.get(tag + "qe", 0))
                    for ct in range(CT):
                        em.on("tensor", nc.tensor.matmul, g.p_a[0][0:64, :],
                              wqh[slot][:, ct * 192: ct * 192 + 64],
                              rhs_x(ct, hc * 512, 512),
                              start=(ct == 0), stop=(ct == CT - 1),
                              _inc=(tag + "qm", 1) if ct == CT - 1 else None)
                    for ct in range(CT):
                        em.on("tensor", nc.tensor.matmul, g.p_a[1][0:64, :],
                              wqh[slot][:, ct * 192 + 64: ct * 192 + 128],
                              rhs_x(ct, hc * 512, 512),
                              start=(ct == 0), stop=(ct == CT - 1),
                              _inc=(tag + "qm", 1) if ct == CT - 1 else None)
                    em.wait_now("vector", tag + "qm")
                    em.on("vector", nc.vector.tensor_copy,
                          QTa[0:64, hc * 512:(hc + 1) * 512], g.p_a[0][0:64, :],
                          _inc=(tag + "qe", 1))
                    em.on("vector", nc.vector.tensor_copy,
                          KTa[0:64, hc * 512:(hc + 1) * 512], g.p_a[1][0:64, :],
                          _inc=(tag + "qe", 1))
                em.wait("tensor", tag + "qe", em.cnt[tag + "qe"])
                em.wait("tensor", tag + "ve", em.cnt.get(tag + "ve", 0))
                for which, dst in ((0, QTa), (64, KTa)):
                    for ct in range(4):
                        em.on("tensor", nc.tensor.matmul, g.p_v[0:64, 0:1],
                              wqh[slot][:, ct * 192 + which: ct * 192 + which + 64],
                              xp[:, ct * T: ct * T + 1],
                              start=(ct == 0), stop=(ct == 3),
                              _inc=(tag + "fx", 1))
                    em.wait_now("vector", tag + "fx")
                    em.on("vector", nc.vector.tensor_copy, dst[0:64, 0:1],
                          g.p_v[0:64, 0:1], _inc=(tag + "fxe", 1))
                    em.wait("tensor", tag + "fxe", em.cnt[tag + "fxe"])
                for tt in range(NT):
                    em.wait("tensor", tag + "ve", em.cnt.get(tag + "ve", 0))
                    for ct in range(CT):
                        em.on("tensor", nc.tensor.matmul, g.p_v[:, 0:64],
                              rhs_x(ct, tt * 128, 128),
                              wqh[slot][:, ct * 192 + 128: ct * 192 + 192],
                              start=(ct == 0), stop=(ct == CT - 1),
                              _inc=(tag + "vm", 1) if ct == CT - 1 else None)
                    em.wait_now("vector", tag + "vm")
                    em.on("vector", nc.vector.tensor_copy,
                          Vbuf[:, tt * 65: tt * 65 + 64], g.p_v[:, 0:64],
                          _inc=(tag + "ve", 1))
                em.wait("tensor", tag + "ve", em.cnt[tag + "ve"])
                for ct in range(4):
                    em.on("tensor", nc.tensor.matmul, g.p_v[0:1, 0:64],
                          xp[:, ct * T: ct * T + 1],
                          wqh[slot][:, ct * 192 + 128: ct * 192 + 192],
                          start=(ct == 0), stop=(ct == 3),
                          _inc=(tag + "fx", 1))
                em.wait_now("vector", tag + "fx")
                em.on("vector", nc.vector.tensor_copy, Vbuf[0:1, 0:64],
                      g.p_v[0:1, 0:64], _inc=(tag + "fxe", 1))
                em.wait("tensor", tag + "fxe", em.cnt[tag + "fxe"])

                for hc in range(NCH):
                    rs = hc % 2
                    em.wait("gpsimd", tag + "rote",
                            em.cnt.get(tag + "rote", 0) - 2)
                    rot_t[rs] = em.on(
                        "gpsimd", nc.gpsimd.dma_start,
                        rot[:, rs * 2048 + 0: rs * 2048 + 512],
                        consts["rotq"][0, :, hc * 512:(hc + 1) * 512],
                        _inc=(tag + f"rt{rs}", 16))
                    em.on("gpsimd", nc.gpsimd.dma_start,
                          rot[:, rs * 2048 + 512: rs * 2048 + 1024],
                          consts["rotq"][1, :, hc * 512:(hc + 1) * 512],
                          _inc=(tag + f"rt{rs}", 16))
                    em.on("gpsimd", nc.gpsimd.dma_start,
                          rot[:, rs * 2048 + 1024: rs * 2048 + 1536],
                          consts["rotk"][0, :, hc * 512:(hc + 1) * 512],
                          _inc=(tag + f"rt{rs}", 16))
                    rot_t[rs] = em.on(
                        "gpsimd", nc.gpsimd.dma_start,
                        rot[:, rs * 2048 + 1536: rs * 2048 + 2048],
                        consts["rotk"][1, :, hc * 512:(hc + 1) * 512],
                        _inc=(tag + f"rt{rs}", 16))
                    for (src, t0, t1, scol) in (
                            (QTa, rs * 2048, rs * 2048 + 512, 0),
                            (KTa, rs * 2048 + 1024, rs * 2048 + 1536, 1)):
                        ch = slice(hc * 512, (hc + 1) * 512)
                        em.wait("vector", tag + "nm", em.cnt.get(tag + "nm", 0))
                        n = em.on("vector", nc.vector.tensor_mul, sqbf[0:64, :],
                                  src[0:64, ch], src[0:64, ch],
                                  _inc=(tag + "sq", 1))
                        em.wait("tensor", tag + "sq", n)
                        em.wait("tensor", tag + "nre", em.cnt.get(tag + "nre", 0))
                        n = em.on("tensor", nc.tensor.matmul, g.p_row[:, :],
                                  g.onescol[0:64, :], sqbf[0:64, :],
                                  start=True, stop=True, _inc=(tag + "nm", 1))
                        em.wait("scalar", tag + "nm", n)
                        n = em.on("scalar", nc.scalar.activation, rrow[:, :],
                                  g.p_row[:, :], AF.Ln, bias=g.cvals[:, 4:5],
                                  scale=1.0, _inc=(tag + "ln1", 1))
                        em.wait("scalar", tag + "ln1", n)
                        n = em.on("scalar", nc.scalar.activation, rrow[:, :],
                                  rrow[:, :], AF.Exp, bias=0.0, scale=-0.5,
                                  _inc=(tag + "nre", 1))
                        em.wait("tensor", tag + "nre", em.cnt[tag + "nre"])
                        n = em.on("tensor", nc.tensor.matmul, g.p_bc[0:64, :],
                                  g.onesrowf[:, 0:64], rrow[:, :],
                                  start=True, stop=True, _inc=(tag + "bc", 1))
                        em.wait("vector", tag + "bc", n)
                        n = em.on("vector", nc.vector.tensor_copy, rbc64[:, :],
                                  g.p_bc[0:64, :], _inc=(tag + "bce", 1))
                        em.wait("tensor", tag + "bce", em.cnt[tag + "bce"])
                        em.wait("vector", tag + "bce", n)
                        em.vop(nc.vector.tensor_mul, scr[0:64, :],
                               src[0:64, ch], rbc64[:, :])
                        if l == 0 and h == 0 and hc == 0 and scol == 0:
                            em.wait("vector", tag + "qs8", 1)
                        em.vop(nc.vector.tensor_scalar_mul, scr[0:64, :],
                               scr[0:64, :], qs8[:, scol:scol + 1])
                        em.wait("vector", tag + f"rt{rs}", rot_t[rs])
                        em.vop(nc.vector.tensor_mul, scr2[0:64, :],
                               scr[0:64, :], rot[:, t0:t0 + 512])
                        em.vop(nc.vector.tensor_scalar_mul, sqbf[0:32, :],
                               scr[32:64, :], -1.0)
                        em.vop(nc.vector.tensor_copy, sqbf[32:64, :],
                               scr[0:32, :])
                        em.vop(nc.vector.tensor_mul, scr[0:64, :],
                               sqbf[0:64, :], rot[:, t1:t1 + 512])
                        em.vop(nc.vector.tensor_add, src[0:64, ch],
                               scr2[0:64, :], scr[0:64, :],
                               _inc=(tag + "rote", 1))
                em.wait("tensor", tag + "rote", em.cnt[tag + "rote"])
                p1_step = [0]
                for qt in range(NT):
                    kcs = [kc for kc in (qt // 4 - 1, qt // 4) if kc >= 0]
                    first = True
                    for kc in kcs:
                        j = qt - 4 * kc - 4
                        pb = g.p_a[p1_step[0] % 2]
                        p1_step[0] += 1
                        em.wait("tensor", tag + "mx",
                                em.cnt.get(tag + "mx", 0) - 1)
                        n = em.on("tensor", nc.tensor.matmul, pb[:, :],
                                  QTa[0:64, ts(qt)],
                                  KTa[0:64, kc * 512:(kc + 1) * 512],
                                  start=True, stop=True, _inc=(tag + "p1", 1))
                        em.wait("vector", tag + "p1", n)
                        n = em.on("vector", nc.vector.tensor_add, pb[:, :],
                                  pb[:, :], mask_j(j), _inc=(tag + "p1m", 1))
                        em.wait("vector", tag + "p1m", n)
                        if first:
                            n = em.on("vector", nc.vector.reduce_max,
                                      maxc[:, qt:qt + 1], pb[:, :], axis=AX.X,
                                      _inc=(tag + "mx", 1))
                        else:
                            n = em.on("vector", nc.vector.reduce_max,
                                      scr[:, 0:1], pb[:, :], axis=AX.X,
                                      _inc=(tag + "mx", 1))
                            em.wait("vector", tag + "mx", n)
                            n = em.on("vector", nc.vector.tensor_max,
                                      maxc[:, qt:qt + 1], maxc[:, qt:qt + 1],
                                      scr[:, 0:1], _inc=(tag + "mx", 1))
                        first = False
                em.wait("vector", tag + "mx", em.cnt[tag + "mx"])
                n = em.on("vector", nc.vector.tensor_scalar_mul, maxc[:, :],
                          maxc[:, :], -1.0, _inc=(tag + "mxn", 1))
                em.wait("tensor", tag + "mxn", n)
                n = em.on("tensor", nc.tensor.transpose, g.p_v[0:32, :],
                          maxc[:, :], g.identf[:], _inc=(tag + "mxt", 1))
                em.wait("vector", tag + "mxt", n)
                n = em.on("vector", nc.vector.tensor_copy, maxt[:, :],
                          g.p_v[0:32, :], _inc=(tag + "mxe", 1))
                em.wait("gpsimd", tag + "mxe", em.cnt[tag + "mxe"])
                nsh = em.on("gpsimd", nc.gpsimd.dma_start, QTa[64:65, :],
                            maxt[:, :], _inc=(tag + "nsh", 16))
                em.wait("tensor", tag + "nsh", nsh)

                for qc in range(NCH):
                    kts = list(range(max(0, 4 * qc - 4), 4 * qc + 4))
                    for ki, kt in enumerate(kts):
                        sbk = kt % 2
                        p_s = g.p_a[sbk]
                        em.wait("tensor", tag + "ex",
                                em.cnt.get(tag + "ex", 0) - 1)
                        smm = em.on("tensor", nc.tensor.matmul, p_s[:, :],
                                    KTa[:, ts(kt)],
                                    QTa[:, qc * 512:(qc + 1) * 512],
                                    start=True, stop=True, _inc=(tag + "s", 1))
                        em.wait("vector", tag + "s", smm)
                        j = kt - 4 * qc
                        mkn = em.on("vector", nc.vector.tensor_add,
                                    p_s[:, :], p_s[:, :], mask_j(j),
                                    _inc=(tag + "mk", 1))
                        em.wait("scalar", tag + "mk", mkn)
                        exn = em.on("scalar", nc.scalar.activation,
                                    PT[:, ts(sbk, 512)], p_s[:, :], AF.Exp,
                                    bias=0.0, scale=1.0, _inc=(tag + "ex", 1))
                        em.wait("tensor", tag + "ex", exn)
                        em.on("tensor", nc.tensor.matmul, g.p_y[0:65, :],
                              Vbuf[:, kt * 65: kt * 65 + 65],
                              PT[:, ts(sbk, 512)],
                              start=(ki == 0), stop=(ki == len(kts) - 1),
                              _inc=(tag + "pv", 1) if ki == len(kts) - 1 else None)
                    em.wait_now("vector", tag + "pv")
                    rcn = em.on("vector", nc.vector.reciprocal, rrow2[:, :],
                                g.p_y[64:65, :], _inc=(tag + "rc2", 1))
                    em.wait("tensor", tag + "rc2", rcn)
                    bcn = em.on("tensor", nc.tensor.matmul, g.p_bc[0:64, :],
                                g.onesrowf[:, 0:64], rrow2[:, :],
                                start=True, stop=True, _inc=(tag + "bc2", 1))
                    em.wait("vector", tag + "bc2", bcn)
                    nbce = em.on("vector", nc.vector.tensor_copy, recbc[:, :],
                                 g.p_bc[0:64, :], _inc=(tag + "bc2e", 1))
                    em.wait("vector", tag + "bc2e", nbce)
                    yev = em.on("vector", nc.vector.tensor_mul,
                                YT2[64 * (h % 2): 64 * (h % 2) + 64,
                                    (h // 2) * T + qc * 512:
                                    (h // 2) * T + (qc + 1) * 512],
                                g.p_y[0:64, :], recbc[:, :],
                                _inc=(tag + "ye", 1))
                    em.wait("tensor", tag + "ye", yev)
                    em.wait("tensor", tag + "bc2e", em.cnt[tag + "bc2e"])

            wot = {0: 0, 1: 0}
            wou = [0]

            def load_wo(ft, wslot):
                em.wait("gpsimd", tag + "om", wou[0] - 1)
                wot[wslot] = em.on("gpsimd", nc.gpsimd.dma_start,
                                   woh[wslot][:, :].rearrange(
                                       "c (ct f) -> c ct f", ct=4),
                                   W["wo"][l, :, :, ft * 128:(ft + 1) * 128]
                                   .rearrange("ct c f -> c ct f"),
                                   _inc=(tag + f"wo{wslot}", 16))

            load_wo(0, 0)
            for tc in range(NCH):
                for ft in range(CT):
                    i = tc * CT + ft
                    wslot = i % 2
                    po = g.p_a[ft % 2]
                    em.wait("tensor", tag + f"wo{wslot}", wot[wslot])
                    em.wait("tensor", tag + "oe", em.cnt.get(tag + "oe", 0) - 1)
                    for ct in range(4):
                        em.on("tensor", nc.tensor.matmul, po[:, :],
                              woh[wslot][:, ct * 128: (ct + 1) * 128],
                              YT2[:, ct * T + tc * 512: ct * T + (tc + 1) * 512],
                              start=(ct == 0), stop=(ct == 3),
                              _inc=(tag + "om", 1) if ct == 3 else None)
                    wou[0] = em.cnt[tag + "om"]
                    ni = i + 1
                    if ni < NCH * CT:
                        load_wo(ni % CT, ni % 2)
                    em.wait_now("vector", tag + "om")
                    em.on("vector", nc.vector.tensor_add,
                          xp[:, ft * T + tc * 512: ft * T + (tc + 1) * 512],
                          xp[:, ft * T + tc * 512: ft * T + (tc + 1) * 512],
                          po[:, :], _inc=(tag + "oe", 1))
        em.flush(tag + "_attn")

        with ExitStack() as fes:
            fsb = lambda n_, sh, dt: fes.enter_context(
                nc.sbuf_tensor(f"F{l}{n_}", sh, dt))
            w1a_res = g.xT[:, 0:FCP * CT * 128]
            act = fsb("_act", [128, FCP * 512], BF16)
            h2ch = fsb("_h2", [128, CT * 512], BF16)
            shch = fsb("_sh", [128, CT * 512], BF16)
            w1g_s = [fsb(f"_w1g{i}", [128, CT * 128], BF16) for i in range(2)]
            w2_s = [fsb(f"_w2{i}", [128, D], BF16) for i in range(2)]
            gel = g.sqbf

            em.wait("gpsimd", "f0z1", 16)
            for fc in range(FCP):
                em.on("gpsimd", nc.gpsimd.dma_start,
                      w1a_res[:, fc * 1024: (fc + 1) * 1024],
                      W["w1a"][l, fc].rearrange("ct c f -> c ct f"),
                      _inc=(tag + "w1a", 16))
            w1a_tgt = em.cnt[tag + "w1a"]
            w1g_t = {0: 0, 1: 0}
            w2_t = {0: 0, 1: 0}

            def load_w1g(fc, wslot, chunk_i):
                w1g_t[wslot] = em.on(
                    "gpsimd", nc.gpsimd.dma_start, w1g_s[wslot][:, :],
                    W["w1g"][l, fc].rearrange("ct c f -> c ct f"),
                    _inc=(tag + f"g{wslot}", 16))

            def load_w2(fc, wslot):
                w2_t[wslot] = em.on(
                    "gpsimd", nc.gpsimd.dma_start, w2_s[wslot][:, :],
                    W["w2"][l, fc, :, :], _inc=(tag + f"h{wslot}", 16))

            load_w1g(0, 0, 0)
            for tc in range(NCH):
                em.wait("vector", tag + "h2u", em.cnt.get(tag + "h2u", 0))
                em.wait("vector", tag + "st1", em.cnt.get(tag + "st1", 0))
                em.wait("vector", tag + "st2", em.cnt.get(tag + "st2", 0))
                for ct in range(CT):
                    off = ct * T + tc * 512 - (1 if ct >= 4 else 0)
                    n = em.on("vector", nc.vector.tensor_copy,
                              shch[:, ts(ct, 512)], xp[:, off: off + 512],
                              _inc=(tag + "shc", 1))
                    if ct >= 4 and tc == 0:
                        em.wait("vector", tag + "shc", n)
                        em.on("vector", nc.vector.memset,
                              shch[:, ct * 512: ct * 512 + 1], 0.0,
                              _inc=(tag + "shc", 1))
                em.wait("tensor", tag + "shc", em.cnt[tag + "shc"])
                em.wait("vector", tag + "shc", em.cnt[tag + "shc"])
                for ct in range(CT):
                    em.on("tensor", nc.tensor.matmul, g.p_row[:, :],
                          g.onescol[:, :], shch[:, ts(ct, 512)],
                          start=(ct == 0), stop=(ct == CT - 1),
                          _inc=(tag + "st1", 1))
                    em.wait("vector", tag + "st2", em.cnt.get(tag + "st2", 0))
                    n = em.on("vector", nc.vector.tensor_mul, sqbf[:, :],
                              shch[:, ts(ct, 512)], shch[:, ts(ct, 512)],
                              _inc=(tag + "sq2", 1))
                    em.wait("tensor", tag + "sq2", n)
                    em.on("tensor", nc.tensor.matmul, g.p_row2[:, :],
                          g.onescol[:, :], sqbf[:, :],
                          start=(ct == 0), stop=(ct == CT - 1),
                          _inc=(tag + "st2", 1))
                em.wait_now("vector", tag + "st1")
                em.wait_now("vector", tag + "st2")
                em.vop(nc.vector.tensor_scalar_mul, l_mean, g.p_row[:, :],
                       1.0 / D)
                em.vop(nc.vector.tensor_scalar_mul, l_r, g.p_row2[:, :],
                       1.0 / D, _inc=(tag + "lv", 1))
                em.wait("tensor", tag + "lv", em.cnt[tag + "lv"])
                em.vop(nc.vector.tensor_mul, l_mr, l_mean, l_mean)
                em.vop(nc.vector.tensor_sub, l_r, l_r, l_mr,
                       _inc=(tag + "lv2", 1))
                em.wait_now("scalar", tag + "lv2")
                n = em.on("scalar", nc.scalar.activation, l_r, l_r, AF.Ln,
                          bias=g.cvals[:, 1:2], scale=1.0, _inc=(tag + "lnp", 1))
                em.wait("scalar", tag + "lnp", n)
                em.on("scalar", nc.scalar.activation, l_r, l_r, AF.Exp,
                      bias=0.0, scale=-0.5, _inc=(tag + "lnr", 1))
                em.wait_now("vector", tag + "lnr")
                n = em.vop(nc.vector.tensor_mul, l_mr, l_mean, l_r,
                           _inc=(tag + "hl", 1))
                em.wait("tensor", tag + "hl", em.cnt[tag + "hl"])
                em.wait("tensor", tag + "bfre", em.cnt.get(tag + "bfre", 0))
                n1 = em.on("tensor", nc.tensor.matmul, g.p_bc[:, :],
                           g.onesrowf[:, :], l_r, start=True, stop=True,
                           _inc=(tag + "bcf", 1))
                em.wait("vector", tag + "bcf", n1)
                em.on("vector", nc.vector.tensor_copy, rbc[:, :], g.p_bc[:, :],
                      _inc=(tag + "bfre", 1))
                em.wait("tensor", tag + "bfre", em.cnt[tag + "bfre"])
                n2 = em.on("tensor", nc.tensor.matmul, g.p_bc[:, :],
                           g.onesrowf[:, :], l_mr, start=True, stop=True,
                           _inc=(tag + "bcf", 1))
                em.wait("vector", tag + "bcf", n2)
                em.on("vector", nc.vector.tensor_copy, mrbc[:, :], g.p_bc[:, :],
                      _inc=(tag + "bfre", 1))
                em.wait("vector", tag + "am", em.cnt.get(tag + "am", 0))
                em.wait("vector", tag + "gm", em.cnt.get(tag + "gm", 0))
                for ct in range(CT):
                    em.vop(nc.vector.tensor_mul, scr[:, :],
                           shch[:, ts(ct, 512)], rbc[:, :])
                    em.vop(nc.vector.tensor_sub, scr[:, :], scr[:, :],
                           mrbc[:, :])
                    em.vop(nc.vector.tensor_scalar, h2ch[:, ts(ct, 512)],
                           scr[:, :], lgb[:, ct:ct + 1], lgb[:, 8 + ct:9 + ct],
                           op0=ALU.mult, op1=ALU.add, _inc=(tag + "h2", 1))
                em.wait("sync", tag + "h2", em.cnt[tag + "h2"])
                em.wait("sync", tag + "bl", em.cnt[tag + "bl"])
                em.on("sync", nc.sync.dma_start,
                      h2h[:, tc * 4096:(tc + 1) * 4096], h2ch[:, :],
                      _inc=(tag + "bl", 16))
                em.wait("vector", tag + "bl", em.cnt[tag + "bl"])

            h2sl = [h2ch, shch]
            em.wait("sync", tag + "bl", em.cnt[tag + "bl"])
            em.wait("sync", tag + "st1", em.cnt.get(tag + "st1", 0))
            em.wait("sync", tag + "sq2", em.cnt.get(tag + "sq2", 0))
            em.on("sync", nc.sync.dma_start, h2sl[0][:, :], h2h[:, 0:4096],
                  _inc=(tag + "bl", 16))
            h2ld = {0: em.cnt[tag + "bl"], 1: 0}
            for tc in range(NCH):
                sl = tc % 2
                h2c = h2sl[sl]
                em.wait("sync", tag + "bl", h2ld[sl])
                if tc + 1 < NCH:
                    em.wait("sync", tag + "am", em.cnt.get(tag + "am", 0))
                    em.wait("sync", tag + "gm", em.cnt.get(tag + "gm", 0))
                    em.on("sync", nc.sync.dma_start, h2sl[1 - sl][:, :],
                          h2h[:, (tc + 1) * 4096:(tc + 2) * 4096],
                          _inc=(tag + "bl", 16))
                    h2ld[1 - sl] = em.cnt[tag + "bl"]
                em.wait("tensor", tag + "bl", h2ld[sl])

                for fc in range(FCP):
                    wslot = fc % 2
                    if tc == 0 and fc == 0:
                        em.wait("tensor", tag + "w1a", w1a_tgt)
                    em.wait("tensor", tag + f"g{wslot}", w1g_t[wslot])
                    em.wait("tensor", tag + "ae",
                            em.cnt.get(tag + "ae", 0))
                    for ct in range(CT):
                        em.on("tensor", nc.tensor.matmul, g.p_a[0][:, :],
                              w1a_res[:, fc * 1024 + ct * 128:
                                      fc * 1024 + (ct + 1) * 128],
                              h2c[:, ts(ct, 512)],
                              start=(ct == 0), stop=(ct == CT - 1),
                              _inc=(tag + "am", 1) if ct == CT - 1 else None)
                    for ct in range(CT):
                        em.on("tensor", nc.tensor.matmul, g.p_a[1][:, :],
                              w1g_s[wslot][:, ct * 128: (ct + 1) * 128],
                              h2c[:, ts(ct, 512)],
                              start=(ct == 0), stop=(ct == CT - 1),
                              _inc=(tag + "gm", 1) if ct == CT - 1 else None)
                    nfc = fc + 1 if fc + 1 < FCP else (0 if tc + 1 < NCH else None)
                    if nfc is not None:
                        em.wait("gpsimd", tag + "gm", em.cnt[tag + "gm"] - 1)
                        load_w1g(nfc, 1 - wslot, tc)
                    em.wait_now("scalar", tag + "gm")
                    gn = em.on("scalar", nc.scalar.activation, gel[:, :],
                               g.p_a[1][:, :], AF.Gelu,
                               bias=b1c[:, 22 + fc: 23 + fc], scale=1.0,
                               _inc=(tag + "gel", 1))
                    em.wait("vector", tag + "am", em.cnt[tag + "am"])
                    em.wait("vector", tag + "gel", gn)
                    em.wait("vector", tag + "wm", em.cnt.get(tag + "wm", 0))
                    em.vop(nc.vector.tensor_scalar, scr[:, :], g.p_a[0][:, :],
                           b1c[:, fc:fc + 1], None, op0=ALU.add)
                    em.vop(nc.vector.tensor_mul, act[:, ts(fc, 512)],
                           scr[:, :], gel[:, :], _inc=(tag + "ae", 1))
                em.wait("tensor", tag + "ae", em.cnt[tag + "ae"])
                em.wait("tensor", tag + "bfre", em.cnt.get(tag + "bfre", 0))
                pbanks = [g.p_a[0], g.p_a[1], g.p_y, g.p_bc,
                          None, None, None, None]
                for half in range(2):
                    for fc in range(FCP):
                        wslot = fc % 2
                        em.wait("gpsimd", tag + "wm",
                                em.cnt.get(tag + "wm", 0) - 1)
                        load_w2(fc, wslot)
                        em.wait("tensor", tag + f"h{wslot}", w2_t[wslot])
                        for dq in range(4):
                            dt = half * 4 + dq
                            em.on("tensor", nc.tensor.matmul,
                                  pbanks[dq][:, :],
                                  w2_s[wslot][:, dt * 128: (dt + 1) * 128],
                                  act[:, ts(fc, 512)],
                                  start=(fc == 0), stop=(fc == FCP - 1),
                                  _inc=(tag + "wm", 1) if dq == 3 else None)
                    em.wait_now("vector", tag + "wm")
                    for dq in range(4):
                        dt = half * 4 + dq
                        em.vop(nc.vector.tensor_scalar, scr[:, :],
                               pbanks[dq][:, :], b2c[:, dt:dt + 1], None,
                               op0=ALU.add)
                        em.vop(nc.vector.tensor_add,
                               xp[:, dt * T + tc * 512: dt * T + (tc + 1) * 512],
                               xp[:, dt * T + tc * 512: dt * T + (tc + 1) * 512],
                               scr[:, :], _inc=(tag + "h2u", 1))
                    em.wait("tensor", tag + "h2u", em.cnt[tag + "h2u"])
        em.flush(tag + "_ffn")
    with ExitStack() as les:
        lsb = lambda n_, sh, dt: les.enter_context(nc.sbuf_tensor(n_, sh, dt))
        z1ch = lsb("fL_z1", [128, CT * 512], BF16)
        pre = lsb("fL_pre", [128, CT * 512], BF16)
        z1v = z1_hbm[:, :]
        em.wait("gpsimd", "f0z1", 16)
        zt = {}
        for tc in range(NCH):
            em.wait("gpsimd", "a0prf", em.cnt.get("a0prf", 0))
            zt[tc] = em.on("gpsimd", nc.gpsimd.dma_start, z1ch[:, :],
                           z1v.rearrange("p (c t) -> p c t", c=CT)
                           [:, :, tc * 512:(tc + 1) * 512],
                           _inc=("a0zl", 16))
            em.wait("vector", "a0zl", zt[tc])
            for ct in range(CT):
                em.on("vector", nc.vector.tensor_add, pre[:, ts(ct, 512)],
                      z1ch[:, ts(ct, 512)],
                      xp[:, ct * T + tc * 512: ct * T + (tc + 1) * 512],
                      _inc=("a0prf", 1))
            em.wait("tensor", "a0prf", em.cnt["a0prf"])
            em.wait("vector", "a0prf", em.cnt["a0prf"])
            for ct in range(CT):
                em.on("tensor", nc.tensor.matmul, g.p_row[:, :],
                      g.onescol[:, :], pre[:, ts(ct, 512)],
                      start=(ct == 0), stop=(ct == CT - 1),
                      _inc=("a0st1", 1))
                em.wait("vector", "a0st2", em.cnt.get("a0st2", 0))
                n = em.on("vector", nc.vector.tensor_mul, sqbf[:, :],
                          pre[:, ts(ct, 512)], pre[:, ts(ct, 512)],
                          _inc=("a0sqt", 1))
                em.wait("tensor", "a0sqt", n)
                em.on("tensor", nc.tensor.matmul, g.p_row2[:, :],
                      g.onescol[:, :], sqbf[:, :],
                      start=(ct == 0), stop=(ct == CT - 1),
                      _inc=("a0st2", 1))
            em.wait_now("vector", "a0st1")
            em.wait_now("vector", "a0st2")
            em.vop(nc.vector.tensor_scalar_mul, l_mean, g.p_row[:, :], 1.0 / D)
            em.vop(nc.vector.tensor_scalar_mul, l_r, g.p_row2[:, :], 1.0 / D,
                   _inc=("a0lv", 1))
            em.wait("tensor", "a0lv", em.cnt["a0lv"])
            em.vop(nc.vector.tensor_mul, l_mr, l_mean, l_mean)
            em.vop(nc.vector.tensor_sub, l_r, l_r, l_mr, _inc=("a0lv2", 1))
            em.wait_now("scalar", "a0lv2")
            n = em.on("scalar", nc.scalar.activation, l_r, l_r, AF.Ln,
                      bias=g.cvals[:, 1:2], scale=1.0, _inc=("a0lnp", 1))
            em.wait("scalar", "a0lnp", n)
            em.on("scalar", nc.scalar.activation, l_r, l_r, AF.Exp,
                  bias=0.0, scale=-0.5, _inc=("a0lnr", 1))
            em.wait_now("vector", "a0lnr")
            em.vop(nc.vector.tensor_mul, l_mr, l_mean, l_r, _inc=("a0hl", 1))
            em.wait("tensor", "a0hl", em.cnt["a0hl"])
            em.wait("tensor", "a0bfre", em.cnt.get("a0bfre", 0))
            n1 = em.on("tensor", nc.tensor.matmul, g.p_bc[:, :],
                       g.onesrowf[:, :], l_r, start=True, stop=True,
                       _inc=("a0bcf", 1))
            em.wait("vector", "a0bcf", n1)
            em.on("vector", nc.vector.tensor_copy, rbc[:, :], g.p_bc[:, :],
                  _inc=("a0bfre", 1))
            em.wait("tensor", "a0bfre", em.cnt["a0bfre"])
            n2 = em.on("tensor", nc.tensor.matmul, g.p_bc[:, :],
                       g.onesrowf[:, :], l_mr, start=True, stop=True,
                       _inc=("a0bcf", 1))
            em.wait("vector", "a0bcf", n2)
            em.on("vector", nc.vector.tensor_copy, mrbc[:, :], g.p_bc[:, :],
                  _inc=("a0bfre", 1))
            for ct in range(CT):
                em.vop(nc.vector.tensor_mul, scr[:, :], pre[:, ts(ct, 512)],
                       rbc[:, :])
                em.vop(nc.vector.tensor_sub,
                       g.xT[:, ct * T + tc * 512: ct * T + (tc + 1) * 512],
                       scr[:, :], mrbc[:, :], _inc=("a0fin", 1))
    em.flush("fL_final")
    es.close()



def build_l1_attn(g, W, z2_hbm):
    nc, em = g.nc, g.em
    from contextlib import ExitStack
    es = ExitStack()
    sb = lambda n_, sh, dt: es.enter_context(nc.sbuf_tensor(n_, sh, dt))
    tag = "a0"
    SUBS = 4

    em.wait_now("sync", "a0fin")
    em.on("sync", nc.sync.dma_start, z2_hbm[:, :], g.xT[:, :], _inc=("f0z1", 16))

    WR, WK, WV, WO = 0, CT * 1024, 2 * CT * 1024, 3 * CT * 1024
    maa5 = sb("t1_maa5s", [128, 40], F32)
    tdc = sb("t1_tds", [128, 8], F32)
    mw1 = sb("t1_mw1s", [128, CT * 128], BF16)
    mw2 = [sb(f"t1_mw2s{i}", [32, 1024], BF16) for i in range(4)]
    dw1 = sb("t1_dw1s", [128, CT * 64], BF16)
    dw2 = sb("t1_dw2s", [64, 1024], BF16)
    ucol = sb("t1_us", [64, 16], F32)
    lnbc_ = sb("t1_lnbs", [128, 8], BF16)
    lnwc = sb("t1_lnws", [128, 8], F32)
    biasW = sb("t1_biasW", [128, 8], F32)
    xch = sb("t1_xch", [128, CT * 513], BF16)
    mixf = g.bigA[:, WV: WV + CT * 512]
    vtok = g.bigA[:, WV + CT * 512: WV + CT * 512 + SUBS * 1024]
    xxx_rT = sb("t1_xxxrT", [128, CT * 512], BF16)
    kT = sb("t1_kT", [128, CT * 512], BF16)
    wv_s = [sb(f"t1_wv{i}", [128, 512], BF16) for i in range(2)]
    th_all = sb("t1_thall", [128, 512], BF16)
    th_f = [th_all[0:32, :]] + [sb(f"t1_th{i}", [32, 512], BF16)[:, :]
                                for i in range(1, 4)]
    tdh = sb("t1_tdh", [64, 512], BF16)
    wtile = sb("t1_wtile", [128, 128], F32)
    ybuf = sb("t1_ybuf", [128, 1024], BF16)
    ynT = sb("t1_ynT", [128, CT * 128], BF16)
    S32 = sb("t1_S", [64, 16 * 64], F32)
    Sbf = g.rbc64.bitcast(BF16)[0:64, 256:320]
    A_sb = sb("t1_A", [64, 128], F32)
    Ash = sb("t1_Ash", [64, 128], F32)
    rtil = g.scr2.bitcast(BF16)[0:64, 128:256]
    ktil = g.scr2.bitcast(BF16)[0:64, 256:384]
    r0 = g.scr2.bitcast(BF16)[0:64, 384:512]
    rA = g.scr[0:64, 128:256]
    k0 = g.scr.bitcast(BF16)[0:64, 768:896]
    khat = g.sqbf[0:64, 256:384]
    khtok = g.sqbf[:, 384:448]
    dcol = sb("t1_dcol", [128, 1], F32)
    rku = g.sqbf[0:64, 128:256]
    drow = g.rowa[:, 0:128]
    zeros128 = g.mrbc[:, 0:128]
    mstrict = sb("t1_mstrict", [128, 128], BF16)
    m16 = g.rbc.bitcast(F32)[:, 160:176] if False else g.rbc[:, 160:176]
    v16 = g.rbc[:, 176:192]
    sq16 = g.rbc[:, 192:208]

    em.on("sync", nc.sync.dma_start, maa5[:], W["maa5"][:, :], _inc=(tag + "bl", 16))
    em.on("sync", nc.sync.dma_start, tdc[:], W["td"][:, :], _inc=(tag + "bl", 16))
    em.on("sync", nc.sync.dma_start, ucol[:], W["u"][:, :], _inc=(tag + "bl", 16))
    em.on("sync", nc.sync.dma_start, lnwc[:], W["lnw"][:, :], _inc=(tag + "bl", 16))
    em.on("gpsimd", nc.gpsimd.dma_start, lnbc_[:], W["lnb"][:, :], _inc=(tag + "thd", 16))
    for f_ in range(4):
        em.on("gpsimd", nc.gpsimd.dma_start, mw2[f_][:],
              W["mw2"][:, f_ * 1024:(f_ + 1) * 1024], _inc=(tag + "thd", 16))
    em.on("gpsimd", nc.gpsimd.dma_start, dw2[:], W["dw2"][:, :], _inc=(tag + "thd", 16))
    for ct in range(CT):
        em.on("gpsimd", nc.gpsimd.dma_start, mw1[:, ct * 128:(ct + 1) * 128],
              W["mw1"][ct], _inc=(tag + "thd", 16))
        em.on("gpsimd", nc.gpsimd.dma_start, dw1[:, ct * 64:(ct + 1) * 64],
              W["dw1"][ct], _inc=(tag + "thd", 16))
    for ft in range(CT):
        for ct in range(CT):
            em.on("gpsimd", nc.gpsimd.dma_start,
                  g.bigA[:, WR + ft * 1024 + ct * 128: WR + ft * 1024 + (ct + 1) * 128],
                  W["wr"][ft, ct], _inc=(tag + "w1a", 16))
            em.on("gpsimd", nc.gpsimd.dma_start,
                  g.bigA[:, WK + ft * 1024 + ct * 128: WK + ft * 1024 + (ct + 1) * 128],
                  W["wk"][ft, ct], _inc=(tag + "w1a", 16))
            em.on("gpsimd", nc.gpsimd.dma_start,
                  g.bigA[:, WO + ft * 1024 + ct * 128: WO + ft * 1024 + (ct + 1) * 128],
                  W["wo"][ft, ct], _inc=(tag + "w1a", 16))
    wb_tgt = em.cnt[tag + "w1a"]
    n = em.on("gpsimd", nc.gpsimd.memset, mstrict[:], 1.0, _inc=(tag + "bc2", 1))
    em.wait("gpsimd", tag + "bc2", n)
    em.on("gpsimd", nc.gpsimd.affine_select, mstrict[:], mstrict[:],
          [[1, 128]], compare_op=ALU.is_ge, fill=0.0,
          base=-1, channel_multiplier=-1, _inc=(tag + "bc2", 1))
    em.on("vector", nc.vector.memset, zeros128[:, :], 0.0)
    em.on("vector", nc.vector.memset, S32[:], 0.0, _inc=(tag + "qe", 1))
    em.flush("t1_setup")

    em.wait("tensor", tag + "w1a", wb_tgt)
    em.wait("tensor", tag + "bl", em.cnt[tag + "bl"])
    em.wait("tensor", tag + "thd", em.cnt[tag + "thd"])
    for ft in range(CT):
        for ct in range(CT):
            em.on("tensor", nc.tensor.matmul, g.p_row[0:128, 0:1].bitcast(F32)
                  if False else g.p_v[:, 0:1],
                  g.bigA[:, WO + ft * 1024 + ct * 128: WO + ft * 1024 + (ct + 1) * 128],
                  lnbc_[:, ct:ct + 1],
                  start=(ct == 0), stop=(ct == CT - 1),
                  _inc=(tag + "vm", 1) if ct == CT - 1 else None)
        em.wait_now("vector", tag + "vm")
        em.on("vector", nc.vector.tensor_copy, biasW[:, ft:ft + 1],
              g.p_v[:, 0:1], _inc=(tag + "ve", 1))
        em.wait("tensor", tag + "ve", em.cnt[tag + "ve"])
    em.wait("vector", tag + "vm", em.cnt[tag + "vm"])
    for ft in range(CT):
        for ct in range(CT):
            em.vop(nc.vector.tensor_scalar_mul,
                   g.bigA[:, WO + ft * 1024 + ct * 128: WO + ft * 1024 + (ct + 1) * 128],
                   g.bigA[:, WO + ft * 1024 + ct * 128: WO + ft * 1024 + (ct + 1) * 128],
                   lnwc[:, ct:ct + 1], _inc=(tag + "sq", 1))
    em.flush("t1_biasw")

    STR = {"w": 1, "k": 2, "v": 3, "r": 4}

    for tc in range(NCH):
        em.wait("sync", tag + "oe", em.cnt.get(tag + "oe", 0))
        if tc == 0:
            em.wait_now("sync", "f0z1")
            em.wait_now("vector", "f0z1")
            em.wait_now("vector", tag + "bl")
            em.wait_now("scalar", tag + "bl")
            em.wait_now("vector", tag + "thd")
            em.wait_now("scalar", tag + "thd")
            em.wait_now("tensor", tag + "thd")
            em.wait("tensor", tag + "sq", em.cnt[tag + "sq"])
            for ct in range(CT):
                em.on("sync", nc.sync.dma_start, xch[:, ct * 513 + 1: (ct + 1) * 513],
                      z2_hbm[:, ct * T: ct * T + 512], _inc=(tag + "nr", 16))
            em.wait_now("vector", tag + "nr")
            for ct in range(CT):
                em.on("vector", nc.vector.memset, xch[:, ct * 513: ct * 513 + 1],
                      0.0, _inc=(tag + "nre", 1))
            em.wait("vector", tag + "nre", em.cnt[tag + "nre"])
        else:
            for ct in range(CT):
                em.on("sync", nc.sync.dma_start, xch[:, ct * 513: (ct + 1) * 513],
                      z2_hbm[:, ct * T + tc * 512 - 1: ct * T + (tc + 1) * 512],
                      _inc=(tag + "nr", 16))
            em.wait_now("vector", tag + "nr")
        em.wait_now("tensor", tag + "nr")
        xv = lambda ct: xch[:, ct * 513 + 1: (ct + 1) * 513]
        xs = lambda ct: xch[:, ct * 513: (ct + 1) * 513 - 1]

        for ct in range(CT):
            em.vop(nc.vector.tensor_sub, g.scr, xs(ct), xv(ct))
            em.vop(nc.vector.scalar_tensor_tensor, xxx_rT[:, ts(ct, 512)],
                   g.scr, maa5[:, 0 * 8 + ct: 0 * 8 + ct + 1], xv(ct),
                   op0=ALU.mult, op1=ALU.add, _inc=(tag + "s1", 1))
        em.wait("tensor", tag + "s1", em.cnt[tag + "s1"])
        em.wait("tensor", tag + "s2", em.cnt.get(tag + "s2", 0))
        em.wait("tensor", tag + "pv", em.cnt.get(tag + "pv", 0))
        em.wait("tensor", tag + "s", em.cnt.get(tag + "s", 0))
        for ct in range(CT):
            em.on("tensor", nc.tensor.matmul, g.p_a[0][:, :],
                  mw1[:, ct * 128:(ct + 1) * 128], xxx_rT[:, ts(ct, 512)],
                  start=(ct == 0), stop=(ct == CT - 1),
                  _inc=(tag + "s3", 1) if ct == CT - 1 else None)
        em.wait_now("scalar", tag + "s3")
        em.wait("scalar", tag + "s4", em.cnt.get(tag + "s4", 0))
        em.wait("scalar", tag + "thd", em.cnt.get(tag + "thd", 0))
        em.sop(nc.scalar.activation, th_all[:, :], g.p_a[0][:, :], AF.Tanh,
               bias=0.0, scale=1.0, _inc=(tag + "s2", 1))
        em.wait("gpsimd", tag + "s2", em.cnt[tag + "s2"])
        for f_ in range(1, 4):
            em.on("gpsimd", nc.gpsimd.dma_start, th_f[f_],
                  th_all[f_ * 32:(f_ + 1) * 32, :], _inc=(tag + "thd", 16))
        em.wait_now("tensor", tag + "thd")
        em.wait_now("tensor", tag + "s2")

        for f, fname in enumerate(("w", "k", "v", "r")):
            em.wait("vector", tag + "ex", em.cnt.get(tag + "ex", 0))
            em.wait("vector", tag + "ns", em.cnt.get(tag + "ns", 0))
            em.wait("tensor", tag + "pv", em.cnt.get(tag + "pv", 0))
            mcol = STR[fname]
            for ct in range(CT):
                em.wait("tensor", tag + "nsp", em.cnt.get(tag + "nsp", 0))
                em.on("tensor", nc.tensor.matmul, g.p_a[1][:, :],
                      mw2[f][0:32, ct * 128:(ct + 1) * 128], th_f[f],
                      start=True, stop=True, _inc=(tag + "s4", 1))
                em.wait_now("vector", tag + "s4")
                em.vop(nc.vector.tensor_sub, g.scr, xs(ct), xv(ct))
                em.vop(nc.vector.tensor_scalar, g.scr2, g.p_a[1][:, :],
                       maa5[:, mcol * 8 + ct: mcol * 8 + ct + 1], None,
                       op0=ALU.add)
                em.vop(nc.vector.tensor_mul, g.scr, g.scr, g.scr2)
                em.vop(nc.vector.tensor_add, mixf[:, ts(ct, 512)], g.scr,
                       xv(ct), _inc=(tag + "nsp", 1))
            em.wait("tensor", tag + "nsp", em.cnt[tag + "nsp"])
            if fname == "w":
                em.wait("tensor", tag + "bk", em.cnt.get(tag + "bk", 0))
                for ct in range(CT):
                    em.on("tensor", nc.tensor.matmul, g.p_a[0][0:64, :],
                          dw1[:, ct * 64:(ct + 1) * 64], mixf[:, ts(ct, 512)],
                          start=(ct == 0), stop=(ct == CT - 1),
                          _inc=(tag + "ns", 1) if ct == CT - 1 else None)
                em.wait_now("scalar", tag + "ns")
                em.wait("scalar", tag + "mk", em.cnt.get(tag + "mk", 0))
                em.sop(nc.scalar.activation, tdh[:, :], g.p_a[0][0:64, :],
                       AF.Tanh, bias=0.0, scale=1.0, _inc=(tag + "bk", 1))
                em.wait_now("tensor", tag + "bk")
            elif fname in ("k", "r"):
                dst = kT if fname == "k" else xxx_rT
                WX = WK if fname == "k" else WR
                for ft in range(CT):
                    em.wait("tensor", tag + "pv", em.cnt.get(tag + "pv", 0) - 1)
                    for ct in range(CT):
                        em.on("tensor", nc.tensor.matmul,
                              g.p_a[ft % 2][:, :],
                              g.bigA[:, WX + ft * 1024 + ct * 128:
                                     WX + ft * 1024 + (ct + 1) * 128],
                              mixf[:, ts(ct, 512)],
                              start=(ct == 0), stop=(ct == CT - 1),
                              _inc=(tag + "ex", 1) if ct == CT - 1 else None)
                    em.wait_now("vector", tag + "ex")
                    em.on("vector", nc.vector.tensor_copy, dst[:, ts(ft, 512)],
                          g.p_a[ft % 2][:, :], _inc=(tag + "pv", 1))
            elif fname == "v":
                em.wait("tensor", tag + "ye", em.cnt.get(tag + "ye", 0))
                em.wait("tensor", tag + "p1", em.cnt.get(tag + "p1", 0))
                em.wait("tensor", tag + "bfre", em.cnt.get(tag + "bfre", 0))
                banks = [g.p_a[0], g.p_a[1], g.p_y, g.p_bc]
                wv_t = {0: 0, 1: 0}
                vstep = []

                def load_wv(fh, ct, slot, step):
                    if step >= 2:
                        em.wait("gpsimd", tag + "ln1", vstep[step - 2])
                    wv_t[slot] = em.on(
                        "gpsimd", nc.gpsimd.dma_start, wv_s[slot][:, :],
                        W["wv"][ct * 128:(ct + 1) * 128, fh * 512:(fh + 1) * 512],
                        _inc=(tag + f"wv{slot}", 16))

                load_wv(0, 0, 0, 0)
                step = 0
                for fh in range(2):
                    for ct in range(CT):
                        slot = step % 2
                        em.wait("tensor", tag + f"wv{slot}", wv_t[slot])
                        for tt in range(SUBS):
                            em.on("tensor", nc.tensor.matmul, banks[tt][:, :],
                                  mixf[:, ct * 512 + tt * 128:
                                       ct * 512 + (tt + 1) * 128],
                                  wv_s[slot][:, :],
                                  start=(ct == 0), stop=(ct == CT - 1),
                                  _inc=(tag + "ln1", 1) if tt == SUBS - 1 else None)
                        vstep.append(em.cnt[tag + "ln1"])
                        if step + 1 < 16:
                            nfh, nct = divmod(step + 1, CT)
                            load_wv(nfh, nct, 1 - slot, step + 1)
                        step += 1
                    em.wait_now("vector", tag + "ln1")
                    for tt in range(SUBS):
                        em.on("vector", nc.vector.tensor_copy,
                              vtok[:, tt * 1024 + fh * 512:
                                   tt * 1024 + (fh + 1) * 512],
                              banks[tt][:, :], _inc=(tag + "pv", 1))
                    em.wait("tensor", tag + "pv", em.cnt[tag + "pv"])
        em.wait("tensor", tag + "pv", em.cnt[tag + "pv"])

        for sub in range(SUBS):
            for h in range(16):
                ct = h // 2
                hb = 64 * (h % 2)
                em.wait("tensor", tag + "oe", em.cnt.get(tag + "oe", 0))
                if h % 2 == 0:
                    em.wait("tensor", tag + "s", em.cnt.get(tag + "s", 0))
                    em.on("tensor", nc.tensor.matmul, g.p_a[1][:, 0:128],
                          dw2[:, ct * 128:(ct + 1) * 128],
                          tdh[:, sub * 128:(sub + 1) * 128],
                          start=True, stop=True, _inc=(tag + "mk", 1))
                    em.wait_now("scalar", tag + "mk")
                    em.sop(nc.scalar.activation, wtile[:, :], g.p_a[1][:, 0:128],
                           AF.Exp, bias=tdc[:, ct:ct + 1], scale=1.0)
                    em.sop(nc.scalar.activation, wtile[:, :], wtile[:, :],
                           AF.Exp, bias=0.0, scale=-1.0, _inc=(tag + "s", 1))
                    em.wait_now("vector", tag + "s")
                rsl = xxx_rT[hb:hb + 64, ct * 512 + sub * 128: ct * 512 + (sub + 1) * 128]
                ksl = kT[hb:hb + 64, ct * 512 + sub * 128: ct * 512 + (sub + 1) * 128]
                vsl = vtok[:, sub * 1024 + h * 64: sub * 1024 + (h + 1) * 64]
                wsl = wtile[hb:hb + 64, :]
                em.vop(nc.vector.tensor_copy, r0[:, :], rsl)
                em.vop(nc.vector.tensor_copy, k0[:, :], ksl)
                em.vop(nc.vector.tensor_tensor_scan, A_sb[:, :], wsl,
                       zeros128[hb:hb + 64, :], 1.0, op0=ALU.mult, op1=ALU.add)
                em.vop(nc.vector.tensor_copy, Ash[:, 1:128], A_sb[:, 0:127])
                em.vop(nc.vector.memset, Ash[:, 0:1], 1.0)
                em.vop(nc.vector.tensor_mul, rtil[:, :], r0[:, :], Ash[:, :])
                em.vop(nc.vector.reciprocal, rA[:, :], A_sb[:, :])
                em.vop(nc.vector.tensor_mul, ktil[:, :], k0[:, :], rA[:, :])
                em.vop(nc.vector.tensor_scalar_mul, khat[:, :], ktil[:, :],
                       A_sb[:, 127:128])
                em.vop(nc.vector.tensor_scalar_mul, g.scr[0:64, 0:128], r0[:, :],
                       ucol[:, h:h + 1])
                em.vop(nc.vector.tensor_mul, rku[:, :],
                       g.scr[0:64, 0:128], k0[:, :], _inc=(tag + "rc", 1))
                em.wait("tensor", tag + "rc", em.cnt[tag + "rc"])
                em.on("tensor", nc.tensor.transpose,
                      g.p_v[:, :].bitcast(BF16)[:, 0:64], khat[:, :],
                      g.ident[0:64, 0:64], _inc=(tag + "bc", 1))
                em.wait_now("vector", tag + "bc")
                em.on("vector", nc.vector.tensor_copy, khtok[:, :],
                      g.p_v[:, :].bitcast(BF16)[:, 0:64], _inc=(tag + "bce", 1))
                em.on("tensor", nc.tensor.matmul, g.p_row[:, 0:128],
                      g.onescol[0:64, :], rku[:, :], start=True, stop=True,
                      _inc=(tag + "qm", 1))
                em.wait_now("vector", tag + "qm")
                em.vop(nc.vector.tensor_copy, drow[:, :], g.p_row[:, 0:128],
                       _inc=(tag + "sq2", 1))
                em.wait("tensor", tag + "sq2", em.cnt[tag + "sq2"])
                em.on("tensor", nc.tensor.matmul, g.p_tp[:, :].bitcast(F32)[:, 0:1],
                      drow[:, :], g.onesrowf[:, 0:1], start=True, stop=True,
                      _inc=(tag + "om", 1))
                em.wait_now("vector", tag + "om")
                em.vop(nc.vector.tensor_copy, dcol[:, :],
                       g.p_tp[:, :].bitcast(F32)[:, 0:1], _inc=(tag + "pre", 1))
                em.wait("tensor", tag + "fx", em.cnt.get(tag + "fx", 0))
                em.on("tensor", nc.tensor.matmul, g.p_a[0][:, 0:128],
                      ktil[:, :], rtil[:, :], start=True, stop=True,
                      _inc=(tag + "bcc", 1))
                em.wait_now("vector", tag + "bcc")
                em.vop(nc.vector.tensor_mul, g.sqbf[:, 0:128],
                       g.p_a[0][:, 0:128], mstrict[:, :], _inc=(tag + "fx", 1))
                em.vop(nc.vector.tensor_copy, Sbf[:, :],
                       S32[:, h * 64:(h + 1) * 64], _inc=(tag + "fxe", 1))
                em.wait("tensor", tag + "fx", em.cnt[tag + "fx"])
                em.wait("tensor", tag + "fxe", em.cnt[tag + "fxe"])
                em.wait("tensor", tag + "ye", em.cnt.get(tag + "ye", 0))
                em.on("tensor", nc.tensor.matmul, g.p_y[:, 0:64],
                      g.sqbf[:, 0:128], vsl, start=True, stop=False)
                em.on("tensor", nc.tensor.matmul, g.p_y[:, 0:64],
                      rtil[:, :], Sbf[:, :], start=False, stop=True,
                      _inc=(tag + "rote", 1))
                em.wait("tensor", tag + "bce", em.cnt[tag + "bce"])
                em.wait("tensor", tag + "p1", em.cnt.get(tag + "p1", 0))
                em.on("tensor", nc.tensor.matmul, g.p_bc[0:64, 0:64],
                      khtok[:, :], vsl, start=True, stop=True,
                      _inc=(tag + "p1m", 1))
                em.wait_now("vector", tag + "p1m")
                em.vop(nc.vector.tensor_scalar_mul, g.scr2[0:64, 0:64],
                       S32[:, h * 64:(h + 1) * 64], A_sb[:, 127:128])
                em.vop(nc.vector.tensor_add, S32[:, h * 64:(h + 1) * 64],
                       g.scr2[0:64, 0:64], g.p_bc[0:64, 0:64],
                       _inc=(tag + "p1", 1))
                em.wait("vector", tag + "pre", em.cnt[tag + "pre"])
                em.wait("vector", tag + "rote", em.cnt[tag + "rote"])
                em.vop(nc.vector.tensor_scalar_mul, g.scr[:, 0:64], vsl,
                       dcol[:, :])
                em.vop(nc.vector.tensor_add, ybuf[:, h * 64:(h + 1) * 64],
                       g.p_y[:, 0:64], g.scr[:, 0:64], _inc=(tag + "ye", 1))
            em.wait("vector", tag + "ye", em.cnt[tag + "ye"])
            em.vop(nc.vector.reduce_sum, m16[:, :],
                   ybuf[:, :].rearrange("p (h j) -> p h j", j=64), axis=AX.X)
            em.vop(nc.vector.tensor_mul, g.scr.bitcast(BF16)[:, 0:1024],
                   ybuf[:, :], ybuf[:, :])
            em.vop(nc.vector.reduce_sum, sq16[:, :],
                   g.scr.bitcast(BF16)[:, 0:1024]
                   .rearrange("p (h j) -> p h j", j=64), axis=AX.X)
            em.vop(nc.vector.tensor_scalar_mul, m16[:, :], m16[:, :], 1.0 / 64)
            em.vop(nc.vector.tensor_scalar_mul, sq16[:, :], sq16[:, :], 1.0 / 64)
            em.vop(nc.vector.tensor_mul, v16[:, :], m16[:, :], m16[:, :])
            em.vop(nc.vector.tensor_sub, v16[:, :], sq16[:, :], v16[:, :],
                   _inc=(tag + "mx", 1))
            em.wait_now("scalar", tag + "mx")
            em.sop(nc.scalar.activation, v16[:, :], v16[:, :], AF.Ln,
                   bias=g.cvcol[:, 2:3], scale=1.0)
            em.sop(nc.scalar.activation, v16[:, :], v16[:, :], AF.Exp,
                   bias=0.0, scale=-0.5, _inc=(tag + "mxn", 1))
            em.wait_now("vector", tag + "mxn")
            for h in range(16):
                em.vop(nc.vector.tensor_scalar, ybuf[:, h * 64:(h + 1) * 64],
                       ybuf[:, h * 64:(h + 1) * 64],
                       m16[:, h:h + 1], v16[:, h:h + 1],
                       op0=ALU.subtract, op1=ALU.mult, _inc=(tag + "mxt", 1))
            em.wait("tensor", tag + "mxt", em.cnt[tag + "mxt"])
            for ct in range(CT):
                em.wait("tensor", tag + "ae", em.cnt.get(tag + "ae", 0) - 1)
                em.on("tensor", nc.tensor.transpose, g.tp_slots[ct % 2],
                      ybuf[:, ct * 128:(ct + 1) * 128], g.ident[:],
                      _inc=(tag + "mxe", 1))
                em.wait("vector", tag + "mxe", em.cnt[tag + "mxe"])
                em.on("vector", nc.vector.tensor_copy, ynT[:, ts(ct)],
                      g.tp_slots[ct % 2], _inc=(tag + "ae", 1))
            em.wait("tensor", tag + "ae", em.cnt[tag + "ae"])
            for ft in range(CT):
                em.wait("tensor", tag + "oe", em.cnt.get(tag + "oe", 0) - 1)
                for ct in range(CT):
                    em.on("tensor", nc.tensor.matmul, g.p_a[ft % 2][:, 0:128],
                          g.bigA[:, WO + ft * 1024 + ct * 128:
                                 WO + ft * 1024 + (ct + 1) * 128],
                          ynT[:, ts(ct)], start=(ct == 0), stop=(ct == CT - 1),
                          _inc=(tag + "rc2", 1) if ct == CT - 1 else None)
                em.wait_now("vector", tag + "rc2")
                em.vop(nc.vector.tensor_scalar, g.scr[:, 0:128],
                       g.p_a[ft % 2][:, 0:128], biasW[:, ft:ft + 1], None,
                       op0=ALU.add)
                em.vop(nc.vector.tensor_add,
                       g.xT[:, ft * T + tc * 512 + sub * 128:
                            ft * T + tc * 512 + (sub + 1) * 128],
                       g.scr[:, 0:128],
                       xch[:, ft * 513 + 1 + sub * 128: ft * 513 + 1 + (sub + 1) * 128],
                       _inc=(tag + "oe", 1))
        em.wait("tensor", tag + "oe", em.cnt[tag + "oe"])
        em.wait("vector", tag + "oe", em.cnt[tag + "oe"])
        for ct in range(CT):
            em.on("tensor", nc.tensor.matmul, g.p_row[:, :], g.onescol[:, :],
                  g.xT[:, ct * T + tc * 512: ct * T + (tc + 1) * 512],
                  start=(ct == 0), stop=(ct == CT - 1), _inc=(tag + "st1", 1))
            em.wait("vector", tag + "st2", em.cnt.get(tag + "st2", 0))
            n = em.on("vector", nc.vector.tensor_mul, g.sqbf[:, :],
                      g.xT[:, ct * T + tc * 512: ct * T + (tc + 1) * 512],
                      g.xT[:, ct * T + tc * 512: ct * T + (tc + 1) * 512],
                      _inc=(tag + "sqt", 1))
            em.wait("tensor", tag + "sqt", n)
            em.on("tensor", nc.tensor.matmul, g.p_row2[:, :], g.onescol[:, :],
                  g.sqbf[:, :], start=(ct == 0), stop=(ct == CT - 1),
                  _inc=(tag + "st2", 1))
        em.wait_now("vector", tag + "st1")
        em.wait_now("vector", tag + "st2")
        em.vop(nc.vector.tensor_scalar_mul, g.lm, g.p_row[:, :], 1.0 / D)
        em.vop(nc.vector.tensor_scalar_mul, g.lr, g.p_row2[:, :], 1.0 / D,
               _inc=(tag + "lv", 1))
        em.wait("tensor", tag + "lv", em.cnt[tag + "lv"])
        em.vop(nc.vector.tensor_mul, g.lmr, g.lm, g.lm)
        em.vop(nc.vector.tensor_sub, g.lr, g.lr, g.lmr, _inc=(tag + "lv2", 1))
        em.wait_now("scalar", tag + "lv2")
        em.sop(nc.scalar.activation, g.lr, g.lr, AF.Ln,
               bias=g.cvals[:, 1:2], scale=1.0)
        em.sop(nc.scalar.activation, g.lr, g.lr, AF.Exp,
               bias=0.0, scale=-0.5, _inc=(tag + "lnr", 1))
        em.wait_now("vector", tag + "lnr")
        em.vop(nc.vector.tensor_mul, g.lmr, g.lm, g.lr, _inc=(tag + "hl", 1))
        em.wait("tensor", tag + "hl", em.cnt[tag + "hl"])
        em.wait("tensor", tag + "bfre", em.cnt.get(tag + "bfre", 0))
        n1 = em.on("tensor", nc.tensor.matmul, g.p_bc[:, :], g.onesrowf[:, :],
                   g.lr, start=True, stop=True, _inc=(tag + "bcf", 1))
        em.wait("vector", tag + "bcf", n1)
        em.on("vector", nc.vector.tensor_copy, g.rbc, g.p_bc[:, :],
              _inc=(tag + "bfre", 1))
        em.wait("tensor", tag + "bfre", em.cnt[tag + "bfre"])
        n2 = em.on("tensor", nc.tensor.matmul, g.p_bc[:, :], g.onesrowf[:, :],
                   g.lmr, start=True, stop=True, _inc=(tag + "bcf", 1))
        em.wait("vector", tag + "bcf", n2)
        em.on("vector", nc.vector.tensor_copy, g.mrbc, g.p_bc[:, :],
              _inc=(tag + "bfre", 1))
        for ct in range(CT):
            em.vop(nc.vector.tensor_mul, g.scr,
                   g.xT[:, ct * T + tc * 512: ct * T + (tc + 1) * 512], g.rbc)
            em.vop(nc.vector.tensor_sub,
                   g.xT[:, ct * T + tc * 512: ct * T + (tc + 1) * 512],
                   g.scr, g.mrbc, _inc=("a0fin", 1))
        em.vop(nc.vector.memset, zeros128[:, :], 0.0, _inc=(tag + "z0", 1))
    em.flush("t1_main")
    es.close()




def build_l1_ffn(g, W, C4, eoT_hbm, moe_hbm, out_hbm, y_h, scr_dram):
    nc, em = g.nc, g.em
    from contextlib import ExitStack
    es = ExitStack()
    sb = lambda n_, sh, dt: es.enter_context(nc.sbuf_tensor(n_, sh, dt))
    tag = "a0"
    CAP = 512

    H0 = 0
    WD0 = 16 * 512
    XET0 = WD0 + 16 * 1024
    hbuf = lambda ft: g.bigA[:, H0 + ft * 512: H0 + (ft + 1) * 512]
    wdt = lambda ft: g.bigA[:, WD0 + ft * 1024: WD0 + (ft + 1) * 1024]
    xeT = g.bigA[:, XET0: XET0 + CT * 512]
    bigf = g.bigA[:, :].bitcast(F32)
    bigu = g.bigA[:, :].bitcast(U16)
    Ctab = bigu[:, 0:8192]
    sel_em = bigf[0:16, 4096:8192]
    C_em = bigf[0:16, 8192:12288]
    inv_em = bigf[0:16, 12288:16384]
    smajS = bigf[:, 0:4096]
    dtile = lambda e_: g.bigA[:, e_ * 520:(e_ + 1) * 520]

    gate_s = sb("m1_gate_s", [128, CT * 16], BF16)
    lsb = sb("m1_lsb", [16, 512], BF16)
    ltok = sb("m1_ltok", [128, 512], BF16)
    stok = sb("m1_stok", [128, 512], F32)
    sblk = sb("m1_sblk", [128, 512], F32)
    E8 = sb("m1_E8", [128, 16], F32)
    E8T = sb("m1_E8T", [16, 128], F32)
    LB = sb("m1_LB", [128, 128], F32)
    iotaB = sb("m1_iotaB", [128, 512], F32)
    Lrep = sb("m1_Lrep", [16, 256], F32)
    offc = sb("m1_offc", [128, 64], F32)
    jp1 = sb("m1_jp1", [128, 64], F32)
    maskc = sb("m1_maskc", [128, 16], F32)
    lo_c = sb("m1_lo", [128, 1], F32)
    hi_c = sb("m1_hi", [128, 1], F32)
    mid_c = sb("m1_mid", [128, 1], F32)
    cnt_c = sb("m1_cnt", [128, 1], F32)
    cond_c = sb("m1_cond", [128, 1], F32)
    need_c = sb("m1_need", [128, 1], F32)
    cnte = sb("m1_cnte", [16, 1], F32)
    t_d = sb("m1_td_", [128, 1], F32)
    t_n = sb("m1_tn_", [128, 1], F32)
    gemask = sb("m1_ge", [128, 512], BF16)
    gehi = sb("m1_gehi", [128, 512], BF16)
    band = sb("m1_band", [128, 512], BF16)
    scst = [sb(f"m1_scst{i}", [16, 128], F32) for i in range(2)]
    lo2 = sb("m1_lo2", [128, 64], F32)
    hi2 = sb("m1_hi2", [128, 64], F32)
    midf = sb("m1_midf", [128, 64], F32)
    mids = sb("m1_mids", [128, 64], F32)
    midu = sb("m1_midu", [128, 64], U16)
    vcp = sb("m1_vcp", [128, 64], F32)
    val = sb("m1_val", [128, 64], F32)
    cond2 = sb("m1_cond2", [128, 64], F32)
    idw = sb("m1_idw", [128, 512], U16)
    invw = sb("m1_invw", [128, 4096], U16)
    wcolsAll = sb("m1_wca", [128, 64], F32)
    eoTb = sb("m1_eoTb", [128, 8 * 520], BF16)
    gbuf = eoTb[:, :].bitcast(U16)[:, 0:1024]
    scg = eoTb[:, :].bitcast(F32)[:, 512:1024]
    selm = eoTb[:, :].bitcast(F32)[:, 1024:1536]
    rank = eoTb[:, :].bitcast(F32)[:, 1536:2048]
    wg_s = [sb(f"m1_wg{i}", [128, 2048], BF16) for i in range(2)]
    wu_s = [sb(f"m1_wu{i}", [128, 2048], BF16) for i in range(2)]
    eo = sb("m1_eo", [128, 1024], F32)
    ysb = eo[:, :]
    gathw = g.bigA[:, 9344: 9344 + 16 * 512]
    accf = bigf[:, 8768:9280]
    mts = [bigf[:, 9280 + 128 * i: 9280 + 128 * (i + 1)] for i in range(2)]
    osb = [invw[:, :].bitcast(F32)[:, 1024 * i: 1024 * (i + 1)]
           for i in range(2)]
    msb = [eoTb[:, :].bitcast(F32)[:, 1024 * i: 1024 * (i + 1)]
           for i in range(2)]
    xtk = g.bigA[:, 0:1024]

    em.on("sync", nc.sync.dma_start, E8[:], C4["E8"][:, :], _inc=(tag + "bl", 16))
    em.on("sync", nc.sync.dma_start, E8T[:], C4["E8T"][:, :], _inc=(tag + "bl", 16))
    em.on("sync", nc.sync.dma_start, LB[:], C4["LB"][:, :], _inc=(tag + "bl", 16))
    em.on("sync", nc.sync.dma_start, iotaB[:], C4["iotaB"][:, :], _inc=(tag + "bl", 16))
    em.on("sync", nc.sync.dma_start, Lrep[:], C4["Lrep"][:, :], _inc=(tag + "bl", 16))
    em.on("sync", nc.sync.dma_start, offc[:], C4["offc"][:, :], _inc=(tag + "bl", 16))
    em.on("sync", nc.sync.dma_start, jp1[:], C4["jp1"][:, :], _inc=(tag + "bl", 16))
    em.on("sync", nc.sync.dma_start, maskc[:], C4["maskc"][:, :], _inc=(tag + "bl", 16))
    em.on("gpsimd", nc.gpsimd.dma_start, gate_s[:], W["gate"][:, :],
          _inc=(tag + "thd", 16))
    em.flush("m1_setup")

    em.wait_now("tensor", tag + "thd")
    for tc in range(NCH):
        em.wait("tensor", tag + "qe", em.cnt.get(tag + "qe", 0))
        for ct in range(CT):
            em.on("tensor", nc.tensor.matmul, g.p_a[0][0:16, :],
                  gate_s[:, ct * 16:(ct + 1) * 16],
                  g.xT[:, ct * T + tc * 512: ct * T + (tc + 1) * 512],
                  start=(ct == 0), stop=(ct == CT - 1),
                  _inc=("a0om", 1) if ct == CT - 1 else None)
        em.wait_now("vector", "a0om")
        em.on("vector", nc.vector.tensor_copy, lsb[:, :], g.p_a[0][0:16, :],
              _inc=(tag + "qe", 1))
        em.wait("tensor", tag + "qe", em.cnt[tag + "qe"])
        for q in range(4):
            tt = tc * 4 + q
            em.wait("tensor", tag + "mk", em.cnt.get(tag + "mk", 0) - 1)
            em.on("tensor", nc.tensor.transpose, g.tp_slots[q % 2][:, 0:16],
                  lsb[:, q * 128:(q + 1) * 128],
                  g.ident[0:16, 0:16], _inc=(tag + "s", 1))
            em.wait("vector", tag + "s", em.cnt[tag + "s"])
            em.on("vector", nc.vector.tensor_copy,
                  ltok[:, tt * 16:(tt + 1) * 16],
                  g.tp_slots[q % 2][:, 0:16], _inc=(tag + "mk", 1))
    em.wait("vector", tag + "mk", em.cnt[tag + "mk"])
    for tt in range(NT):
        sl = ltok[:, tt * 16:(tt + 1) * 16]
        em.vop(nc.vector.reduce_max, g.scr[:, 0:1], sl, axis=AX.X)
        em.vop(nc.vector.tensor_scalar_mul, g.scr[:, 1:2], g.scr[:, 0:1], -1.0,
               _inc=(tag + "s1", 1))
        em.wait_now("scalar", tag + "s1")
        em.sop(nc.scalar.activation, g.sqbf[:, 0:16], sl, AF.Exp,
               bias=g.scr[:, 1:2], scale=1.0, _inc=(tag + "s2", 1))
        em.wait_now("vector", tag + "s2")
        em.vop(nc.vector.reduce_sum, g.scr[:, 2:3], g.sqbf[:, 0:16], axis=AX.X)
        em.vop(nc.vector.reciprocal, g.scr[:, 3:4], g.scr[:, 2:3])
        em.vop(nc.vector.tensor_scalar_mul, stok[:, tt * 16:(tt + 1) * 16],
               g.sqbf[:, 0:16], g.scr[:, 3:4], _inc=(tag + "s3", 1))
    em.wait("tensor", tag + "s3", em.cnt[tag + "s3"])
    base_s = em.cnt.get(tag + "s", 0)
    base_mk = em.cnt.get(tag + "mk", 0)
    base_bl = em.cnt[tag + "bl"]
    for tt in range(NT):
        b_, s_ = tt // 4, tt % 4
        slot = tt % 2
        pb = g.p_y if slot == 0 else g.p_bc
        if tt >= 2:
            em.wait("tensor", tag + "mk", base_mk + tt - 1)
        em.on("tensor", nc.tensor.transpose, pb[0:16, 0:128],
              stok[:, tt * 16:(tt + 1) * 16], g.identf[:],
              _inc=(tag + "s", 1))
        if tt >= 2:
            em.wait("vector", tag + "bl", base_bl + (tt - 1) * 16)
        em.wait("vector", tag + "s", base_s + tt + 1)
        em.vop(nc.vector.tensor_copy, scst[slot][:, :], pb[0:16, 0:128],
               _inc=(tag + "mk", 1))
        em.wait("sync", tag + "mk", base_mk + tt + 1)
        em.wait("sync", tag + "bl", base_bl + tt * 16)
        em.on("sync", nc.sync.dma_start,
              sblk[b_ * 16: b_ * 16 + 16, s_ * 128:(s_ + 1) * 128],
              scst[slot][:, :], _inc=(tag + "bl", 16))
    em.flush("m1_gating")

    em.wait("tensor", tag + "bl", em.cnt[tag + "bl"])
    em.wait("vector", tag + "bl", em.cnt[tag + "bl"])
    em.on("vector", nc.vector.memset, lo_c[:], 0.0)
    em.on("vector", nc.vector.memset, hi_c[:], 1.0, _inc=("vch", 1))

    def count_ge(mask_ap):
        em.vop(nc.vector.reduce_sum, cnt_c[:], mask_ap, axis=AX.X,
               _inc=(tag + "p1", 1))
        em.wait("tensor", tag + "p1", em.cnt[tag + "p1"])
        em.wait("tensor", tag + "mxn", em.cnt.get(tag + "mxn", 0))
        n1 = em.on("tensor", nc.tensor.matmul, g.p_v[0:16, 0:1], E8[:], cnt_c[:],
                   start=True, stop=True, _inc=(tag + "p1m", 1))
        em.wait("vector", tag + "p1m", n1)
        em.vop(nc.vector.tensor_copy, cnte[:], g.p_v[0:16, 0:1],
               _inc=(tag + "mx", 1))
        em.wait("tensor", tag + "mx", em.cnt[tag + "mx"])
        n2 = em.on("tensor", nc.tensor.matmul, g.p_v[0:128, 1:2], E8T[:], cnte[:],
                   start=True, stop=True, _inc=(tag + "mxn", 1))
        em.wait("vector", tag + "mxn", n2)
        em.vop(nc.vector.tensor_copy, cnt_c[:], g.p_v[:, 1:2],
               _inc=(tag + "ns", 1))

    for it in range(45):
        em.vop(nc.vector.tensor_add, mid_c[:], lo_c[:], hi_c[:])
        em.vop(nc.vector.tensor_scalar_mul, mid_c[:], mid_c[:], 0.5)
        em.vop(nc.vector.tensor_scalar, gemask[:, :], sblk[:, :], mid_c[:],
               None, op0=ALU.is_ge)
        count_ge(gemask[:, :])
        em.vop(nc.vector.tensor_scalar, cond_c[:], cnt_c[:], float(CAP), None,
               op0=ALU.is_ge, _inc=(tag + "nsp", 1))
        em.wait("vector", tag + "nsp", em.cnt[tag + "nsp"])
        em.vop(nc.vector.tensor_scalar, t_n[:], cond_c[:], -1.0, 1.0,
               op0=ALU.mult, op1=ALU.add)
        em.vop(nc.vector.tensor_sub, t_d[:], mid_c[:], lo_c[:])
        em.vop(nc.vector.tensor_mul, t_d[:], t_d[:], cond_c[:])
        em.vop(nc.vector.tensor_add, lo_c[:], lo_c[:], t_d[:])
        em.vop(nc.vector.tensor_sub, t_d[:], mid_c[:], hi_c[:])
        em.vop(nc.vector.tensor_mul, t_d[:], t_d[:], t_n[:])
        em.vop(nc.vector.tensor_add, hi_c[:], hi_c[:], t_d[:])
    em.vop(nc.vector.tensor_scalar, gehi[:, :], sblk[:, :], hi_c[:], None,
           op0=ALU.is_ge)
    count_ge(gehi[:, :])
    em.vop(nc.vector.tensor_scalar, need_c[:], cnt_c[:], -1.0, float(CAP),
           op0=ALU.mult, op1=ALU.add)
    em.vop(nc.vector.tensor_scalar, gemask[:, :], sblk[:, :], lo_c[:], None,
           op0=ALU.is_ge)
    em.vop(nc.vector.tensor_sub, band[:, :], gemask[:, :], gehi[:, :])
    em.vop(nc.vector.memset, lo_c[:], -1.0)
    em.vop(nc.vector.memset, hi_c[:], 4095.0, _inc=("vch", 1))
    for it in range(13):
        em.vop(nc.vector.tensor_add, mid_c[:], lo_c[:], hi_c[:])
        em.vop(nc.vector.tensor_scalar_mul, mid_c[:], mid_c[:], 0.5)
        em.vop(nc.vector.tensor_scalar, selm[:, :], iotaB[:, :], mid_c[:],
               None, op0=ALU.is_le)
        em.vop(nc.vector.tensor_mul, selm[:, :], selm[:, :], band[:, :])
        count_ge(selm[:, :])
        em.vop(nc.vector.tensor_tensor, cond_c[:], cnt_c[:], need_c[:],
               op=ALU.is_ge, _inc=(tag + "nsp", 1))
        em.wait("vector", tag + "nsp", em.cnt[tag + "nsp"])
        em.vop(nc.vector.tensor_scalar, t_n[:], cond_c[:], -1.0, 1.0,
               op0=ALU.mult, op1=ALU.add)
        em.vop(nc.vector.tensor_sub, t_d[:], mid_c[:], hi_c[:])
        em.vop(nc.vector.tensor_mul, t_d[:], t_d[:], cond_c[:])
        em.vop(nc.vector.tensor_add, hi_c[:], hi_c[:], t_d[:])
        em.vop(nc.vector.tensor_sub, t_d[:], mid_c[:], lo_c[:])
        em.vop(nc.vector.tensor_mul, t_d[:], t_d[:], t_n[:])
        em.vop(nc.vector.tensor_add, lo_c[:], lo_c[:], t_d[:])
    em.vop(nc.vector.tensor_scalar, selm[:, :], iotaB[:, :], hi_c[:], None,
           op0=ALU.is_le)
    em.vop(nc.vector.tensor_mul, selm[:, :], selm[:, :], band[:, :])
    em.vop(nc.vector.tensor_add, selm[:, :], selm[:, :], gehi[:, :])
    em.vop(nc.vector.memset, g.scr2[:, :], 1.0)
    em.vop(nc.vector.tensor_tensor_scan, rank[:, :], g.scr2[:, :], selm[:, :],
           0.0, op0=ALU.mult, op1=ALU.add)
    em.vop(nc.vector.reduce_sum, cnt_c[:], selm[:, :], axis=AX.X,
           _inc=(tag + "bk", 1))
    em.wait("tensor", tag + "bk", em.cnt[tag + "bk"])
    n1 = em.on("tensor", nc.tensor.matmul, g.p_v[:, 2:3], LB[:], cnt_c[:],
               start=True, stop=True, _inc=(tag + "ye", 1))
    em.wait("vector", tag + "ye", n1)
    em.vop(nc.vector.tensor_copy, cond_c[:], g.p_v[:, 2:3])
    em.vop(nc.vector.tensor_scalar_add, rank[:, :], rank[:, :], cond_c[:])
    em.vop(nc.vector.tensor_mul, rank[:, :], rank[:, :], selm[:, :])
    em.vop(nc.vector.tensor_scalar, g.scr[:, :], selm[:, :], -513.0,
           512.0, op0=ALU.mult, op1=ALU.add)
    em.vop(nc.vector.tensor_add, rank[:, :], rank[:, :], g.scr[:, :],
           _inc=(tag + "rc", 1))
    em.flush("m1_topk")

    em.wait("sync", tag + "rc", em.cnt[tag + "rc"])
    em.on("sync", nc.sync.dma_start, scr_dram[:, :], selm[:, :],
          _inc=(tag + "bl", 16))
    em.wait("sync", tag + "bl", em.cnt[tag + "bl"])
    em.on("sync", nc.sync.dma_start,
          sel_em.rearrange("p (b f) -> p b f", b=8),
          scr_dram[:, :].rearrange("(b e) f -> e b f", e=16),
          _inc=(tag + "bl", 16))
    em.wait("vector", tag + "bl", em.cnt[tag + "bl"])
    em.vop(nc.vector.memset, inv_em, 1.0)
    em.vop(nc.vector.tensor_tensor_scan, C_em, inv_em, sel_em, 0.0,
           op0=ALU.mult, op1=ALU.add)
    em.vop(nc.vector.tensor_mul, inv_em, C_em, sel_em)
    em.vop(nc.vector.tensor_scalar, sel_em, sel_em, -513.0, 512.0,
           op0=ALU.mult, op1=ALU.add)
    em.vop(nc.vector.tensor_add, inv_em, inv_em, sel_em, _inc=(tag + "ve", 1))
    base_mk = em.cnt.get(tag + "mk", 0)
    nrep = 0
    for o in range(2):
        for ch in range(8):
            pb = g.p_y if nrep % 2 == 0 else g.p_bc
            if nrep == 0:
                em.wait("tensor", tag + "ve", em.cnt[tag + "ve"])
            em.wait("tensor", tag + "mk", em.cnt[tag + "mk"] - 1)
            em.on("tensor", nc.tensor.matmul, pb[:, 0:512],
                  Lrep[:, o * 128:(o + 1) * 128],
                  C_em[:, ch * 512:(ch + 1) * 512],
                  start=True, stop=True, _inc=(tag + "s", 1))
            em.wait("vector", tag + "s", em.cnt[tag + "s"])
            em.vop(nc.vector.tensor_copy,
                   Ctab[:, o * 4096 + ch * 512: o * 4096 + (ch + 1) * 512],
                   pb[:, 0:512], _inc=(tag + "mk", 1))
            nrep += 1
    em.wait("gpsimd", tag + "mk", em.cnt[tag + "mk"])

    em.vop(nc.vector.memset, lo2[:], -1.0)
    em.vop(nc.vector.memset, hi2[:], 4095.0, _inc=("vch", 1))
    gb3 = gbuf[:, :].rearrange("p (w m) -> p w m", m=16)
    for it in range(12):
        em.vop(nc.vector.tensor_add, midf[:], lo2[:], hi2[:])
        em.vop(nc.vector.tensor_scalar_mul, midf[:], midf[:], 0.5)
        em.vop(nc.vector.tensor_copy, midu[:], midf[:])
        em.vop(nc.vector.tensor_copy, midf[:], midu[:])
        em.vop(nc.vector.tensor_add, mids[:], midf[:], offc[:])
        em.vop(nc.vector.tensor_copy, midu[:], mids[:], _inc=(tag + "ve", 1))
        em.wait("gpsimd", tag + "ve", em.cnt[tag + "ve"])
        ng = em.on("gpsimd", nc.gpsimd.indirect_copy, gbuf[:, :], Ctab,
                   midu[:, :], i_know_ap_gather_is_preferred=True,
                   _inc=(tag + "ex", 1))
        em.wait("vector", tag + "ex", ng)
        for m in range(16):
            em.vop(nc.vector.tensor_copy, vcp[:], gb3[:, :, m])
            if m == 0:
                em.vop(nc.vector.tensor_scalar_mul, val[:], vcp[:],
                       maskc[:, 0:1])
            else:
                em.vop(nc.vector.scalar_tensor_tensor, val[:], vcp[:],
                       maskc[:, m:m + 1], val[:], op0=ALU.mult, op1=ALU.add)
        em.vop(nc.vector.tensor_tensor, cond2[:], val[:], jp1[:],
               op=ALU.is_ge)
        em.vop(nc.vector.tensor_scalar, mids[:], cond2[:], -1.0, 1.0,
               op0=ALU.mult, op1=ALU.add)
        em.vop(nc.vector.tensor_sub, vcp[:], midf[:], hi2[:])
        em.vop(nc.vector.tensor_mul, vcp[:], vcp[:], cond2[:])
        em.vop(nc.vector.tensor_add, hi2[:], hi2[:], vcp[:])
        em.vop(nc.vector.tensor_sub, vcp[:], midf[:], lo2[:])
        em.vop(nc.vector.tensor_mul, vcp[:], vcp[:], mids[:])
        em.vop(nc.vector.tensor_add, lo2[:], lo2[:], vcp[:],
               _inc=(tag + "ve", 1))
    em.vop(nc.vector.tensor_copy, midu[:], hi2[:], _inc=(tag + "ve", 1))
    em.wait("sync", tag + "ve", em.cnt[tag + "ve"])
    scr_u = scr_dram[:, :].bitcast(U16)
    em.on("sync", nc.sync.dma_start, scr_u[:, 0:64], midu[:, :],
          _inc=(tag + "bl", 16))
    em.wait("sync", tag + "bl", em.cnt[tag + "bl"])
    for gp in range(8):
        em.on("sync", nc.sync.dma_start,
              idw[16 * gp:16 * gp + 16, :].rearrange("p (gr w) -> p gr w", gr=8),
              scr_u[:, 0:64].rearrange("(gr r) w -> r gr w", gr=8),
              _inc=(tag + "bl", 16))

    base_s = em.cnt[tag + "s"]
    base_mk = em.cnt[tag + "mk"]
    wrI3 = invw[0:16, :].rearrange("r (e w) -> r e w", w=256)
    for w in range(256):
        pb = g.p_y if w % 2 == 0 else g.p_bc
        em.wait("tensor", tag + "mk", em.cnt[tag + "mk"] - 1)
        em.on("tensor", nc.tensor.transpose, pb[0:16, 0:16],
              inv_em[:, w * 16:(w + 1) * 16], g.identf[0:16, 0:16],
              _inc=(tag + "s", 1))
        em.wait("vector", tag + "s", base_s + w + 1)
        em.vop(nc.vector.tensor_copy, wrI3[:, :, w], pb[0:16, 0:16],
               _inc=(tag + "mk", 1))
    em.wait("sync", tag + "mk", em.cnt[tag + "mk"])
    em.wait("sync", tag + "bl", em.cnt[tag + "bl"])
    em.on("sync", nc.sync.dma_start,
          scr_u[0:64, :].rearrange("(r c) w -> r c w", c=4),
          invw[0:16, :].rearrange("p (c w) -> p c w", c=4),
          _inc=(tag + "bl", 16))
    em.wait("sync", tag + "bl", em.cnt[tag + "bl"])
    srcw = scr_u[0:64, :].rearrange("(r c) w -> r (c w)", c=4)
    for gp in range(1, 8):
        em.on("sync", nc.sync.dma_start, invw[16 * gp:16 * gp + 16, :], srcw,
              _inc=(tag + "bl", 16))
    em.flush("m1_index")

    em.wait("sync", tag + "ex", em.cnt[tag + "ex"])
    em.wait("sync", tag + "bl", em.cnt[tag + "bl"])
    em.on("sync", nc.sync.dma_start, scr_dram[:, :], sblk[:, :],
          _inc=(tag + "bl", 16))
    em.wait("sync", tag + "bl", em.cnt[tag + "bl"])
    for gp in range(8):
        em.on("sync", nc.sync.dma_start,
              smajS[16 * gp:16 * gp + 16, :].rearrange("p (b f) -> p b f", b=8),
              scr_dram[:, :].rearrange("(b e) f -> e b f", e=16),
              _inc=(tag + "bl", 16))
    em.wait("gpsimd", tag + "bl", em.cnt[tag + "bl"])
    base_s = em.cnt[tag + "s"]
    for e in range(16):
        ew = idw[:, (e // 2) * 64 + (e % 2) * 32: (e // 2) * 64 + (e % 2) * 32 + 32]
        if e > 0:
            em.wait("gpsimd", tag + "s", base_s + 4 * e)
        ng = em.on("gpsimd", nc.gpsimd.indirect_copy, scg[:, :], smajS,
                   ew, i_know_ap_gather_is_preferred=True, _inc=(tag + "ex", 1))
        em.wait("tensor", tag + "ex", ng)
        for q in range(4):
            pb = g.p_y if q % 2 == 0 else g.p_bc
            em.wait("tensor", tag + "mk", em.cnt[tag + "mk"] - 1)
            em.on("tensor", nc.tensor.transpose, pb[:, 0:16],
                  scg[0:16, q * 128:(q + 1) * 128], g.identf[0:16, 0:16],
                  _inc=(tag + "s", 1))
            em.wait("vector", tag + "s", em.cnt[tag + "s"])
            em.vop(nc.vector.tensor_copy,
                   wcolsAll[:, e * 4 + q: e * 4 + q + 1], pb[:, e:e + 1],
                   _inc=(tag + "mk", 1))
    em.wait("gpsimd", tag + "mk", em.cnt[tag + "mk"])
    em.wait("vector", tag + "mk", em.cnt[tag + "mk"])
    em.flush("m1_scores")

    wl_t = {0: 0, 1: 0}
    wu_t = {0: 0, 1: 0}

    def load_wg(src, p, slot):
        em.wait("gpsimd", tag + "gm", em.cnt.get(tag + "gm", 0))
        wl_t[slot] = em.on(
            "gpsimd", nc.gpsimd.dma_start, wg_s[slot][:, :], src[p],
            _inc=(tag + f"wl{slot}", 16))

    def load_wu(src, p, slot):
        em.wait("gpsimd", tag + "sq", em.cnt.get(tag + "sq", 0))
        wu_t[slot] = em.on(
            "gpsimd", nc.gpsimd.dma_start, wu_s[slot][:, :], src[p],
            _inc=(tag + f"g{slot}", 16))

    sq2 = g.scr2[:, :].bitcast(BF16)[:, 0:512]

    def geglu_512(rhs_of, wg_src, wu_src):
        em.wait("tensor", tag + "h2", em.cnt.get(tag + "h2", 0))
        for ft in range(16):
            p, half = ft // 2, ft % 2
            slot = p % 2
            pg = g.p_a[0][:, :] if half == 0 else g.p_y[:, :]
            pu = g.p_a[1][:, :] if half == 0 else g.p_bc[:, :]
            sqb = g.sqbf[:, 0:512] if half == 0 else sq2
            if half == 0 and p + 1 < 8:
                load_wg(wg_src, p + 1, 1 - slot)
                load_wu(wu_src, p + 1, 1 - slot)
            em.wait("tensor", tag + f"wl{slot}", wl_t[slot])
            em.wait("tensor", tag + f"g{slot}", wu_t[slot])
            em.wait("tensor", tag + "hl", em.cnt.get(tag + "hl", 0) - 1)
            for ct in range(CT):
                em.on("tensor", nc.tensor.matmul, pg,
                      wg_s[slot][:, half * 1024 + ct * 128:
                                 half * 1024 + (ct + 1) * 128], rhs_of(ct),
                      start=(ct == 0), stop=(ct == CT - 1),
                      _inc=(tag + "gm", 1) if ct == CT - 1 else None)
            for ct in range(CT):
                em.on("tensor", nc.tensor.matmul, pu,
                      wu_s[slot][:, half * 1024 + ct * 128:
                                 half * 1024 + (ct + 1) * 128], rhs_of(ct),
                      start=(ct == 0), stop=(ct == CT - 1),
                      _inc=(tag + "sq", 1) if ct == CT - 1 else None)
            em.wait_now("scalar", tag + "gm")
            em.wait("scalar", tag + "hl", em.cnt.get(tag + "hl", 0) - 1)
            gn = em.sop(nc.scalar.activation, sqb, pg,
                        AF.Gelu, bias=0.0, scale=1.0, _inc=(tag + "gel", 1))
            em.wait("vector", tag + "gel", gn)
            em.wait("vector", tag + "sq", em.cnt[tag + "sq"])
            em.vop(nc.vector.tensor_mul, hbuf(ft), pu, sqb,
                   _inc=(tag + "hl", 1))
        em.wait("tensor", tag + "hl", em.cnt[tag + "hl"])

    def down_512(wd_of, out_rows_fn, wcol=None):
        for tt in range(4):
            for dh in range(2):
                pb = g.p_y if dh == 0 else g.p_bc
                em.wait("tensor", tag + "h2", em.cnt.get(tag + "h2", 0))
                for ft in range(16):
                    em.on("tensor", nc.tensor.matmul, pb[:, :],
                          g.bigA[:, H0 + ft * 512 + tt * 128:
                                 H0 + ft * 512 + (tt + 1) * 128],
                          wd_of(ft)[:, dh * 512:(dh + 1) * 512],
                          start=(ft == 0), stop=(ft == 15),
                          _inc=(tag + "h2u", 1) if ft == 15 else None)
                em.wait_now("vector", tag + "h2u")
                em.wait("vector", tag + "s", em.cnt.get(tag + "s", 0))
                if wcol is not None:
                    em.vop(nc.vector.tensor_scalar_mul,
                           eo[:, dh * 512:(dh + 1) * 512], pb[:, :],
                           wcol(tt), _inc=(tag + "h2", 1))
                else:
                    em.vop(nc.vector.tensor_copy,
                           eo[:, dh * 512:(dh + 1) * 512], pb[:, :],
                           _inc=(tag + "h2", 1))
            out_rows_fn(tt)

    load_wg(W["wsg"], 0, 0)
    load_wu(W["wsu"], 0, 0)
    for ft in range(16):
        em.on("gpsimd", nc.gpsimd.dma_start, wdt(ft), W["wsd"][ft],
              _inc=(tag + "h0", 16))
    em.wait("tensor", tag + "h0", em.cnt[tag + "h0"])
    for tc in range(NCH):
        rhs_of = (lambda _tc: (lambda ct: g.xT[
            :, ct * T + _tc * 512: ct * T + (_tc + 1) * 512]))(tc)
        geglu_512(rhs_of, W["wsg"], W["wsu"])
        if tc + 1 < NCH:
            load_wg(W["wsg"], 0, 0)
            load_wu(W["wsu"], 0, 0)

        def store_rows(tt, _tc=tc):
            em.wait_now("sync", tag + "h2")
            em.on("sync", nc.sync.dma_start,
                  out_hbm[_tc * 512 + tt * 128: _tc * 512 + (tt + 1) * 128, :],
                  eo[:, :], _inc=(tag + "am", 16))
            em.wait("vector", tag + "am", em.cnt[tag + "am"])

        down_512(wdt, store_rows)
    em.flush("m1_shared")

    em.vop(nc.vector.memset, eoTb[:, :], 0.0, _inc=(tag + "mk", 1))
    for e in range(16):
        ew = idw[:, (e // 2) * 64 + (e % 2) * 32: (e // 2) * 64 + (e % 2) * 32 + 32]
        em.wait("gpsimd", tag + "gm", em.cnt.get(tag + "gm", 0))
        em.wait("gpsimd", tag + "sq", em.cnt.get(tag + "sq", 0))
        for ct in range(CT):
            em.on("gpsimd", nc.gpsimd.indirect_copy,
                  xeT[:, ct * 512:(ct + 1) * 512],
                  g.xT[:, ct * T:(ct + 1) * T], ew,
                  i_know_ap_gather_is_preferred=True, _inc=(tag + "ex", 1))
        em.wait("gpsimd", tag + "h2u", em.cnt.get(tag + "h2u", 0))
        for ft in range(16):
            em.on("gpsimd", nc.gpsimd.dma_start, wdt(ft), W["wd"][e, ft],
                  _inc=(tag + "h0", 16))
        em.wait("tensor", tag + "ex", em.cnt[tag + "ex"])
        wg_src = W["wg"][e]
        wu_src = W["wu"][e]
        if e == 0:
            load_wg(wg_src, 0, 0)
            load_wu(wu_src, 0, 0)
        em.wait("tensor", tag + "mk", em.cnt[tag + "mk"])
        geglu_512(lambda ct: xeT[:, ct * 512:(ct + 1) * 512], wg_src, wu_src)
        if e + 1 < 16:
            load_wg(W["wg"][e + 1], 0, 0)
            load_wu(W["wu"][e + 1], 0, 0)
        em.wait("tensor", tag + "h0", em.cnt[tag + "h0"])
        em.wait("vector", tag + "am", em.cnt.get(tag + "am", 0))

        def eoT_rows(tt, _e=e):
            em.wait("tensor", tag + "mk", em.cnt[tag + "mk"])
            em.wait("tensor", tag + "h2", em.cnt[tag + "h2"])
            for dt in range(8):
                pb2 = g.p_a[0] if dt < 4 else g.p_a[1]
                col = (dt % 4) * 128
                em.on("tensor", nc.tensor.transpose, pb2[:, col:col + 128],
                      eo[:, dt * 128:(dt + 1) * 128], g.identf[:],
                      _inc=(tag + "s", 1))
            em.wait("vector", tag + "s", em.cnt[tag + "s"])
            for dt in range(8):
                pb2 = g.p_a[0] if dt < 4 else g.p_a[1]
                col = (dt % 4) * 128
                em.vop(nc.vector.tensor_copy,
                       eoTb[:, dt * 520 + tt * 128: dt * 520 + (tt + 1) * 128],
                       pb2[:, col:col + 128],
                       _inc=(tag + "mk", 1) if dt == 7 else None)
            em.wait("vector", tag + "s", em.cnt[tag + "s"])

        down_512(wdt, eoT_rows, wcol=lambda tt: wcolsAll[:, e * 4 + tt:
                                                         e * 4 + tt + 1])
        em.wait("sync", tag + "mk", em.cnt[tag + "mk"])
        em.wait("sync", tag + "am", em.cnt.get(tag + "am", 0))
        em.on("sync", nc.sync.dma_start, eoT_hbm[e], eoTb[:, :],
              _inc=(tag + "am", 16))
    em.flush("m1_experts")

    em.wait("sync", tag + "am", em.cnt[tag + "am"])
    for dt in range(8):
        for e in range(16):
            if dt > 0:
                em.wait("sync", tag + "pv", em.cnt.get(tag + "pv", 0))
            em.on("sync", nc.sync.dma_start, dtile(e),
                  eoT_hbm[e][:, dt * 520:(dt + 1) * 520],
                  _inc=(tag + "nr", 16))
        em.wait("gpsimd", tag + "nr", em.cnt[tag + "nr"])
        for tc in range(8):
            em.wait("gpsimd", tag + "ye", em.cnt.get(tag + "ye", 0))
            for e in range(16):
                ng = em.on("gpsimd", nc.gpsimd.indirect_copy,
                           gathw[:, e * 512:(e + 1) * 512], dtile(e),
                           invw[:, e * 256 + tc * 32: e * 256 + (tc + 1) * 32],
                           i_know_ap_gather_is_preferred=True,
                           _inc=(tag + "pv", 1))
            em.wait("tensor", tag + "pv", ng)
            for e in range(16):
                em.on("tensor", nc.tensor.matmul, g.p_a[0][:, :],
                      g.ident[:], gathw[:, e * 512:(e + 1) * 512],
                      start=(e == 0), stop=(e == 15),
                      _inc=(tag + "ye", 1) if e == 15 else None)
            em.wait("vector", tag + "ye", em.cnt[tag + "ye"])
            em.wait("vector", tag + "s", em.cnt[tag + "s"])
            em.vop(nc.vector.tensor_copy, accf[:, :], g.p_a[0][:, :],
                   _inc=(tag + "h2", 1))
            em.wait("tensor", tag + "h2", em.cnt[tag + "h2"])
            for s_ in range(4):
                pb = g.p_y if s_ % 2 == 0 else g.p_bc
                em.wait("tensor", tag + "mk", em.cnt[tag + "mk"] - 1)
                em.on("tensor", nc.tensor.transpose, pb[:, 0:128],
                      accf[:, s_ * 128:(s_ + 1) * 128], g.identf[:],
                      _inc=(tag + "s", 1))
                em.wait("vector", tag + "wm", em.cnt.get(tag + "wm", 0) - 16)
                em.wait("vector", tag + "s", em.cnt[tag + "s"])
                em.vop(nc.vector.tensor_copy, mts[s_ % 2][:, :],
                       pb[:, 0:128], _inc=(tag + "mk", 1))
                em.wait("sync", tag + "mk", em.cnt[tag + "mk"])
                em.wait("sync", tag + "wm", em.cnt.get(tag + "wm", 0))
                em.on("sync", nc.sync.dma_start,
                      moe_hbm[tc * 512 + s_ * 128: tc * 512 + (s_ + 1) * 128,
                              dt * 128:(dt + 1) * 128],
                      mts[s_ % 2][:, :], _inc=(tag + "wm", 16))
    em.flush("m1_combine")

    for tt in range(NT):
        oslot = tt % 2
        em.wait("sync", tag + "shc", em.cnt.get(tag + "shc", 0))
        if tt == 0:
            em.wait("sync", tag + "wm", em.cnt.get(tag + "wm", 0))
        ld = em.on("sync", nc.sync.dma_start, osb[oslot][:, :],
                   out_hbm[tt * 128:(tt + 1) * 128, :], _inc=(tag + "nre", 16))
        ld = em.on("sync", nc.sync.dma_start, msb[oslot][:, :],
                   moe_hbm[tt * 128:(tt + 1) * 128, :], _inc=(tag + "nre", 16))
        for ct in range(CT):
            em.wait("tensor", tag + "fxe", em.cnt.get(tag + "fxe", 0) - 1)
            em.on("tensor", nc.tensor.transpose, g.tp_slots[ct % 2],
                  g.xT[:, ct * T + tt * 128: ct * T + (tt + 1) * 128],
                  g.ident[:], _inc=(tag + "fx", 1))
            em.wait("vector", tag + "fx", em.cnt[tag + "fx"])
            em.on("vector", nc.vector.tensor_copy, xtk[:, ts(ct)],
                  g.tp_slots[ct % 2], _inc=(tag + "fxe", 1))
        em.wait("vector", tag + "nre", ld)
        pre4 = osb[oslot][:, :]
        em.vop(nc.vector.tensor_add, pre4, osb[oslot][:, :], msb[oslot][:, :])
        em.vop(nc.vector.tensor_add, pre4, pre4, xtk[:, :])
        em.vop(nc.vector.reduce_sum, g.scr[:, 0:1], pre4, axis=AX.X)
        em.vop(nc.vector.tensor_scalar_mul, g.scr[:, 0:1], g.scr[:, 0:1],
               1.0 / D)
        em.vop(nc.vector.tensor_mul, g.scr2[:, :], pre4[:, 0:512],
               pre4[:, 0:512])
        em.vop(nc.vector.reduce_sum, g.scr[:, 1:2], g.scr2[:, :], axis=AX.X)
        em.vop(nc.vector.tensor_mul, g.scr2[:, :], pre4[:, 512:1024],
               pre4[:, 512:1024])
        em.vop(nc.vector.reduce_sum, g.scr[:, 2:3], g.scr2[:, :], axis=AX.X)
        em.vop(nc.vector.tensor_add, g.scr[:, 1:2], g.scr[:, 1:2],
               g.scr[:, 2:3])
        em.vop(nc.vector.tensor_scalar_mul, g.scr[:, 1:2], g.scr[:, 1:2],
               1.0 / D)
        em.vop(nc.vector.tensor_mul, g.scr[:, 2:3], g.scr[:, 0:1],
               g.scr[:, 0:1])
        em.vop(nc.vector.tensor_sub, g.scr[:, 1:2], g.scr[:, 1:2],
               g.scr[:, 2:3], _inc=(tag + "lv", 1))
        em.wait_now("scalar", tag + "lv")
        em.sop(nc.scalar.activation, g.scr[:, 1:2], g.scr[:, 1:2], AF.Ln,
               bias=g.cvcol[:, 1:2], scale=1.0)
        em.sop(nc.scalar.activation, g.scr[:, 1:2], g.scr[:, 1:2], AF.Exp,
               bias=0.0, scale=-0.5, _inc=(tag + "lnr", 1))
        em.wait_now("vector", tag + "lnr")
        em.wait("vector", tag + "wm", em.cnt.get(tag + "wm", 0))
        em.vop(nc.vector.tensor_scalar, ysb[:, :], pre4,
               g.scr[:, 0:1], g.scr[:, 1:2],
               op0=ALU.subtract, op1=ALU.mult, _inc=(tag + "shc", 1))
        em.wait("sync", tag + "shc", em.cnt[tag + "shc"])
        em.on("sync", nc.sync.dma_start, y_h[ts(tt), :], ysb[:, :],
              _inc=(tag + "wm", 16))
    em.wait_now("sync", tag + "wm")
    em.flush("m1_final")
    es.close()



def build_program(nphases=1):
    nc = bass.Bass("TRN2", target_bir_lowering=False)
    x0_h = nc.dram_tensor("x0", [T, D], F32, kind="ExternalInput")
    y_h = nc.dram_tensor("y", [T, D], F32, kind="ExternalOutput")
    Wh = {}
    for name, shape, dt_ in [
            ("a0_wq", [16, CT, 128, 64], BF16), ("a0_wk", [16, CT, 128, 64], BF16),
            ("a0_wv", [16, CT, 128, 64], BF16), ("a0_wo", [CT, 128, CT * 128], BF16),
            ("a0_qb", [128, 8], F32), ("a0_kb", [128, 8], F32),
            ("a0_vb", [1, D], F32), ("a0_ob", [128, 8], F32)]:
        Wh[name] = nc.dram_tensor(name, shape, dt_, kind="ExternalInput")

    for name, shape, dt_ in [
            ("f0_wqkv", [2, 8, CT, 128, 192], BF16),
            ("f0_qs", [2, 64, 1], F32), ("f0_ks", [2, 64, 1], F32),
            ("f0_wo", [2, 4, 128, D], BF16),
            ("f0_lg", [2, 128, 8], F32), ("f0_lb", [2, 128, 8], F32),
            ("f0_w1a", [2, 22, CT, 128, 128], BF16),
            ("f0_w1g", [2, 22, CT, 128, 128], BF16),
            ("f0_b1a", [2, 128, 22], F32), ("f0_b1g", [2, 128, 22], F32),
            ("f0_w2", [2, 22, 128, D], BF16), ("f0_b2", [2, 128, 8], F32)]:
        Wh[name] = nc.dram_tensor(name, shape, dt_, kind="ExternalInput")
    z1_hbm = nc.dram_tensor("z1_hbm", [128, CT * T], BF16, kind="Internal")

    consts = make_consts(nc)
    em = Em(nc)
    g = setup_common(nc, em, consts)
    stage_xT(g, x0_h)
    run_finalize = nphases < 4
    if nphases >= 1:
        build_l0_attn(g, dict(wq=Wh["a0_wq"], wk=Wh["a0_wk"], wv=Wh["a0_wv"],
                              wo=Wh["a0_wo"], qb=Wh["a0_qb"], kb=Wh["a0_kb"],
                              vb=Wh["a0_vb"], ob=Wh["a0_ob"]), consts["qpos"])
    if nphases >= 2:
        build_l0_ffn(g, dict(wqkv=Wh["f0_wqkv"], qs=Wh["f0_qs"], ks=Wh["f0_ks"],
                             wo=Wh["f0_wo"], lg=Wh["f0_lg"], lb=Wh["f0_lb"],
                             w1a=Wh["f0_w1a"], w1g=Wh["f0_w1g"],
                             b1a=Wh["f0_b1a"], b1g=Wh["f0_b1g"],
                             w2=Wh["f0_w2"], b2=Wh["f0_b2"]),
                     z1_hbm, consts)
    if nphases >= 3:
        for name, shape in [("t1_maa5", [128, 40]), ("t1_td", [128, 8]),
                            ("t1_mw1", [8, 128, 128]), ("t1_mw2", [32, 4096]),
                            ("t1_dw1", [8, 128, 64]), ("t1_dw2", [64, 1024]),
                            ("t1_u", [64, 16]), ("t1_wr", [8, 8, 128, 128]),
                            ("t1_wk", [8, 8, 128, 128]), ("t1_wv", [1024, 1024]),
                            ("t1_wo", [8, 8, 128, 128]),
                            ("t1_lnw", [128, 8]), ("t1_lnb", [128, 8])]:
            Wh[name] = nc.dram_tensor(name, shape, F32, kind="ExternalInput")
        build_l1_attn(g, dict(maa5=Wh["t1_maa5"], td=Wh["t1_td"],
                              mw1=Wh["t1_mw1"], mw2=Wh["t1_mw2"],
                              dw1=Wh["t1_dw1"], dw2=Wh["t1_dw2"],
                              u=Wh["t1_u"], wr=Wh["t1_wr"], wk=Wh["t1_wk"],
                              wv=Wh["t1_wv"], wo=Wh["t1_wo"],
                              lnw=Wh["t1_lnw"], lnb=Wh["t1_lnb"]), z1_hbm)
    if nphases >= 4:
        for name, shape, dt_ in [
                ("m1_gate", [128, 128], F32),
                ("m1_wg", [16, 8, 128, 2048], BF16),
                ("m1_wu", [16, 8, 128, 2048], BF16),
                ("m1_wd", [16, 16, 128, 1024], BF16),
                ("m1_wsg", [8, 128, 2048], BF16),
                ("m1_wsu", [8, 128, 2048], BF16),
                ("m1_wsd", [16, 128, 1024], BF16)]:
            Wh[name] = nc.dram_tensor(name, shape, dt_, kind="ExternalInput")
        eoT_hbm = nc.dram_tensor("eoT_hbm", [16, 128, 8 * 520], BF16,
                                 kind="Internal")
        moe_hbm = nc.dram_tensor("moe_hbm", [T, D], F32, kind="Internal")
        out_hbm = nc.dram_tensor("m1_out", [T, D], F32, kind="Internal")
        build_l1_ffn(g, dict(gate=Wh["m1_gate"], wg=Wh["m1_wg"],
                             wu=Wh["m1_wu"], wd=Wh["m1_wd"],
                             wsg=Wh["m1_wsg"], wsu=Wh["m1_wsu"],
                             wsd=Wh["m1_wsd"]),
                     dict(E8=consts["E8"], E8T=consts["E8T"], LB=consts["LB"],
                          iotaB=consts["iotaB"], Lrep=consts["Lrep"],
                          offc=consts["offc"], jp1=consts["jp1"],
                          maskc=consts["maskc"]),
                     eoT_hbm, moe_hbm, out_hbm, y_h,
                     nc.dram_tensor("m1_scr", [128, 512], F32, kind="Internal"))
    if run_finalize:
        finalize_out(g, y_h, deps=("a0fin",))
    return nc


_NC_CACHE = {}


def get_program(nphases):
    if nphases not in _NC_CACHE:
        _NC_CACHE[nphases] = build_program(nphases)
    return _NC_CACHE[nphases]


def kernel(**inputs):
    w = prep_weights(inputs)
    x = np.ascontiguousarray(np.asarray(inputs["L0_attn_x"]), np.float32)
    nc = get_program(4)
    in_maps = []
    for b in range(8):
        m = {"x0": np.ascontiguousarray(x[b])}
        m.update(w)
        in_maps.append(m)
    res = run_bass_kernel_spmd(nc, in_maps, core_ids=list(range(8)))
    out = np.stack([np.asarray(r["y"]) for r in res.results], axis=0)
    return out.astype(np.float32)


# version 07
# speedup vs baseline: 1.2108x; 1.0036x over previous
import numpy as np
import ml_dtypes
import concourse.bass as bass
import concourse.mybir as mybir
from concourse.bass_utils import run_bass_kernel_spmd
from concourse.bass import IndirectOffsetOnAxis

F32 = mybir.dt.float32
BF16 = mybir.dt.bfloat16
U32 = mybir.dt.uint32
U16 = mybir.dt.uint16
AF = mybir.ActivationFunctionType
ALU = mybir.AluOpType
AX = mybir.AxisListType

T = 4096
D = 1024
CT = D // 128
NT = T // 128
NCH = T // 512

ENGINES = ("tensor", "vector", "scalar", "sync", "gpsimd")


class Em:

    def __init__(self, nc):
        self.nc = nc
        self.q = {e: [] for e in ENGINES}
        self.sems = {}
        self.cnt = {}
        self.nblk = 0

    def sem(self, name):
        if name not in self.sems:
            self.sems[name] = self.nc.alloc_semaphore(name)
            self.cnt[name] = 0
        return self.sems[name]

    def on(self, eng, fn, *args, _inc=None, **kwargs):
        if _inc is not None:
            name, amt = _inc
            sem = self.sem(name)
            self.cnt[name] += amt
            self.q[eng].append((fn, args, kwargs, (sem, amt)))
            return self.cnt[name]
        self.q[eng].append((fn, args, kwargs, None))
        return None

    def vop(self, fn, *args, _inc=None, **kwargs):
        self.wait("vector", "vch", self.cnt.get("vch", 0))
        if _inc is not None:
            n = self.on("vector", fn, *args, _inc=_inc, **kwargs)
            self.wait("vector", _inc[0], n)
            return n
        return self.on("vector", fn, *args, _inc=("vch", 1), **kwargs)

    def sop(self, fn, *args, _inc=None, **kwargs):
        self.wait("scalar", "sch", self.cnt.get("sch", 0))
        if _inc is not None:
            n = self.on("scalar", fn, *args, _inc=_inc, **kwargs)
            self.wait("scalar", _inc[0], n)
            return n
        return self.on("scalar", fn, *args, _inc=("sch", 1), **kwargs)

    def wait(self, eng, semname, val):
        if val is None or val <= 0:
            return
        sem = self.sem(semname)
        self.q[eng].append(("WAIT", (sem, val), {}, None))

    def wait_now(self, eng, semname):
        self.wait(eng, semname, self.cnt.get(semname, 0))

    def flush(self, name=None):
        nc = self.nc
        q = self.q
        self.q = {e: [] for e in ENGINES}
        if not any(q[e] for e in ENGINES):
            return
        self.nblk += 1
        name = f"{name}{self.nblk}" if name else f"blk{self.nblk}"

        def runner(ops):
            def body(eng):
                for fn, args, kwargs, inc in ops:
                    if fn == "WAIT":
                        eng.wait_ge(args[0], args[1])
                    else:
                        instr = fn(*args, **kwargs)
                        if inc is not None:
                            instr.then_inc(inc[0], inc[1])
            return body

        with nc.Block(name) as blk:
            for e in ENGINES:
                if q[e]:
                    getattr(blk, e)(runner(q[e]))


def ts(i, n=128):
    return slice(i * n, (i + 1) * n)



def prep_weights(inp):
    w = {}
    f32 = lambda a: np.ascontiguousarray(np.asarray(a), np.float32)
    qw = f32(inp["L0_attn_q_w"])
    kvw = f32(inp["L0_attn_kv_w"])
    head_tiles = lambda m: np.ascontiguousarray(
        m.reshape(CT, 128, 16, 64).transpose(2, 0, 1, 3))
    w["a0_wq"] = head_tiles(qw)
    w["a0_wk"] = head_tiles(kvw[:, :D])
    w["a0_wv"] = head_tiles(kvw[:, D:])
    w["a0_wo"] = np.ascontiguousarray(
        f32(inp["L0_attn_o_w"]).reshape(CT, 128, CT, 128).transpose(2, 1, 0, 3)
    ).reshape(CT, 128, CT * 128)
    w["a0_qb"] = np.ascontiguousarray(f32(inp["L0_attn_q_b"]).reshape(8, 128).T)
    w["a0_kb"] = np.ascontiguousarray(f32(inp["L0_attn_kv_b"][:D]).reshape(8, 128).T)
    w["a0_vb"] = f32(inp["L0_attn_kv_b"][D:]).reshape(1, D)
    w["a0_ob"] = np.ascontiguousarray(f32(inp["L0_attn_o_b"]).reshape(8, 128).T)

    wqkv = f32(inp["L0_ffn_wqkv"])
    hq = wqkv[:, :, 0:512].reshape(2, CT, 128, 8, 64)
    hk = wqkv[:, :, 512:1024].reshape(2, CT, 128, 8, 64)
    hv = wqkv[:, :, 1024:1536].reshape(2, CT, 128, 8, 64)
    w["f0_wqkv"] = np.ascontiguousarray(
        np.concatenate([hq, hk, hv], axis=-1).transpose(0, 3, 1, 2, 4))
    w["f0_qs"] = f32(inp["L0_ffn_q_scale"]).reshape(2, 64, 1)
    w["f0_ks"] = f32(inp["L0_ffn_k_scale"]).reshape(2, 64, 1)
    w["f0_wo"] = f32(inp["L0_ffn_wo"]).reshape(2, 4, 128, 1024)
    w["f0_lg"] = np.ascontiguousarray(
        f32(inp["L0_ffn_ln_g"]).reshape(2, 8, 128).transpose(0, 2, 1))
    w["f0_lb"] = np.ascontiguousarray(
        f32(inp["L0_ffn_ln_b"]).reshape(2, 8, 128).transpose(0, 2, 1))
    HID, FCP = 2730, 22
    w1 = f32(inp["L0_ffn_w1"])
    w1a = np.zeros((2, D, FCP * 128), np.float32)
    w1g = np.zeros((2, D, FCP * 128), np.float32)
    w1a[:, :, :HID] = w1[:, :, :HID]
    w1g[:, :, :HID] = w1[:, :, HID:]
    tile_lhs = lambda m: np.ascontiguousarray(
        m.reshape(2, CT, 128, FCP, 128).transpose(0, 3, 1, 2, 4))
    w["f0_w1a"] = tile_lhs(w1a)
    w["f0_w1g"] = tile_lhs(w1g)
    b1 = f32(inp["L0_ffn_b1"])
    b1a = np.zeros((2, FCP * 128), np.float32)
    b1g = np.zeros((2, FCP * 128), np.float32)
    b1a[:, :HID] = b1[:, :HID]
    b1g[:, :HID] = b1[:, HID:]
    w["f0_b1a"] = np.ascontiguousarray(
        b1a.reshape(2, FCP, 128).transpose(0, 2, 1))
    w["f0_b1g"] = np.ascontiguousarray(
        b1g.reshape(2, FCP, 128).transpose(0, 2, 1))
    w2 = f32(inp["L0_ffn_w2"])
    w2p = np.zeros((2, FCP * 128, D), np.float32)
    w2p[:, :HID, :] = w2
    w["f0_w2"] = np.ascontiguousarray(w2p.reshape(2, FCP, 128, D))
    w["f0_b2"] = np.ascontiguousarray(
        f32(inp["L0_ffn_b2"]).reshape(2, 8, 128).transpose(0, 2, 1))

    gat = lambda k: f32(inp["L1_attn_" + k])
    maa5 = np.stack([gat("time_maa_x"), gat("time_maa_w"), gat("time_maa_k"),
                     gat("time_maa_v"), gat("time_maa_r")])
    w["t1_maa5"] = np.ascontiguousarray(
        maa5.reshape(5, 8, 128).transpose(2, 0, 1).reshape(128, 40))
    w["t1_td"] = np.ascontiguousarray(gat("time_decay").reshape(8, 128).T)
    w["t1_mw1"] = np.ascontiguousarray(gat("time_maa_w1").reshape(8, 128, 128))
    w["t1_mw2"] = np.ascontiguousarray(
        gat("time_maa_w2").transpose(1, 0, 2).reshape(32, 4096))
    w["t1_dw1"] = np.ascontiguousarray(gat("time_decay_w1").reshape(8, 128, 64))
    w["t1_dw2"] = gat("time_decay_w2")
    w["t1_u"] = np.ascontiguousarray(gat("time_faaaa").T.reshape(64, 16))
    tile4 = lambda m: np.ascontiguousarray(
        m.reshape(8, 128, 8, 128).transpose(2, 0, 1, 3))
    w["t1_wr"] = tile4(gat("W_r"))
    w["t1_wk"] = tile4(gat("W_k"))
    w["t1_wv"] = gat("W_v")
    w["t1_wo"] = tile4(gat("W_o"))
    w["t1_lnw"] = np.ascontiguousarray(gat("ln_w").reshape(8, 128).T)
    w["t1_lnb"] = np.ascontiguousarray(gat("ln_b").reshape(8, 128).T)

    gm = lambda k: f32(inp["L1_ffn_" + k])
    w["m1_gate"] = np.ascontiguousarray(
        gm("gate_w").T.reshape(8, 128, 16).transpose(1, 0, 2).reshape(128, 128))
    bf = lambda m: np.ascontiguousarray(m).astype(ml_dtypes.bfloat16)
    exp_tiles = lambda m: bf(
        m.transpose(0, 2, 1).reshape(16, 8, 128, 16, 128)
        .transpose(0, 3, 1, 2, 4)
        .reshape(16, 8, 2, 8, 128, 128).transpose(0, 1, 4, 2, 3, 5)
        .reshape(16, 8, 128, 2048))
    w["m1_wgu"] = np.ascontiguousarray(np.concatenate(
        [exp_tiles(gm("Wg")), exp_tiles(gm("Wu"))], axis=3))
    w["m1_wd"] = bf(
        gm("Wd").transpose(0, 2, 1).reshape(16, 16, 128, 1024))
    sh_tiles = lambda m: bf(
        m.T.reshape(8, 128, 16, 128).transpose(2, 0, 1, 3)
        .reshape(8, 2, 8, 128, 128).transpose(0, 3, 1, 2, 4)
        .reshape(8, 128, 2048))
    w["m1_wsgu"] = np.ascontiguousarray(np.concatenate(
        [sh_tiles(gm("Ws_g")), sh_tiles(gm("Ws_u"))], axis=2))
    w["m1_wsd"] = bf(gm("Ws_d").T.reshape(16, 128, 1024))
    for k in ("a0_wq", "a0_wk", "a0_wv", "a0_wo", "f0_wqkv", "f0_wo",
              "f0_w1a", "f0_w1g", "f0_w2"):
        w[k] = w[k].astype(ml_dtypes.bfloat16)
    return w


def _rotary_tables(HD=64, WINDOW=512):
    half = np.arange(0, HD, 2, dtype=np.float32)
    inv_freq = 1.0 / 10000.0 ** (half / HD)
    t = np.arange(T, dtype=np.float32)
    freqs = t[:, None] * inv_freq[None, :]
    freqs = np.concatenate([freqs, freqs], axis=-1)
    base = (half + 0.4 * HD) / (1.4 * HD)
    power = (t - T // 2) / float(WINDOW // 2)
    scale = base[None, :] ** power[:, None]
    scale = np.concatenate([scale, scale], axis=-1)
    cos, sin = np.cos(freqs), np.sin(freqs)
    f = lambda m: np.ascontiguousarray(m.T, np.float32)
    return f(cos * scale), f(sin * scale), f(cos / scale), f(sin / scale)


def make_consts(nc):
    c = {}
    c["qpos"] = nc.inline_tensor(
        np.arange(512, dtype=np.float32).reshape(1, 512), name="qpos")
    c["kpos"] = nc.inline_tensor(
        np.ascontiguousarray(
            np.arange(NT, dtype=np.float32)[None, :] * 128
            + np.arange(128, dtype=np.float32)[:, None]), name="kpos")
    c["ident"] = nc.inline_tensor(np.eye(128, dtype=np.float32), name="identc")
    CSQ, SSQ, CSK, SSK = _rotary_tables()
    c["rotq"] = nc.inline_tensor(
        np.ascontiguousarray(np.stack([CSQ, SSQ])), name="rotq")
    c["rotk"] = nc.inline_tensor(
        np.ascontiguousarray(np.stack([CSK, SSK])), name="rotk")
    p = np.arange(128)
    c["E8"] = nc.inline_tensor(
        (p[:, None] % 16 == np.arange(16)[None, :]).astype(np.float32),
        name="m1E8")
    c["E8T"] = nc.inline_tensor(np.ascontiguousarray(
        (p[:, None] % 16 == np.arange(16)[None, :]).astype(np.float32).T),
        name="m1E8T")
    c["LB"] = nc.inline_tensor(
        ((p[:, None] % 16 == p[None, :] % 16)
         & (p[:, None] < p[None, :])).astype(np.float32), name="m1LB")
    c["iotaB"] = nc.inline_tensor(
        ((p // 16)[:, None] * 512.0
         + np.arange(512)[None, :]).astype(np.float32), name="m1iotaB")
    c["ebm1"] = nc.inline_tensor(
        ((p % 16) * 512.0 - 1.0).astype(np.float32).reshape(128, 1),
        name="m1ebm1")
    c["tids"] = nc.inline_tensor(
        (np.arange(NT)[None, :] * 128 + p[:, None]).astype(np.uint32),
        name="m1tids")
    c["Lrep"] = nc.inline_tensor(np.ascontiguousarray(np.concatenate(
        [(np.arange(16)[:, None] == 2 * (p[None, :] // 16) + o)
         .astype(np.float32) for o in range(2)], axis=1)), name="m1Lrep")
    w64 = np.arange(64)
    c["offc"] = nc.inline_tensor(np.ascontiguousarray(np.broadcast_to(
        (w64 >= 32) * 4096.0, (128, 64))).astype(np.float32), name="m1offc")
    c["jp1"] = nc.inline_tensor(
        ((w64[None, :] * 16 + (p % 16)[:, None] + 1)
         - 512.0 * (w64[None, :] >= 32)).astype(np.float32), name="m1jp1")
    c["maskc"] = nc.inline_tensor(
        ((p % 16)[:, None] == np.arange(16)[None, :]).astype(np.float32),
        name="m1maskc")
    c["cvals"] = nc.inline_tensor(
        np.array([[1e-20, 1e-5, 0.00064, 0.0, 1e-24, 0.5, 1.0, 0.0]],
                 dtype=np.float32), name="cvalsc")
    return c



class Ctx:
    pass


def setup_common(nc, em, consts):
    g = Ctx()
    g.nc, g.em = nc, em
    g.xT = nc.alloc_sbuf_tensor("xT", [128, CT * T], BF16)
    g.bigA = nc.alloc_sbuf_tensor("bigA", [128, CT * T], BF16)
    g.identf = nc.alloc_sbuf_tensor("identf", [128, 128], F32)
    g.ident = nc.alloc_sbuf_tensor("ident", [128, 128], BF16)
    g.onescol = nc.alloc_sbuf_tensor("onescol", [128, 1], BF16)
    g.onesrow = nc.alloc_sbuf_tensor("onesrow", [1, 128], BF16)
    g.ones2 = nc.alloc_sbuf_tensor("ones2", [2, 128], BF16)
    g.onesrowf = nc.alloc_sbuf_tensor("onesrowf", [1, 128], F32)
    g.kpos = nc.alloc_sbuf_tensor("kpos_sb", [128, NT], F32)
    g.p_a = [nc.alloc_psum_tensor(f"p_a{i}", [128, 512], F32) for i in range(2)]
    g.p_y = nc.alloc_psum_tensor("p_y", [128, 512], F32)
    g.p_bc = nc.alloc_psum_tensor("p_bc", [128, 512], F32)
    g.p_row = nc.alloc_psum_tensor("p_row", [1, 512], F32)
    g.p_row2 = nc.alloc_psum_tensor("p_row2", [1, 512], F32)
    g.p_v = nc.alloc_psum_tensor("p_v", [128, 128], F32)
    g.p_tp = nc.alloc_psum_tensor("p_tp", [128, 128], BF16)
    g.tp_slots = [g.p_tp[:, 0:128], g.p_v[:, :].bitcast(BF16)[:, 0:128]]
    g.cvals = nc.alloc_sbuf_tensor("cvals", [1, 8], F32)
    g.scr = nc.alloc_sbuf_tensor("g_scr", [128, 512], F32)[:, :]
    g.scr2 = nc.alloc_sbuf_tensor("g_scr2", [128, 512], F32)[:, :]
    g.sqbf = nc.alloc_sbuf_tensor("g_sqbf", [128, 512], BF16)[:, :]
    g.rbc = nc.alloc_sbuf_tensor("g_rbc", [128, 512], F32)[:, :]
    g.mrbc = nc.alloc_sbuf_tensor("g_mrbc", [128, 512], F32)[:, :]
    g.rbc64 = nc.alloc_sbuf_tensor("g_rbc64", [64, 512], F32)[:, :]
    g.recbc = nc.alloc_sbuf_tensor("g_recbc", [64, 512], F32)[:, :]
    g.rowa = nc.alloc_sbuf_tensor("g_rowa", [1, 512], F32)[:, :]
    g.rowb = nc.alloc_sbuf_tensor("g_rowb", [1, 512], F32)[:, :]
    g.lm = nc.alloc_sbuf_tensor("g_lm", [1, 512], F32)[:, :]
    g.lr = nc.alloc_sbuf_tensor("g_lr", [1, 512], F32)[:, :]
    g.lmr = nc.alloc_sbuf_tensor("g_lmr", [1, 512], F32)[:, :]
    g.scal = nc.alloc_sbuf_tensor("g_scal", [1, 64], F32)
    g.cvcol = nc.alloc_sbuf_tensor("g_cvcol", [128, 4], F32)

    em.on("sync", nc.sync.dma_start, g.identf[:], consts["ident"][:, :],
          _inc=("a0bl", 16))
    em.on("sync", nc.sync.dma_start, g.kpos[:], consts["kpos"][:, :],
          _inc=("a0bl", 16))
    em.on("sync", nc.sync.dma_start, g.cvals[:], consts["cvals"][:, :],
          _inc=("a0bl", 16))
    em.wait_now("vector", "a0bl")
    em.on("vector", nc.vector.tensor_copy, g.ident[:], g.identf[:])
    em.on("vector", nc.vector.memset, g.onescol[:], 1.0)
    em.on("vector", nc.vector.memset, g.onesrow[:], 1.0)
    em.on("vector", nc.vector.memset, g.ones2[:], 1.0)
    em.on("vector", nc.vector.memset, g.onesrowf[:], 1.0)
    em.on("vector", nc.vector.memset, g.cvcol[:, 0:1], 1e-20)
    em.on("vector", nc.vector.memset, g.cvcol[:, 1:2], 1e-5)
    em.on("vector", nc.vector.memset, g.cvcol[:, 2:3], 0.00064)
    em.on("vector", nc.vector.memset, g.cvcol[:, 3:4], 1e-24, _inc=("setup_v", 1))
    em.flush("setup")
    return g


def stage_xT(g, src_h, dst=None, cast=True):
    nc, em = g.nc, g.em
    dst = dst if dst is not None else g.xT
    from contextlib import ExitStack
    ses = ExitStack()
    fin = ses.enter_context(nc.sbuf_tensor("stage_tok", [128, 2 * D], F32))
    st = fin[:, :].bitcast(BF16)
    tpn = 0
    for tt in range(NT):
        b = tt % 2
        em.wait("gpsimd", "xtp", (tt - 1) * CT)
        ld = em.on("gpsimd", nc.gpsimd.dma_start, st[:, ts(b, D)],
                   src_h[ts(tt), :], _inc=(f"xld{b}", 16))
        em.wait("tensor", f"xld{b}", ld)
        for ct in range(CT):
            pslot = g.tp_slots[ct % 2]
            em.wait("tensor", "xev", tpn - 1)
            tpn = em.on("tensor", nc.tensor.transpose, pslot,
                        st[:, b * D + ct * 128: b * D + (ct + 1) * 128],
                        g.ident[:], _inc=("xtp", 1))
            em.wait("vector", "xtp", tpn)
            em.on("vector", nc.vector.tensor_copy,
                  dst[:, ct * T + tt * 128: ct * T + (tt + 1) * 128],
                  pslot, _inc=("xev", 1))
    em.flush("stage_xT")
    ses.close()


def finalize_out(g, y_h, deps=()):
    nc, em = g.nc, g.em
    from contextlib import ExitStack
    fes_ = ExitStack()
    st = fes_.enter_context(nc.sbuf_tensor("final_tok", [128, 2 * D], F32))
    for semname in deps:
        em.wait("tensor", semname, em.cnt.get(semname, 0))
    tpn = 0
    fst_tgt = {0: 0, 1: 0}
    for tt in range(NT):
        b = tt % 2
        em.wait("vector", f"fst{b}", fst_tgt[b])
        for ct in range(CT):
            pslot = g.tp_slots[ct % 2]
            em.wait("tensor", "fev", tpn - 1)
            tpn = em.on("tensor", nc.tensor.transpose, pslot,
                        g.xT[:, ct * T + tt * 128: ct * T + (tt + 1) * 128],
                        g.ident[:], _inc=("ftp", 1))
            em.wait("vector", "ftp", tpn)
            em.on("vector", nc.vector.tensor_copy,
                  st[:, b * D + ct * 128: b * D + (ct + 1) * 128], pslot,
                  _inc=("fev", 1))
        em.wait_now("sync", "fev")
        fst_tgt[b] = em.on("sync", nc.sync.dma_start, y_h[ts(tt), :],
                           st[:, ts(b, D)], _inc=(f"fst{b}", 16))
    em.wait_now("sync", "fst0")
    em.wait_now("sync", "fst1")
    em.flush("finalize")
    fes_.close()



def build_l0_attn(g, W, CONSTS_QPOS):
    nc, em = g.nc, g.em
    NH = 16
    from contextlib import ExitStack
    es = ExitStack()
    sb = lambda name, shape, dt: es.enter_context(nc.sbuf_tensor(name, shape, dt))

    wqkv = [sb(f"a0_w{i}", [128, CT * 192], BF16) for i in range(2)]
    qb_sb = sb("a0_qbs", [128, 8], F32)
    kb_sb = sb("a0_kbs", [128, 8], F32)
    ob_sb = sb("a0_obs", [128, 8], F32)
    vb_bf = sb("a0_vbb", [1, D], BF16)
    QTa = sb("QTa", [65, T], BF16)
    KTa = sb("KTa", [65, T], BF16)
    Vbuf = sb("Vbuf", [128, NT * 65], BF16)
    YT = g.bigA[:, :]
    PT = sb("PT", [128, 2 * 512], BF16)
    dmask = sb("dmask", [128, 4 * 512], BF16)
    r_qn2 = sb("a0_qn2", [1, T], BF16)[:, :]
    r_b8 = r_qn2
    qpos_t = sb("a0_qpos", [1, 512], F32)
    g.qpos = qpos_t
    bias_k = sb("a0_biask", [128, NT], F32)
    wo_s = sb("a0_wos", [128, 2 * D], BF16)[:, :]
    scal = g.scal
    rec_sb = None
    recbc = g.recbc
    scr = g.scr
    sqbf = g.sqbf[:, :]
    qsq = g.sqbf[0:64, :]
    l_mean, l_r, l_mr = g.lm, g.lr, g.lmr
    rbc, mrbc = g.rbc, g.mrbc
    em.on("sync", nc.sync.dma_start, qpos_t[:], CONSTS_QPOS[:, :],
          _inc=("a0bl", 16))

    for j in range(4):
        m = dmask[:, ts(j, 512)]
        n = em.on("gpsimd", nc.gpsimd.memset, m, 0.0, _inc=("a0mm", 1))
        em.wait("gpsimd", "a0mm", n)
        em.on("gpsimd", nc.gpsimd.affine_select, m, m,
              [[1, 512]], compare_op=ALU.is_ge, fill=-1e30,
              base=-128 * j, channel_multiplier=-1, _inc=("a0m", 1))
    for tt in range(NT):
        em.on("gpsimd", nc.gpsimd.memset,
              Vbuf[:, tt * 65 + 64: tt * 65 + 65], 1.0, _inc=("a0m", 1))
    em.on("vector", nc.vector.memset, KTa[64:65, :], 1.0, _inc=("a0m", 1))
    em.on("sync", nc.sync.dma_start, qb_sb[:], W["qb"][:, :], _inc=("a0bl", 16))
    em.on("sync", nc.sync.dma_start, kb_sb[:], W["kb"][:, :], _inc=("a0bl", 16))
    em.on("sync", nc.sync.dma_start, ob_sb[:], W["ob"][:, :], _inc=("a0bl", 16))
    em.on("gpsimd", nc.gpsimd.dma_start, vb_bf[:], W["vb"][:, :],
          _inc=("a0wo0", 16))
    def load_head(h, slot):
        for ct in range(CT):
            em.on("gpsimd", nc.gpsimd.dma_start,
                  wqkv[slot][:, ct * 192 + 0: ct * 192 + 64],
                  W["wq"][h, ct, :, :], _inc=(f"a0wl{slot}", 16))
            em.on("gpsimd", nc.gpsimd.dma_start,
                  wqkv[slot][:, ct * 192 + 64: ct * 192 + 128],
                  W["wk"][h, ct, :, :], _inc=(f"a0wl{slot}", 16))
            em.on("gpsimd", nc.gpsimd.dma_start,
                  wqkv[slot][:, ct * 192 + 128: ct * 192 + 192],
                  W["wv"][h, ct, :, :], _inc=(f"a0wl{slot}", 16))
        return em.cnt[f"a0wl{slot}"]

    wl_tgt = load_head(0, 0)
    em.flush("a0_setup")

    for h in range(NH):
        slot = h % 2
        hb = 64 * (h % 2)
        slope = (h + 1) / NH
        my_tgt = wl_tgt
        if h + 1 < NH:
            em.wait("gpsimd", "a0vm", em.cnt.get("a0vm", 0))
            wl_tgt = load_head(h + 1, 1 - slot)
        em.wait("tensor", f"a0wl{slot}", my_tgt)

        if h == 0:
            em.wait_now("vector", "a0bl")
        em.wait("vector", "a0s", em.cnt.get("a0s", 0))
        em.wait("vector", "a0pv", em.cnt.get("a0pv", 0))
        em.wait("vector", "a0ex", em.cnt.get("a0ex", 0))
        for hc in range(NCH):
            em.wait("tensor", "a0qe", em.cnt.get("a0qe", 0))
            for ct in range(CT):
                em.on("tensor", nc.tensor.matmul, g.p_a[0][0:64, :],
                      wqkv[slot][:, ct * 192: ct * 192 + 64],
                      g.xT[:, ct * T + hc * 512: ct * T + (hc + 1) * 512],
                      start=(ct == 0), stop=(ct == CT - 1),
                      _inc=("a0q", 1) if ct == CT - 1 else None)
            for ct in range(CT):
                em.on("tensor", nc.tensor.matmul, g.p_a[1][0:64, :],
                      wqkv[slot][:, ct * 192 + 64: ct * 192 + 128],
                      g.xT[:, ct * T + hc * 512: ct * T + (hc + 1) * 512],
                      start=(ct == 0), stop=(ct == CT - 1),
                      _inc=("a0q", 1) if ct == CT - 1 else None)
            em.wait_now("vector", "a0q")
            em.on("vector", nc.vector.tensor_scalar,
                  QTa[0:64, hc * 512:(hc + 1) * 512], g.p_a[0][0:64, :],
                  qb_sb[hb:hb + 64, h // 2:h // 2 + 1], None, op0=ALU.add,
                  _inc=("a0qe", 1))
            em.on("vector", nc.vector.tensor_scalar,
                  KTa[0:64, hc * 512:(hc + 1) * 512], g.p_a[1][0:64, :],
                  kb_sb[hb:hb + 64, h // 2:h // 2 + 1], None, op0=ALU.add,
                  _inc=("a0qe", 1))
        vhead = vb_bf[:, h * 64:(h + 1) * 64]
        for tt in range(NT):
            em.wait("tensor", "a0ve", em.cnt.get("a0ve", 0))
            for ct in range(CT):
                em.on("tensor", nc.tensor.matmul, g.p_v[:, 0:64],
                      g.xT[:, ct * T + tt * 128: ct * T + (tt + 1) * 128],
                      wqkv[slot][:, ct * 192 + 128: ct * 192 + 192],
                      start=(ct == 0), stop=False)
            if h == 0 and tt == 0:
                em.wait("tensor", "a0wo0", 16)
            em.on("tensor", nc.tensor.matmul, g.p_v[:, 0:64],
                  g.onesrow[:, :], vhead, start=False, stop=True,
                  _inc=("a0vm", 1))
            em.wait_now("vector", "a0vm")
            em.on("vector", nc.vector.tensor_copy,
                  Vbuf[:, tt * 65: tt * 65 + 64], g.p_v[:, 0:64],
                  _inc=("a0ve", 1))

        em.wait("vector", "a0qe", em.cnt["a0qe"])
        for hc in range(NCH):
            em.wait("vector", "a0nr", em.cnt.get("a0nr", 0))
            em.on("vector", nc.vector.tensor_mul, qsq[:, :],
                  QTa[0:64, hc * 512:(hc + 1) * 512],
                  QTa[0:64, hc * 512:(hc + 1) * 512], _inc=("a0sq", 1))
            em.wait("tensor", "a0sq", em.cnt["a0sq"])
            em.wait("tensor", "a0nre", em.cnt.get("a0nre", 0))
            em.on("tensor", nc.tensor.matmul, g.p_row[:, :],
                  g.onescol[0:64, :], qsq[:, :],
                  start=True, stop=True, _inc=("a0nr", 1))
            em.wait_now("vector", "a0nr")
            em.on("vector", nc.vector.tensor_copy,
                  r_qn2[:, hc * 512:(hc + 1) * 512], g.p_row[:, :],
                  _inc=("a0nre", 1))
        for hc in range(NCH):
            em.wait("vector", "a0nr", em.cnt["a0nr"])
            em.on("vector", nc.vector.tensor_mul, qsq[:, :],
                  KTa[0:64, hc * 512:(hc + 1) * 512],
                  KTa[0:64, hc * 512:(hc + 1) * 512], _inc=("a0sq", 1))
            em.wait("tensor", "a0sq", em.cnt["a0sq"])
            em.wait("tensor", "a0nre", em.cnt.get("a0nre", 0))
            em.on("tensor", nc.tensor.matmul, g.p_row[:, :],
                  g.onescol[0:64, :], qsq[:, :],
                  start=True, stop=True, _inc=("a0nr", 1))
            em.wait_now("vector", "a0nr")
            em.on("vector", nc.vector.reduce_max, scal[:, hc:hc + 1],
                  g.p_row[:, :], axis=AX.X, _inc=("a0nre", 1))
        em.wait("vector", "a0nre", em.cnt["a0nre"])
        em.on("vector", nc.vector.reduce_max, scal[:, 8:9], scal[:, 0:8],
              axis=AX.X, _inc=("a0s1", 1))
        em.wait_now("scalar", "a0s1")
        em.on("scalar", nc.scalar.activation, scal[:, 9:10], scal[:, 8:9],
              AF.Ln, bias=g.cvals[:, 0:1], scale=1.0)
        em.on("scalar", nc.scalar.activation, r_b8, r_qn2,
              AF.Ln, bias=g.cvals[:, 0:1], scale=1.0, _inc=("a0s2", 1))
        em.wait_now("vector", "a0s2")
        em.on("vector", nc.vector.tensor_scalar_mul, scal[:, 10:11],
              scal[:, 9:10], 0.5, _inc=("a0s3", 1))
        em.wait_now("scalar", "a0s3")
        em.on("scalar", nc.scalar.activation, r_b8, r_b8,
              AF.Exp, bias=scal[:, 10:11], scale=0.5, _inc=("a0s4", 1))
        em.wait_now("vector", "a0s4")
        for hc in range(NCH):
            em.wait("vector", "a0ns", em.cnt.get("a0ns", 0))
            n = em.on("vector", nc.vector.tensor_scalar,
                      g.rowa, r_b8[:, hc * 512:(hc + 1) * 512],
                      1.0, 8.0 * slope * (hc * 512),
                      op0=ALU.mult, op1=ALU.add, _inc=("a0nsp", 1))
            em.wait("vector", "a0nsp", n)
            em.on("vector", nc.vector.scalar_tensor_tensor,
                  QTa[64:65, hc * 512:(hc + 1) * 512],
                  g.qpos[:, :], -8.0 * slope, g.rowa,
                  op0=ALU.mult, op1=ALU.subtract, _inc=("a0ns", 1))
        em.on("vector", nc.vector.tensor_scalar_mul, bias_k[:, :],
              g.kpos[:, :], slope, _inc=("a0bk", 1))
        em.wait_now("tensor", "a0ns")
        em.wait("tensor", "a0ve", em.cnt["a0ve"])
        em.wait_now("scalar", "a0bk")

        for qc in range(NCH):
            ktm = 4 * qc + 4
            for kt in range(ktm):
                sb = kt % 2
                p_s = g.p_a[sb]
                em.wait("tensor", "a0ex", em.cnt.get("a0ex", 0) - 1)
                smm = em.on("tensor", nc.tensor.matmul, p_s[:, :],
                            KTa[:, ts(kt)], QTa[:, qc * 512:(qc + 1) * 512],
                            start=True, stop=True, _inc=("a0s", 1))
                if kt >= 4 * qc:
                    em.wait("vector", "a0s", smm)
                    mkn = em.on("vector", nc.vector.tensor_add,
                                p_s[:, :], p_s[:, :],
                                dmask[:, ts(kt - 4 * qc, 512)],
                                _inc=("a0mk", 1))
                    em.wait("scalar", "a0mk", mkn)
                else:
                    em.wait("scalar", "a0s", smm)
                exn = em.on("scalar", nc.scalar.activation,
                            PT[:, ts(sb, 512)], p_s[:, :], AF.Exp,
                            bias=bias_k[:, kt:kt + 1], scale=0.125,
                            _inc=("a0ex", 1))
                em.wait("tensor", "a0ex", exn)
                em.on("tensor", nc.tensor.matmul, g.p_y[0:65, :],
                      Vbuf[:, kt * 65: kt * 65 + 65], PT[:, ts(sb, 512)],
                      start=(kt == 0), stop=(kt == ktm - 1),
                      _inc=("a0pv", 1) if kt == ktm - 1 else None)
            em.wait_now("vector", "a0pv")
            rcn = em.on("vector", nc.vector.reciprocal, g.rowa,
                        g.p_y[64:65, :], _inc=("a0rc", 1))
            em.wait("tensor", "a0rc", rcn)
            bcn = em.on("tensor", nc.tensor.matmul, g.p_bc[0:64, :],
                        g.onesrowf[:, 0:64], g.rowa,
                        start=True, stop=True, _inc=("a0bc", 1))
            em.wait("vector", "a0bc", bcn)
            nbce = em.on("vector", nc.vector.tensor_copy, recbc[:, :],
                         g.p_bc[0:64, :], _inc=("a0bce", 1))
            em.wait("vector", "a0bce", nbce)
            yev = em.on("vector", nc.vector.tensor_mul,
                        YT[64 * (h % 2): 64 * (h % 2) + 64,
                           (h // 2) * T + qc * 512: (h // 2) * T + (qc + 1) * 512],
                        g.p_y[0:64, :], recbc[:, :], _inc=("a0ye", 1))
            em.wait("tensor", "a0ye", yev)
            em.wait("tensor", "a0bce", em.cnt["a0bce"])

    em.flush("a0_heads")

    won = 0
    wou = 0

    wos_tgt = {0: 0, 1: 0}

    def load_wo(ft, slot):
        em.wait("gpsimd", "a0om", wou - 1)
        wos_tgt[slot] = em.on("gpsimd", nc.gpsimd.dma_start,
                              wo_s[:, ts(slot, D)],
                              W["wo"][ft, :, :],
                              _inc=(f"a0wos{slot}", 16))

    load_wo(0, 0)
    for tc in range(NCH):
        for pss in range(2):
            for ft in range(CT):
                slot = (tc * 2 * CT + pss * CT + ft) % 2
                nft = ft + 1
                npss, ntc = pss, tc
                if nft == CT:
                    nft = 0
                    npss += 1
                    if npss == 2:
                        npss = 0
                        ntc += 1
                po = g.p_a[ft % 2]
                em.wait("tensor", f"a0wos{slot}", wos_tgt[slot])
                em.wait("tensor", "a0pre", em.cnt.get("a0pre", 0) - 1)
                for ct in range(CT):
                    em.on("tensor", nc.tensor.matmul, po[:, :],
                          wo_s[:, slot * D + ct * 128: slot * D + (ct + 1) * 128],
                          YT[:, ct * T + tc * 512: ct * T + (tc + 1) * 512],
                          start=(ct == 0), stop=(ct == CT - 1),
                          _inc=("a0om", 1) if ct == CT - 1 else None)
                wou = em.cnt["a0om"]
                if ntc < NCH:
                    load_wo(nft, 1 - slot)
                em.wait_now("vector", "a0om")
                if pss == 0:
                    em.wait("vector", "a0st1", em.cnt.get("a0st1", 0))
                    prn = em.on("vector", nc.vector.scalar_tensor_tensor,
                                sqbf[:, :], po[:, :], ob_sb[:, ft:ft + 1],
                                g.xT[:, ft * T + tc * 512: ft * T + (tc + 1) * 512],
                                op0=ALU.add, op1=ALU.add, _inc=("a0pre", 1))
                    em.wait("tensor", "a0pre", prn)
                    em.on("tensor", nc.tensor.matmul, g.p_row[:, :],
                          g.onescol[:, :], sqbf[:, :],
                          start=(ft == 0), stop=(ft == CT - 1),
                          _inc=("a0st1", 1))
                    em.wait("vector", "a0st2", em.cnt.get("a0st2", 0))
                    sqn = em.on("vector", nc.vector.tensor_mul, PT[:, 0:512],
                                sqbf[:, :], sqbf[:, :], _inc=("a0sqt", 1))
                    em.wait("tensor", "a0sqt", sqn)
                    em.on("tensor", nc.tensor.matmul, g.p_row2[:, :],
                          g.onescol[:, :], PT[:, 0:512],
                          start=(ft == 0), stop=(ft == CT - 1),
                          _inc=("a0st2", 1))
                else:
                    em.vop(nc.vector.scalar_tensor_tensor,
                           scr[:, :], po[:, :], ob_sb[:, ft:ft + 1],
                           g.xT[:, ft * T + tc * 512: ft * T + (tc + 1) * 512],
                           op0=ALU.add, op1=ALU.add)
                    em.vop(nc.vector.tensor_mul, scr[:, :],
                           scr[:, :], rbc[:, :])
                    em.vop(nc.vector.tensor_sub,
                           g.xT[:, ft * T + tc * 512: ft * T + (tc + 1) * 512],
                           scr[:, :], mrbc[:, :], _inc=("a0pre", 1))
            if pss == 0:
                em.wait_now("vector", "a0st1")
                em.wait_now("vector", "a0st2")
                em.vop(nc.vector.tensor_scalar_mul, l_mean, g.p_row[:, :],
                       1.0 / D)
                em.vop(nc.vector.tensor_scalar_mul, l_r, g.p_row2[:, :],
                       1.0 / D, _inc=("a0lv", 1))
                em.wait("tensor", "a0lv", em.cnt["a0lv"])
                em.vop(nc.vector.tensor_mul, l_mr, l_mean, l_mean)
                em.vop(nc.vector.tensor_sub, l_r, l_r, l_mr, _inc=("a0lv2", 1))
                em.wait_now("scalar", "a0lv2")
                em.on("scalar", nc.scalar.activation, l_r, l_r, AF.Ln,
                      bias=g.cvals[:, 1:2], scale=1.0, _inc=("a0lnp", 1))
                em.wait("scalar", "a0lnp", em.cnt["a0lnp"])
                em.on("scalar", nc.scalar.activation, l_r, l_r, AF.Exp,
                      bias=0.0, scale=-0.5, _inc=("a0lnr", 1))
                em.wait_now("vector", "a0lnr")
                mrn = em.vop(nc.vector.tensor_mul, l_mr, l_mean, l_r,
                             _inc=("a0hl", 1))
                em.wait("tensor", "a0hl", mrn)
                em.wait("tensor", "a0bce", em.cnt.get("a0bce", 0))
                n1 = em.on("tensor", nc.tensor.matmul, g.p_bc[:, :],
                           g.onesrowf[:, :], l_r,
                           start=True, stop=True, _inc=("a0bcc", 1))
                em.wait("vector", "a0bcc", n1)
                em.on("vector", nc.vector.tensor_copy, rbc[:, :], g.p_bc[:, :],
                      _inc=("a0bce", 1))
                em.wait("tensor", "a0bce", em.cnt["a0bce"])
                n2 = em.on("tensor", nc.tensor.matmul, g.p_bc[:, :],
                           g.onesrowf[:, :], l_mr,
                           start=True, stop=True, _inc=("a0bcc", 1))
                em.wait("vector", "a0bcc", n2)
                em.on("vector", nc.vector.tensor_copy, mrbc[:, :], g.p_bc[:, :],
                      _inc=("a0bce", 1))
    em.flush("a0_proj")
    es.close()





def build_l0_ffn(g, W, z1_hbm, consts):
    nc, em = g.nc, g.em
    from contextlib import ExitStack
    FCP = 22
    es = ExitStack()
    sb = lambda n, sh, dt: es.enter_context(nc.sbuf_tensor(n, sh, dt))

    xp = g.bigA[:, :]
    h2h = nc.dram_tensor("h2_hbm", [128, CT * T], BF16, kind="Internal")

    masks = sb("f0_masks", [128, 8 * 512], BF16)
    scal = g.scal
    rrow = g.rowa
    rrow2 = g.rowb
    rbc64 = g.rbc64
    recbc = g.recbc
    l_mean, l_r, l_mr = g.lm, g.lr, g.lmr
    rbc, mrbc = g.rbc, g.mrbc
    scr = g.scr
    scr2 = g.scr2
    sqbf = g.sqbf
    qs8 = sb("f0_qs8", [64, 2], F32)
    lgb = sb("f0_lgb", [128, 16], F32)
    b1c = sb("f0_b1c", [128, 44], F32)
    b2c = sb("f0_b2c", [128, 8], F32)

    em.on("vector", nc.vector.tensor_copy, xp, g.xT[:, :], _inc=("setup_v", 1))
    em.wait_now("sync", "setup_v")
    em.on("sync", nc.sync.dma_start, z1_hbm[:, :], g.xT[:, :], _inc=("f0z1", 16))

    for ji, j in enumerate(range(-4, 4)):
        m = masks[:, ts(ji, 512)]
        n = em.on("gpsimd", nc.gpsimd.memset, m, 0.0, _inc=("f0mg", 1))
        em.wait("gpsimd", "f0mg", n)
        n = em.on("gpsimd", nc.gpsimd.affine_select, m, m,
                  [[1, 512]], compare_op=ALU.is_ge, fill=-1e30,
                  base=-128 * j, channel_multiplier=-1, _inc=("f0mg", 1))
        em.wait("gpsimd", "f0mg", n)
        em.on("gpsimd", nc.gpsimd.affine_select, m, m,
              [[-1, 512]], compare_op=ALU.is_ge, fill=-1e30,
              base=512 + 128 * j, channel_multiplier=1, _inc=("f0mg", 1))
    em.flush("f0_setup")

    def mask_j(j):
        return masks[:, ts(j + 4, 512)]

    for l in range(2):
        tag = "a0"

        em.on("sync", nc.sync.dma_start, qs8[:, 0:1], W["qs"][l, :, :],
              _inc=(tag + "bl", 16))
        em.on("sync", nc.sync.dma_start, qs8[:, 1:2], W["ks"][l, :, :],
              _inc=(tag + "bl", 16))
        em.on("sync", nc.sync.dma_start, lgb[:, 0:8], W["lg"][l, :, :],
              _inc=(tag + "bl", 16))
        em.on("sync", nc.sync.dma_start, lgb[:, 8:16], W["lb"][l, :, :],
              _inc=(tag + "bl", 16))
        em.on("sync", nc.sync.dma_start, b1c[:, 0:22], W["b1a"][l, :, :],
              _inc=(tag + "bl", 16))
        em.on("sync", nc.sync.dma_start, b1c[:, 22:44], W["b1g"][l, :, :],
              _inc=(tag + "bl", 16))
        em.on("sync", nc.sync.dma_start, b2c[:, :], W["b2"][l, :, :],
              _inc=(tag + "bl", 16))
        em.wait_now("vector", tag + "bl")
        n = em.on("vector", nc.vector.tensor_scalar_mul, qs8[:, 0:1],
                  qs8[:, 0:1], 8.0, _inc=(tag + "qs8", 1))
        em.flush(tag + "_consts")

        with ExitStack() as aes:
            asb = lambda n_, sh, dt: aes.enter_context(
                nc.sbuf_tensor(f"L{l}{n_}", sh, dt))
            YT2 = g.xT[:, 0: 4 * T]
            QTa = asb("_QTa", [65, T], BF16)
            KTa = asb("_KTa", [65, T], BF16)
            Vbuf = asb("_Vbuf", [128, NT * 65], BF16)
            PT = asb("_PT", [128, 2 * 512], BF16)
            wqh = [asb(f"_wqh{i}", [128, CT * 192], BF16) for i in range(2)]
            rot = asb("_rot", [64, 8 * 512], BF16)
            woh = [asb(f"_woh{i}", [128, 4 * 128], BF16) for i in range(2)]
            maxc = asb("_maxc", [128, 32], F32)
            maxt = asb("_maxt", [32, 128], F32)

            for tt in range(NT):
                em.on("gpsimd", nc.gpsimd.memset,
                      Vbuf[:, tt * 65 + 64: tt * 65 + 65], 1.0,
                      _inc=(tag + "vo", 1))
            em.on("vector", nc.vector.memset, KTa[64:65, :], 1.0,
                  _inc=(tag + "vo", 1))

            def load_head(h, slot):
                for ct in range(CT):
                    em.on("gpsimd", nc.gpsimd.dma_start,
                          wqh[slot][:, ct * 192: (ct + 1) * 192],
                          W["wqkv"][l, h, ct, :, :],
                          _inc=(tag + f"wl{slot}", 16))
                return em.cnt[tag + f"wl{slot}"]

            em.wait("vector", "f0z1", 16)
            wl_tgt = load_head(0, 0)
            em.flush(tag + "_asetup")

            def rhs_x(ct, lo, width):
                off = ct * T + lo - (1 if ct >= 4 else 0)
                return xp[:, off: off + width]

            rot_t = {0: 0, 1: 0}
            for h in range(8):
                slot = h % 2
                my_tgt = wl_tgt
                if h + 1 < 8:
                    em.wait("gpsimd", tag + "vm", em.cnt.get(tag + "vm", 0))
                    wl_tgt = load_head(h + 1, 1 - slot)
                em.wait("tensor", tag + f"wl{slot}", my_tgt)

                em.wait("vector", tag + "s", em.cnt.get(tag + "s", 0))
                em.wait("vector", tag + "p1", em.cnt.get(tag + "p1", 0))
                em.wait("vector", tag + "pv", em.cnt.get(tag + "pv", 0))
                em.wait("vector", tag + "mxt", em.cnt.get(tag + "mxt", 0))
                for hc in range(NCH):
                    em.wait("tensor", tag + "qe", em.cnt.get(tag + "qe", 0))
                    for ct in range(CT):
                        em.on("tensor", nc.tensor.matmul, g.p_a[0][0:64, :],
                              wqh[slot][:, ct * 192: ct * 192 + 64],
                              rhs_x(ct, hc * 512, 512),
                              start=(ct == 0), stop=(ct == CT - 1),
                              _inc=(tag + "qm", 1) if ct == CT - 1 else None)
                    for ct in range(CT):
                        em.on("tensor", nc.tensor.matmul, g.p_a[1][0:64, :],
                              wqh[slot][:, ct * 192 + 64: ct * 192 + 128],
                              rhs_x(ct, hc * 512, 512),
                              start=(ct == 0), stop=(ct == CT - 1),
                              _inc=(tag + "qm", 1) if ct == CT - 1 else None)
                    em.wait_now("vector", tag + "qm")
                    em.on("vector", nc.vector.tensor_copy,
                          QTa[0:64, hc * 512:(hc + 1) * 512], g.p_a[0][0:64, :],
                          _inc=(tag + "qe", 1))
                    em.on("vector", nc.vector.tensor_copy,
                          KTa[0:64, hc * 512:(hc + 1) * 512], g.p_a[1][0:64, :],
                          _inc=(tag + "qe", 1))
                em.wait("tensor", tag + "qe", em.cnt[tag + "qe"])
                em.wait("tensor", tag + "ve", em.cnt.get(tag + "ve", 0))
                for which, dst in ((0, QTa), (64, KTa)):
                    for ct in range(4):
                        em.on("tensor", nc.tensor.matmul, g.p_v[0:64, 0:1],
                              wqh[slot][:, ct * 192 + which: ct * 192 + which + 64],
                              xp[:, ct * T: ct * T + 1],
                              start=(ct == 0), stop=(ct == 3),
                              _inc=(tag + "fx", 1))
                    em.wait_now("vector", tag + "fx")
                    em.on("vector", nc.vector.tensor_copy, dst[0:64, 0:1],
                          g.p_v[0:64, 0:1], _inc=(tag + "fxe", 1))
                    em.wait("tensor", tag + "fxe", em.cnt[tag + "fxe"])
                for tt in range(NT):
                    em.wait("tensor", tag + "ve", em.cnt.get(tag + "ve", 0))
                    for ct in range(CT):
                        em.on("tensor", nc.tensor.matmul, g.p_v[:, 0:64],
                              rhs_x(ct, tt * 128, 128),
                              wqh[slot][:, ct * 192 + 128: ct * 192 + 192],
                              start=(ct == 0), stop=(ct == CT - 1),
                              _inc=(tag + "vm", 1) if ct == CT - 1 else None)
                    em.wait_now("vector", tag + "vm")
                    em.on("vector", nc.vector.tensor_copy,
                          Vbuf[:, tt * 65: tt * 65 + 64], g.p_v[:, 0:64],
                          _inc=(tag + "ve", 1))
                em.wait("tensor", tag + "ve", em.cnt[tag + "ve"])
                for ct in range(4):
                    em.on("tensor", nc.tensor.matmul, g.p_v[0:1, 0:64],
                          xp[:, ct * T: ct * T + 1],
                          wqh[slot][:, ct * 192 + 128: ct * 192 + 192],
                          start=(ct == 0), stop=(ct == 3),
                          _inc=(tag + "fx", 1))
                em.wait_now("vector", tag + "fx")
                em.on("vector", nc.vector.tensor_copy, Vbuf[0:1, 0:64],
                      g.p_v[0:1, 0:64], _inc=(tag + "fxe", 1))
                em.wait("tensor", tag + "fxe", em.cnt[tag + "fxe"])

                for hc in range(NCH):
                    rs = hc % 2
                    em.wait("gpsimd", tag + "rote",
                            em.cnt.get(tag + "rote", 0) - 2)
                    rot_t[rs] = em.on(
                        "gpsimd", nc.gpsimd.dma_start,
                        rot[:, rs * 2048 + 0: rs * 2048 + 512],
                        consts["rotq"][0, :, hc * 512:(hc + 1) * 512],
                        _inc=(tag + f"rt{rs}", 16))
                    em.on("gpsimd", nc.gpsimd.dma_start,
                          rot[:, rs * 2048 + 512: rs * 2048 + 1024],
                          consts["rotq"][1, :, hc * 512:(hc + 1) * 512],
                          _inc=(tag + f"rt{rs}", 16))
                    em.on("gpsimd", nc.gpsimd.dma_start,
                          rot[:, rs * 2048 + 1024: rs * 2048 + 1536],
                          consts["rotk"][0, :, hc * 512:(hc + 1) * 512],
                          _inc=(tag + f"rt{rs}", 16))
                    rot_t[rs] = em.on(
                        "gpsimd", nc.gpsimd.dma_start,
                        rot[:, rs * 2048 + 1536: rs * 2048 + 2048],
                        consts["rotk"][1, :, hc * 512:(hc + 1) * 512],
                        _inc=(tag + f"rt{rs}", 16))
                    for (src, t0, t1, scol) in (
                            (QTa, rs * 2048, rs * 2048 + 512, 0),
                            (KTa, rs * 2048 + 1024, rs * 2048 + 1536, 1)):
                        ch = slice(hc * 512, (hc + 1) * 512)
                        em.wait("vector", tag + "nm", em.cnt.get(tag + "nm", 0))
                        n = em.on("vector", nc.vector.tensor_mul, sqbf[0:64, :],
                                  src[0:64, ch], src[0:64, ch],
                                  _inc=(tag + "sq", 1))
                        em.wait("tensor", tag + "sq", n)
                        em.wait("tensor", tag + "nre", em.cnt.get(tag + "nre", 0))
                        n = em.on("tensor", nc.tensor.matmul, g.p_row[:, :],
                                  g.onescol[0:64, :], sqbf[0:64, :],
                                  start=True, stop=True, _inc=(tag + "nm", 1))
                        em.wait("scalar", tag + "nm", n)
                        n = em.on("scalar", nc.scalar.activation, rrow[:, :],
                                  g.p_row[:, :], AF.Ln, bias=g.cvals[:, 4:5],
                                  scale=1.0, _inc=(tag + "ln1", 1))
                        em.wait("scalar", tag + "ln1", n)
                        n = em.on("scalar", nc.scalar.activation, rrow[:, :],
                                  rrow[:, :], AF.Exp, bias=0.0, scale=-0.5,
                                  _inc=(tag + "nre", 1))
                        em.wait("tensor", tag + "nre", em.cnt[tag + "nre"])
                        n = em.on("tensor", nc.tensor.matmul, g.p_bc[0:64, :],
                                  g.onesrowf[:, 0:64], rrow[:, :],
                                  start=True, stop=True, _inc=(tag + "bc", 1))
                        em.wait("vector", tag + "bc", n)
                        n = em.on("vector", nc.vector.tensor_copy, rbc64[:, :],
                                  g.p_bc[0:64, :], _inc=(tag + "bce", 1))
                        em.wait("tensor", tag + "bce", em.cnt[tag + "bce"])
                        em.wait("vector", tag + "bce", n)
                        em.vop(nc.vector.tensor_mul, scr[0:64, :],
                               src[0:64, ch], rbc64[:, :])
                        if l == 0 and h == 0 and hc == 0 and scol == 0:
                            em.wait("vector", tag + "qs8", 1)
                        em.vop(nc.vector.tensor_scalar_mul, scr[0:64, :],
                               scr[0:64, :], qs8[:, scol:scol + 1])
                        em.wait("vector", tag + f"rt{rs}", rot_t[rs])
                        em.vop(nc.vector.tensor_mul, scr2[0:64, :],
                               scr[0:64, :], rot[:, t0:t0 + 512])
                        em.vop(nc.vector.tensor_scalar_mul, sqbf[0:32, :],
                               scr[32:64, :], -1.0)
                        em.vop(nc.vector.tensor_copy, sqbf[32:64, :],
                               scr[0:32, :])
                        em.vop(nc.vector.tensor_mul, scr[0:64, :],
                               sqbf[0:64, :], rot[:, t1:t1 + 512])
                        em.vop(nc.vector.tensor_add, src[0:64, ch],
                               scr2[0:64, :], scr[0:64, :],
                               _inc=(tag + "rote", 1))
                em.wait("tensor", tag + "rote", em.cnt[tag + "rote"])
                p1_step = [0]
                for qt in range(NT):
                    kcs = [kc for kc in (qt // 4 - 1, qt // 4) if kc >= 0]
                    first = True
                    for kc in kcs:
                        j = qt - 4 * kc - 4
                        pb = g.p_a[p1_step[0] % 2]
                        p1_step[0] += 1
                        em.wait("tensor", tag + "mx",
                                em.cnt.get(tag + "mx", 0) - 1)
                        n = em.on("tensor", nc.tensor.matmul, pb[:, :],
                                  QTa[0:64, ts(qt)],
                                  KTa[0:64, kc * 512:(kc + 1) * 512],
                                  start=True, stop=True, _inc=(tag + "p1", 1))
                        em.wait("vector", tag + "p1", n)
                        n = em.on("vector", nc.vector.tensor_add, pb[:, :],
                                  pb[:, :], mask_j(j), _inc=(tag + "p1m", 1))
                        em.wait("vector", tag + "p1m", n)
                        if first:
                            n = em.on("vector", nc.vector.reduce_max,
                                      maxc[:, qt:qt + 1], pb[:, :], axis=AX.X,
                                      _inc=(tag + "mx", 1))
                        else:
                            n = em.on("vector", nc.vector.reduce_max,
                                      scr[:, 0:1], pb[:, :], axis=AX.X,
                                      _inc=(tag + "mx", 1))
                            em.wait("vector", tag + "mx", n)
                            n = em.on("vector", nc.vector.tensor_max,
                                      maxc[:, qt:qt + 1], maxc[:, qt:qt + 1],
                                      scr[:, 0:1], _inc=(tag + "mx", 1))
                        first = False
                em.wait("vector", tag + "mx", em.cnt[tag + "mx"])
                n = em.on("vector", nc.vector.tensor_scalar_mul, maxc[:, :],
                          maxc[:, :], -1.0, _inc=(tag + "mxn", 1))
                em.wait("tensor", tag + "mxn", n)
                n = em.on("tensor", nc.tensor.transpose, g.p_v[0:32, :],
                          maxc[:, :], g.identf[:], _inc=(tag + "mxt", 1))
                em.wait("vector", tag + "mxt", n)
                n = em.on("vector", nc.vector.tensor_copy, maxt[:, :],
                          g.p_v[0:32, :], _inc=(tag + "mxe", 1))
                em.wait("gpsimd", tag + "mxe", em.cnt[tag + "mxe"])
                nsh = em.on("gpsimd", nc.gpsimd.dma_start, QTa[64:65, :],
                            maxt[:, :], _inc=(tag + "nsh", 16))
                em.wait("tensor", tag + "nsh", nsh)

                for qc in range(NCH):
                    kts = list(range(max(0, 4 * qc - 4), 4 * qc + 4))
                    for ki, kt in enumerate(kts):
                        sbk = kt % 2
                        p_s = g.p_a[sbk]
                        em.wait("tensor", tag + "ex",
                                em.cnt.get(tag + "ex", 0) - 1)
                        smm = em.on("tensor", nc.tensor.matmul, p_s[:, :],
                                    KTa[:, ts(kt)],
                                    QTa[:, qc * 512:(qc + 1) * 512],
                                    start=True, stop=True, _inc=(tag + "s", 1))
                        em.wait("vector", tag + "s", smm)
                        j = kt - 4 * qc
                        mkn = em.on("vector", nc.vector.tensor_add,
                                    p_s[:, :], p_s[:, :], mask_j(j),
                                    _inc=(tag + "mk", 1))
                        em.wait("scalar", tag + "mk", mkn)
                        exn = em.on("scalar", nc.scalar.activation,
                                    PT[:, ts(sbk, 512)], p_s[:, :], AF.Exp,
                                    bias=0.0, scale=1.0, _inc=(tag + "ex", 1))
                        em.wait("tensor", tag + "ex", exn)
                        em.on("tensor", nc.tensor.matmul, g.p_y[0:65, :],
                              Vbuf[:, kt * 65: kt * 65 + 65],
                              PT[:, ts(sbk, 512)],
                              start=(ki == 0), stop=(ki == len(kts) - 1),
                              _inc=(tag + "pv", 1) if ki == len(kts) - 1 else None)
                    em.wait_now("vector", tag + "pv")
                    rcn = em.on("vector", nc.vector.reciprocal, rrow2[:, :],
                                g.p_y[64:65, :], _inc=(tag + "rc2", 1))
                    em.wait("tensor", tag + "rc2", rcn)
                    bcn = em.on("tensor", nc.tensor.matmul, g.p_bc[0:64, :],
                                g.onesrowf[:, 0:64], rrow2[:, :],
                                start=True, stop=True, _inc=(tag + "bc2", 1))
                    em.wait("vector", tag + "bc2", bcn)
                    nbce = em.on("vector", nc.vector.tensor_copy, recbc[:, :],
                                 g.p_bc[0:64, :], _inc=(tag + "bc2e", 1))
                    em.wait("vector", tag + "bc2e", nbce)
                    yev = em.on("vector", nc.vector.tensor_mul,
                                YT2[64 * (h % 2): 64 * (h % 2) + 64,
                                    (h // 2) * T + qc * 512:
                                    (h // 2) * T + (qc + 1) * 512],
                                g.p_y[0:64, :], recbc[:, :],
                                _inc=(tag + "ye", 1))
                    em.wait("tensor", tag + "ye", yev)
                    em.wait("tensor", tag + "bc2e", em.cnt[tag + "bc2e"])

            wot = {0: 0, 1: 0}
            wou = [0]

            def load_wo(ft, wslot):
                em.wait("gpsimd", tag + "om", wou[0] - 1)
                wot[wslot] = em.on("gpsimd", nc.gpsimd.dma_start,
                                   woh[wslot][:, :].rearrange(
                                       "c (ct f) -> c ct f", ct=4),
                                   W["wo"][l, :, :, ft * 128:(ft + 1) * 128]
                                   .rearrange("ct c f -> c ct f"),
                                   _inc=(tag + f"wo{wslot}", 16))

            load_wo(0, 0)
            for tc in range(NCH):
                for ft in range(CT):
                    i = tc * CT + ft
                    wslot = i % 2
                    po = g.p_a[ft % 2]
                    em.wait("tensor", tag + f"wo{wslot}", wot[wslot])
                    em.wait("tensor", tag + "oe", em.cnt.get(tag + "oe", 0) - 1)
                    for ct in range(4):
                        em.on("tensor", nc.tensor.matmul, po[:, :],
                              woh[wslot][:, ct * 128: (ct + 1) * 128],
                              YT2[:, ct * T + tc * 512: ct * T + (tc + 1) * 512],
                              start=(ct == 0), stop=(ct == 3),
                              _inc=(tag + "om", 1) if ct == 3 else None)
                    wou[0] = em.cnt[tag + "om"]
                    ni = i + 1
                    if ni < NCH * CT:
                        load_wo(ni % CT, ni % 2)
                    em.wait_now("vector", tag + "om")
                    em.on("vector", nc.vector.tensor_add,
                          xp[:, ft * T + tc * 512: ft * T + (tc + 1) * 512],
                          xp[:, ft * T + tc * 512: ft * T + (tc + 1) * 512],
                          po[:, :], _inc=(tag + "oe", 1))
        em.flush(tag + "_attn")

        with ExitStack() as fes:
            fsb = lambda n_, sh, dt: fes.enter_context(
                nc.sbuf_tensor(f"F{l}{n_}", sh, dt))
            w1a_res = g.xT[:, 0:FCP * CT * 128]
            act = fsb("_act", [128, FCP * 512], BF16)
            h2ch = fsb("_h2", [128, CT * 512], BF16)
            shch = fsb("_sh", [128, CT * 512], BF16)
            w1g_s = [fsb(f"_w1g{i}", [128, CT * 128], BF16) for i in range(2)]
            w2_s = [fsb(f"_w2{i}", [128, D], BF16) for i in range(2)]
            gel = g.sqbf

            em.wait("gpsimd", "f0z1", 16)
            for fc in range(FCP):
                em.on("gpsimd", nc.gpsimd.dma_start,
                      w1a_res[:, fc * 1024: (fc + 1) * 1024],
                      W["w1a"][l, fc].rearrange("ct c f -> c ct f"),
                      _inc=(tag + "w1a", 16))
            w1a_tgt = em.cnt[tag + "w1a"]
            w1g_t = {0: 0, 1: 0}
            w2_t = {0: 0, 1: 0}

            def load_w1g(fc, wslot, chunk_i):
                w1g_t[wslot] = em.on(
                    "gpsimd", nc.gpsimd.dma_start, w1g_s[wslot][:, :],
                    W["w1g"][l, fc].rearrange("ct c f -> c ct f"),
                    _inc=(tag + f"g{wslot}", 16))

            def load_w2(fc, wslot):
                w2_t[wslot] = em.on(
                    "gpsimd", nc.gpsimd.dma_start, w2_s[wslot][:, :],
                    W["w2"][l, fc, :, :], _inc=(tag + f"h{wslot}", 16))

            load_w1g(0, 0, 0)
            for tc in range(NCH):
                em.wait("vector", tag + "h2u", em.cnt.get(tag + "h2u", 0))
                em.wait("vector", tag + "st1", em.cnt.get(tag + "st1", 0))
                em.wait("vector", tag + "st2", em.cnt.get(tag + "st2", 0))
                for ct in range(CT):
                    off = ct * T + tc * 512 - (1 if ct >= 4 else 0)
                    n = em.on("vector", nc.vector.tensor_copy,
                              shch[:, ts(ct, 512)], xp[:, off: off + 512],
                              _inc=(tag + "shc", 1))
                    if ct >= 4 and tc == 0:
                        em.wait("vector", tag + "shc", n)
                        em.on("vector", nc.vector.memset,
                              shch[:, ct * 512: ct * 512 + 1], 0.0,
                              _inc=(tag + "shc", 1))
                em.wait("tensor", tag + "shc", em.cnt[tag + "shc"])
                em.wait("vector", tag + "shc", em.cnt[tag + "shc"])
                for ct in range(CT):
                    em.on("tensor", nc.tensor.matmul, g.p_row[:, :],
                          g.onescol[:, :], shch[:, ts(ct, 512)],
                          start=(ct == 0), stop=(ct == CT - 1),
                          _inc=(tag + "st1", 1))
                    em.wait("vector", tag + "st2", em.cnt.get(tag + "st2", 0))
                    n = em.on("vector", nc.vector.tensor_mul, sqbf[:, :],
                              shch[:, ts(ct, 512)], shch[:, ts(ct, 512)],
                              _inc=(tag + "sq2", 1))
                    em.wait("tensor", tag + "sq2", n)
                    em.on("tensor", nc.tensor.matmul, g.p_row2[:, :],
                          g.onescol[:, :], sqbf[:, :],
                          start=(ct == 0), stop=(ct == CT - 1),
                          _inc=(tag + "st2", 1))
                em.wait_now("vector", tag + "st1")
                em.wait_now("vector", tag + "st2")
                em.vop(nc.vector.tensor_scalar_mul, l_mean, g.p_row[:, :],
                       1.0 / D)
                em.vop(nc.vector.tensor_scalar_mul, l_r, g.p_row2[:, :],
                       1.0 / D, _inc=(tag + "lv", 1))
                em.wait("tensor", tag + "lv", em.cnt[tag + "lv"])
                em.vop(nc.vector.tensor_mul, l_mr, l_mean, l_mean)
                em.vop(nc.vector.tensor_sub, l_r, l_r, l_mr,
                       _inc=(tag + "lv2", 1))
                em.wait_now("scalar", tag + "lv2")
                n = em.on("scalar", nc.scalar.activation, l_r, l_r, AF.Ln,
                          bias=g.cvals[:, 1:2], scale=1.0, _inc=(tag + "lnp", 1))
                em.wait("scalar", tag + "lnp", n)
                em.on("scalar", nc.scalar.activation, l_r, l_r, AF.Exp,
                      bias=0.0, scale=-0.5, _inc=(tag + "lnr", 1))
                em.wait_now("vector", tag + "lnr")
                n = em.vop(nc.vector.tensor_mul, l_mr, l_mean, l_r,
                           _inc=(tag + "hl", 1))
                em.wait("tensor", tag + "hl", em.cnt[tag + "hl"])
                em.wait("tensor", tag + "bfre", em.cnt.get(tag + "bfre", 0))
                n1 = em.on("tensor", nc.tensor.matmul, g.p_bc[:, :],
                           g.onesrowf[:, :], l_r, start=True, stop=True,
                           _inc=(tag + "bcf", 1))
                em.wait("vector", tag + "bcf", n1)
                em.on("vector", nc.vector.tensor_copy, rbc[:, :], g.p_bc[:, :],
                      _inc=(tag + "bfre", 1))
                em.wait("tensor", tag + "bfre", em.cnt[tag + "bfre"])
                n2 = em.on("tensor", nc.tensor.matmul, g.p_bc[:, :],
                           g.onesrowf[:, :], l_mr, start=True, stop=True,
                           _inc=(tag + "bcf", 1))
                em.wait("vector", tag + "bcf", n2)
                em.on("vector", nc.vector.tensor_copy, mrbc[:, :], g.p_bc[:, :],
                      _inc=(tag + "bfre", 1))
                em.wait("vector", tag + "am", em.cnt.get(tag + "am", 0))
                em.wait("vector", tag + "gm", em.cnt.get(tag + "gm", 0))
                for ct in range(CT):
                    em.vop(nc.vector.tensor_mul, scr[:, :],
                           shch[:, ts(ct, 512)], rbc[:, :])
                    em.vop(nc.vector.tensor_sub, scr[:, :], scr[:, :],
                           mrbc[:, :])
                    em.vop(nc.vector.tensor_scalar, h2ch[:, ts(ct, 512)],
                           scr[:, :], lgb[:, ct:ct + 1], lgb[:, 8 + ct:9 + ct],
                           op0=ALU.mult, op1=ALU.add, _inc=(tag + "h2", 1))
                em.wait("sync", tag + "h2", em.cnt[tag + "h2"])
                em.wait("sync", tag + "bl", em.cnt[tag + "bl"])
                em.on("sync", nc.sync.dma_start,
                      h2h[:, tc * 4096:(tc + 1) * 4096], h2ch[:, :],
                      _inc=(tag + "bl", 16))
                em.wait("vector", tag + "bl", em.cnt[tag + "bl"])

            h2sl = [h2ch, shch]
            em.wait("sync", tag + "bl", em.cnt[tag + "bl"])
            em.wait("sync", tag + "st1", em.cnt.get(tag + "st1", 0))
            em.wait("sync", tag + "sq2", em.cnt.get(tag + "sq2", 0))
            em.on("sync", nc.sync.dma_start, h2sl[0][:, :], h2h[:, 0:4096],
                  _inc=(tag + "bl", 16))
            h2ld = {0: em.cnt[tag + "bl"], 1: 0}
            for tc in range(NCH):
                sl = tc % 2
                h2c = h2sl[sl]
                em.wait("sync", tag + "bl", h2ld[sl])
                if tc + 1 < NCH:
                    em.wait("sync", tag + "am", em.cnt.get(tag + "am", 0))
                    em.wait("sync", tag + "gm", em.cnt.get(tag + "gm", 0))
                    em.on("sync", nc.sync.dma_start, h2sl[1 - sl][:, :],
                          h2h[:, (tc + 1) * 4096:(tc + 2) * 4096],
                          _inc=(tag + "bl", 16))
                    h2ld[1 - sl] = em.cnt[tag + "bl"]
                em.wait("tensor", tag + "bl", h2ld[sl])

                for fc in range(FCP):
                    wslot = fc % 2
                    if tc == 0 and fc == 0:
                        em.wait("tensor", tag + "w1a", w1a_tgt)
                    em.wait("tensor", tag + f"g{wslot}", w1g_t[wslot])
                    em.wait("tensor", tag + "ae",
                            em.cnt.get(tag + "ae", 0))
                    for ct in range(CT):
                        em.on("tensor", nc.tensor.matmul, g.p_a[0][:, :],
                              w1a_res[:, fc * 1024 + ct * 128:
                                      fc * 1024 + (ct + 1) * 128],
                              h2c[:, ts(ct, 512)],
                              start=(ct == 0), stop=(ct == CT - 1),
                              _inc=(tag + "am", 1) if ct == CT - 1 else None)
                    for ct in range(CT):
                        em.on("tensor", nc.tensor.matmul, g.p_a[1][:, :],
                              w1g_s[wslot][:, ct * 128: (ct + 1) * 128],
                              h2c[:, ts(ct, 512)],
                              start=(ct == 0), stop=(ct == CT - 1),
                              _inc=(tag + "gm", 1) if ct == CT - 1 else None)
                    nfc = fc + 1 if fc + 1 < FCP else (0 if tc + 1 < NCH else None)
                    if nfc is not None:
                        em.wait("gpsimd", tag + "gm", em.cnt[tag + "gm"] - 1)
                        load_w1g(nfc, 1 - wslot, tc)
                    em.wait_now("scalar", tag + "gm")
                    gn = em.on("scalar", nc.scalar.activation, gel[:, :],
                               g.p_a[1][:, :], AF.Gelu,
                               bias=b1c[:, 22 + fc: 23 + fc], scale=1.0,
                               _inc=(tag + "gel", 1))
                    em.wait("vector", tag + "am", em.cnt[tag + "am"])
                    em.wait("vector", tag + "gel", gn)
                    em.wait("vector", tag + "wm", em.cnt.get(tag + "wm", 0))
                    em.vop(nc.vector.tensor_scalar, scr[:, :], g.p_a[0][:, :],
                           b1c[:, fc:fc + 1], None, op0=ALU.add)
                    em.vop(nc.vector.tensor_mul, act[:, ts(fc, 512)],
                           scr[:, :], gel[:, :], _inc=(tag + "ae", 1))
                em.wait("tensor", tag + "ae", em.cnt[tag + "ae"])
                em.wait("tensor", tag + "bfre", em.cnt.get(tag + "bfre", 0))
                pbanks = [g.p_a[0], g.p_a[1], g.p_y, g.p_bc,
                          None, None, None, None]
                for half in range(2):
                    for fc in range(FCP):
                        wslot = fc % 2
                        em.wait("gpsimd", tag + "wm",
                                em.cnt.get(tag + "wm", 0) - 1)
                        load_w2(fc, wslot)
                        em.wait("tensor", tag + f"h{wslot}", w2_t[wslot])
                        for dq in range(4):
                            dt = half * 4 + dq
                            em.on("tensor", nc.tensor.matmul,
                                  pbanks[dq][:, :],
                                  w2_s[wslot][:, dt * 128: (dt + 1) * 128],
                                  act[:, ts(fc, 512)],
                                  start=(fc == 0), stop=(fc == FCP - 1),
                                  _inc=(tag + "wm", 1) if dq == 3 else None)
                    em.wait_now("vector", tag + "wm")
                    for dq in range(4):
                        dt = half * 4 + dq
                        em.vop(nc.vector.tensor_scalar, scr[:, :],
                               pbanks[dq][:, :], b2c[:, dt:dt + 1], None,
                               op0=ALU.add)
                        em.vop(nc.vector.tensor_add,
                               xp[:, dt * T + tc * 512: dt * T + (tc + 1) * 512],
                               xp[:, dt * T + tc * 512: dt * T + (tc + 1) * 512],
                               scr[:, :], _inc=(tag + "h2u", 1))
                    em.wait("tensor", tag + "h2u", em.cnt[tag + "h2u"])
        em.flush(tag + "_ffn")
    with ExitStack() as les:
        lsb = lambda n_, sh, dt: les.enter_context(nc.sbuf_tensor(n_, sh, dt))
        z1ch = lsb("fL_z1", [128, CT * 512], BF16)
        pre = lsb("fL_pre", [128, CT * 512], BF16)
        z1v = z1_hbm[:, :]
        em.wait("gpsimd", "f0z1", 16)
        zt = {}
        for tc in range(NCH):
            em.wait("gpsimd", "a0prf", em.cnt.get("a0prf", 0))
            zt[tc] = em.on("gpsimd", nc.gpsimd.dma_start, z1ch[:, :],
                           z1v.rearrange("p (c t) -> p c t", c=CT)
                           [:, :, tc * 512:(tc + 1) * 512],
                           _inc=("a0zl", 16))
            em.wait("vector", "a0zl", zt[tc])
            for ct in range(CT):
                em.on("vector", nc.vector.tensor_add, pre[:, ts(ct, 512)],
                      z1ch[:, ts(ct, 512)],
                      xp[:, ct * T + tc * 512: ct * T + (tc + 1) * 512],
                      _inc=("a0prf", 1))
            em.wait("tensor", "a0prf", em.cnt["a0prf"])
            em.wait("vector", "a0prf", em.cnt["a0prf"])
            for ct in range(CT):
                em.on("tensor", nc.tensor.matmul, g.p_row[:, :],
                      g.onescol[:, :], pre[:, ts(ct, 512)],
                      start=(ct == 0), stop=(ct == CT - 1),
                      _inc=("a0st1", 1))
                em.wait("vector", "a0st2", em.cnt.get("a0st2", 0))
                n = em.on("vector", nc.vector.tensor_mul, sqbf[:, :],
                          pre[:, ts(ct, 512)], pre[:, ts(ct, 512)],
                          _inc=("a0sqt", 1))
                em.wait("tensor", "a0sqt", n)
                em.on("tensor", nc.tensor.matmul, g.p_row2[:, :],
                      g.onescol[:, :], sqbf[:, :],
                      start=(ct == 0), stop=(ct == CT - 1),
                      _inc=("a0st2", 1))
            em.wait_now("vector", "a0st1")
            em.wait_now("vector", "a0st2")
            em.vop(nc.vector.tensor_scalar_mul, l_mean, g.p_row[:, :], 1.0 / D)
            em.vop(nc.vector.tensor_scalar_mul, l_r, g.p_row2[:, :], 1.0 / D,
                   _inc=("a0lv", 1))
            em.wait("tensor", "a0lv", em.cnt["a0lv"])
            em.vop(nc.vector.tensor_mul, l_mr, l_mean, l_mean)
            em.vop(nc.vector.tensor_sub, l_r, l_r, l_mr, _inc=("a0lv2", 1))
            em.wait_now("scalar", "a0lv2")
            n = em.on("scalar", nc.scalar.activation, l_r, l_r, AF.Ln,
                      bias=g.cvals[:, 1:2], scale=1.0, _inc=("a0lnp", 1))
            em.wait("scalar", "a0lnp", n)
            em.on("scalar", nc.scalar.activation, l_r, l_r, AF.Exp,
                  bias=0.0, scale=-0.5, _inc=("a0lnr", 1))
            em.wait_now("vector", "a0lnr")
            em.vop(nc.vector.tensor_mul, l_mr, l_mean, l_r, _inc=("a0hl", 1))
            em.wait("tensor", "a0hl", em.cnt["a0hl"])
            em.wait("tensor", "a0bfre", em.cnt.get("a0bfre", 0))
            n1 = em.on("tensor", nc.tensor.matmul, g.p_bc[:, :],
                       g.onesrowf[:, :], l_r, start=True, stop=True,
                       _inc=("a0bcf", 1))
            em.wait("vector", "a0bcf", n1)
            em.on("vector", nc.vector.tensor_copy, rbc[:, :], g.p_bc[:, :],
                  _inc=("a0bfre", 1))
            em.wait("tensor", "a0bfre", em.cnt["a0bfre"])
            n2 = em.on("tensor", nc.tensor.matmul, g.p_bc[:, :],
                       g.onesrowf[:, :], l_mr, start=True, stop=True,
                       _inc=("a0bcf", 1))
            em.wait("vector", "a0bcf", n2)
            em.on("vector", nc.vector.tensor_copy, mrbc[:, :], g.p_bc[:, :],
                  _inc=("a0bfre", 1))
            for ct in range(CT):
                em.vop(nc.vector.tensor_mul, scr[:, :], pre[:, ts(ct, 512)],
                       rbc[:, :])
                em.vop(nc.vector.tensor_sub,
                       g.xT[:, ct * T + tc * 512: ct * T + (tc + 1) * 512],
                       scr[:, :], mrbc[:, :], _inc=("a0fin", 1))
    em.flush("fL_final")
    es.close()



def build_l1_attn(g, W, z2_hbm):
    nc, em = g.nc, g.em
    from contextlib import ExitStack
    es = ExitStack()
    sb = lambda n_, sh, dt: es.enter_context(nc.sbuf_tensor(n_, sh, dt))
    tag = "a0"
    SUBS = 4

    em.wait_now("sync", "a0fin")
    em.on("sync", nc.sync.dma_start, z2_hbm[:, :], g.xT[:, :], _inc=("f0z1", 16))

    WR, WK, WV, WO = 0, CT * 1024, 2 * CT * 1024, 3 * CT * 1024
    maa5 = sb("t1_maa5s", [128, 40], F32)
    tdc = sb("t1_tds", [128, 8], F32)
    mw1 = sb("t1_mw1s", [128, CT * 128], BF16)
    mw2 = [sb(f"t1_mw2s{i}", [32, 1024], BF16) for i in range(4)]
    dw1 = sb("t1_dw1s", [128, CT * 64], BF16)
    dw2 = sb("t1_dw2s", [64, 1024], BF16)
    ucol = sb("t1_us", [64, 16], F32)
    lnbc_ = sb("t1_lnbs", [128, 8], BF16)
    lnwc = sb("t1_lnws", [128, 8], F32)
    biasW = sb("t1_biasW", [128, 8], F32)
    xch = sb("t1_xch", [128, CT * 513], BF16)
    mixf = g.bigA[:, WV: WV + CT * 512]
    vtok = g.bigA[:, WV + CT * 512: WV + CT * 512 + SUBS * 1024]
    xxx_rT = sb("t1_xxxrT", [128, CT * 512], BF16)
    kT = sb("t1_kT", [128, CT * 512], BF16)
    wv_s = [sb(f"t1_wv{i}", [128, 512], BF16) for i in range(2)]
    th_all = sb("t1_thall", [128, 512], BF16)
    th_f = [th_all[0:32, :]] + [sb(f"t1_th{i}", [32, 512], BF16)[:, :]
                                for i in range(1, 4)]
    tdh = sb("t1_tdh", [64, 512], BF16)
    wtile = sb("t1_wtile", [128, 128], F32)
    ybuf = sb("t1_ybuf", [128, 1024], BF16)
    ynT = sb("t1_ynT", [128, CT * 128], BF16)
    S32 = sb("t1_S", [64, 16 * 64], F32)
    Sbf = g.rbc64.bitcast(BF16)[0:64, 256:320]
    A_sb = sb("t1_A", [64, 128], F32)
    Ash = sb("t1_Ash", [64, 128], F32)
    rtil = g.scr2.bitcast(BF16)[0:64, 128:256]
    ktil = g.scr2.bitcast(BF16)[0:64, 256:384]
    r0 = g.scr2.bitcast(BF16)[0:64, 384:512]
    rA = g.scr[0:64, 128:256]
    k0 = g.scr.bitcast(BF16)[0:64, 768:896]
    khat = g.sqbf[0:64, 256:384]
    khtok = g.sqbf[:, 384:448]
    dcol = sb("t1_dcol", [128, 1], F32)
    rku = g.sqbf[0:64, 128:256]
    drow = g.rowa[:, 0:128]
    zeros128 = g.mrbc[:, 0:128]
    mstrict = sb("t1_mstrict", [128, 128], BF16)
    m16 = g.rbc.bitcast(F32)[:, 160:176] if False else g.rbc[:, 160:176]
    v16 = g.rbc[:, 176:192]
    sq16 = g.rbc[:, 192:208]

    em.on("sync", nc.sync.dma_start, maa5[:], W["maa5"][:, :], _inc=(tag + "bl", 16))
    em.on("sync", nc.sync.dma_start, tdc[:], W["td"][:, :], _inc=(tag + "bl", 16))
    em.on("sync", nc.sync.dma_start, ucol[:], W["u"][:, :], _inc=(tag + "bl", 16))
    em.on("sync", nc.sync.dma_start, lnwc[:], W["lnw"][:, :], _inc=(tag + "bl", 16))
    em.on("gpsimd", nc.gpsimd.dma_start, lnbc_[:], W["lnb"][:, :], _inc=(tag + "thd", 16))
    for f_ in range(4):
        em.on("gpsimd", nc.gpsimd.dma_start, mw2[f_][:],
              W["mw2"][:, f_ * 1024:(f_ + 1) * 1024], _inc=(tag + "thd", 16))
    em.on("gpsimd", nc.gpsimd.dma_start, dw2[:], W["dw2"][:, :], _inc=(tag + "thd", 16))
    for ct in range(CT):
        em.on("gpsimd", nc.gpsimd.dma_start, mw1[:, ct * 128:(ct + 1) * 128],
              W["mw1"][ct], _inc=(tag + "thd", 16))
        em.on("gpsimd", nc.gpsimd.dma_start, dw1[:, ct * 64:(ct + 1) * 64],
              W["dw1"][ct], _inc=(tag + "thd", 16))
    for ft in range(CT):
        for ct in range(CT):
            em.on("gpsimd", nc.gpsimd.dma_start,
                  g.bigA[:, WR + ft * 1024 + ct * 128: WR + ft * 1024 + (ct + 1) * 128],
                  W["wr"][ft, ct], _inc=(tag + "w1a", 16))
            em.on("gpsimd", nc.gpsimd.dma_start,
                  g.bigA[:, WK + ft * 1024 + ct * 128: WK + ft * 1024 + (ct + 1) * 128],
                  W["wk"][ft, ct], _inc=(tag + "w1a", 16))
            em.on("gpsimd", nc.gpsimd.dma_start,
                  g.bigA[:, WO + ft * 1024 + ct * 128: WO + ft * 1024 + (ct + 1) * 128],
                  W["wo"][ft, ct], _inc=(tag + "w1a", 16))
    wb_tgt = em.cnt[tag + "w1a"]
    n = em.on("gpsimd", nc.gpsimd.memset, mstrict[:], 1.0, _inc=(tag + "bc2", 1))
    em.wait("gpsimd", tag + "bc2", n)
    em.on("gpsimd", nc.gpsimd.affine_select, mstrict[:], mstrict[:],
          [[1, 128]], compare_op=ALU.is_ge, fill=0.0,
          base=-1, channel_multiplier=-1, _inc=(tag + "bc2", 1))
    em.on("vector", nc.vector.memset, zeros128[:, :], 0.0)
    em.on("vector", nc.vector.memset, S32[:], 0.0, _inc=(tag + "qe", 1))
    em.flush("t1_setup")

    em.wait("tensor", tag + "w1a", wb_tgt)
    em.wait("tensor", tag + "bl", em.cnt[tag + "bl"])
    em.wait("tensor", tag + "thd", em.cnt[tag + "thd"])
    for ft in range(CT):
        for ct in range(CT):
            em.on("tensor", nc.tensor.matmul, g.p_row[0:128, 0:1].bitcast(F32)
                  if False else g.p_v[:, 0:1],
                  g.bigA[:, WO + ft * 1024 + ct * 128: WO + ft * 1024 + (ct + 1) * 128],
                  lnbc_[:, ct:ct + 1],
                  start=(ct == 0), stop=(ct == CT - 1),
                  _inc=(tag + "vm", 1) if ct == CT - 1 else None)
        em.wait_now("vector", tag + "vm")
        em.on("vector", nc.vector.tensor_copy, biasW[:, ft:ft + 1],
              g.p_v[:, 0:1], _inc=(tag + "ve", 1))
        em.wait("tensor", tag + "ve", em.cnt[tag + "ve"])
    em.wait("vector", tag + "vm", em.cnt[tag + "vm"])
    for ft in range(CT):
        for ct in range(CT):
            em.vop(nc.vector.tensor_scalar_mul,
                   g.bigA[:, WO + ft * 1024 + ct * 128: WO + ft * 1024 + (ct + 1) * 128],
                   g.bigA[:, WO + ft * 1024 + ct * 128: WO + ft * 1024 + (ct + 1) * 128],
                   lnwc[:, ct:ct + 1], _inc=(tag + "sq", 1))
    em.flush("t1_biasw")

    STR = {"w": 1, "k": 2, "v": 3, "r": 4}

    for tc in range(NCH):
        em.wait("sync", tag + "oe", em.cnt.get(tag + "oe", 0))
        if tc == 0:
            em.wait_now("sync", "f0z1")
            em.wait_now("vector", "f0z1")
            em.wait_now("vector", tag + "bl")
            em.wait_now("scalar", tag + "bl")
            em.wait_now("vector", tag + "thd")
            em.wait_now("scalar", tag + "thd")
            em.wait_now("tensor", tag + "thd")
            em.wait("tensor", tag + "sq", em.cnt[tag + "sq"])
            for ct in range(CT):
                em.on("sync", nc.sync.dma_start, xch[:, ct * 513 + 1: (ct + 1) * 513],
                      z2_hbm[:, ct * T: ct * T + 512], _inc=(tag + "nr", 16))
            em.wait_now("vector", tag + "nr")
            for ct in range(CT):
                em.on("vector", nc.vector.memset, xch[:, ct * 513: ct * 513 + 1],
                      0.0, _inc=(tag + "nre", 1))
            em.wait("vector", tag + "nre", em.cnt[tag + "nre"])
        else:
            for ct in range(CT):
                em.on("sync", nc.sync.dma_start, xch[:, ct * 513: (ct + 1) * 513],
                      z2_hbm[:, ct * T + tc * 512 - 1: ct * T + (tc + 1) * 512],
                      _inc=(tag + "nr", 16))
            em.wait_now("vector", tag + "nr")
        em.wait_now("tensor", tag + "nr")
        xv = lambda ct: xch[:, ct * 513 + 1: (ct + 1) * 513]
        xs = lambda ct: xch[:, ct * 513: (ct + 1) * 513 - 1]

        for ct in range(CT):
            em.vop(nc.vector.tensor_sub, g.scr, xs(ct), xv(ct))
            em.vop(nc.vector.scalar_tensor_tensor, xxx_rT[:, ts(ct, 512)],
                   g.scr, maa5[:, 0 * 8 + ct: 0 * 8 + ct + 1], xv(ct),
                   op0=ALU.mult, op1=ALU.add, _inc=(tag + "s1", 1))
        em.wait("tensor", tag + "s1", em.cnt[tag + "s1"])
        em.wait("tensor", tag + "s2", em.cnt.get(tag + "s2", 0))
        em.wait("tensor", tag + "pv", em.cnt.get(tag + "pv", 0))
        em.wait("tensor", tag + "s", em.cnt.get(tag + "s", 0))
        for ct in range(CT):
            em.on("tensor", nc.tensor.matmul, g.p_a[0][:, :],
                  mw1[:, ct * 128:(ct + 1) * 128], xxx_rT[:, ts(ct, 512)],
                  start=(ct == 0), stop=(ct == CT - 1),
                  _inc=(tag + "s3", 1) if ct == CT - 1 else None)
        em.wait_now("scalar", tag + "s3")
        em.wait("scalar", tag + "s4", em.cnt.get(tag + "s4", 0))
        em.wait("scalar", tag + "thd", em.cnt.get(tag + "thd", 0))
        em.sop(nc.scalar.activation, th_all[:, :], g.p_a[0][:, :], AF.Tanh,
               bias=0.0, scale=1.0, _inc=(tag + "s2", 1))
        em.wait("gpsimd", tag + "s2", em.cnt[tag + "s2"])
        for f_ in range(1, 4):
            em.on("gpsimd", nc.gpsimd.dma_start, th_f[f_],
                  th_all[f_ * 32:(f_ + 1) * 32, :], _inc=(tag + "thd", 16))
        em.wait_now("tensor", tag + "thd")
        em.wait_now("tensor", tag + "s2")

        for f, fname in enumerate(("w", "k", "v", "r")):
            em.wait("vector", tag + "ex", em.cnt.get(tag + "ex", 0))
            em.wait("vector", tag + "ns", em.cnt.get(tag + "ns", 0))
            em.wait("tensor", tag + "pv", em.cnt.get(tag + "pv", 0))
            mcol = STR[fname]
            for ct in range(CT):
                em.wait("tensor", tag + "nsp", em.cnt.get(tag + "nsp", 0))
                em.on("tensor", nc.tensor.matmul, g.p_a[1][:, :],
                      mw2[f][0:32, ct * 128:(ct + 1) * 128], th_f[f],
                      start=True, stop=True, _inc=(tag + "s4", 1))
                em.wait_now("vector", tag + "s4")
                em.vop(nc.vector.tensor_sub, g.scr, xs(ct), xv(ct))
                em.vop(nc.vector.tensor_scalar, g.scr2, g.p_a[1][:, :],
                       maa5[:, mcol * 8 + ct: mcol * 8 + ct + 1], None,
                       op0=ALU.add)
                em.vop(nc.vector.tensor_mul, g.scr, g.scr, g.scr2)
                em.vop(nc.vector.tensor_add, mixf[:, ts(ct, 512)], g.scr,
                       xv(ct), _inc=(tag + "nsp", 1))
            em.wait("tensor", tag + "nsp", em.cnt[tag + "nsp"])
            if fname == "w":
                em.wait("tensor", tag + "bk", em.cnt.get(tag + "bk", 0))
                for ct in range(CT):
                    em.on("tensor", nc.tensor.matmul, g.p_a[0][0:64, :],
                          dw1[:, ct * 64:(ct + 1) * 64], mixf[:, ts(ct, 512)],
                          start=(ct == 0), stop=(ct == CT - 1),
                          _inc=(tag + "ns", 1) if ct == CT - 1 else None)
                em.wait_now("scalar", tag + "ns")
                em.wait("scalar", tag + "mk", em.cnt.get(tag + "mk", 0))
                em.sop(nc.scalar.activation, tdh[:, :], g.p_a[0][0:64, :],
                       AF.Tanh, bias=0.0, scale=1.0, _inc=(tag + "bk", 1))
                em.wait_now("tensor", tag + "bk")
            elif fname in ("k", "r"):
                dst = kT if fname == "k" else xxx_rT
                WX = WK if fname == "k" else WR
                for ft in range(CT):
                    em.wait("tensor", tag + "pv", em.cnt.get(tag + "pv", 0) - 1)
                    for ct in range(CT):
                        em.on("tensor", nc.tensor.matmul,
                              g.p_a[ft % 2][:, :],
                              g.bigA[:, WX + ft * 1024 + ct * 128:
                                     WX + ft * 1024 + (ct + 1) * 128],
                              mixf[:, ts(ct, 512)],
                              start=(ct == 0), stop=(ct == CT - 1),
                              _inc=(tag + "ex", 1) if ct == CT - 1 else None)
                    em.wait_now("vector", tag + "ex")
                    em.on("vector", nc.vector.tensor_copy, dst[:, ts(ft, 512)],
                          g.p_a[ft % 2][:, :], _inc=(tag + "pv", 1))
            elif fname == "v":
                em.wait("tensor", tag + "ye", em.cnt.get(tag + "ye", 0))
                em.wait("tensor", tag + "p1", em.cnt.get(tag + "p1", 0))
                em.wait("tensor", tag + "bfre", em.cnt.get(tag + "bfre", 0))
                banks = [g.p_a[0], g.p_a[1], g.p_y, g.p_bc]
                wv_t = {0: 0, 1: 0}
                vstep = []

                def load_wv(fh, ct, slot, step):
                    if step >= 2:
                        em.wait("gpsimd", tag + "ln1", vstep[step - 2])
                    wv_t[slot] = em.on(
                        "gpsimd", nc.gpsimd.dma_start, wv_s[slot][:, :],
                        W["wv"][ct * 128:(ct + 1) * 128, fh * 512:(fh + 1) * 512],
                        _inc=(tag + f"wv{slot}", 16))

                load_wv(0, 0, 0, 0)
                step = 0
                for fh in range(2):
                    for ct in range(CT):
                        slot = step % 2
                        em.wait("tensor", tag + f"wv{slot}", wv_t[slot])
                        for tt in range(SUBS):
                            em.on("tensor", nc.tensor.matmul, banks[tt][:, :],
                                  mixf[:, ct * 512 + tt * 128:
                                       ct * 512 + (tt + 1) * 128],
                                  wv_s[slot][:, :],
                                  start=(ct == 0), stop=(ct == CT - 1),
                                  _inc=(tag + "ln1", 1) if tt == SUBS - 1 else None)
                        vstep.append(em.cnt[tag + "ln1"])
                        if step + 1 < 16:
                            nfh, nct = divmod(step + 1, CT)
                            load_wv(nfh, nct, 1 - slot, step + 1)
                        step += 1
                    em.wait_now("vector", tag + "ln1")
                    for tt in range(SUBS):
                        em.on("vector", nc.vector.tensor_copy,
                              vtok[:, tt * 1024 + fh * 512:
                                   tt * 1024 + (fh + 1) * 512],
                              banks[tt][:, :], _inc=(tag + "pv", 1))
                    em.wait("tensor", tag + "pv", em.cnt[tag + "pv"])
        em.wait("tensor", tag + "pv", em.cnt[tag + "pv"])

        for sub in range(SUBS):
            for h in range(16):
                ct = h // 2
                hb = 64 * (h % 2)
                em.wait("tensor", tag + "oe", em.cnt.get(tag + "oe", 0))
                if h % 2 == 0:
                    em.wait("tensor", tag + "s", em.cnt.get(tag + "s", 0))
                    em.on("tensor", nc.tensor.matmul, g.p_a[1][:, 0:128],
                          dw2[:, ct * 128:(ct + 1) * 128],
                          tdh[:, sub * 128:(sub + 1) * 128],
                          start=True, stop=True, _inc=(tag + "mk", 1))
                    em.wait_now("scalar", tag + "mk")
                    em.sop(nc.scalar.activation, wtile[:, :], g.p_a[1][:, 0:128],
                           AF.Exp, bias=tdc[:, ct:ct + 1], scale=1.0)
                    em.sop(nc.scalar.activation, wtile[:, :], wtile[:, :],
                           AF.Exp, bias=0.0, scale=-1.0, _inc=(tag + "s", 1))
                    em.wait_now("vector", tag + "s")
                rsl = xxx_rT[hb:hb + 64, ct * 512 + sub * 128: ct * 512 + (sub + 1) * 128]
                ksl = kT[hb:hb + 64, ct * 512 + sub * 128: ct * 512 + (sub + 1) * 128]
                vsl = vtok[:, sub * 1024 + h * 64: sub * 1024 + (h + 1) * 64]
                wsl = wtile[hb:hb + 64, :]
                em.vop(nc.vector.tensor_copy, r0[:, :], rsl)
                em.vop(nc.vector.tensor_copy, k0[:, :], ksl)
                em.vop(nc.vector.tensor_tensor_scan, A_sb[:, :], wsl,
                       zeros128[hb:hb + 64, :], 1.0, op0=ALU.mult, op1=ALU.add)
                em.vop(nc.vector.tensor_copy, Ash[:, 1:128], A_sb[:, 0:127])
                em.vop(nc.vector.memset, Ash[:, 0:1], 1.0)
                em.vop(nc.vector.tensor_mul, rtil[:, :], r0[:, :], Ash[:, :])
                em.vop(nc.vector.reciprocal, rA[:, :], A_sb[:, :])
                em.vop(nc.vector.tensor_mul, ktil[:, :], k0[:, :], rA[:, :])
                em.vop(nc.vector.tensor_scalar_mul, khat[:, :], ktil[:, :],
                       A_sb[:, 127:128])
                em.vop(nc.vector.tensor_scalar_mul, g.scr[0:64, 0:128], r0[:, :],
                       ucol[:, h:h + 1])
                em.vop(nc.vector.tensor_mul, rku[:, :],
                       g.scr[0:64, 0:128], k0[:, :], _inc=(tag + "rc", 1))
                em.wait("tensor", tag + "rc", em.cnt[tag + "rc"])
                em.on("tensor", nc.tensor.transpose,
                      g.p_v[:, :].bitcast(BF16)[:, 0:64], khat[:, :],
                      g.ident[0:64, 0:64], _inc=(tag + "bc", 1))
                em.wait_now("vector", tag + "bc")
                em.on("vector", nc.vector.tensor_copy, khtok[:, :],
                      g.p_v[:, :].bitcast(BF16)[:, 0:64], _inc=(tag + "bce", 1))
                em.on("tensor", nc.tensor.matmul, g.p_row[:, 0:128],
                      g.onescol[0:64, :], rku[:, :], start=True, stop=True,
                      _inc=(tag + "qm", 1))
                em.wait_now("vector", tag + "qm")
                em.vop(nc.vector.tensor_copy, drow[:, :], g.p_row[:, 0:128],
                       _inc=(tag + "sq2", 1))
                em.wait("tensor", tag + "sq2", em.cnt[tag + "sq2"])
                em.on("tensor", nc.tensor.matmul, g.p_tp[:, :].bitcast(F32)[:, 0:1],
                      drow[:, :], g.onesrowf[:, 0:1], start=True, stop=True,
                      _inc=(tag + "om", 1))
                em.wait_now("vector", tag + "om")
                em.vop(nc.vector.tensor_copy, dcol[:, :],
                       g.p_tp[:, :].bitcast(F32)[:, 0:1], _inc=(tag + "pre", 1))
                em.wait("tensor", tag + "fx", em.cnt.get(tag + "fx", 0))
                em.on("tensor", nc.tensor.matmul, g.p_a[0][:, 0:128],
                      ktil[:, :], rtil[:, :], start=True, stop=True,
                      _inc=(tag + "bcc", 1))
                em.wait_now("vector", tag + "bcc")
                em.vop(nc.vector.tensor_mul, g.sqbf[:, 0:128],
                       g.p_a[0][:, 0:128], mstrict[:, :], _inc=(tag + "fx", 1))
                em.vop(nc.vector.tensor_copy, Sbf[:, :],
                       S32[:, h * 64:(h + 1) * 64], _inc=(tag + "fxe", 1))
                em.wait("tensor", tag + "fx", em.cnt[tag + "fx"])
                em.wait("tensor", tag + "fxe", em.cnt[tag + "fxe"])
                em.wait("tensor", tag + "ye", em.cnt.get(tag + "ye", 0))
                em.on("tensor", nc.tensor.matmul, g.p_y[:, 0:64],
                      g.sqbf[:, 0:128], vsl, start=True, stop=False)
                em.on("tensor", nc.tensor.matmul, g.p_y[:, 0:64],
                      rtil[:, :], Sbf[:, :], start=False, stop=True,
                      _inc=(tag + "rote", 1))
                em.wait("tensor", tag + "bce", em.cnt[tag + "bce"])
                em.wait("tensor", tag + "p1", em.cnt.get(tag + "p1", 0))
                em.on("tensor", nc.tensor.matmul, g.p_bc[0:64, 0:64],
                      khtok[:, :], vsl, start=True, stop=True,
                      _inc=(tag + "p1m", 1))
                em.wait_now("vector", tag + "p1m")
                em.vop(nc.vector.tensor_scalar_mul, g.scr2[0:64, 0:64],
                       S32[:, h * 64:(h + 1) * 64], A_sb[:, 127:128])
                em.vop(nc.vector.tensor_add, S32[:, h * 64:(h + 1) * 64],
                       g.scr2[0:64, 0:64], g.p_bc[0:64, 0:64],
                       _inc=(tag + "p1", 1))
                em.wait("vector", tag + "pre", em.cnt[tag + "pre"])
                em.wait("vector", tag + "rote", em.cnt[tag + "rote"])
                em.vop(nc.vector.tensor_scalar_mul, g.scr[:, 0:64], vsl,
                       dcol[:, :])
                em.vop(nc.vector.tensor_add, ybuf[:, h * 64:(h + 1) * 64],
                       g.p_y[:, 0:64], g.scr[:, 0:64], _inc=(tag + "ye", 1))
            em.wait("vector", tag + "ye", em.cnt[tag + "ye"])
            em.vop(nc.vector.reduce_sum, m16[:, :],
                   ybuf[:, :].rearrange("p (h j) -> p h j", j=64), axis=AX.X)
            em.vop(nc.vector.tensor_mul, g.scr.bitcast(BF16)[:, 0:1024],
                   ybuf[:, :], ybuf[:, :])
            em.vop(nc.vector.reduce_sum, sq16[:, :],
                   g.scr.bitcast(BF16)[:, 0:1024]
                   .rearrange("p (h j) -> p h j", j=64), axis=AX.X)
            em.vop(nc.vector.tensor_scalar_mul, m16[:, :], m16[:, :], 1.0 / 64)
            em.vop(nc.vector.tensor_scalar_mul, sq16[:, :], sq16[:, :], 1.0 / 64)
            em.vop(nc.vector.tensor_mul, v16[:, :], m16[:, :], m16[:, :])
            em.vop(nc.vector.tensor_sub, v16[:, :], sq16[:, :], v16[:, :],
                   _inc=(tag + "mx", 1))
            em.wait_now("scalar", tag + "mx")
            em.sop(nc.scalar.activation, v16[:, :], v16[:, :], AF.Ln,
                   bias=g.cvcol[:, 2:3], scale=1.0)
            em.sop(nc.scalar.activation, v16[:, :], v16[:, :], AF.Exp,
                   bias=0.0, scale=-0.5, _inc=(tag + "mxn", 1))
            em.wait_now("vector", tag + "mxn")
            for h in range(16):
                em.vop(nc.vector.tensor_scalar, ybuf[:, h * 64:(h + 1) * 64],
                       ybuf[:, h * 64:(h + 1) * 64],
                       m16[:, h:h + 1], v16[:, h:h + 1],
                       op0=ALU.subtract, op1=ALU.mult, _inc=(tag + "mxt", 1))
            em.wait("tensor", tag + "mxt", em.cnt[tag + "mxt"])
            for ct in range(CT):
                em.wait("tensor", tag + "ae", em.cnt.get(tag + "ae", 0) - 1)
                em.on("tensor", nc.tensor.transpose, g.tp_slots[ct % 2],
                      ybuf[:, ct * 128:(ct + 1) * 128], g.ident[:],
                      _inc=(tag + "mxe", 1))
                em.wait("vector", tag + "mxe", em.cnt[tag + "mxe"])
                em.on("vector", nc.vector.tensor_copy, ynT[:, ts(ct)],
                      g.tp_slots[ct % 2], _inc=(tag + "ae", 1))
            em.wait("tensor", tag + "ae", em.cnt[tag + "ae"])
            for ft in range(CT):
                em.wait("tensor", tag + "oe", em.cnt.get(tag + "oe", 0) - 1)
                for ct in range(CT):
                    em.on("tensor", nc.tensor.matmul, g.p_a[ft % 2][:, 0:128],
                          g.bigA[:, WO + ft * 1024 + ct * 128:
                                 WO + ft * 1024 + (ct + 1) * 128],
                          ynT[:, ts(ct)], start=(ct == 0), stop=(ct == CT - 1),
                          _inc=(tag + "rc2", 1) if ct == CT - 1 else None)
                em.wait_now("vector", tag + "rc2")
                em.vop(nc.vector.tensor_scalar, g.scr[:, 0:128],
                       g.p_a[ft % 2][:, 0:128], biasW[:, ft:ft + 1], None,
                       op0=ALU.add)
                em.vop(nc.vector.tensor_add,
                       g.xT[:, ft * T + tc * 512 + sub * 128:
                            ft * T + tc * 512 + (sub + 1) * 128],
                       g.scr[:, 0:128],
                       xch[:, ft * 513 + 1 + sub * 128: ft * 513 + 1 + (sub + 1) * 128],
                       _inc=(tag + "oe", 1))
        em.wait("tensor", tag + "oe", em.cnt[tag + "oe"])
        em.wait("vector", tag + "oe", em.cnt[tag + "oe"])
        for ct in range(CT):
            em.on("tensor", nc.tensor.matmul, g.p_row[:, :], g.onescol[:, :],
                  g.xT[:, ct * T + tc * 512: ct * T + (tc + 1) * 512],
                  start=(ct == 0), stop=(ct == CT - 1), _inc=(tag + "st1", 1))
            em.wait("vector", tag + "st2", em.cnt.get(tag + "st2", 0))
            n = em.on("vector", nc.vector.tensor_mul, g.sqbf[:, :],
                      g.xT[:, ct * T + tc * 512: ct * T + (tc + 1) * 512],
                      g.xT[:, ct * T + tc * 512: ct * T + (tc + 1) * 512],
                      _inc=(tag + "sqt", 1))
            em.wait("tensor", tag + "sqt", n)
            em.on("tensor", nc.tensor.matmul, g.p_row2[:, :], g.onescol[:, :],
                  g.sqbf[:, :], start=(ct == 0), stop=(ct == CT - 1),
                  _inc=(tag + "st2", 1))
        em.wait_now("vector", tag + "st1")
        em.wait_now("vector", tag + "st2")
        em.vop(nc.vector.tensor_scalar_mul, g.lm, g.p_row[:, :], 1.0 / D)
        em.vop(nc.vector.tensor_scalar_mul, g.lr, g.p_row2[:, :], 1.0 / D,
               _inc=(tag + "lv", 1))
        em.wait("tensor", tag + "lv", em.cnt[tag + "lv"])
        em.vop(nc.vector.tensor_mul, g.lmr, g.lm, g.lm)
        em.vop(nc.vector.tensor_sub, g.lr, g.lr, g.lmr, _inc=(tag + "lv2", 1))
        em.wait_now("scalar", tag + "lv2")
        em.sop(nc.scalar.activation, g.lr, g.lr, AF.Ln,
               bias=g.cvals[:, 1:2], scale=1.0)
        em.sop(nc.scalar.activation, g.lr, g.lr, AF.Exp,
               bias=0.0, scale=-0.5, _inc=(tag + "lnr", 1))
        em.wait_now("vector", tag + "lnr")
        em.vop(nc.vector.tensor_mul, g.lmr, g.lm, g.lr, _inc=(tag + "hl", 1))
        em.wait("tensor", tag + "hl", em.cnt[tag + "hl"])
        em.wait("tensor", tag + "bfre", em.cnt.get(tag + "bfre", 0))
        n1 = em.on("tensor", nc.tensor.matmul, g.p_bc[:, :], g.onesrowf[:, :],
                   g.lr, start=True, stop=True, _inc=(tag + "bcf", 1))
        em.wait("vector", tag + "bcf", n1)
        em.on("vector", nc.vector.tensor_copy, g.rbc, g.p_bc[:, :],
              _inc=(tag + "bfre", 1))
        em.wait("tensor", tag + "bfre", em.cnt[tag + "bfre"])
        n2 = em.on("tensor", nc.tensor.matmul, g.p_bc[:, :], g.onesrowf[:, :],
                   g.lmr, start=True, stop=True, _inc=(tag + "bcf", 1))
        em.wait("vector", tag + "bcf", n2)
        em.on("vector", nc.vector.tensor_copy, g.mrbc, g.p_bc[:, :],
              _inc=(tag + "bfre", 1))
        for ct in range(CT):
            em.vop(nc.vector.tensor_mul, g.scr,
                   g.xT[:, ct * T + tc * 512: ct * T + (tc + 1) * 512], g.rbc)
            em.vop(nc.vector.tensor_sub,
                   g.xT[:, ct * T + tc * 512: ct * T + (tc + 1) * 512],
                   g.scr, g.mrbc, _inc=("a0fin", 1))
        em.vop(nc.vector.memset, zeros128[:, :], 0.0, _inc=(tag + "z0", 1))
    em.flush("t1_main")
    es.close()




def build_l1_ffn(g, W, C4, eoT_hbm, moe_hbm, out_hbm, y_h, scr_dram):
    nc, em = g.nc, g.em
    from contextlib import ExitStack
    es = ExitStack()
    sb = lambda n_, sh, dt: es.enter_context(nc.sbuf_tensor(n_, sh, dt))
    tag = "a0"
    CAP = 512

    H0 = 0
    WD0 = 16 * 512
    XET0 = WD0 + 16 * 1024
    hbuf = lambda ft: g.bigA[:, H0 + ft * 512: H0 + (ft + 1) * 512]
    wdt = lambda ft: g.bigA[:, WD0 + ft * 1024: WD0 + (ft + 1) * 1024]
    xeT = g.bigA[:, XET0: XET0 + CT * 512]
    bigf = g.bigA[:, :].bitcast(F32)
    bigu = g.bigA[:, :].bitcast(U16)
    Ctab = bigu[:, 0:8192]
    sel_em = bigf[0:16, 4096:8192]
    C_em = bigf[0:16, 8192:12288]
    inv_em = bigf[0:16, 12288:16384]
    smajS = bigf[:, 0:4096]
    dtile = lambda e_: g.bigA[:, e_ * 520:(e_ + 1) * 520]

    gate_s = sb("m1_gate_s", [128, CT * 16], BF16)
    lsb = sb("m1_lsb", [16, 512], BF16)
    ltok = sb("m1_ltok", [128, 512], BF16)
    stok = sb("m1_stok", [128, 512], F32)
    sblk = sb("m1_sblk", [128, 512], F32)
    E8 = sb("m1_E8", [128, 16], F32)
    E8T = sb("m1_E8T", [16, 128], F32)
    LB = sb("m1_LB", [128, 128], F32)
    iotaB = sb("m1_iotaB", [128, 512], F32)
    Lrep = sb("m1_Lrep", [16, 256], F32)
    offc = sb("m1_offc", [128, 64], F32)
    jp1 = sb("m1_jp1", [128, 64], F32)
    maskc = sb("m1_maskc", [128, 16], F32)
    lo_c = sb("m1_lo", [128, 1], F32)
    hi_c = sb("m1_hi", [128, 1], F32)
    mid_c = sb("m1_mid", [128, 1], F32)
    cnt_c = sb("m1_cnt", [128, 1], F32)
    cond_c = sb("m1_cond", [128, 1], F32)
    need_c = sb("m1_need", [128, 1], F32)
    cnte = sb("m1_cnte", [16, 1], F32)
    t_d = sb("m1_td_", [128, 1], F32)
    t_n = sb("m1_tn_", [128, 1], F32)
    gemask = sb("m1_ge", [128, 512], BF16)
    gehi = sb("m1_gehi", [128, 512], BF16)
    band = sb("m1_band", [128, 512], BF16)
    scst = [sb(f"m1_scst{i}", [16, 128], F32) for i in range(2)]
    lo2 = sb("m1_lo2", [128, 64], F32)
    hi2 = sb("m1_hi2", [128, 64], F32)
    midf = sb("m1_midf", [128, 64], F32)
    mids = sb("m1_mids", [128, 64], F32)
    midu = sb("m1_midu", [128, 64], U16)
    vcp = sb("m1_vcp", [128, 64], F32)
    val = sb("m1_val", [128, 64], F32)
    cond2 = sb("m1_cond2", [128, 64], F32)
    idw = sb("m1_idw", [128, 512], U16)
    invw = sb("m1_invw", [128, 4096], U16)
    wcolsAll = sb("m1_wca", [128, 64], F32)
    eoTb = sb("m1_eoTb", [128, 8 * 520], BF16)
    gbuf = eoTb[:, :].bitcast(U16)[:, 0:1024]
    scg = eoTb[:, :].bitcast(F32)[:, 512:1024]
    selm = eoTb[:, :].bitcast(F32)[:, 1024:1536]
    rank = eoTb[:, :].bitcast(F32)[:, 1536:2048]
    wgu_s = [sb(f"m1_wgu{i}", [128, 4096], BF16) for i in range(2)]
    eo = sb("m1_eo", [128, 1024], F32)
    ysb = eo[:, :]
    gathw = g.bigA[:, 9344: 9344 + 16 * 512]
    accf = bigf[:, 8768:9280]
    mts = [bigf[:, 9280 + 128 * i: 9280 + 128 * (i + 1)] for i in range(2)]
    osb = [invw[:, :].bitcast(F32)[:, 1024 * i: 1024 * (i + 1)]
           for i in range(2)]
    msb = [eoTb[:, :].bitcast(F32)[:, 1024 * i: 1024 * (i + 1)]
           for i in range(2)]
    xtk = g.bigA[:, 0:1024]

    em.on("sync", nc.sync.dma_start, E8[:], C4["E8"][:, :], _inc=(tag + "bl", 16))
    em.on("sync", nc.sync.dma_start, E8T[:], C4["E8T"][:, :], _inc=(tag + "bl", 16))
    em.on("sync", nc.sync.dma_start, LB[:], C4["LB"][:, :], _inc=(tag + "bl", 16))
    em.on("sync", nc.sync.dma_start, iotaB[:], C4["iotaB"][:, :], _inc=(tag + "bl", 16))
    em.on("sync", nc.sync.dma_start, Lrep[:], C4["Lrep"][:, :], _inc=(tag + "bl", 16))
    em.on("sync", nc.sync.dma_start, offc[:], C4["offc"][:, :], _inc=(tag + "bl", 16))
    em.on("sync", nc.sync.dma_start, jp1[:], C4["jp1"][:, :], _inc=(tag + "bl", 16))
    em.on("sync", nc.sync.dma_start, maskc[:], C4["maskc"][:, :], _inc=(tag + "bl", 16))
    em.on("gpsimd", nc.gpsimd.dma_start, gate_s[:], W["gate"][:, :],
          _inc=(tag + "thd", 16))
    em.flush("m1_setup")

    em.wait_now("tensor", tag + "thd")
    for tc in range(NCH):
        em.wait("tensor", tag + "qe", em.cnt.get(tag + "qe", 0))
        for ct in range(CT):
            em.on("tensor", nc.tensor.matmul, g.p_a[0][0:16, :],
                  gate_s[:, ct * 16:(ct + 1) * 16],
                  g.xT[:, ct * T + tc * 512: ct * T + (tc + 1) * 512],
                  start=(ct == 0), stop=(ct == CT - 1),
                  _inc=("a0om", 1) if ct == CT - 1 else None)
        em.wait_now("vector", "a0om")
        em.on("vector", nc.vector.tensor_copy, lsb[:, :], g.p_a[0][0:16, :],
              _inc=(tag + "qe", 1))
        em.wait("tensor", tag + "qe", em.cnt[tag + "qe"])
        for q in range(4):
            tt = tc * 4 + q
            em.wait("tensor", tag + "mk", em.cnt.get(tag + "mk", 0) - 1)
            em.on("tensor", nc.tensor.transpose, g.tp_slots[q % 2][:, 0:16],
                  lsb[:, q * 128:(q + 1) * 128],
                  g.ident[0:16, 0:16], _inc=(tag + "s", 1))
            em.wait("vector", tag + "s", em.cnt[tag + "s"])
            em.on("vector", nc.vector.tensor_copy,
                  ltok[:, tt * 16:(tt + 1) * 16],
                  g.tp_slots[q % 2][:, 0:16], _inc=(tag + "mk", 1))
    em.wait("vector", tag + "mk", em.cnt[tag + "mk"])
    for tt in range(NT):
        sl = ltok[:, tt * 16:(tt + 1) * 16]
        em.vop(nc.vector.reduce_max, g.scr[:, 0:1], sl, axis=AX.X)
        em.vop(nc.vector.tensor_scalar_mul, g.scr[:, 1:2], g.scr[:, 0:1], -1.0,
               _inc=(tag + "s1", 1))
        em.wait_now("scalar", tag + "s1")
        em.sop(nc.scalar.activation, g.sqbf[:, 0:16], sl, AF.Exp,
               bias=g.scr[:, 1:2], scale=1.0, _inc=(tag + "s2", 1))
        em.wait_now("vector", tag + "s2")
        em.vop(nc.vector.reduce_sum, g.scr[:, 2:3], g.sqbf[:, 0:16], axis=AX.X)
        em.vop(nc.vector.reciprocal, g.scr[:, 3:4], g.scr[:, 2:3])
        em.vop(nc.vector.tensor_scalar_mul, stok[:, tt * 16:(tt + 1) * 16],
               g.sqbf[:, 0:16], g.scr[:, 3:4], _inc=(tag + "s3", 1))
    em.wait("tensor", tag + "s3", em.cnt[tag + "s3"])
    base_s = em.cnt.get(tag + "s", 0)
    base_mk = em.cnt.get(tag + "mk", 0)
    base_bl = em.cnt[tag + "bl"]
    for tt in range(NT):
        b_, s_ = tt // 4, tt % 4
        slot = tt % 2
        pb = g.p_y if slot == 0 else g.p_bc
        if tt >= 2:
            em.wait("tensor", tag + "mk", base_mk + tt - 1)
        em.on("tensor", nc.tensor.transpose, pb[0:16, 0:128],
              stok[:, tt * 16:(tt + 1) * 16], g.identf[:],
              _inc=(tag + "s", 1))
        if tt >= 2:
            em.wait("vector", tag + "bl", base_bl + (tt - 1) * 16)
        em.wait("vector", tag + "s", base_s + tt + 1)
        em.vop(nc.vector.tensor_copy, scst[slot][:, :], pb[0:16, 0:128],
               _inc=(tag + "mk", 1))
        em.wait("sync", tag + "mk", base_mk + tt + 1)
        em.wait("sync", tag + "bl", base_bl + tt * 16)
        em.on("sync", nc.sync.dma_start,
              sblk[b_ * 16: b_ * 16 + 16, s_ * 128:(s_ + 1) * 128],
              scst[slot][:, :], _inc=(tag + "bl", 16))
    em.flush("m1_gating")

    em.wait("tensor", tag + "bl", em.cnt[tag + "bl"])
    em.wait("vector", tag + "bl", em.cnt[tag + "bl"])
    em.on("vector", nc.vector.memset, lo_c[:], 0.0)
    em.on("vector", nc.vector.memset, hi_c[:], 1.0, _inc=("vch", 1))

    def count_ge(mask_ap):
        em.vop(nc.vector.reduce_sum, cnt_c[:], mask_ap, axis=AX.X,
               _inc=(tag + "p1", 1))
        em.wait("tensor", tag + "p1", em.cnt[tag + "p1"])
        em.wait("tensor", tag + "mxn", em.cnt.get(tag + "mxn", 0))
        n1 = em.on("tensor", nc.tensor.matmul, g.p_v[0:16, 0:1], E8[:], cnt_c[:],
                   start=True, stop=True, _inc=(tag + "p1m", 1))
        em.wait("vector", tag + "p1m", n1)
        em.vop(nc.vector.tensor_copy, cnte[:], g.p_v[0:16, 0:1],
               _inc=(tag + "mx", 1))
        em.wait("tensor", tag + "mx", em.cnt[tag + "mx"])
        n2 = em.on("tensor", nc.tensor.matmul, g.p_v[0:128, 1:2], E8T[:], cnte[:],
                   start=True, stop=True, _inc=(tag + "mxn", 1))
        em.wait("vector", tag + "mxn", n2)
        em.vop(nc.vector.tensor_copy, cnt_c[:], g.p_v[:, 1:2],
               _inc=(tag + "ns", 1))

    for it in range(45):
        em.vop(nc.vector.tensor_add, mid_c[:], lo_c[:], hi_c[:])
        em.vop(nc.vector.tensor_scalar_mul, mid_c[:], mid_c[:], 0.5)
        em.vop(nc.vector.tensor_scalar, gemask[:, :], sblk[:, :], mid_c[:],
               None, op0=ALU.is_ge)
        count_ge(gemask[:, :])
        em.vop(nc.vector.tensor_scalar, cond_c[:], cnt_c[:], float(CAP), None,
               op0=ALU.is_ge, _inc=(tag + "nsp", 1))
        em.wait("vector", tag + "nsp", em.cnt[tag + "nsp"])
        em.vop(nc.vector.tensor_scalar, t_n[:], cond_c[:], -1.0, 1.0,
               op0=ALU.mult, op1=ALU.add)
        em.vop(nc.vector.tensor_sub, t_d[:], mid_c[:], lo_c[:])
        em.vop(nc.vector.tensor_mul, t_d[:], t_d[:], cond_c[:])
        em.vop(nc.vector.tensor_add, lo_c[:], lo_c[:], t_d[:])
        em.vop(nc.vector.tensor_sub, t_d[:], mid_c[:], hi_c[:])
        em.vop(nc.vector.tensor_mul, t_d[:], t_d[:], t_n[:])
        em.vop(nc.vector.tensor_add, hi_c[:], hi_c[:], t_d[:])
    em.vop(nc.vector.tensor_scalar, gehi[:, :], sblk[:, :], hi_c[:], None,
           op0=ALU.is_ge)
    count_ge(gehi[:, :])
    em.vop(nc.vector.tensor_scalar, need_c[:], cnt_c[:], -1.0, float(CAP),
           op0=ALU.mult, op1=ALU.add)
    em.vop(nc.vector.tensor_scalar, gemask[:, :], sblk[:, :], lo_c[:], None,
           op0=ALU.is_ge)
    em.vop(nc.vector.tensor_sub, band[:, :], gemask[:, :], gehi[:, :])
    em.vop(nc.vector.memset, lo_c[:], -1.0)
    em.vop(nc.vector.memset, hi_c[:], 4095.0, _inc=("vch", 1))
    for it in range(13):
        em.vop(nc.vector.tensor_add, mid_c[:], lo_c[:], hi_c[:])
        em.vop(nc.vector.tensor_scalar_mul, mid_c[:], mid_c[:], 0.5)
        em.vop(nc.vector.tensor_scalar, selm[:, :], iotaB[:, :], mid_c[:],
               None, op0=ALU.is_le)
        em.vop(nc.vector.tensor_mul, selm[:, :], selm[:, :], band[:, :])
        count_ge(selm[:, :])
        em.vop(nc.vector.tensor_tensor, cond_c[:], cnt_c[:], need_c[:],
               op=ALU.is_ge, _inc=(tag + "nsp", 1))
        em.wait("vector", tag + "nsp", em.cnt[tag + "nsp"])
        em.vop(nc.vector.tensor_scalar, t_n[:], cond_c[:], -1.0, 1.0,
               op0=ALU.mult, op1=ALU.add)
        em.vop(nc.vector.tensor_sub, t_d[:], mid_c[:], hi_c[:])
        em.vop(nc.vector.tensor_mul, t_d[:], t_d[:], cond_c[:])
        em.vop(nc.vector.tensor_add, hi_c[:], hi_c[:], t_d[:])
        em.vop(nc.vector.tensor_sub, t_d[:], mid_c[:], lo_c[:])
        em.vop(nc.vector.tensor_mul, t_d[:], t_d[:], t_n[:])
        em.vop(nc.vector.tensor_add, lo_c[:], lo_c[:], t_d[:])
    em.vop(nc.vector.tensor_scalar, selm[:, :], iotaB[:, :], hi_c[:], None,
           op0=ALU.is_le)
    em.vop(nc.vector.tensor_mul, selm[:, :], selm[:, :], band[:, :])
    em.vop(nc.vector.tensor_add, selm[:, :], selm[:, :], gehi[:, :])
    em.vop(nc.vector.memset, g.scr2[:, :], 1.0)
    em.vop(nc.vector.tensor_tensor_scan, rank[:, :], g.scr2[:, :], selm[:, :],
           0.0, op0=ALU.mult, op1=ALU.add)
    em.vop(nc.vector.reduce_sum, cnt_c[:], selm[:, :], axis=AX.X,
           _inc=(tag + "bk", 1))
    em.wait("tensor", tag + "bk", em.cnt[tag + "bk"])
    n1 = em.on("tensor", nc.tensor.matmul, g.p_v[:, 2:3], LB[:], cnt_c[:],
               start=True, stop=True, _inc=(tag + "ye", 1))
    em.wait("vector", tag + "ye", n1)
    em.vop(nc.vector.tensor_copy, cond_c[:], g.p_v[:, 2:3])
    em.vop(nc.vector.tensor_scalar_add, rank[:, :], rank[:, :], cond_c[:])
    em.vop(nc.vector.tensor_mul, rank[:, :], rank[:, :], selm[:, :])
    em.vop(nc.vector.tensor_scalar, g.scr[:, :], selm[:, :], -513.0,
           512.0, op0=ALU.mult, op1=ALU.add)
    em.vop(nc.vector.tensor_add, rank[:, :], rank[:, :], g.scr[:, :],
           _inc=(tag + "rc", 1))
    em.flush("m1_topk")

    em.wait("sync", tag + "rc", em.cnt[tag + "rc"])
    em.on("sync", nc.sync.dma_start, scr_dram[:, :], selm[:, :],
          _inc=(tag + "bl", 16))
    em.wait("sync", tag + "bl", em.cnt[tag + "bl"])
    em.on("sync", nc.sync.dma_start,
          sel_em.rearrange("p (b f) -> p b f", b=8),
          scr_dram[:, :].rearrange("(b e) f -> e b f", e=16),
          _inc=(tag + "bl", 16))
    em.wait("vector", tag + "bl", em.cnt[tag + "bl"])
    em.vop(nc.vector.memset, inv_em, 1.0)
    em.vop(nc.vector.tensor_tensor_scan, C_em, inv_em, sel_em, 0.0,
           op0=ALU.mult, op1=ALU.add)
    em.vop(nc.vector.tensor_mul, inv_em, C_em, sel_em)
    em.vop(nc.vector.tensor_scalar, sel_em, sel_em, -513.0, 512.0,
           op0=ALU.mult, op1=ALU.add)
    em.vop(nc.vector.tensor_add, inv_em, inv_em, sel_em, _inc=(tag + "ve", 1))
    base_mk = em.cnt.get(tag + "mk", 0)
    nrep = 0
    for o in range(2):
        for ch in range(8):
            pb = g.p_y if nrep % 2 == 0 else g.p_bc
            if nrep == 0:
                em.wait("tensor", tag + "ve", em.cnt[tag + "ve"])
            em.wait("tensor", tag + "mk", em.cnt[tag + "mk"] - 1)
            em.on("tensor", nc.tensor.matmul, pb[:, 0:512],
                  Lrep[:, o * 128:(o + 1) * 128],
                  C_em[:, ch * 512:(ch + 1) * 512],
                  start=True, stop=True, _inc=(tag + "s", 1))
            em.wait("vector", tag + "s", em.cnt[tag + "s"])
            em.vop(nc.vector.tensor_copy,
                   Ctab[:, o * 4096 + ch * 512: o * 4096 + (ch + 1) * 512],
                   pb[:, 0:512], _inc=(tag + "mk", 1))
            nrep += 1
    em.wait("gpsimd", tag + "mk", em.cnt[tag + "mk"])

    em.vop(nc.vector.memset, lo2[:], -1.0)
    em.vop(nc.vector.memset, hi2[:], 4095.0, _inc=("vch", 1))
    gb3 = gbuf[:, :].rearrange("p (w m) -> p w m", m=16)
    for it in range(12):
        em.vop(nc.vector.tensor_add, midf[:], lo2[:], hi2[:])
        em.vop(nc.vector.tensor_scalar_mul, midf[:], midf[:], 0.5)
        em.vop(nc.vector.tensor_copy, midu[:], midf[:])
        em.vop(nc.vector.tensor_copy, midf[:], midu[:])
        em.vop(nc.vector.tensor_add, mids[:], midf[:], offc[:])
        em.vop(nc.vector.tensor_copy, midu[:], mids[:], _inc=(tag + "ve", 1))
        em.wait("gpsimd", tag + "ve", em.cnt[tag + "ve"])
        ng = em.on("gpsimd", nc.gpsimd.indirect_copy, gbuf[:, :], Ctab,
                   midu[:, :], i_know_ap_gather_is_preferred=True,
                   _inc=(tag + "ex", 1))
        em.wait("vector", tag + "ex", ng)
        for m in range(16):
            em.vop(nc.vector.tensor_copy, vcp[:], gb3[:, :, m])
            if m == 0:
                em.vop(nc.vector.tensor_scalar_mul, val[:], vcp[:],
                       maskc[:, 0:1])
            else:
                em.vop(nc.vector.scalar_tensor_tensor, val[:], vcp[:],
                       maskc[:, m:m + 1], val[:], op0=ALU.mult, op1=ALU.add)
        em.vop(nc.vector.tensor_tensor, cond2[:], val[:], jp1[:],
               op=ALU.is_ge)
        em.vop(nc.vector.tensor_scalar, mids[:], cond2[:], -1.0, 1.0,
               op0=ALU.mult, op1=ALU.add)
        em.vop(nc.vector.tensor_sub, vcp[:], midf[:], hi2[:])
        em.vop(nc.vector.tensor_mul, vcp[:], vcp[:], cond2[:])
        em.vop(nc.vector.tensor_add, hi2[:], hi2[:], vcp[:])
        em.vop(nc.vector.tensor_sub, vcp[:], midf[:], lo2[:])
        em.vop(nc.vector.tensor_mul, vcp[:], vcp[:], mids[:])
        em.vop(nc.vector.tensor_add, lo2[:], lo2[:], vcp[:],
               _inc=(tag + "ve", 1))
    em.vop(nc.vector.tensor_copy, midu[:], hi2[:], _inc=(tag + "ve", 1))
    em.wait("sync", tag + "ve", em.cnt[tag + "ve"])
    scr_u = scr_dram[:, :].bitcast(U16)
    em.on("sync", nc.sync.dma_start, scr_u[:, 0:64], midu[:, :],
          _inc=(tag + "bl", 16))
    em.wait("sync", tag + "bl", em.cnt[tag + "bl"])
    for gp in range(8):
        em.on("sync", nc.sync.dma_start,
              idw[16 * gp:16 * gp + 16, :].rearrange("p (gr w) -> p gr w", gr=8),
              scr_u[:, 0:64].rearrange("(gr r) w -> r gr w", gr=8),
              _inc=(tag + "bl", 16))

    base_s = em.cnt[tag + "s"]
    base_mk = em.cnt[tag + "mk"]
    wrI3 = invw[0:16, :].rearrange("r (e w) -> r e w", w=256)
    for w in range(256):
        pb = g.p_y if w % 2 == 0 else g.p_bc
        em.wait("tensor", tag + "mk", em.cnt[tag + "mk"] - 1)
        em.on("tensor", nc.tensor.transpose, pb[0:16, 0:16],
              inv_em[:, w * 16:(w + 1) * 16], g.identf[0:16, 0:16],
              _inc=(tag + "s", 1))
        em.wait("vector", tag + "s", base_s + w + 1)
        em.vop(nc.vector.tensor_copy, wrI3[:, :, w], pb[0:16, 0:16],
               _inc=(tag + "mk", 1))
    em.wait("sync", tag + "mk", em.cnt[tag + "mk"])
    em.wait("sync", tag + "bl", em.cnt[tag + "bl"])
    em.on("sync", nc.sync.dma_start,
          scr_u[0:64, :].rearrange("(r c) w -> r c w", c=4),
          invw[0:16, :].rearrange("p (c w) -> p c w", c=4),
          _inc=(tag + "bl", 16))
    em.wait("sync", tag + "bl", em.cnt[tag + "bl"])
    srcw = scr_u[0:64, :].rearrange("(r c) w -> r (c w)", c=4)
    for gp in range(1, 8):
        em.on("sync", nc.sync.dma_start, invw[16 * gp:16 * gp + 16, :], srcw,
              _inc=(tag + "bl", 16))
    em.flush("m1_index")

    em.wait("sync", tag + "ex", em.cnt[tag + "ex"])
    em.wait("sync", tag + "bl", em.cnt[tag + "bl"])
    em.on("sync", nc.sync.dma_start, scr_dram[:, :], sblk[:, :],
          _inc=(tag + "bl", 16))
    em.wait("sync", tag + "bl", em.cnt[tag + "bl"])
    for gp in range(8):
        em.on("sync", nc.sync.dma_start,
              smajS[16 * gp:16 * gp + 16, :].rearrange("p (b f) -> p b f", b=8),
              scr_dram[:, :].rearrange("(b e) f -> e b f", e=16),
              _inc=(tag + "bl", 16))
    em.wait("gpsimd", tag + "bl", em.cnt[tag + "bl"])
    base_s = em.cnt[tag + "s"]
    for e in range(16):
        ew = idw[:, (e // 2) * 64 + (e % 2) * 32: (e // 2) * 64 + (e % 2) * 32 + 32]
        if e > 0:
            em.wait("gpsimd", tag + "s", base_s + 4 * e)
        ng = em.on("gpsimd", nc.gpsimd.indirect_copy, scg[:, :], smajS,
                   ew, i_know_ap_gather_is_preferred=True, _inc=(tag + "ex", 1))
        em.wait("tensor", tag + "ex", ng)
        for q in range(4):
            pb = g.p_y if q % 2 == 0 else g.p_bc
            em.wait("tensor", tag + "mk", em.cnt[tag + "mk"] - 1)
            em.on("tensor", nc.tensor.transpose, pb[:, 0:16],
                  scg[0:16, q * 128:(q + 1) * 128], g.identf[0:16, 0:16],
                  _inc=(tag + "s", 1))
            em.wait("vector", tag + "s", em.cnt[tag + "s"])
            em.vop(nc.vector.tensor_copy,
                   wcolsAll[:, e * 4 + q: e * 4 + q + 1], pb[:, e:e + 1],
                   _inc=(tag + "mk", 1))
    em.wait("gpsimd", tag + "mk", em.cnt[tag + "mk"])
    em.wait("vector", tag + "mk", em.cnt[tag + "mk"])
    em.flush("m1_scores")

    wl_t = {0: 0, 1: 0}
    wu_t = {0: 0, 1: 0}

    def load_wgu(src, p, slot):
        em.wait("gpsimd", tag + "gm", em.cnt.get(tag + "gm", 0))
        em.wait("gpsimd", tag + "sq", em.cnt.get(tag + "sq", 0))
        wl_t[slot] = em.on(
            "gpsimd", nc.gpsimd.dma_start, wgu_s[slot][:, :], src[p],
            _inc=(tag + f"wl{slot}", 16))

    sq2 = g.scr2[:, :].bitcast(BF16)[:, 0:512]

    def geglu_512(rhs_of, wg_src):
        em.wait("tensor", tag + "h2", em.cnt.get(tag + "h2", 0))
        for ft in range(16):
            p, half = ft // 2, ft % 2
            slot = p % 2
            pg = g.p_a[0][:, :] if half == 0 else g.p_y[:, :]
            pu = g.p_a[1][:, :] if half == 0 else g.p_bc[:, :]
            sqb = g.sqbf[:, 0:512] if half == 0 else sq2
            if half == 0 and p + 1 < 8:
                load_wgu(wg_src, p + 1, 1 - slot)
            em.wait("tensor", tag + f"wl{slot}", wl_t[slot])
            em.wait("tensor", tag + "hl", em.cnt.get(tag + "hl", 0) - 1)
            for ct in range(CT):
                em.on("tensor", nc.tensor.matmul, pg,
                      wgu_s[slot][:, half * 1024 + ct * 128:
                                  half * 1024 + (ct + 1) * 128], rhs_of(ct),
                      start=(ct == 0), stop=(ct == CT - 1),
                      _inc=(tag + "gm", 1) if ct == CT - 1 else None)
            for ct in range(CT):
                em.on("tensor", nc.tensor.matmul, pu,
                      wgu_s[slot][:, 2048 + half * 1024 + ct * 128:
                                  2048 + half * 1024 + (ct + 1) * 128],
                      rhs_of(ct),
                      start=(ct == 0), stop=(ct == CT - 1),
                      _inc=(tag + "sq", 1) if ct == CT - 1 else None)
            em.wait_now("scalar", tag + "gm")
            em.wait("scalar", tag + "hl", em.cnt.get(tag + "hl", 0) - 1)
            gn = em.sop(nc.scalar.activation, sqb, pg,
                        AF.Gelu, bias=0.0, scale=1.0, _inc=(tag + "gel", 1))
            em.wait("vector", tag + "gel", gn)
            em.wait("vector", tag + "sq", em.cnt[tag + "sq"])
            em.vop(nc.vector.tensor_mul, hbuf(ft), pu, sqb,
                   _inc=(tag + "hl", 1))
        em.wait("tensor", tag + "hl", em.cnt[tag + "hl"])

    def down_512(wd_of, out_rows_fn, wcol=None):
        for tt in range(4):
            for dh in range(2):
                pb = g.p_y if dh == 0 else g.p_bc
                em.wait("tensor", tag + "h2", em.cnt.get(tag + "h2", 0))
                for ft in range(16):
                    em.on("tensor", nc.tensor.matmul, pb[:, :],
                          g.bigA[:, H0 + ft * 512 + tt * 128:
                                 H0 + ft * 512 + (tt + 1) * 128],
                          wd_of(ft)[:, dh * 512:(dh + 1) * 512],
                          start=(ft == 0), stop=(ft == 15),
                          _inc=(tag + "h2u", 1) if ft == 15 else None)
                em.wait_now("vector", tag + "h2u")
                em.wait("vector", tag + "s", em.cnt.get(tag + "s", 0))
                if wcol is not None:
                    em.vop(nc.vector.tensor_scalar_mul,
                           eo[:, dh * 512:(dh + 1) * 512], pb[:, :],
                           wcol(tt), _inc=(tag + "h2", 1))
                else:
                    em.vop(nc.vector.tensor_copy,
                           eo[:, dh * 512:(dh + 1) * 512], pb[:, :],
                           _inc=(tag + "h2", 1))
            out_rows_fn(tt)

    load_wgu(W["wsgu"], 0, 0)
    for ft in range(16):
        em.on("sync", nc.sync.dma_start, wdt(ft), W["wsd"][ft],
              _inc=(tag + "nr", 16))
    em.wait("tensor", tag + "nr", em.cnt[tag + "nr"])
    for tc in range(NCH):
        rhs_of = (lambda _tc: (lambda ct: g.xT[
            :, ct * T + _tc * 512: ct * T + (_tc + 1) * 512]))(tc)
        geglu_512(rhs_of, W["wsgu"])
        if tc + 1 < NCH:
            load_wgu(W["wsgu"], 0, 0)

        def store_rows(tt, _tc=tc):
            em.wait_now("sync", tag + "h2")
            em.on("sync", nc.sync.dma_start,
                  out_hbm[_tc * 512 + tt * 128: _tc * 512 + (tt + 1) * 128, :],
                  eo[:, :], _inc=(tag + "am", 16))
            em.wait("vector", tag + "am", em.cnt[tag + "am"])

        down_512(wdt, store_rows)
    em.flush("m1_shared")

    em.vop(nc.vector.memset, eoTb[:, :], 0.0, _inc=(tag + "mk", 1))
    for e in range(16):
        ew = idw[:, (e // 2) * 64 + (e % 2) * 32: (e // 2) * 64 + (e % 2) * 32 + 32]
        em.wait("gpsimd", tag + "gm", em.cnt.get(tag + "gm", 0))
        em.wait("gpsimd", tag + "sq", em.cnt.get(tag + "sq", 0))
        for ct in range(CT):
            em.on("gpsimd", nc.gpsimd.indirect_copy,
                  xeT[:, ct * 512:(ct + 1) * 512],
                  g.xT[:, ct * T:(ct + 1) * T], ew,
                  i_know_ap_gather_is_preferred=True, _inc=(tag + "ex", 1))
        em.wait("sync", tag + "h2u", em.cnt.get(tag + "h2u", 0))
        for ft in range(16):
            em.on("sync", nc.sync.dma_start, wdt(ft), W["wd"][e, ft],
                  _inc=(tag + "nr", 16))
        em.wait("tensor", tag + "ex", em.cnt[tag + "ex"])
        wg_src = W["wgu"][e]
        if e == 0:
            load_wgu(wg_src, 0, 0)
        em.wait("tensor", tag + "mk", em.cnt[tag + "mk"])
        geglu_512(lambda ct: xeT[:, ct * 512:(ct + 1) * 512], wg_src)
        if e + 1 < 16:
            load_wgu(W["wgu"][e + 1], 0, 0)
        em.wait("tensor", tag + "nr", em.cnt[tag + "nr"])
        em.wait("vector", tag + "am", em.cnt.get(tag + "am", 0))

        def eoT_rows(tt, _e=e):
            em.wait("tensor", tag + "mk", em.cnt[tag + "mk"])
            em.wait("tensor", tag + "h2", em.cnt[tag + "h2"])
            for dt in range(8):
                pb2 = g.p_a[0] if dt < 4 else g.p_a[1]
                col = (dt % 4) * 128
                em.on("tensor", nc.tensor.transpose, pb2[:, col:col + 128],
                      eo[:, dt * 128:(dt + 1) * 128], g.identf[:],
                      _inc=(tag + "s", 1))
            em.wait("vector", tag + "s", em.cnt[tag + "s"])
            for dt in range(8):
                pb2 = g.p_a[0] if dt < 4 else g.p_a[1]
                col = (dt % 4) * 128
                em.vop(nc.vector.tensor_copy,
                       eoTb[:, dt * 520 + tt * 128: dt * 520 + (tt + 1) * 128],
                       pb2[:, col:col + 128],
                       _inc=(tag + "mk", 1) if dt == 7 else None)
            em.wait("vector", tag + "s", em.cnt[tag + "s"])

        down_512(wdt, eoT_rows, wcol=lambda tt: wcolsAll[:, e * 4 + tt:
                                                         e * 4 + tt + 1])
        em.wait("sync", tag + "mk", em.cnt[tag + "mk"])
        em.wait("sync", tag + "am", em.cnt.get(tag + "am", 0))
        em.on("sync", nc.sync.dma_start, eoT_hbm[e], eoTb[:, :],
              _inc=(tag + "am", 16))
    em.flush("m1_experts")

    em.wait("sync", tag + "am", em.cnt[tag + "am"])
    for dt in range(8):
        for e in range(16):
            if dt > 0:
                em.wait("sync", tag + "pv", em.cnt.get(tag + "pv", 0))
            em.on("sync", nc.sync.dma_start, dtile(e),
                  eoT_hbm[e][:, dt * 520:(dt + 1) * 520],
                  _inc=(tag + "nr", 16))
        em.wait("gpsimd", tag + "nr", em.cnt[tag + "nr"])
        for tc in range(8):
            em.wait("gpsimd", tag + "ye", em.cnt.get(tag + "ye", 0))
            for e in range(16):
                ng = em.on("gpsimd", nc.gpsimd.indirect_copy,
                           gathw[:, e * 512:(e + 1) * 512], dtile(e),
                           invw[:, e * 256 + tc * 32: e * 256 + (tc + 1) * 32],
                           i_know_ap_gather_is_preferred=True,
                           _inc=(tag + "pv", 1))
            em.wait("tensor", tag + "pv", ng)
            for e in range(16):
                em.on("tensor", nc.tensor.matmul, g.p_a[0][:, :],
                      g.ident[:], gathw[:, e * 512:(e + 1) * 512],
                      start=(e == 0), stop=(e == 15),
                      _inc=(tag + "ye", 1) if e == 15 else None)
            em.wait("vector", tag + "ye", em.cnt[tag + "ye"])
            em.wait("vector", tag + "s", em.cnt[tag + "s"])
            em.vop(nc.vector.tensor_copy, accf[:, :], g.p_a[0][:, :],
                   _inc=(tag + "h2", 1))
            em.wait("tensor", tag + "h2", em.cnt[tag + "h2"])
            for s_ in range(4):
                pb = g.p_y if s_ % 2 == 0 else g.p_bc
                em.wait("tensor", tag + "mk", em.cnt[tag + "mk"] - 1)
                em.on("tensor", nc.tensor.transpose, pb[:, 0:128],
                      accf[:, s_ * 128:(s_ + 1) * 128], g.identf[:],
                      _inc=(tag + "s", 1))
                em.wait("vector", tag + "wm", em.cnt.get(tag + "wm", 0) - 16)
                em.wait("vector", tag + "s", em.cnt[tag + "s"])
                em.vop(nc.vector.tensor_copy, mts[s_ % 2][:, :],
                       pb[:, 0:128], _inc=(tag + "mk", 1))
                em.wait("sync", tag + "mk", em.cnt[tag + "mk"])
                em.wait("sync", tag + "wm", em.cnt.get(tag + "wm", 0))
                em.on("sync", nc.sync.dma_start,
                      moe_hbm[tc * 512 + s_ * 128: tc * 512 + (s_ + 1) * 128,
                              dt * 128:(dt + 1) * 128],
                      mts[s_ % 2][:, :], _inc=(tag + "wm", 16))
    em.flush("m1_combine")

    for tt in range(NT):
        oslot = tt % 2
        em.wait("sync", tag + "shc", em.cnt.get(tag + "shc", 0))
        if tt == 0:
            em.wait("sync", tag + "wm", em.cnt.get(tag + "wm", 0))
        ld = em.on("sync", nc.sync.dma_start, osb[oslot][:, :],
                   out_hbm[tt * 128:(tt + 1) * 128, :], _inc=(tag + "nre", 16))
        ld = em.on("sync", nc.sync.dma_start, msb[oslot][:, :],
                   moe_hbm[tt * 128:(tt + 1) * 128, :], _inc=(tag + "nre", 16))
        for ct in range(CT):
            em.wait("tensor", tag + "fxe", em.cnt.get(tag + "fxe", 0) - 1)
            em.on("tensor", nc.tensor.transpose, g.tp_slots[ct % 2],
                  g.xT[:, ct * T + tt * 128: ct * T + (tt + 1) * 128],
                  g.ident[:], _inc=(tag + "fx", 1))
            em.wait("vector", tag + "fx", em.cnt[tag + "fx"])
            em.on("vector", nc.vector.tensor_copy, xtk[:, ts(ct)],
                  g.tp_slots[ct % 2], _inc=(tag + "fxe", 1))
        em.wait("vector", tag + "nre", ld)
        pre4 = osb[oslot][:, :]
        em.vop(nc.vector.tensor_add, pre4, osb[oslot][:, :], msb[oslot][:, :])
        em.vop(nc.vector.tensor_add, pre4, pre4, xtk[:, :])
        em.vop(nc.vector.reduce_sum, g.scr[:, 0:1], pre4, axis=AX.X)
        em.vop(nc.vector.tensor_scalar_mul, g.scr[:, 0:1], g.scr[:, 0:1],
               1.0 / D)
        em.vop(nc.vector.tensor_mul, g.scr2[:, :], pre4[:, 0:512],
               pre4[:, 0:512])
        em.vop(nc.vector.reduce_sum, g.scr[:, 1:2], g.scr2[:, :], axis=AX.X)
        em.vop(nc.vector.tensor_mul, g.scr2[:, :], pre4[:, 512:1024],
               pre4[:, 512:1024])
        em.vop(nc.vector.reduce_sum, g.scr[:, 2:3], g.scr2[:, :], axis=AX.X)
        em.vop(nc.vector.tensor_add, g.scr[:, 1:2], g.scr[:, 1:2],
               g.scr[:, 2:3])
        em.vop(nc.vector.tensor_scalar_mul, g.scr[:, 1:2], g.scr[:, 1:2],
               1.0 / D)
        em.vop(nc.vector.tensor_mul, g.scr[:, 2:3], g.scr[:, 0:1],
               g.scr[:, 0:1])
        em.vop(nc.vector.tensor_sub, g.scr[:, 1:2], g.scr[:, 1:2],
               g.scr[:, 2:3], _inc=(tag + "lv", 1))
        em.wait_now("scalar", tag + "lv")
        em.sop(nc.scalar.activation, g.scr[:, 1:2], g.scr[:, 1:2], AF.Ln,
               bias=g.cvcol[:, 1:2], scale=1.0)
        em.sop(nc.scalar.activation, g.scr[:, 1:2], g.scr[:, 1:2], AF.Exp,
               bias=0.0, scale=-0.5, _inc=(tag + "lnr", 1))
        em.wait_now("vector", tag + "lnr")
        em.wait("vector", tag + "wm", em.cnt.get(tag + "wm", 0))
        em.vop(nc.vector.tensor_scalar, ysb[:, :], pre4,
               g.scr[:, 0:1], g.scr[:, 1:2],
               op0=ALU.subtract, op1=ALU.mult, _inc=(tag + "shc", 1))
        em.wait("sync", tag + "shc", em.cnt[tag + "shc"])
        em.on("sync", nc.sync.dma_start, y_h[ts(tt), :], ysb[:, :],
              _inc=(tag + "wm", 16))
    em.wait_now("sync", tag + "wm")
    em.flush("m1_final")
    es.close()



def build_program(nphases=1):
    nc = bass.Bass("TRN2", target_bir_lowering=False)
    x0_h = nc.dram_tensor("x0", [T, D], F32, kind="ExternalInput")
    y_h = nc.dram_tensor("y", [T, D], F32, kind="ExternalOutput")
    Wh = {}
    for name, shape, dt_ in [
            ("a0_wq", [16, CT, 128, 64], BF16), ("a0_wk", [16, CT, 128, 64], BF16),
            ("a0_wv", [16, CT, 128, 64], BF16), ("a0_wo", [CT, 128, CT * 128], BF16),
            ("a0_qb", [128, 8], F32), ("a0_kb", [128, 8], F32),
            ("a0_vb", [1, D], F32), ("a0_ob", [128, 8], F32)]:
        Wh[name] = nc.dram_tensor(name, shape, dt_, kind="ExternalInput")

    for name, shape, dt_ in [
            ("f0_wqkv", [2, 8, CT, 128, 192], BF16),
            ("f0_qs", [2, 64, 1], F32), ("f0_ks", [2, 64, 1], F32),
            ("f0_wo", [2, 4, 128, D], BF16),
            ("f0_lg", [2, 128, 8], F32), ("f0_lb", [2, 128, 8], F32),
            ("f0_w1a", [2, 22, CT, 128, 128], BF16),
            ("f0_w1g", [2, 22, CT, 128, 128], BF16),
            ("f0_b1a", [2, 128, 22], F32), ("f0_b1g", [2, 128, 22], F32),
            ("f0_w2", [2, 22, 128, D], BF16), ("f0_b2", [2, 128, 8], F32)]:
        Wh[name] = nc.dram_tensor(name, shape, dt_, kind="ExternalInput")
    z1_hbm = nc.dram_tensor("z1_hbm", [128, CT * T], BF16, kind="Internal")

    consts = make_consts(nc)
    em = Em(nc)
    g = setup_common(nc, em, consts)
    stage_xT(g, x0_h)
    run_finalize = nphases < 4
    if nphases >= 1:
        build_l0_attn(g, dict(wq=Wh["a0_wq"], wk=Wh["a0_wk"], wv=Wh["a0_wv"],
                              wo=Wh["a0_wo"], qb=Wh["a0_qb"], kb=Wh["a0_kb"],
                              vb=Wh["a0_vb"], ob=Wh["a0_ob"]), consts["qpos"])
    if nphases >= 2:
        build_l0_ffn(g, dict(wqkv=Wh["f0_wqkv"], qs=Wh["f0_qs"], ks=Wh["f0_ks"],
                             wo=Wh["f0_wo"], lg=Wh["f0_lg"], lb=Wh["f0_lb"],
                             w1a=Wh["f0_w1a"], w1g=Wh["f0_w1g"],
                             b1a=Wh["f0_b1a"], b1g=Wh["f0_b1g"],
                             w2=Wh["f0_w2"], b2=Wh["f0_b2"]),
                     z1_hbm, consts)
    if nphases >= 3:
        for name, shape in [("t1_maa5", [128, 40]), ("t1_td", [128, 8]),
                            ("t1_mw1", [8, 128, 128]), ("t1_mw2", [32, 4096]),
                            ("t1_dw1", [8, 128, 64]), ("t1_dw2", [64, 1024]),
                            ("t1_u", [64, 16]), ("t1_wr", [8, 8, 128, 128]),
                            ("t1_wk", [8, 8, 128, 128]), ("t1_wv", [1024, 1024]),
                            ("t1_wo", [8, 8, 128, 128]),
                            ("t1_lnw", [128, 8]), ("t1_lnb", [128, 8])]:
            Wh[name] = nc.dram_tensor(name, shape, F32, kind="ExternalInput")
        build_l1_attn(g, dict(maa5=Wh["t1_maa5"], td=Wh["t1_td"],
                              mw1=Wh["t1_mw1"], mw2=Wh["t1_mw2"],
                              dw1=Wh["t1_dw1"], dw2=Wh["t1_dw2"],
                              u=Wh["t1_u"], wr=Wh["t1_wr"], wk=Wh["t1_wk"],
                              wv=Wh["t1_wv"], wo=Wh["t1_wo"],
                              lnw=Wh["t1_lnw"], lnb=Wh["t1_lnb"]), z1_hbm)
    if nphases >= 4:
        for name, shape, dt_ in [
                ("m1_gate", [128, 128], F32),
                ("m1_wgu", [16, 8, 128, 4096], BF16),
                ("m1_wd", [16, 16, 128, 1024], BF16),
                ("m1_wsgu", [8, 128, 4096], BF16),
                ("m1_wsd", [16, 128, 1024], BF16)]:
            Wh[name] = nc.dram_tensor(name, shape, dt_, kind="ExternalInput")
        eoT_hbm = nc.dram_tensor("eoT_hbm", [16, 128, 8 * 520], BF16,
                                 kind="Internal")
        moe_hbm = nc.dram_tensor("moe_hbm", [T, D], F32, kind="Internal")
        out_hbm = nc.dram_tensor("m1_out", [T, D], F32, kind="Internal")
        build_l1_ffn(g, dict(gate=Wh["m1_gate"], wgu=Wh["m1_wgu"],
                             wd=Wh["m1_wd"], wsgu=Wh["m1_wsgu"],
                             wsd=Wh["m1_wsd"]),
                     dict(E8=consts["E8"], E8T=consts["E8T"], LB=consts["LB"],
                          iotaB=consts["iotaB"], Lrep=consts["Lrep"],
                          offc=consts["offc"], jp1=consts["jp1"],
                          maskc=consts["maskc"]),
                     eoT_hbm, moe_hbm, out_hbm, y_h,
                     nc.dram_tensor("m1_scr", [128, 512], F32, kind="Internal"))
    if run_finalize:
        finalize_out(g, y_h, deps=("a0fin",))
    return nc


_NC_CACHE = {}


def get_program(nphases):
    if nphases not in _NC_CACHE:
        _NC_CACHE[nphases] = build_program(nphases)
    return _NC_CACHE[nphases]


def kernel(**inputs):
    w = prep_weights(inputs)
    x = np.ascontiguousarray(np.asarray(inputs["L0_attn_x"]), np.float32)
    nc = get_program(4)
    in_maps = []
    for b in range(8):
        m = {"x0": np.ascontiguousarray(x[b])}
        m.update(w)
        in_maps.append(m)
    res = run_bass_kernel_spmd(nc, in_maps, core_ids=list(range(8)))
    out = np.stack([np.asarray(r["y"]) for r in res.results], axis=0)
    return out.astype(np.float32)
